# version 07
# baseline (speedup 1.0000x reference)
import math
import numpy as np
import bass_rust
import concourse.bass as bass
import concourse.mybir as mybir
from concourse.bass_utils import run_bass_kernel_spmd

F32 = mybir.dt.float32
BF16 = mybir.dt.bfloat16
AF = mybir.ActivationFunctionType
OP = mybir.AluOpType

N_CORES = 8
B_PER = 4
S = 2048
D = 1024
H = 16
DK = 64
NT = S // 128
NCH = D // 128
NREV = 511
C_SHIFT = 12.0
EPS = 1e-5

NUM_BUCKETS = 32
MAX_DISTANCE = 128

DBG_ITEMS = B_PER
DBG_STAGE = "full"
DBG_PREFIX = None
DBG_DUMP = False
DBG_DUMP_ITEM = 0
DBG_NO_ROT = False
DBG_OLD_RECIP = False
DBG_ACT_FLUSH = False


def _rel_bucket_host(d):
    import jax
    import jax.numpy as jnp
    with jax.default_device(jax.devices("cpu")[0]):
        num_buckets = NUM_BUCKETS // 2
        rel = jnp.asarray(d, dtype=jnp.int32)
        relative_buckets = (rel > 0).astype(jnp.int32) * num_buckets
        rp = jnp.abs(rel)
        max_exact = num_buckets // 2
        is_small = rp < max_exact
        rp_large = max_exact + (
            jnp.log(rp.astype(jnp.float32) / max_exact)
            / math.log(MAX_DISTANCE / max_exact)
            * (num_buckets - max_exact)
        ).astype(jnp.int32)
        rp_large = jnp.minimum(rp_large, num_buckets - 1)
        relative_buckets = relative_buckets + jnp.where(is_small, rp, rp_large)
        return np.asarray(relative_buckets)


def _host_consts():
    d = np.arange(NREV, dtype=np.int64) - 255
    bk = _rel_bucket_host(d)
    onehot = np.zeros((NUM_BUCKETS, NREV), dtype=np.float32)
    onehot[bk, np.arange(NREV)] = 1.0
    eye16 = np.eye(16, dtype=np.float32)
    return onehot, eye16


def build_program():
    nc = bass.Bass("TRN2", target_bir_lowering=False)

    x8 = nc.dram_tensor("x", [B_PER * S, D], F32, kind="ExternalInput").ap()
    wq_d = nc.dram_tensor("wq", [D, D], F32, kind="ExternalInput").ap()
    wk_d = nc.dram_tensor("wk", [D, D], F32, kind="ExternalInput").ap()
    wv_d = nc.dram_tensor("wv", [D, D], F32, kind="ExternalInput").ap()
    wo_d = nc.dram_tensor("wo", [D, D], F32, kind="ExternalInput").ap()
    wf_d = nc.dram_tensor("wf", [D, D], F32, kind="ExternalInput").ap()
    wa1_d = nc.dram_tensor("wa1", [D, D], F32, kind="ExternalInput").ap()
    wa2_d = nc.dram_tensor("wa2", [D, D], F32, kind="ExternalInput").ap()
    fb_d = nc.dram_tensor("fbias", [1, D], F32, kind="ExternalInput").ap()
    rel_d = nc.dram_tensor("rel", [NUM_BUCKETS, H], F32, kind="ExternalInput").ap()
    oh_d = nc.dram_tensor("onehot", [NUM_BUCKETS, NREV], F32, kind="ExternalInput").ap()
    eye_d = nc.dram_tensor("eye16", [16, 16], F32, kind="ExternalInput").ap()
    y8 = nc.dram_tensor("y", [B_PER * S, D], F32, kind="ExternalOutput").ap()

    dbg = {}
    if DBG_DUMP:
        for nm, shp, dt in [("d_ln1t", [128, NCH * S], BF16),
                            ("d_qt", [128, 2 * S], BF16),
                            ("d_kt", [128, 2 * S], BF16),
                            ("d_vh", [128, 8320], BF16),
                            ("d_pt", [128, 4096], BF16),
                            ("d_ctxt", [128, NCH * S], BF16),
                            ("d_rev", [16, NREV], F32),
                            ("d_ca", [128, 32], F32),
                            ("d_pp", [128, 768], BF16)]:
            dbg[nm] = nc.dram_tensor(nm, shp, dt, kind="ExternalOutput").ap()

    qdup_d = nc.dram_tensor("qdup", [NCH, 128, H, 128], BF16, kind="Internal").ap()
    kdup_d = nc.dram_tensor("kdup", [NCH, 128, H, 128], BF16, kind="Internal").ap()
    x2_d = nc.dram_tensor("x2scr", [S, D], F32, kind="Internal").ap()

    A = nc.alloc_sbuf_tensor
    ln1T = A("ln1T", [128, NCH * S], BF16)
    ctxT = A("ctxT", [128, NCH * S], BF16)
    wv_sb = A("wv_sb", [128, NCH * D], BF16)
    wo_sb = A("wo_sb", [128, NCH * D], BF16)
    m_sb = A("m_sb", [128, NCH * D], BF16)
    qt_sb = A("qt_sb", [128, 2 * S], BF16)
    kt_sb = A("kt_sb", [128, 2 * S], BF16)
    qs_sb = A("qs_sb", [128, 2 * NCH * 128], BF16)
    ks_sb = A("ks_sb", [128, 2 * NCH * 128], BF16)
    vh_sb = A("vh_sb", [128, 2 * 4 * (NT * 65)], BF16)
    pt_sb = A("pt_sb", [128, 4 * 1024], BF16)
    pp_sb = A("pp_sb", [128, 2 * 384], BF16)
    xa_sb = A("xa_sb", [128, 2 * 1024], F32)
    xb_sb = A("xb_sb", [128, 2 * 1024], F32)
    ctxu_sb = A("ctxu_sb", [128, 1024], F32)
    aux_sb = A("aux_sb", [128, 1024], F32)
    ln_sb = A("ln_sb", [128, 2 * 1024], BF16)
    lt_sb = A("lt_sb", [128, 2 * 1024], BF16)
    br_sb = A("br_sb", [128, D], F32)
    ca_sb = A("ca_sb", [128, 32], F32)
    st_sb = A("st_sb", [128, 160], F32)
    rev_sb = A("rev_sb", [16, NREV], F32)
    oh_sb = A("oh_sb", [NUM_BUCKETS, NREV], F32)
    rel_sb = A("rel_sb", [NUM_BUCKETS, H], F32)
    eye_sb = A("eye_sb", [16, 16], F32)
    or_sb = A("or_sb", [1, 128], F32)
    oc_sb = A("oc_sb", [128, 64], F32)
    eps_sb = A("eps_sb", [128, 1], F32)

    P = nc.alloc_psum_tensor
    s2_ps = P("s2_ps", [128, 2048], F32)
    ctx_ps = P("ctx_ps", [128, 1024], F32)
    qkv_ps = P("qkv_ps", [128, 512], F32)
    msc_ps = P("msc_ps", [128, 512], F32)

    sem_names = ["ssm", "sxa0", "sxa1", "sxb0", "sxb1",
                 "tpa0", "tpa1", "tpd0", "tpd1", "tpw",
                 "gq0", "gq1", "gw0", "gw1", "gpp0", "gpp1",
                 "gdiv", "gc0", "gc1", "gx0", "gx1", "gy0", "gy1", "gdbg",
                 "pe", "act", "dve"]
    sems = {n: nc.alloc_semaphore(n) for n in sem_names}
    cnt = {n: 0 for n in sem_names}
    OPS = []

    def emit(engine, waits, incs, body):
        wl = []
        for s, v in waits:
            assert v <= cnt[s], f"wait on future: {s} {v} > {cnt[s]}"
            if v > 0:
                wl.append((sems[s], v))

        def fn(eng, wl=wl, incs=incs, body=body):
            for sh, v in wl:
                eng.wait_ge(sh, v)
            inst = body(eng)
            for s, k in incs:
                inst = inst.then_inc(sems[s], k)
        OPS.append((engine, fn))
        for s, k in incs:
            cnt[s] += k

    def mk(s):
        return (s, cnt[s])

    def ap3(t, off, dims):
        return bass_rust.AP(t[:].tensor, off, dims)

    from concourse.hw_specs import get_activation_tables
    tabs = list(get_activation_tables(nc.m.arch).items())
    id_nle = next(i for i, (n, f) in enumerate(tabs) if AF.Exp in f and AF.Ln in f)
    id_exp = next(i for i, (n, f) in enumerate(tabs) if AF.Exp in f and AF.Tanh in f)

    def load_table(idx):
        emit("scalar", [], [], lambda e, idx=idx: e.add_instruction(
            mybir.InstLoadActFuncSet(name=nc.get_next_instruction_name(),
                                     act_func_set_id=idx, ins=[], outs=[])))

    emit("vector", [], [], lambda e: e.memset(or_sb[0:1, :], 1.0))
    emit("vector", [], [], lambda e: e.memset(oc_sb[:, :], 1.0))
    for g2 in range(2):
        for hl in range(4):
            off = g2 * 4160 + hl * 1040 + 64
            apm = ap3(vh_sb, off, [[8320, 128], [65, NT]])
            emit("vector", [], [], lambda e, apm=apm: e.memset(apm, 1.0))
    emit("vector", [], [], lambda e: e.memset(eps_sb[:, :], EPS))
    emit("vector", [], [("dve", 1)], lambda e: e.memset(st_sb[:, :], 0.0))
    ones_done = mk("dve")

    load_table(id_nle)

    emit("sync", [], [("ssm", 16)], lambda e: e.dma_start(out=oh_sb[:, :], in_=oh_d))
    emit("sync", [], [("ssm", 16)], lambda e: e.dma_start(out=rel_sb[:, :], in_=rel_d))
    emit("sync", [], [("ssm", 16)], lambda e: e.dma_start(out=eye_sb[:, :], in_=eye_d))
    emit("sync", [], [("ssm", 16)], lambda e: e.dma_start(out=ctxu_sb[0:1, 0:D], in_=fb_d))
    small_ld = mk("ssm")

    emit("tensor", [small_ld], [("pe", 1)],
         lambda e: e.matmul(msc_ps[0:16, 0:NREV], rel_sb[0:32, 0:16],
                            oh_sb[0:32, 0:NREV], start=True, stop=True))
    emit("scalar", [mk("pe")], [("act", 1)],
         lambda e: e.activation(rev_sb[0:16, :], msc_ps[0:16, 0:NREV], AF.Copy,
                                bias=-C_SHIFT, scale=1.0))
    rev_done = mk("act")

    pp_ready = [None, None]
    pp_freed = [("dve", 0), ("dve", 0)]

    def emit_patp(h, waits):
        pbuf = h % 2
        sem = f"gpp{pbuf}"
        src = ap3(rev_sb, h * NREV, [[NREV, 1], [1, 128], [1, 384]])
        dst = pp_sb[:, pbuf * 384:(pbuf + 1) * 384]
        emit("gpsimd", waits, [(sem, 16)],
             lambda e, dst=dst, src=src: e.dma_start(out=dst, in_=src))
        pp_ready[pbuf] = mk(sem)

    emit_patp(0, [rev_done])
    emit_patp(1, [rev_done])
    patp_prologue = [mk("gpp0"), mk("gpp1")]

    emit("tensor", [rev_done], [],
         lambda e: e.matmul(msc_ps[0:1, 0:16], rev_sb[0:16, 0:1],
                            eye_sb[0:16, 0:16], start=True, stop=True))
    emit("tensor", [], [("pe", 1)],
         lambda e: e.matmul(msc_ps[0:1, 16:32], rev_sb[0:16, 510:511],
                            eye_sb[0:16, 0:16], start=True, stop=True))
    emit("vector", [mk("pe"), ones_done], [("dve", 1)],
         lambda e: e.tensor_copy(st_sb[0:1, 0:32], msc_ps[0:1, 0:32]))
    emit("tensor", [mk("dve")], [("pe", 1)],
         lambda e: e.matmul(msc_ps[0:128, 0:32], or_sb[0:1, 0:128],
                            st_sb[0:1, 0:32], start=True, stop=True))
    emit("vector", [mk("pe")], [("dve", 1)],
         lambda e: e.tensor_copy(ca_sb[:, :], msc_ps[:, 0:32]))

    for half in range(2):
        emit("tensor", [small_ld, ones_done, mk("dve")], [("pe", 1)],
             lambda e, half=half: e.matmul(
                 qkv_ps[:, 0:512], or_sb[0:1, 0:128],
                 ctxu_sb[0:1, half * 512:(half + 1) * 512], start=True, stop=True))
        emit("vector", [mk("pe")], [("dve", 1)],
             lambda e, half=half: e.tensor_copy(
                 br_sb[:, half * 512:(half + 1) * 512], qkv_ps[:, 0:512]))
    pro1_done = mk("dve")

    wbf, abf = ctxT, ln1T
    xa_free = [pro1_done, pro1_done]

    def load_cast(src_d, dst_t, dst_off):
        for ch in range(NCH):
            buf = ch % 2
            emit("sync", [xa_free[buf]], [(f"sxa{buf}", 16)],
                 lambda e, ch=ch, buf=buf, src_d=src_d: e.dma_start(
                     out=xa_sb[:, buf * 1024:(buf + 1) * 1024],
                     in_=src_d[ch * 128:(ch + 1) * 128, :]))
            emit("vector", [mk(f"sxa{buf}")], [("dve", 1)],
                 lambda e, ch=ch, buf=buf, dst_t=dst_t, dst_off=dst_off:
                 e.tensor_copy(dst_t[:, dst_off + ch * 1024: dst_off + (ch + 1) * 1024],
                               xa_sb[:, buf * 1024:(buf + 1) * 1024]))
            xa_free[buf] = mk("dve")
        return mk("dve")

    w_cast = load_cast(wf_d, wbf, 0)
    for i in range(NCH):
        src = ap3(wbf, i * 1024, [[NCH * S, 128], [128, 8], [1, 128]])
        dst = ap3(wbf, 8192 + i * 128, [[NCH * S, 128], [1024, 8], [1, 128]])
        w = [("dve", w_cast[1])]
        if i == 0:
            w.extend(patp_prologue)
        emit("sync", w, [("tpw", 16)],
             lambda e, dst=dst, src=src: e.dma_start(out=dst, in_=src, transpose=True))
    wt_done = mk("tpw")
    a1_done = load_cast(wa1_d, abf, 0)

    def mm_chain(lhs_fn, rhs_fn, ps, dst_t, dst_fn, extra_first_waits, gate0):
        gate = gate0
        for ot in range(NCH):
            for ob in range(2):
                for ch in range(NCH):
                    w = []
                    if ch == 0:
                        w = list(extra_first_waits) + [gate]
                    emit("tensor", w, [("pe", 1)] if ch == NCH - 1 else [],
                         lambda e, ot=ot, ob=ob, ch=ch, ps=ps: e.matmul(
                             ps[:, 0:512], lhs_fn(ch, ot), rhs_fn(ch, ob),
                             start=(ch == 0), stop=(ch == NCH - 1)))
                emit("vector", [mk("pe")], [("dve", 1)],
                     lambda e, ot=ot, ob=ob, dst_t=dst_t: e.tensor_copy(
                         dst_fn(dst_t, ot, ob), qkv_ps[:, 0:512] if ps is qkv_ps
                         else msc_ps[:, 0:512]))
                gate = mk("dve")
        return mk("dve")

    m1t_done = mm_chain(
        lambda ch, jt: abf[:, ch * 1024 + jt * 128: ch * 1024 + jt * 128 + 128],
        lambda ch, cb: wbf[:, 8192 + ch * 1024 + cb * 512: 8192 + ch * 1024 + cb * 512 + 512],
        msc_ps, abf,
        lambda t, jt, cb: t[:, 8192 + jt * 1024 + cb * 512: 8192 + jt * 1024 + cb * 512 + 512],
        [("dve", a1_done[1]), wt_done], ("dve", a1_done[1]))
    a2_done = load_cast(wa2_d, abf, 0)
    m_done = mm_chain(
        lambda ch, ct: abf[:, 8192 + ch * 1024 + ct * 128: 8192 + ch * 1024 + ct * 128 + 128],
        lambda ch, jb: abf[:, ch * 1024 + jb * 512: ch * 1024 + jb * 512 + 512],
        qkv_ps, m_sb,
        lambda t, ct, jb: t[:, ct * 1024 + jb * 512: ct * 1024 + jb * 512 + 512],
        [("dve", a2_done[1])], ("dve", a2_done[1]))


    if DBG_DUMP:
        emit("gpsimd", [mk("dve"), mk("act")], [("gdbg", 16)],
             lambda e: e.dma_start(out=dbg["d_rev"], in_=rev_sb[:, :]))
        emit("gpsimd", [], [("gdbg", 16)],
             lambda e: e.dma_start(out=dbg["d_ca"], in_=ca_sb[:, :]))
        emit("gpsimd", [], [("gdbg", 16)],
             lambda e: e.dma_start(out=dbg["d_pp"], in_=pp_sb[:, :]))

    s_free = {"A": ("act", 0), "B": ("act", 0)}
    pt_reader = [("pe", 0)] * 4
    kt_slot = [0]
    qs_ready = [None, None]
    qs_freed = [("pe", 0), ("pe", 0)]
    qt_freed = [("pe", 0), ("pe", 0)]
    qkv_done = {}
    vh_ready = [("dve", 0), ("dve", 0)]
    vh_free = [("pe", 0), ("pe", 0)]
    qkv_gate = [m_done]
    lnT_tp = [("tpa0", 0)] * NT
    ctx_gate = [("dve", 0)]
    ctx_done = [("pe", 0)]
    div_tt_gate = [("dve", 0)]
    misc_gate = [m1t_done]
    acc_i = [0]
    div_r2_free = [("gdiv", 0)]
    div_pending = []
    bgq = []


    def bgq_pop(n):
        for _ in range(min(n, len(bgq))):
            bgq.pop(0)()

    def bgq_flush():
        while bgq:
            bgq.pop(0)()

    stage_free = {0: m_done, 1: m_done}
    wv_done_c = [None]
    wo_done_c = [None]
    wdup_done_c = [None]

    def q_cast_chunk(src_d, dst_t, dst_off, ch, done_cell):
        def go():
            buf = ch % 2
            emit("sync", [xa_free[buf]], [(f"sxa{buf}", 16)],
                 lambda e: e.dma_start(
                     out=xa_sb[:, buf * 1024:(buf + 1) * 1024],
                     in_=src_d[ch * 128:(ch + 1) * 128, :]))
            emit("vector", [mk(f"sxa{buf}")], [("dve", 1)],
                 lambda e: e.tensor_copy(
                     dst_t[:, dst_off + ch * 1024: dst_off + (ch + 1) * 1024],
                     xa_sb[:, buf * 1024:(buf + 1) * 1024]))
            xa_free[buf] = mk("dve")
            if done_cell is not None and ch == NCH - 1:
                done_cell[0] = mk("dve")
        return go

    def q_dup_chunk(src_d, dst_d, ch, done_cell):
        def go():
            buf = ch % 2
            st_t = lt_sb if buf == 0 else pt_sb
            emit("sync", [xa_free[buf]], [(f"sxa{buf}", 16)],
                 lambda e: e.dma_start(
                     out=xa_sb[:, buf * 1024: buf * 1024 + 1024],
                     in_=src_d[ch * 128:(ch + 1) * 128, :]))
            sy_m = mk(f"sxa{buf}")
            pitch = NCH * 512 if st_t is pt_sb else 2048
            for rep in range(2):
                src_ap = ap3(xa_sb, buf * 1024, [[2048, 128], [64, 16], [1, 64]])
                dst_ap = ap3(st_t, rep * 64, [[pitch, 128], [128, 16], [1, 64]])
                emit("vector", [sy_m, stage_free[buf]], [("dve", 1)],
                     lambda e, dst_ap=dst_ap, src_ap=src_ap:
                     e.tensor_copy(dst_ap, src_ap))
            xa_free[buf] = mk("dve")
            emit("gpsimd", [mk("dve")], [(f"gw{buf}", 16)],
                 lambda e: e.dma_start(
                     out=dst_d[ch],
                     in_=st_t[:, 0:2048].rearrange("p (h c) -> p h c", h=16)))
            stage_free[buf] = mk(f"gw{buf}")
            if done_cell is not None and ch == NCH - 1:
                done_cell[0] = None
        return go

    for _ch in range(NCH):
        bgq.append(q_dup_chunk(wq_d, qdup_d, _ch, None))
    for _ch in range(NCH):
        bgq.append(q_dup_chunk(wk_d, kdup_d, _ch, wdup_done_c))
    for _ch in range(NCH):
        bgq.append(q_cast_chunk(wv_d, wv_sb, 0, _ch, wv_done_c))
    for _ch in range(NCH):
        bgq.append(q_cast_chunk(wo_d, wo_sb, 0, _ch, wo_done_c))

    def emit_qk_stream(h):
        buf = h % 2
        for (src_d, dst_t) in ((qdup_d, qs_sb), (kdup_d, ks_sb)):
            src = bass_rust.AP(src_d.tensor, h * 128,
                               [[H * 128, 128], [128 * H * 128, NCH], [1, 128]])
            dst = ap3(dst_t, buf * 1024, [[2048, 128], [128, 8], [1, 128]])
            emit("gpsimd", [qs_freed[buf]] + wdup_done_c[0], [(f"gq{buf}", 16)],
                 lambda e, dst=dst, src=src: e.dma_start(out=dst, in_=src))
        qs_ready[buf] = mk(f"gq{buf}")

    def acc_pick():
        i = (acc_i[0] % 2) if not DBG_NO_ROT else 0
        acc_i[0] += 1
        if i == 0:
            return qkv_ps, (lambda: qkv_gate[0]), qkv_gate
        return msc_ps, (lambda: misc_gate[0]), misc_gate

    def emit_qkv_group(h, proj, qb):
        buf = h % 2
        w_src = qs_sb if proj == 0 else ks_sb
        t_dst = qt_sb if proj == 0 else kt_sb
        ps, gate_get, gate_cell = acc_pick()
        for ch in range(NCH):
            w = []
            if ch == 0:
                w = [qs_ready[buf], lnT_tp[qb * 4 + 2],
                     lnT_tp[qb * 4 + 3], gate_get()]
            emit("tensor", w, [("pe", 1)] if ch == NCH - 1 else [],
                 lambda e, ch=ch, qb=qb, buf=buf, w_src=w_src, ps=ps:
                 e.matmul(ps[:, 0:512],
                          w_src[:, buf * 1024 + ch * 128: buf * 1024 + ch * 128 + 128],
                          ln1T[:, ch * S + qb * 512: ch * S + qb * 512 + 512],
                          start=(ch == 0), stop=(ch == NCH - 1)))
        emit("vector", [mk("pe")], [("dve", 1)],
             lambda e, qb=qb, buf=buf, t_dst=t_dst, ps=ps:
             e.tensor_copy(t_dst[:, buf * S + qb * 512: buf * S + qb * 512 + 512],
                           ps[:, 0:512]))
        gate_cell[0] = mk("dve")
        if proj == 1 and qb == 3:
            qkv_done[h] = mk("dve")

    def emit_v_tile(g, tt):
        gb = g % 2
        ps, gate_get, gate_cell = acc_pick()
        for ch in range(NCH):
            w = []
            if ch == 0:
                w = [lnT_tp[tt], gate_get(), wv_done_c[0]]
            emit("tensor", w, [("pe", 1)] if ch == NCH - 1 else [],
                 lambda e, ch=ch, tt=tt, g=g, ps=ps:
                 e.matmul(ps[:, 0:256],
                          ln1T[:, ch * S + tt * 128: ch * S + tt * 128 + 128],
                          wv_sb[:, ch * D + g * 256: ch * D + g * 256 + 256],
                          start=(ch == 0), stop=(ch == NCH - 1)))
        src = ap3(ps, 0, [[512, 128], [64, 4], [1, 64]])
        dst = ap3(vh_sb, gb * 4160 + tt * 65, [[8320, 128], [1040, 4], [1, 64]])
        emit("vector", [mk("pe"), vh_free[gb]], [("dve", 1)],
             lambda e, dst=dst, src=src: e.tensor_copy(dst, src))
        gate_cell[0] = mk("dve")
        if tt == NT - 1:
            vh_ready[gb] = mk("dve")

    def queue_head_work(h):
        bgq.append(lambda h=h: emit_qk_stream(h))
        for proj in range(2):
            for qb in range(4):
                bgq.append(lambda h=h, p=proj, q=qb: emit_qkv_group(h, p, q))

    def a_begin(item):
        a_tp_fence[0] = [mk("gpp0"), mk("gpp1"), mk("gc0"), mk("gc1"),
                         mk("gdiv")]
        st8 = dict(item=item, base=item * S,
                   rstd=[("act", 0)] * NT, norm=[("dve", 0)] * NT,
                   tp=[("tpa0", 0)] * NT,
                   ln_fence=[mk("tpa0"), mk("tpa1"), mk("tpd0"), mk("tpd1"),
                             ln_stage_free[0], ln_stage_free[1]],
                   pe_fence=mk("pe"))
        return st8

    def a_tile(st8, tt):
        base = st8["base"]
        buf = tt % 2
        sc = 18 * buf
        w = [xa_free[buf]]
        if tt >= 2:
            w.append(st8["norm"][tt - 2])
        emit("sync", w, [(f"sxa{buf}", 16)],
             lambda e, tt=tt, buf=buf, base=base: e.dma_start(
                 out=xa_sb[:, buf * 1024: buf * 1024 + 1024],
                 in_=x8[base + tt * 128: base + tt * 128 + 128, :]))
        sy_m = mk(f"sxa{buf}")
        wv_ = [sy_m]
        if tt >= 2:
            wv_.append(st8["rstd"][tt - 2])
        emit("vector", wv_, [],
             lambda e, buf=buf, sc=sc: e.bn_stats(
                 st_sb[:, sc:sc + 6], xa_sb[:, buf * 1024: buf * 1024 + 512]))
        emit("vector", [], [("dve", 1)],
             lambda e, buf=buf, sc=sc: e.bn_stats(
                 st_sb[:, sc + 6:sc + 12],
                 xa_sb[:, buf * 1024 + 512: buf * 1024 + 1024]))
        emit("vector", [mk("dve")], [("dve", 1)],
             lambda e, sc=sc: e.bn_aggr(st_sb[:, sc + 12:sc + 14],
                                        st_sb[:, sc:sc + 12]))
        emit("scalar", [mk("dve")], [("act", 1)],
             lambda e, sc=sc: e.activation(st_sb[:, sc + 14:sc + 15],
                                           st_sb[:, sc + 13:sc + 14],
                                           AF.Ln, bias=eps_sb[:, 0:1], scale=1.0))
        emit("scalar", [mk("act")], [("act", 1)],
             lambda e, sc=sc: e.activation(st_sb[:, sc + 15:sc + 16],
                                           st_sb[:, sc + 14:sc + 15],
                                           AF.Exp, scale=-0.5))
        st8["rstd"][tt] = mk("act")
        emit("vector", [st8["rstd"][tt], mk("dve")], [("dve", 1)],
             lambda e, sc=sc: e.tensor_scalar(
                 st_sb[:, sc + 16:sc + 17], st_sb[:, sc + 12:sc + 13],
                 st_sb[:, sc + 15:sc + 16], -1.0, op0=OP.mult, op1=OP.mult))
        wv2 = [st8["tp"][tt - 2], mk("dve")] if tt >= 2 else \
            list(st8["ln_fence"]) + [mk("dve")]
        emit("vector", wv2, [("dve", 1)],
             lambda e, buf=buf, sc=sc: e.tensor_scalar(
                 ln_sb[:, buf * 1024: buf * 1024 + 1024],
                 xa_sb[:, buf * 1024: buf * 1024 + 1024],
                 st_sb[:, sc + 15:sc + 16], st_sb[:, sc + 16:sc + 17],
                 op0=OP.mult, op1=OP.add))
        st8["norm"][tt] = mk("dve")
        xa_free[buf] = st8["norm"][tt]
        src_ = ap3(ln_sb, buf * 1024, [[2048, 128], [128, 8], [1, 128]])
        dst_ = ap3(ln1T, tt * 128, [[NCH * S, 128], [S, 8], [1, 128]])
        wt = [st8["norm"][tt]] + a_tp_fence[0]
        if tt == 0:
            wt.append(st8["pe_fence"])
        emit("sync", wt, [(f"tpa{buf}", 16)],
             lambda e, dst_=dst_, src_=src_: e.dma_start(out=dst_, in_=src_,
                                                         transpose=True))
        st8["tp"][tt] = mk(f"tpa{buf}")
        lnT_tp[tt] = st8["tp"][tt]

    def a_end(st8):
        ln_stage_free[0] = st8["tp"][NT - 1]
        ln_stage_free[1] = st8["tp"][NT - 1]

    def phase_A(item):
        st8 = a_begin(item)
        for tt in range(NT):
            a_tile(st8, tt)
            bgq_pop(4)
        a_end(st8)
        return st8

    a_tp_fence = [[]]
    ln_stage_free = [("gc0", 0), ("gc1", 0)]

    def band_range(kt, qh):
        lo = max(0, 128 * (kt - 1), 1024 * qh)
        hi = min(S, 128 * (kt + 2), 1024 * (qh + 1))
        return lo, hi

    def emit_sweep(item, h, qh):
        buf = h % 2
        g = h // 4
        gb = g % 2
        pbase = buf * S + qh * 1024
        exp_marks = {}
        s_pe = {}
        band_dve = {}

        def s_mms(ktp):
            ktA, ktB = 2 * ktp, 2 * ktp + 1
            for (kt, p0, coff, tp) in ((ktA, 0, 0, (0, 0)), (ktB, 64, 1024, (64, 0))):
                key = "A" if coff == 0 else "B"
                for half in range(2):
                    w = []
                    if half == 0:
                        w = [s_free[key], ("dve", qkv_done[h][1])]
                    emit("tensor", w, [("pe", 1)] if half == 1 else [],
                         lambda e, kt=kt, p0=p0, coff=coff, tp=tp, half=half,
                         buf=buf, pbase=pbase:
                         e.matmul(
                             s2_ps[:, coff + half * 512: coff + half * 512 + 512],
                             kt_sb[p0:p0 + 64, buf * S + kt * 128: buf * S + kt * 128 + 128],
                             qt_sb[p0:p0 + 64, pbase + half * 512: pbase + half * 512 + 512],
                             start=True, stop=True, tile_position=tp))
                s_pe[kt] = mk("pe")

        def band_add(kt, coff):
            lo, hi = band_range(kt, qh)
            if hi <= lo:
                return
            llo, lhi = lo - 1024 * qh, hi - 1024 * qh
            r0 = 128 * kt + 255 - lo
            ln_ = hi - lo
            ppo = (h % 2) * 384
            abs_start = ppo + r0
            abs_stop = ppo + r0 - ln_
            if abs_stop < 0:
                in1 = pp_sb[:, abs_start::-1]
            else:
                in1 = pp_sb[:, abs_start: abs_stop: -1]
            emit("vector", [s_pe[kt], pp_ready[h % 2]], [("dve", 1)],
                 lambda e, coff=coff, llo=llo, lhi=lhi, in1=in1:
                 e.tensor_tensor(s2_ps[:, coff + llo: coff + lhi],
                                 s2_ps[:, coff + llo: coff + lhi], in1, op=OP.add))
            band_dve[kt] = mk("dve")

        def exps(kt, coff, slot):
            lo, hi = band_range(kt, qh)
            segs = []
            if hi <= lo:
                segs = [(0, 1024, "lo" if 128 * (kt + 2) <= 1024 * qh else "hi")]
            else:
                llo, lhi = lo - 1024 * qh, hi - 1024 * qh
                if llo > 0:
                    segs.append((0, llo, "hi"))
                segs.append((llo, lhi, "band"))
                if lhi < 1024:
                    segs.append((lhi, 1024, "lo"))
            w = [s_pe[kt], pt_reader[slot]]
            if kt in band_dve:
                w.append(band_dve[kt])
            nseg = len(segs) + (1 if DBG_ACT_FLUSH else 0)
            for i, (a, b_, kind) in enumerate(segs):
                inc = [("act", 1)] if (i == len(segs) - 1 and not DBG_ACT_FLUSH) else []
                if kind == "band":
                    bias = None
                elif kind == "hi":
                    bias = ca_sb[:, 16 + h:16 + h + 1]
                else:
                    bias = ca_sb[:, h:h + 1]
                if bias is None:
                    emit("scalar", w if i == 0 else [], inc,
                         lambda e, a=a, b_=b_, coff=coff, slot=slot:
                         e.activation(pt_sb[:, slot * 1024 + a: slot * 1024 + b_],
                                      s2_ps[:, coff + a: coff + b_],
                                      AF.Exp, scale=1.0))
                else:
                    emit("scalar", w if i == 0 else [], inc,
                         lambda e, a=a, b_=b_, coff=coff, slot=slot, bias=bias:
                         e.activation(pt_sb[:, slot * 1024 + a: slot * 1024 + b_],
                                      s2_ps[:, coff + a: coff + b_],
                                      AF.Exp, bias=bias, scale=1.0))
            if DBG_ACT_FLUSH:
                emit("scalar", [], [("act", 1)],
                     lambda e, slot=slot: e.activation(
                         st_sb[0:1, 158:159],
                         pt_sb[0:1, slot * 1024: slot * 1024 + 1],
                         AF.Identity, scale=1.0))
            exp_marks[kt] = mk("act")
            key = "A" if coff == 0 else "B"
            s_free[key] = mk("act")

        def ctx_mms(kt, slot):
            vo = gb * 4160 + (h % 4) * 1040 + kt * 65
            for half in range(2):
                w = []
                if half == 0:
                    w = [exp_marks[kt], vh_ready[gb]]
                    if kt == 0:
                        w.append(ctx_gate[0])
                emit("tensor", w, [("pe", 1)] if half == 1 else [],
                     lambda e, kt=kt, slot=slot, half=half, vo=vo:
                     e.matmul(ctx_ps[0:65, half * 512: half * 512 + 512],
                              vh_sb[:, vo: vo + 65],
                              pt_sb[:, slot * 1024 + half * 512: slot * 1024 + half * 512 + 512],
                              start=(kt == 0), stop=(kt == NT - 1)))
            pt_reader[slot] = mk("pe")
            if kt == NT - 1:
                ctx_done[0] = mk("pe")

        prev = None
        for ktp in range(8):
            ktA, ktB = 2 * ktp, 2 * ktp + 1
            slA = kt_slot[0] % 4
            kt_slot[0] += 1
            slB = kt_slot[0] % 4
            kt_slot[0] += 1
            s_mms(ktp)
            band_add(ktA, 0)
            band_add(ktB, 1024)
            exps(ktA, 0, slA)
            exps(ktB, 1024, slB)
            if DBG_DUMP and item == DBG_DUMP_ITEM and h == 0 and qh == 0 and ktp == 0:
                emit("gpsimd", [mk("act")], [("gdbg", 16)],
                     lambda e, slA=slA, slB=slB: e.dma_start(
                         out=dbg["d_pt"], in_=pt_sb[:, :]))
                pt_reader[slA] = mk("gdbg")
                pt_reader[slB] = mk("gdbg")
            if prev is not None:
                ctx_mms(prev[0], prev[2])
                ctx_mms(prev[1], prev[3])
            if div_pending:
                div_pending.pop(0)()
            bgq_pop(2 if (ktp & 1) else 1)
            prev = (ktA, ktB, slA, slB)
        ctx_mms(prev[0], prev[2])
        ctx_mms(prev[1], prev[3])

    def queue_division(item, h, qh):
        ch = h // 2
        odd = h % 2 == 1
        pe_ctx = ctx_done[0]
        st = {}

        def d1():
            emit("vector", [pe_ctx, div_tt_gate[0]], [("dve", 1)],
                 lambda e: e.tensor_copy(ctxu_sb[0:65, :], ctx_ps[0:65, :]))
            ctx_gate[0] = mk("dve")
            st["d1"] = mk("dve")

        def d2():
            if DBG_OLD_RECIP:
                emit("scalar", [st["d1"]], [],
                     lambda e: e.activation(aux_sb[64:65, 0:1024],
                                            ctxu_sb[64:65, 0:1024], AF.Ln, scale=1.0))
                emit("scalar", [], [("act", 1)],
                     lambda e: e.activation(ctxu_sb[64:65, 0:1024],
                                            aux_sb[64:65, 0:1024], AF.Exp, scale=-1.0))
                st["d2"] = mk("act")
                return
            scat_src = ap3(ctxu_sb, 64 * 1024, [[1024, 1], [16, 64], [1, 16]])
            emit("gpsimd", [st["d1"], div_r2_free[0]] + pp_tp_fence[0],
                 [("gdiv", 16)],
                 lambda e: e.dma_start(out=st_sb[64:128, 128:144], in_=scat_src))
            emit("vector", [mk("gdiv")], [("dve", 1)],
                 lambda e: e.reciprocal(st_sb[64:128, 128:144],
                                        st_sb[64:128, 128:144]))
            gat_src = ap3(st_sb, 64 * 160 + 128, [[160, 64], [1, 16]])
            emit("gpsimd", [mk("dve")], [("gdiv", 16)],
                 lambda e: e.dma_start(out=ctxu_sb[64:65, 0:1024], in_=gat_src))
            st["d2"] = mk("gdiv")
            div_r2_free[0] = st["d2"]

        def d3(half):
            emit("tensor", [st["d2"], misc_gate[0]], [("pe", 1)],
                 lambda e, half=half: e.matmul(
                     msc_ps[0:64, 0:512], oc_sb[64:65, 0:64],
                     ctxu_sb[64:65, half * 512: half * 512 + 512],
                     start=True, stop=True))
            st["d3"] = mk("pe")

        def d4(half):
            if odd:
                out = ln_sb[0:64, qh * 1024 + half * 512: qh * 1024 + half * 512 + 512]
            else:
                out = ctxT[0:64, ch * S + qh * 1024 + half * 512:
                           ch * S + qh * 1024 + half * 512 + 512]
            w = [st["d3"]]
            if odd:
                w.append(ln_stage_free[qh])
            emit("vector", w, [("dve", 1)],
                 lambda e, out=out, half=half: e.tensor_tensor(
                     out, ctxu_sb[0:64, half * 512: half * 512 + 512],
                     msc_ps[0:64, 0:512], op=OP.mult))
            misc_gate[0] = mk("dve")
            div_tt_gate[0] = mk("dve")
            st["d4"] = mk("dve")

        def d5():
            emit("gpsimd", [st["d4"]] + pp_tp_fence[0], [(f"gc{qh}", 16)],
                 lambda e: e.dma_start(
                     out=ctxT[64:128, ch * S + qh * 1024: ch * S + qh * 1024 + 1024],
                     in_=ln_sb[0:64, qh * 1024: qh * 1024 + 1024]))
            ln_stage_free[qh] = mk(f"gc{qh}")

        def d34(half):
            d3(half)
            d4(half)
        seq = [d1, d2, lambda: d34(0), lambda: d34(1)]
        if odd:
            seq.append(d5)
        div_pending.extend(seq)

    def phase_B(item):
        for h in range(H):
            buf = h % 2
            if 1 <= h and h + 1 < H:
                queue_head_work(h + 1)
            nh = (h + 1) % H
            emit_patp(nh, [pp_freed[nh % 2]] + pp_tp_fence[0])
            if h % 4 == 1 and h // 4 + 1 <= 3:
                g = h // 4 + 1
                for tt in range(NT):
                    bgq.append(lambda g=g, tt=tt: emit_v_tile(g, tt))
            if h not in qkv_done:
                bgq_flush()
            assert h in qkv_done, f"QKV for head {h} not emitted"
            for qh in range(2):
                emit_sweep(item, h, qh)
                queue_division(item, h, qh)
            pp_freed[h % 2] = mk("dve")
            qt_freed[buf] = mk("pe")
            qs_freed[buf] = mk("pe")
            if h % 4 == 3:
                vh_free[(h // 4) % 2] = mk("pe")
        while div_pending:
            div_pending.pop(0)()
        bgq_flush()

    pp_tp_fence = [[]]

    def phase_C1(item, next_item):
        base = item * S
        div_done = mk("dve")
        div_gp = [mk("gc0"), mk("gc1"), mk("gdiv")]
        misc_gate[0] = div_done
        qkv_gate[0] = max(qkv_gate[0], div_done, key=lambda t: t[1])
        bank_gate = [div_done, div_done]
        xf = [mk("dve"), mk("dve")]
        xbf = [[mk("gy0"), mk("gx0")], [mk("gy1"), mk("gx1")]]
        st8 = a_begin(next_item) if next_item is not None else None
        for tt in range(NT):
            buf = tt % 2
            emit("sync", [xf[buf], xa_free[buf]], [(f"sxa{buf}", 16)],
                 lambda e, tt=tt, buf=buf, base=base: e.dma_start(
                     out=xa_sb[:, buf * 1024: buf * 1024 + 1024],
                     in_=x8[base + tt * 128: base + tt * 128 + 128, :]))
            sy_m = mk(f"sxa{buf}")
            for dh in range(2):
                ps = msc_ps if dh == 0 else qkv_ps
                for ch in range(NCH):
                    w2 = []
                    if ch == 0:
                        w2 = [bank_gate[dh]] + div_gp
                        if dh == 1:
                            w2.append(qkv_gate[0])
                    emit("tensor", w2, [("pe", 1)] if ch == NCH - 1 else [],
                         lambda e, ch=ch, tt=tt, dh=dh, ps=ps: e.matmul(
                             ps[:, 0:512],
                             ctxT[:, ch * S + tt * 128: ch * S + tt * 128 + 128],
                             wo_sb[:, ch * D + dh * 512: ch * D + dh * 512 + 512],
                             start=(ch == 0), stop=(ch == NCH - 1)))
                emit("vector", [mk("pe"), sy_m] + xbf[buf], [("dve", 1)],
                     lambda e, buf=buf, dh=dh, ps=ps: e.tensor_tensor(
                         xb_sb[:, buf * 1024 + dh * 512: buf * 1024 + dh * 512 + 512],
                         ps[:, 0:512],
                         xa_sb[:, buf * 1024 + dh * 512: buf * 1024 + dh * 512 + 512],
                         op=OP.add))
                bank_gate[dh] = mk("dve")
                if dh == 1:
                    qkv_gate[0] = mk("dve")
            xf[buf] = mk("dve")
            xa_free[buf] = mk("dve")
            sc = 64 + 2 * tt
            emit("vector", [mk("dve")], [],
                 lambda e, buf=buf: e.bn_stats(
                     st_sb[:, 36:42], xb_sb[:, buf * 1024: buf * 1024 + 512]))
            emit("vector", [], [("dve", 1)],
                 lambda e, buf=buf: e.bn_stats(
                     st_sb[:, 42:48], xb_sb[:, buf * 1024 + 512: buf * 1024 + 1024]))
            emit("vector", [mk("dve")], [("dve", 1)],
                 lambda e, sc=sc: e.bn_aggr(st_sb[:, sc:sc + 2], st_sb[:, 36:48]))
            emit("gpsimd", [mk("dve")], [(f"gx{buf}", 16)],
                 lambda e, tt=tt, buf=buf: e.dma_start(
                     out=x2_d[tt * 128: tt * 128 + 128, :],
                     in_=xb_sb[:, buf * 1024: buf * 1024 + 1024]))
            xbf[buf] = [mk(f"gx{buf}")]
            if st8 is not None:
                a_tile(st8, tt)
        if st8 is not None:
            a_end(st8)
        emit("scalar", [mk("dve")], [("act", 1)],
             lambda e: e.activation(st_sb[:, 96:112], st_sb[:, 65:97:2],
                                    AF.Ln, bias=eps_sb[:, 0:1], scale=1.0))
        emit("scalar", [mk("act")], [("act", 1)],
             lambda e: e.activation(st_sb[:, 112:128], st_sb[:, 96:112],
                                    AF.Exp, scale=-0.5))
        rstd2 = mk("act")
        emit("vector", [rstd2], [("dve", 1)],
             lambda e: e.scalar_tensor_tensor(
                 st_sb[:, 48:64], st_sb[:, 64:96:2], -1.0, st_sb[:, 112:128],
                 op0=OP.mult, op1=OP.mult))
        div_tt_gate[0] = mk("dve")
        return mk("dve"), [mk("gx0"), mk("gx1")], st8

    def phase_D(item, nm2_done, x2_ready, a_next):
        base = item * S
        load_table(id_exp)
        xbfree = [nm2_done, nm2_done]
        lnfree = [("pe", cnt["pe"]), ("pe", cnt["pe"])]
        ltfree = [("pe", 0), ("pe", 0)]
        misc_gate[0] = mk("dve")
        qkv_gate[0] = mk("dve")
        tanh_done = [("act", 0)]
        tanh_war = [mk("gdiv")]
        yfree = [("gy0", 0), ("gy1", 0)]
        for tt in range(NT):
            buf = tt % 2
            emit("sync", [xbfree[buf]] + x2_ready, [(f"sxb{buf}", 16)],
                 lambda e, tt=tt, buf=buf: e.dma_start(
                     out=xb_sb[:, buf * 1024: buf * 1024 + 1024],
                     in_=x2_d[tt * 128: tt * 128 + 128, :]))
            sy_m = mk(f"sxb{buf}")
            emit("vector", [sy_m, nm2_done, lnfree[buf]], [("dve", 1)],
                 lambda e, tt=tt, buf=buf: e.tensor_scalar(
                     qs_sb[:, buf * 1024: buf * 1024 + 1024],
                     xb_sb[:, buf * 1024: buf * 1024 + 1024],
                     st_sb[:, 112 + tt:113 + tt], st_sb[:, 48 + tt:49 + tt],
                     op0=OP.mult, op1=OP.add))
            dve_m = mk("dve")
            src_ = ap3(qs_sb, buf * 1024, [[2048, 128], [128, 8], [1, 128]])
            dst_ = ap3(lt_sb, buf * 1024, [[2048, 128], [128, 8], [1, 128]])
            wt = [dve_m, ltfree[buf]]
            if tt == 0:
                wt.extend(d_tp_fence[0])
            emit("sync", wt, [(f"tpd{buf}", 16)],
                 lambda e, dst_=dst_, src_=src_: e.dma_start(out=dst_, in_=src_,
                                                             transpose=True))
            tp_m = mk(f"tpd{buf}")
            lnfree[buf] = tp_m
            for dh in range(2):
                ps, gate_get, gate_cell = acc_pick()
                for chm in range(NCH):
                    w3 = []
                    if chm == 0:
                        w3 = [tp_m, gate_get()]
                    emit("tensor", w3, [("pe", 1)] if chm == NCH - 1 else [],
                         lambda e, chm=chm, buf=buf, dh=dh, ps=ps: e.matmul(
                             ps[:, 0:512],
                             lt_sb[:, buf * 1024 + chm * 128: buf * 1024 + chm * 128 + 128],
                             m_sb[:, chm * D + dh * 512: chm * D + dh * 512 + 512],
                             start=(chm == 0), stop=(chm == NCH - 1)))
                pe_m = mk("pe")
                emit("vector", [pe_m, tanh_done[0]], [("dve", 1)],
                     lambda e, dh=dh, ps=ps: e.tensor_tensor(
                         ctxu_sb[:, dh * 512: dh * 512 + 512], ps[:, 0:512],
                         br_sb[:, dh * 512: dh * 512 + 512], op=OP.add))
                gate_cell[0] = mk("dve")
                emit("scalar", [mk("dve"), tanh_war[0]], [("act", 1)],
                     lambda e, dh=dh: e.activation(
                         aux_sb[:, dh * 512: dh * 512 + 512],
                         ctxu_sb[:, dh * 512: dh * 512 + 512],
                         AF.Tanh, scale=1.0))
                tanh_done[0] = mk("act")
                emit("vector", [mk("act"), yfree[buf]], [("dve", 1)],
                     lambda e, buf=buf, dh=dh: e.tensor_tensor(
                         xb_sb[:, buf * 1024 + dh * 512: buf * 1024 + dh * 512 + 512],
                         xb_sb[:, buf * 1024 + dh * 512: buf * 1024 + dh * 512 + 512],
                         aux_sb[:, dh * 512: dh * 512 + 512], op=OP.add))
                tanh_war[0] = mk("dve")
            emit("gpsimd", [mk("dve")], [(f"gy{buf}", 16)],
                 lambda e, tt=tt, buf=buf, base=base: e.dma_start(
                     out=y8[base + tt * 128: base + tt * 128 + 128, :],
                     in_=xb_sb[:, buf * 1024: buf * 1024 + 1024]))
            yfree[buf] = mk(f"gy{buf}")
            xbfree[buf] = mk(f"gy{buf}")
            ltfree[buf] = ("pe", cnt["pe"])
        load_table(id_nle)
        qkv_gate[0] = mk("dve")
        misc_gate[0] = mk("dve")
        qs_freed[0] = mk("tpd0")
        qs_freed[1] = mk("tpd1")
        return

    d_tp_fence = [[]]

    a_next_state = None
    for item in range(min(B_PER, DBG_ITEMS)):
        if DBG_STAGE == "prologue":
            break
        if a_next_state is None:
            a_next_state = phase_A(item)
        if DBG_DUMP and item == DBG_DUMP_ITEM:
            emit("gpsimd", [lnT_tp[NT - 2], lnT_tp[NT - 1]], [("gdbg", 16)],
                 lambda e: e.dma_start(out=dbg["d_ln1t"], in_=ln1T[:, :]))
        if DBG_STAGE == "A":
            a_next_state = None
            continue
        bgq_flush()
        if item == 0:
            wdup_done_c[0] = [mk("gw0"), mk("gw1")]
            pt_reader[0] = mk("gw1")
            pt_reader[1] = mk("gw1")
        emit_qk_stream(0)
        for proj in range(2):
            for qb in range(4):
                emit_qkv_group(0, proj, qb)
        for tt in range(NT):
            emit_v_tile(0, tt)
        if DBG_DUMP and item == DBG_DUMP_ITEM:
            emit("gpsimd", [mk("dve")], [("gdbg", 16)],
                 lambda e: e.dma_start(out=dbg["d_qt"], in_=qt_sb[:, :]))
            emit("gpsimd", [], [("gdbg", 16)],
                 lambda e: e.dma_start(out=dbg["d_kt"], in_=kt_sb[:, :]))
            emit("gpsimd", [], [("gdbg", 16)],
                 lambda e: e.dma_start(out=dbg["d_vh"], in_=vh_sb[:, 0:8320]))
        queue_head_work(1)
        pp_tp_fence[0] = [mk("tpa0"), mk("tpa1"), mk("tpd0"), mk("tpd1"),
                          mk("tpw")]
        phase_B(item)
        if DBG_DUMP and item == DBG_DUMP_ITEM:
            emit("gpsimd", [mk("dve"), mk("gc0"), mk("gc1")], [("gdbg", 16)],
                 lambda e: e.dma_start(out=dbg["d_ctxt"], in_=ctxT[:, :]))
        if DBG_STAGE == "B":
            a_next_state = None
            continue
        nxt = item + 1 if item + 1 < min(B_PER, DBG_ITEMS) else None
        nm2_done, x2_ready, a_next_state = phase_C1(item, nxt)
        if DBG_STAGE == "C1":
            continue
        d_tp_fence[0] = [mk("gpp0"), mk("gpp1"), mk("gc0"), mk("gc1"),
                         mk("gdiv")]
        phase_D(item, nm2_done, x2_ready, a_next_state)

    emit("gpsimd", [], [], lambda e: e.wait_ge(sems["gy0"], cnt["gy0"]))
    emit("gpsimd", [], [], lambda e: e.wait_ge(sems["gy1"], cnt["gy1"]))

    if DBG_PREFIX is not None:
        del OPS[DBG_PREFIX:]
    print(f"[kernel] total ops: {len(OPS)}")

    with nc.Block() as block:
        for eng_name in ["sync", "gpsimd", "tensor", "vector", "scalar"]:
            def make(eng_name):
                def run(eng):
                    for e_, fn in OPS:
                        if e_ == eng_name:
                            fn(eng)
                return run
            getattr(block, eng_name)(make(eng_name))

    return nc


_PROG = None


def kernel(L0_attn_hidden_states, L0_attn_Wq, L0_attn_Wk, L0_attn_Wv, L0_attn_Wo,
           L0_attn_rel_emb, L0_ffn_weight, L0_ffn_aux_weight1, L0_ffn_aux_weight2,
           L0_ffn_bias):
    global _PROG
    x = np.ascontiguousarray(np.asarray(L0_attn_hidden_states, dtype=np.float32))
    onehot, eye16 = _host_consts()
    shared = dict(
        wq=np.ascontiguousarray(np.asarray(L0_attn_Wq, np.float32)),
        wk=np.ascontiguousarray(np.asarray(L0_attn_Wk, np.float32)),
        wv=np.ascontiguousarray(np.asarray(L0_attn_Wv, np.float32)),
        wo=np.ascontiguousarray(np.asarray(L0_attn_Wo, np.float32)),
        wf=np.ascontiguousarray(np.asarray(L0_ffn_weight, np.float32)),
        wa1=np.ascontiguousarray(np.asarray(L0_ffn_aux_weight1, np.float32)),
        wa2=np.ascontiguousarray(np.asarray(L0_ffn_aux_weight2, np.float32)),
        fbias=np.ascontiguousarray(np.asarray(L0_ffn_bias, np.float32).reshape(1, D)),
        rel=np.ascontiguousarray(np.asarray(L0_attn_rel_emb, np.float32)),
        onehot=onehot, eye16=eye16,
    )
    if _PROG is None:
        _PROG = build_program()
    in_maps = []
    for c in range(N_CORES):
        m = dict(shared)
        m["x"] = np.ascontiguousarray(
            x[c * B_PER:(c + 1) * B_PER].reshape(B_PER * S, D))
        in_maps.append(m)
    res = run_bass_kernel_spmd(_PROG, in_maps, core_ids=list(range(N_CORES)))
    out = np.concatenate(
        [np.asarray(r["y"]).reshape(B_PER, S, D) for r in res.results], axis=0)
    return out.astype(np.float32)


# version 08
# speedup vs baseline: 1.1533x; 1.1533x over previous
import math
import numpy as np
import bass_rust
import concourse.bass as bass
import concourse.mybir as mybir
from concourse.bass_utils import run_bass_kernel_spmd

F32 = mybir.dt.float32
BF16 = mybir.dt.bfloat16
AF = mybir.ActivationFunctionType
OP = mybir.AluOpType

N_CORES = 8
B_PER = 4
S = 2048
D = 1024
H = 16
DK = 64
NT = S // 128
NCH = D // 128
NREV = 511
C_SHIFT = 12.0
EPS = 1e-5

NUM_BUCKETS = 32
MAX_DISTANCE = 128

DBG_ITEMS = B_PER
DBG_STAGE = "full"
DBG_PREFIX = None
DBG_DUMP = False
DBG_DUMP_ITEM = 0
DBG_NO_ROT = False
DBG_OLD_RECIP = False
DBG_ACT_FLUSH = False


def _rel_bucket_host(d):
    import jax
    import jax.numpy as jnp
    with jax.default_device(jax.devices("cpu")[0]):
        num_buckets = NUM_BUCKETS // 2
        rel = jnp.asarray(d, dtype=jnp.int32)
        relative_buckets = (rel > 0).astype(jnp.int32) * num_buckets
        rp = jnp.abs(rel)
        max_exact = num_buckets // 2
        is_small = rp < max_exact
        rp_large = max_exact + (
            jnp.log(rp.astype(jnp.float32) / max_exact)
            / math.log(MAX_DISTANCE / max_exact)
            * (num_buckets - max_exact)
        ).astype(jnp.int32)
        rp_large = jnp.minimum(rp_large, num_buckets - 1)
        relative_buckets = relative_buckets + jnp.where(is_small, rp, rp_large)
        return np.asarray(relative_buckets)


def _host_consts():
    d = np.arange(NREV, dtype=np.int64) - 255
    bk = _rel_bucket_host(d)
    onehot = np.zeros((NUM_BUCKETS, NREV), dtype=np.float32)
    onehot[bk, np.arange(NREV)] = 1.0
    eye16 = np.eye(16, dtype=np.float32)
    return onehot, eye16


def build_program():
    nc = bass.Bass("TRN2", target_bir_lowering=False)

    x8 = nc.dram_tensor("x", [B_PER * S, D], F32, kind="ExternalInput").ap()
    wq_d = nc.dram_tensor("wq", [D, D], F32, kind="ExternalInput").ap()
    wk_d = nc.dram_tensor("wk", [D, D], F32, kind="ExternalInput").ap()
    wv_d = nc.dram_tensor("wv", [D, D], F32, kind="ExternalInput").ap()
    wo_d = nc.dram_tensor("wo", [D, D], F32, kind="ExternalInput").ap()
    wf_d = nc.dram_tensor("wf", [D, D], F32, kind="ExternalInput").ap()
    wa1_d = nc.dram_tensor("wa1", [D, D], F32, kind="ExternalInput").ap()
    wa2_d = nc.dram_tensor("wa2", [D, D], F32, kind="ExternalInput").ap()
    fb_d = nc.dram_tensor("fbias", [1, D], F32, kind="ExternalInput").ap()
    rel_d = nc.dram_tensor("rel", [NUM_BUCKETS, H], F32, kind="ExternalInput").ap()
    oh_d = nc.dram_tensor("onehot", [NUM_BUCKETS, NREV], F32, kind="ExternalInput").ap()
    eye_d = nc.dram_tensor("eye16", [16, 16], F32, kind="ExternalInput").ap()
    y8 = nc.dram_tensor("y", [B_PER * S, D], F32, kind="ExternalOutput").ap()

    dbg = {}
    if DBG_DUMP:
        for nm, shp, dt in [("d_ln1t", [128, NCH * S], BF16),
                            ("d_qt", [128, 2 * S], BF16),
                            ("d_kt", [128, 2 * S], BF16),
                            ("d_vh", [128, 8320], BF16),
                            ("d_pt", [128, 4096], BF16),
                            ("d_ctxt", [128, NCH * S], BF16),
                            ("d_rev", [16, NREV], F32),
                            ("d_ca", [128, 32], F32),
                            ("d_pp", [128, 768], BF16)]:
            dbg[nm] = nc.dram_tensor(nm, shp, dt, kind="ExternalOutput").ap()

    qdup_d = nc.dram_tensor("qdup", [NCH, 128, H, 128], BF16, kind="Internal").ap()
    kdup_d = nc.dram_tensor("kdup", [NCH, 128, H, 128], BF16, kind="Internal").ap()
    x2_d = nc.dram_tensor("x2scr", [S, D], F32, kind="Internal").ap()

    A = nc.alloc_sbuf_tensor
    ln1T = A("ln1T", [128, NCH * S], BF16)
    ctxT = A("ctxT", [128, NCH * S], BF16)
    wv_sb = A("wv_sb", [128, NCH * D], BF16)
    wo_sb = A("wo_sb", [128, NCH * D], BF16)
    m_sb = A("m_sb", [128, NCH * D], BF16)
    qt_sb = A("qt_sb", [128, 2 * S], BF16)
    kt_sb = A("kt_sb", [128, 2 * S], BF16)
    qs_sb = A("qs_sb", [128, 2 * NCH * 128], BF16)
    ks_sb = A("ks_sb", [128, 2 * NCH * 128], BF16)
    vh_sb = A("vh_sb", [128, 2 * 4 * (NT * 65)], BF16)
    pt_sb = A("pt_sb", [128, 4 * 1024], BF16)
    pp_sb = A("pp_sb", [128, 2 * 384], BF16)
    xa_sb = A("xa_sb", [128, 2 * 1024], F32)
    xb_sb = A("xb_sb", [128, 2 * 1024], F32)
    ctxu_sb = A("ctxu_sb", [128, 1024], F32)
    aux_sb = A("aux_sb", [128, 1024], F32)
    ln_sb = A("ln_sb", [128, 2 * 1024], BF16)
    lt_sb = A("lt_sb", [128, 2 * 1024], BF16)
    br_sb = A("br_sb", [128, D], F32)
    ca_sb = A("ca_sb", [128, 32], F32)
    st_sb = A("st_sb", [128, 160], F32)
    rev_sb = A("rev_sb", [16, NREV], F32)
    oh_sb = A("oh_sb", [NUM_BUCKETS, NREV], F32)
    rel_sb = A("rel_sb", [NUM_BUCKETS, H], F32)
    eye_sb = A("eye_sb", [16, 16], F32)
    or_sb = A("or_sb", [1, 128], F32)
    oc_sb = A("oc_sb", [128, 64], F32)
    eps_sb = A("eps_sb", [128, 1], F32)

    P = nc.alloc_psum_tensor
    s2_ps = P("s2_ps", [128, 2048], F32)
    ctx_ps = P("ctx_ps", [128, 1024], F32)
    qkv_ps = P("qkv_ps", [128, 512], F32)
    msc_ps = P("msc_ps", [128, 512], F32)

    sem_names = ["ssm", "sxa0", "sxa1", "sxb0", "sxb1",
                 "tpa0", "tpa1", "tpd0", "tpd1", "tpw",
                 "gq0", "gq1", "gw0", "gw1", "gpp0", "gpp1",
                 "gdiv", "gc0", "gc1", "gx0", "gx1", "gy0", "gy1", "gdbg",
                 "gld0", "gld1", "gld2", "gld3", "gld4",
                 "pe", "act", "dve"]
    sems = {n: nc.alloc_semaphore(n) for n in sem_names}
    cnt = {n: 0 for n in sem_names}
    OPS = []

    def emit(engine, waits, incs, body):
        wl = []
        for s, v in waits:
            assert v <= cnt[s], f"wait on future: {s} {v} > {cnt[s]}"
            if v > 0:
                wl.append((sems[s], v))

        def fn(eng, wl=wl, incs=incs, body=body):
            for sh, v in wl:
                eng.wait_ge(sh, v)
            inst = body(eng)
            for s, k in incs:
                inst = inst.then_inc(sems[s], k)
        OPS.append((engine, fn))
        for s, k in incs:
            cnt[s] += k

    def mk(s):
        return (s, cnt[s])

    def ap3(t, off, dims):
        return bass_rust.AP(t[:].tensor, off, dims)

    from concourse.hw_specs import get_activation_tables
    tabs = list(get_activation_tables(nc.m.arch).items())
    id_nle = next(i for i, (n, f) in enumerate(tabs) if AF.Exp in f and AF.Ln in f)
    id_exp = next(i for i, (n, f) in enumerate(tabs) if AF.Exp in f and AF.Tanh in f)

    def load_table(idx):
        emit("scalar", [], [], lambda e, idx=idx: e.add_instruction(
            mybir.InstLoadActFuncSet(name=nc.get_next_instruction_name(),
                                     act_func_set_id=idx, ins=[], outs=[])))

    emit("vector", [], [], lambda e: e.memset(or_sb[0:1, :], 1.0))
    emit("vector", [], [], lambda e: e.memset(oc_sb[:, :], 1.0))
    for g2 in range(2):
        for hl in range(4):
            off = g2 * 4160 + hl * 1040 + 64
            apm = ap3(vh_sb, off, [[8320, 128], [65, NT]])
            emit("vector", [], [], lambda e, apm=apm: e.memset(apm, 1.0))
    emit("vector", [], [], lambda e: e.memset(eps_sb[:, :], EPS))
    emit("vector", [], [("dve", 1)], lambda e: e.memset(st_sb[:, :], 0.0))
    ones_done = mk("dve")

    load_table(id_nle)

    emit("sync", [], [("ssm", 16)], lambda e: e.dma_start(out=oh_sb[:, :], in_=oh_d))
    emit("sync", [], [("ssm", 16)], lambda e: e.dma_start(out=rel_sb[:, :], in_=rel_d))
    emit("sync", [], [("ssm", 16)], lambda e: e.dma_start(out=eye_sb[:, :], in_=eye_d))
    emit("sync", [], [("ssm", 16)], lambda e: e.dma_start(out=ctxu_sb[0:1, 0:D], in_=fb_d))
    small_ld = mk("ssm")

    emit("tensor", [small_ld], [("pe", 1)],
         lambda e: e.matmul(msc_ps[0:16, 0:NREV], rel_sb[0:32, 0:16],
                            oh_sb[0:32, 0:NREV], start=True, stop=True))
    emit("scalar", [mk("pe")], [("act", 1)],
         lambda e: e.activation(rev_sb[0:16, :], msc_ps[0:16, 0:NREV], AF.Copy,
                                bias=-C_SHIFT, scale=1.0))
    rev_done = mk("act")

    pp_ready = [None, None]
    pp_freed = [("dve", 0), ("dve", 0)]

    def emit_patp(h, waits):
        pbuf = h % 2
        sem = f"gpp{pbuf}"
        src = ap3(rev_sb, h * NREV, [[NREV, 1], [1, 128], [1, 384]])
        dst = pp_sb[:, pbuf * 384:(pbuf + 1) * 384]
        emit("gpsimd", waits, [(sem, 16)],
             lambda e, dst=dst, src=src: e.dma_start(out=dst, in_=src))
        pp_ready[pbuf] = mk(sem)

    emit_patp(0, [rev_done])
    emit_patp(1, [rev_done])
    patp_prologue = [mk("gpp0"), mk("gpp1")]

    emit("tensor", [rev_done], [],
         lambda e: e.matmul(msc_ps[0:1, 0:16], rev_sb[0:16, 0:1],
                            eye_sb[0:16, 0:16], start=True, stop=True))
    emit("tensor", [], [("pe", 1)],
         lambda e: e.matmul(msc_ps[0:1, 16:32], rev_sb[0:16, 510:511],
                            eye_sb[0:16, 0:16], start=True, stop=True))
    emit("vector", [mk("pe"), ones_done], [("dve", 1)],
         lambda e: e.tensor_copy(st_sb[0:1, 0:32], msc_ps[0:1, 0:32]))
    emit("tensor", [mk("dve")], [("pe", 1)],
         lambda e: e.matmul(msc_ps[0:128, 0:32], or_sb[0:1, 0:128],
                            st_sb[0:1, 0:32], start=True, stop=True))
    emit("vector", [mk("pe")], [("dve", 1)],
         lambda e: e.tensor_copy(ca_sb[:, :], msc_ps[:, 0:32]))

    for half in range(2):
        emit("tensor", [small_ld, ones_done, mk("dve")], [("pe", 1)],
             lambda e, half=half: e.matmul(
                 qkv_ps[:, 0:512], or_sb[0:1, 0:128],
                 ctxu_sb[0:1, half * 512:(half + 1) * 512], start=True, stop=True))
        emit("vector", [mk("pe")], [("dve", 1)],
             lambda e, half=half: e.tensor_copy(
                 br_sb[:, half * 512:(half + 1) * 512], qkv_ps[:, 0:512]))
    pro1_done = mk("dve")

    wbf, abf = ctxT, ln1T
    xa_free = [pro1_done, pro1_done]

    _gld_i = [0]

    def cast_load(src_d, dst_t, dst_off, waits):
        sem = f"gld{_gld_i[0]}"
        _gld_i[0] += 1
        pitch = dst_t[:].ap[0][0]
        s_ap = bass_rust.AP(src_d.tensor, 0,
                            [[1024, 128], [128 * 1024, NCH], [1, 1024]])
        d_ap = bass_rust.AP(dst_t[:].tensor, dst_off,
                            [[pitch, 128], [1024, NCH], [1, 1024]])
        emit("gpsimd", waits, [(sem, 16)],
             lambda e, d_ap=d_ap, s_ap=s_ap: e.dma_start(out=d_ap, in_=s_ap))
        return mk(sem)

    w_cast = cast_load(wf_d, wbf, 0, [pro1_done])
    for i in range(NCH):
        src = ap3(wbf, i * 1024, [[NCH * S, 128], [128, 8], [1, 128]])
        dst = ap3(wbf, 8192 + i * 128, [[NCH * S, 128], [1024, 8], [1, 128]])
        w = [w_cast]
        if i == 0:
            w.extend(patp_prologue)
        emit("sync", w, [("tpw", 16)],
             lambda e, dst=dst, src=src: e.dma_start(out=dst, in_=src, transpose=True))
    wt_done = mk("tpw")
    a1_done = cast_load(wa1_d, abf, 0, [pro1_done])

    def mm_chain(lhs_fn, rhs_fn, ps, dst_t, dst_fn, extra_first_waits, gate0):
        gate = gate0
        for ot in range(NCH):
            for ob in range(2):
                for ch in range(NCH):
                    w = []
                    if ch == 0:
                        w = list(extra_first_waits) + [gate]
                    emit("tensor", w, [("pe", 1)] if ch == NCH - 1 else [],
                         lambda e, ot=ot, ob=ob, ch=ch, ps=ps: e.matmul(
                             ps[:, 0:512], lhs_fn(ch, ot), rhs_fn(ch, ob),
                             start=(ch == 0), stop=(ch == NCH - 1)))
                emit("vector", [mk("pe")], [("dve", 1)],
                     lambda e, ot=ot, ob=ob, dst_t=dst_t: e.tensor_copy(
                         dst_fn(dst_t, ot, ob), qkv_ps[:, 0:512] if ps is qkv_ps
                         else msc_ps[:, 0:512]))
                gate = mk("dve")
        return mk("dve")

    m1t_done = mm_chain(
        lambda ch, jt: abf[:, ch * 1024 + jt * 128: ch * 1024 + jt * 128 + 128],
        lambda ch, cb: wbf[:, 8192 + ch * 1024 + cb * 512: 8192 + ch * 1024 + cb * 512 + 512],
        msc_ps, abf,
        lambda t, jt, cb: t[:, 8192 + jt * 1024 + cb * 512: 8192 + jt * 1024 + cb * 512 + 512],
        [a1_done, wt_done], a1_done)
    a2_done = cast_load(wa2_d, abf, 0, [m1t_done])
    m_done = mm_chain(
        lambda ch, ct: abf[:, 8192 + ch * 1024 + ct * 128: 8192 + ch * 1024 + ct * 128 + 128],
        lambda ch, jb: abf[:, ch * 1024 + jb * 512: ch * 1024 + jb * 512 + 512],
        qkv_ps, m_sb,
        lambda t, ct, jb: t[:, ct * 1024 + jb * 512: ct * 1024 + jb * 512 + 512],
        [a2_done], a2_done)


    if DBG_DUMP:
        emit("gpsimd", [mk("dve"), mk("act")], [("gdbg", 16)],
             lambda e: e.dma_start(out=dbg["d_rev"], in_=rev_sb[:, :]))
        emit("gpsimd", [], [("gdbg", 16)],
             lambda e: e.dma_start(out=dbg["d_ca"], in_=ca_sb[:, :]))
        emit("gpsimd", [], [("gdbg", 16)],
             lambda e: e.dma_start(out=dbg["d_pp"], in_=pp_sb[:, :]))

    s_free = {"A": ("act", 0), "B": ("act", 0)}
    pt_reader = [("pe", 0)] * 4
    kt_slot = [0]
    qs_ready = [None, None]
    qs_freed = [("pe", 0), ("pe", 0)]
    qt_freed = [("pe", 0), ("pe", 0)]
    qkv_done = {}
    vh_ready = [("dve", 0), ("dve", 0)]
    vh_free = [("pe", 0), ("pe", 0)]
    qkv_gate = [m_done]
    lnT_tp = [("tpa0", 0)] * NT
    ctx_gate = [("dve", 0)]
    ctx_done = [("pe", 0)]
    div_tt_gate = [("dve", 0)]
    misc_gate = [m1t_done]
    acc_i = [0]
    div_r2_free = [("gdiv", 0)]
    div_pending = []
    bgq = []


    def bgq_pop(n):
        for _ in range(min(n, len(bgq))):
            bgq.pop(0)()

    def bgq_flush():
        while bgq:
            bgq.pop(0)()

    stage_free = {0: m_done, 1: m_done}
    wv_done_c = [None]
    wo_done_c = [None]
    wdup_done_c = [None]

    def q_cast_chunk(src_d, dst_t, dst_off, ch, done_cell):
        def go():
            buf = ch % 2
            emit("sync", [xa_free[buf]], [(f"sxa{buf}", 16)],
                 lambda e: e.dma_start(
                     out=xa_sb[:, buf * 1024:(buf + 1) * 1024],
                     in_=src_d[ch * 128:(ch + 1) * 128, :]))
            emit("vector", [mk(f"sxa{buf}")], [("dve", 1)],
                 lambda e: e.tensor_copy(
                     dst_t[:, dst_off + ch * 1024: dst_off + (ch + 1) * 1024],
                     xa_sb[:, buf * 1024:(buf + 1) * 1024]))
            xa_free[buf] = mk("dve")
            if done_cell is not None and ch == NCH - 1:
                done_cell[0] = mk("dve")
        return go

    def q_dup_chunk(src_d, dst_d, ch, done_cell):
        def go():
            buf = ch % 2
            st_t = lt_sb if buf == 0 else pt_sb
            emit("sync", [xa_free[buf]], [(f"sxa{buf}", 16)],
                 lambda e: e.dma_start(
                     out=xa_sb[:, buf * 1024: buf * 1024 + 1024],
                     in_=src_d[ch * 128:(ch + 1) * 128, :]))
            sy_m = mk(f"sxa{buf}")
            pitch = NCH * 512 if st_t is pt_sb else 2048
            for rep in range(2):
                src_ap = ap3(xa_sb, buf * 1024, [[2048, 128], [64, 16], [1, 64]])
                dst_ap = ap3(st_t, rep * 64, [[pitch, 128], [128, 16], [1, 64]])
                emit("vector", [sy_m, stage_free[buf]], [("dve", 1)],
                     lambda e, dst_ap=dst_ap, src_ap=src_ap:
                     e.tensor_copy(dst_ap, src_ap))
            xa_free[buf] = mk("dve")
            emit("gpsimd", [mk("dve")], [(f"gw{buf}", 16)],
                 lambda e: e.dma_start(
                     out=dst_d[ch],
                     in_=st_t[:, 0:2048].rearrange("p (h c) -> p h c", h=16)))
            stage_free[buf] = mk(f"gw{buf}")
            if done_cell is not None and ch == NCH - 1:
                done_cell[0] = None
        return go

    for _ch in range(NCH):
        bgq.append(q_dup_chunk(wq_d, qdup_d, _ch, None))
    for _ch in range(NCH):
        bgq.append(q_dup_chunk(wk_d, kdup_d, _ch, wdup_done_c))
    def _wv_item():
        wv_done_c[0] = cast_load(wv_d, wv_sb, 0, [])
    def _wo_item():
        wo_done_c[0] = cast_load(wo_d, wo_sb, 0, [])
    bgq.append(_wv_item)
    bgq.append(_wo_item)

    def emit_qk_stream(h):
        buf = h % 2
        for (src_d, dst_t) in ((qdup_d, qs_sb), (kdup_d, ks_sb)):
            src = bass_rust.AP(src_d.tensor, h * 128,
                               [[H * 128, 128], [128 * H * 128, NCH], [1, 128]])
            dst = ap3(dst_t, buf * 1024, [[2048, 128], [128, 8], [1, 128]])
            emit("gpsimd", [qs_freed[buf]] + wdup_done_c[0], [(f"gq{buf}", 16)],
                 lambda e, dst=dst, src=src: e.dma_start(out=dst, in_=src))
        qs_ready[buf] = mk(f"gq{buf}")

    def acc_pick():
        i = (acc_i[0] % 2) if not DBG_NO_ROT else 0
        acc_i[0] += 1
        if i == 0:
            return qkv_ps, (lambda: qkv_gate[0]), qkv_gate
        return msc_ps, (lambda: misc_gate[0]), misc_gate

    def emit_qkv_group(h, proj, qb):
        buf = h % 2
        w_src = qs_sb if proj == 0 else ks_sb
        t_dst = qt_sb if proj == 0 else kt_sb
        ps, gate_get, gate_cell = acc_pick()
        for ch in range(NCH):
            w = []
            if ch == 0:
                w = [qs_ready[buf], lnT_tp[qb * 4 + 2],
                     lnT_tp[qb * 4 + 3], gate_get()]
            emit("tensor", w, [("pe", 1)] if ch == NCH - 1 else [],
                 lambda e, ch=ch, qb=qb, buf=buf, w_src=w_src, ps=ps:
                 e.matmul(ps[:, 0:512],
                          w_src[:, buf * 1024 + ch * 128: buf * 1024 + ch * 128 + 128],
                          ln1T[:, ch * S + qb * 512: ch * S + qb * 512 + 512],
                          start=(ch == 0), stop=(ch == NCH - 1)))
        emit("vector", [mk("pe")], [("dve", 1)],
             lambda e, qb=qb, buf=buf, t_dst=t_dst, ps=ps:
             e.tensor_copy(t_dst[:, buf * S + qb * 512: buf * S + qb * 512 + 512],
                           ps[:, 0:512]))
        gate_cell[0] = mk("dve")
        if proj == 1 and qb == 3:
            qkv_done[h] = mk("dve")

    def emit_v_tile(g, tt):
        gb = g % 2
        ps, gate_get, gate_cell = acc_pick()
        for ch in range(NCH):
            w = []
            if ch == 0:
                w = [lnT_tp[tt], gate_get(), wv_done_c[0]]
            emit("tensor", w, [("pe", 1)] if ch == NCH - 1 else [],
                 lambda e, ch=ch, tt=tt, g=g, ps=ps:
                 e.matmul(ps[:, 0:256],
                          ln1T[:, ch * S + tt * 128: ch * S + tt * 128 + 128],
                          wv_sb[:, ch * D + g * 256: ch * D + g * 256 + 256],
                          start=(ch == 0), stop=(ch == NCH - 1)))
        src = ap3(ps, 0, [[512, 128], [64, 4], [1, 64]])
        dst = ap3(vh_sb, gb * 4160 + tt * 65, [[8320, 128], [1040, 4], [1, 64]])
        emit("vector", [mk("pe"), vh_free[gb]], [("dve", 1)],
             lambda e, dst=dst, src=src: e.tensor_copy(dst, src))
        gate_cell[0] = mk("dve")
        if tt == NT - 1:
            vh_ready[gb] = mk("dve")

    def queue_head_work(h):
        bgq.append(lambda h=h: emit_qk_stream(h))
        for proj in range(2):
            for qb in range(4):
                bgq.append(lambda h=h, p=proj, q=qb: emit_qkv_group(h, p, q))

    def a_begin(item):
        a_tp_fence[0] = [mk("gpp0"), mk("gpp1"), mk("gc0"), mk("gc1"),
                         mk("gdiv")]
        st8 = dict(item=item, base=item * S,
                   rstd=[("act", 0)] * NT, norm=[("dve", 0)] * NT,
                   tp=[("tpa0", 0)] * NT,
                   ln_fence=[mk("tpa0"), mk("tpa1"), mk("tpd0"), mk("tpd1"),
                             ln_stage_free[0], ln_stage_free[1]],
                   pe_fence=mk("pe"))
        return st8

    def a_tile(st8, tt):
        base = st8["base"]
        buf = tt % 2
        sc = 18 * buf
        w = [xa_free[buf]]
        if tt >= 2:
            w.append(st8["norm"][tt - 2])
        emit("sync", w, [(f"sxa{buf}", 16)],
             lambda e, tt=tt, buf=buf, base=base: e.dma_start(
                 out=xa_sb[:, buf * 1024: buf * 1024 + 1024],
                 in_=x8[base + tt * 128: base + tt * 128 + 128, :]))
        sy_m = mk(f"sxa{buf}")
        wv_ = [sy_m]
        if tt >= 2:
            wv_.append(st8["rstd"][tt - 2])
        emit("vector", wv_, [],
             lambda e, buf=buf, sc=sc: e.bn_stats(
                 st_sb[:, sc:sc + 6], xa_sb[:, buf * 1024: buf * 1024 + 512]))
        emit("vector", [], [("dve", 1)],
             lambda e, buf=buf, sc=sc: e.bn_stats(
                 st_sb[:, sc + 6:sc + 12],
                 xa_sb[:, buf * 1024 + 512: buf * 1024 + 1024]))
        emit("vector", [mk("dve")], [("dve", 1)],
             lambda e, sc=sc: e.bn_aggr(st_sb[:, sc + 12:sc + 14],
                                        st_sb[:, sc:sc + 12]))
        emit("scalar", [mk("dve")], [("act", 1)],
             lambda e, sc=sc: e.activation(st_sb[:, sc + 14:sc + 15],
                                           st_sb[:, sc + 13:sc + 14],
                                           AF.Ln, bias=eps_sb[:, 0:1], scale=1.0))
        emit("scalar", [mk("act")], [("act", 1)],
             lambda e, sc=sc: e.activation(st_sb[:, sc + 15:sc + 16],
                                           st_sb[:, sc + 14:sc + 15],
                                           AF.Exp, scale=-0.5))
        st8["rstd"][tt] = mk("act")
        emit("vector", [st8["rstd"][tt], mk("dve")], [("dve", 1)],
             lambda e, sc=sc: e.tensor_scalar(
                 st_sb[:, sc + 16:sc + 17], st_sb[:, sc + 12:sc + 13],
                 st_sb[:, sc + 15:sc + 16], -1.0, op0=OP.mult, op1=OP.mult))
        wv2 = [st8["tp"][tt - 2], mk("dve")] if tt >= 2 else \
            list(st8["ln_fence"]) + [mk("dve")]
        emit("vector", wv2, [("dve", 1)],
             lambda e, buf=buf, sc=sc: e.tensor_scalar(
                 ln_sb[:, buf * 1024: buf * 1024 + 1024],
                 xa_sb[:, buf * 1024: buf * 1024 + 1024],
                 st_sb[:, sc + 15:sc + 16], st_sb[:, sc + 16:sc + 17],
                 op0=OP.mult, op1=OP.add))
        st8["norm"][tt] = mk("dve")
        xa_free[buf] = st8["norm"][tt]
        src_ = ap3(ln_sb, buf * 1024, [[2048, 128], [128, 8], [1, 128]])
        dst_ = ap3(ln1T, tt * 128, [[NCH * S, 128], [S, 8], [1, 128]])
        wt = [st8["norm"][tt]] + a_tp_fence[0]
        if tt == 0:
            wt.append(st8["pe_fence"])
        emit("sync", wt, [(f"tpa{buf}", 16)],
             lambda e, dst_=dst_, src_=src_: e.dma_start(out=dst_, in_=src_,
                                                         transpose=True))
        st8["tp"][tt] = mk(f"tpa{buf}")
        lnT_tp[tt] = st8["tp"][tt]

    def a_end(st8):
        ln_stage_free[0] = st8["tp"][NT - 1]
        ln_stage_free[1] = st8["tp"][NT - 1]

    def phase_A(item):
        st8 = a_begin(item)
        for tt in range(NT):
            a_tile(st8, tt)
            bgq_pop(4)
        a_end(st8)
        return st8

    a_tp_fence = [[]]
    ln_stage_free = [("gc0", 0), ("gc1", 0)]

    def band_range(kt, qh):
        lo = max(0, 128 * (kt - 1), 1024 * qh)
        hi = min(S, 128 * (kt + 2), 1024 * (qh + 1))
        return lo, hi

    def emit_sweep(item, h, qh):
        buf = h % 2
        g = h // 4
        gb = g % 2
        pbase = buf * S + qh * 1024
        exp_marks = {}
        s_pe = {}
        band_dve = {}

        def s_mms(ktp):
            ktA, ktB = 2 * ktp, 2 * ktp + 1
            for (kt, p0, coff, tp) in ((ktA, 0, 0, (0, 0)), (ktB, 64, 1024, (64, 0))):
                key = "A" if coff == 0 else "B"
                for half in range(2):
                    w = []
                    if half == 0:
                        w = [s_free[key], ("dve", qkv_done[h][1])]
                    emit("tensor", w, [("pe", 1)] if half == 1 else [],
                         lambda e, kt=kt, p0=p0, coff=coff, tp=tp, half=half,
                         buf=buf, pbase=pbase:
                         e.matmul(
                             s2_ps[:, coff + half * 512: coff + half * 512 + 512],
                             kt_sb[p0:p0 + 64, buf * S + kt * 128: buf * S + kt * 128 + 128],
                             qt_sb[p0:p0 + 64, pbase + half * 512: pbase + half * 512 + 512],
                             start=True, stop=True, tile_position=tp))
                s_pe[kt] = mk("pe")

        def band_add(kt, coff):
            lo, hi = band_range(kt, qh)
            if hi <= lo:
                return
            llo, lhi = lo - 1024 * qh, hi - 1024 * qh
            r0 = 128 * kt + 255 - lo
            ln_ = hi - lo
            ppo = (h % 2) * 384
            abs_start = ppo + r0
            abs_stop = ppo + r0 - ln_
            if abs_stop < 0:
                in1 = pp_sb[:, abs_start::-1]
            else:
                in1 = pp_sb[:, abs_start: abs_stop: -1]
            emit("vector", [s_pe[kt], pp_ready[h % 2]], [("dve", 1)],
                 lambda e, coff=coff, llo=llo, lhi=lhi, in1=in1:
                 e.tensor_tensor(s2_ps[:, coff + llo: coff + lhi],
                                 s2_ps[:, coff + llo: coff + lhi], in1, op=OP.add))
            band_dve[kt] = mk("dve")

        def exps(kt, coff, slot):
            lo, hi = band_range(kt, qh)
            segs = []
            if hi <= lo:
                segs = [(0, 1024, "lo" if 128 * (kt + 2) <= 1024 * qh else "hi")]
            else:
                llo, lhi = lo - 1024 * qh, hi - 1024 * qh
                if llo > 0:
                    segs.append((0, llo, "hi"))
                segs.append((llo, lhi, "band"))
                if lhi < 1024:
                    segs.append((lhi, 1024, "lo"))
            w = [s_pe[kt], pt_reader[slot]]
            if kt in band_dve:
                w.append(band_dve[kt])
            nseg = len(segs) + (1 if DBG_ACT_FLUSH else 0)
            for i, (a, b_, kind) in enumerate(segs):
                inc = [("act", 1)] if (i == len(segs) - 1 and not DBG_ACT_FLUSH) else []
                if kind == "band":
                    bias = None
                elif kind == "hi":
                    bias = ca_sb[:, 16 + h:16 + h + 1]
                else:
                    bias = ca_sb[:, h:h + 1]
                if bias is None:
                    emit("scalar", w if i == 0 else [], inc,
                         lambda e, a=a, b_=b_, coff=coff, slot=slot:
                         e.activation(pt_sb[:, slot * 1024 + a: slot * 1024 + b_],
                                      s2_ps[:, coff + a: coff + b_],
                                      AF.Exp, scale=1.0))
                else:
                    emit("scalar", w if i == 0 else [], inc,
                         lambda e, a=a, b_=b_, coff=coff, slot=slot, bias=bias:
                         e.activation(pt_sb[:, slot * 1024 + a: slot * 1024 + b_],
                                      s2_ps[:, coff + a: coff + b_],
                                      AF.Exp, bias=bias, scale=1.0))
            if DBG_ACT_FLUSH:
                emit("scalar", [], [("act", 1)],
                     lambda e, slot=slot: e.activation(
                         st_sb[0:1, 158:159],
                         pt_sb[0:1, slot * 1024: slot * 1024 + 1],
                         AF.Identity, scale=1.0))
            exp_marks[kt] = mk("act")
            key = "A" if coff == 0 else "B"
            s_free[key] = mk("act")

        def ctx_mms(kt, slot):
            vo = gb * 4160 + (h % 4) * 1040 + kt * 65
            for half in range(2):
                w = []
                if half == 0:
                    w = [exp_marks[kt], vh_ready[gb]]
                    if kt == 0:
                        w.append(ctx_gate[0])
                emit("tensor", w, [("pe", 1)] if half == 1 else [],
                     lambda e, kt=kt, slot=slot, half=half, vo=vo:
                     e.matmul(ctx_ps[0:65, half * 512: half * 512 + 512],
                              vh_sb[:, vo: vo + 65],
                              pt_sb[:, slot * 1024 + half * 512: slot * 1024 + half * 512 + 512],
                              start=(kt == 0), stop=(kt == NT - 1)))
            pt_reader[slot] = mk("pe")
            if kt == NT - 1:
                ctx_done[0] = mk("pe")

        prev = None
        for ktp in range(8):
            ktA, ktB = 2 * ktp, 2 * ktp + 1
            slA = kt_slot[0] % 4
            kt_slot[0] += 1
            slB = kt_slot[0] % 4
            kt_slot[0] += 1
            s_mms(ktp)
            band_add(ktA, 0)
            band_add(ktB, 1024)
            exps(ktA, 0, slA)
            exps(ktB, 1024, slB)
            if DBG_DUMP and item == DBG_DUMP_ITEM and h == 0 and qh == 0 and ktp == 0:
                emit("gpsimd", [mk("act")], [("gdbg", 16)],
                     lambda e, slA=slA, slB=slB: e.dma_start(
                         out=dbg["d_pt"], in_=pt_sb[:, :]))
                pt_reader[slA] = mk("gdbg")
                pt_reader[slB] = mk("gdbg")
            if prev is not None:
                ctx_mms(prev[0], prev[2])
                ctx_mms(prev[1], prev[3])
            if div_pending:
                div_pending.pop(0)()
            bgq_pop(2 if (ktp & 1) else 1)
            prev = (ktA, ktB, slA, slB)
        ctx_mms(prev[0], prev[2])
        ctx_mms(prev[1], prev[3])

    def queue_division(item, h, qh):
        ch = h // 2
        odd = h % 2 == 1
        pe_ctx = ctx_done[0]
        st = {}

        def d1():
            emit("vector", [pe_ctx, div_tt_gate[0]], [("dve", 1)],
                 lambda e: e.tensor_copy(ctxu_sb[0:65, :], ctx_ps[0:65, :]))
            ctx_gate[0] = mk("dve")
            st["d1"] = mk("dve")

        def d2():
            if DBG_OLD_RECIP:
                emit("scalar", [st["d1"]], [],
                     lambda e: e.activation(aux_sb[64:65, 0:1024],
                                            ctxu_sb[64:65, 0:1024], AF.Ln, scale=1.0))
                emit("scalar", [], [("act", 1)],
                     lambda e: e.activation(ctxu_sb[64:65, 0:1024],
                                            aux_sb[64:65, 0:1024], AF.Exp, scale=-1.0))
                st["d2"] = mk("act")
                return
            scat_src = ap3(ctxu_sb, 64 * 1024, [[1024, 1], [16, 64], [1, 16]])
            emit("gpsimd", [st["d1"], div_r2_free[0]] + pp_tp_fence[0],
                 [("gdiv", 16)],
                 lambda e: e.dma_start(out=st_sb[64:128, 128:144], in_=scat_src))
            emit("vector", [mk("gdiv")], [("dve", 1)],
                 lambda e: e.reciprocal(st_sb[64:128, 128:144],
                                        st_sb[64:128, 128:144]))
            gat_src = ap3(st_sb, 64 * 160 + 128, [[160, 64], [1, 16]])
            emit("gpsimd", [mk("dve")], [("gdiv", 16)],
                 lambda e: e.dma_start(out=ctxu_sb[64:65, 0:1024], in_=gat_src))
            st["d2"] = mk("gdiv")
            div_r2_free[0] = st["d2"]

        def d3(half):
            emit("tensor", [st["d2"], misc_gate[0]], [("pe", 1)],
                 lambda e, half=half: e.matmul(
                     msc_ps[0:64, 0:512], oc_sb[64:65, 0:64],
                     ctxu_sb[64:65, half * 512: half * 512 + 512],
                     start=True, stop=True))
            st["d3"] = mk("pe")

        def d4(half):
            if odd:
                out = ln_sb[0:64, qh * 1024 + half * 512: qh * 1024 + half * 512 + 512]
            else:
                out = ctxT[0:64, ch * S + qh * 1024 + half * 512:
                           ch * S + qh * 1024 + half * 512 + 512]
            w = [st["d3"]]
            if odd:
                w.append(ln_stage_free[qh])
            emit("vector", w, [("dve", 1)],
                 lambda e, out=out, half=half: e.tensor_tensor(
                     out, ctxu_sb[0:64, half * 512: half * 512 + 512],
                     msc_ps[0:64, 0:512], op=OP.mult))
            misc_gate[0] = mk("dve")
            div_tt_gate[0] = mk("dve")
            st["d4"] = mk("dve")

        def d5():
            emit("gpsimd", [st["d4"]] + pp_tp_fence[0], [(f"gc{qh}", 16)],
                 lambda e: e.dma_start(
                     out=ctxT[64:128, ch * S + qh * 1024: ch * S + qh * 1024 + 1024],
                     in_=ln_sb[0:64, qh * 1024: qh * 1024 + 1024]))
            ln_stage_free[qh] = mk(f"gc{qh}")

        def d34(half):
            d3(half)
            d4(half)
        seq = [d1, d2, lambda: d34(0), lambda: d34(1)]
        if odd:
            seq.append(d5)
        div_pending.extend(seq)

    def phase_B(item):
        for h in range(H):
            buf = h % 2
            if 1 <= h and h + 1 < H:
                queue_head_work(h + 1)
            nh = (h + 1) % H
            emit_patp(nh, [pp_freed[nh % 2]] + pp_tp_fence[0])
            if h % 4 == 1 and h // 4 + 1 <= 3:
                g = h // 4 + 1
                for tt in range(NT):
                    bgq.append(lambda g=g, tt=tt: emit_v_tile(g, tt))
            if h not in qkv_done:
                bgq_flush()
            assert h in qkv_done, f"QKV for head {h} not emitted"
            for qh in range(2):
                emit_sweep(item, h, qh)
                queue_division(item, h, qh)
            pp_freed[h % 2] = mk("dve")
            qt_freed[buf] = mk("pe")
            qs_freed[buf] = mk("pe")
            if h % 4 == 3:
                vh_free[(h // 4) % 2] = mk("pe")
        while div_pending:
            div_pending.pop(0)()
        bgq_flush()

    pp_tp_fence = [[]]

    def phase_C1(item, next_item):
        base = item * S
        div_done = mk("dve")
        div_gp = [mk("gc0"), mk("gc1"), mk("gdiv"), wo_done_c[0]]
        misc_gate[0] = div_done
        qkv_gate[0] = max(qkv_gate[0], div_done, key=lambda t: t[1])
        bank_gate = [div_done, div_done]
        xf = [mk("dve"), mk("dve")]
        xbf = [[mk("gy0"), mk("gx0")], [mk("gy1"), mk("gx1")]]
        st8 = a_begin(next_item) if next_item is not None else None
        for tt in range(NT):
            buf = tt % 2
            emit("sync", [xf[buf], xa_free[buf]], [(f"sxa{buf}", 16)],
                 lambda e, tt=tt, buf=buf, base=base: e.dma_start(
                     out=xa_sb[:, buf * 1024: buf * 1024 + 1024],
                     in_=x8[base + tt * 128: base + tt * 128 + 128, :]))
            sy_m = mk(f"sxa{buf}")
            for dh in range(2):
                ps = msc_ps if dh == 0 else qkv_ps
                for ch in range(NCH):
                    w2 = []
                    if ch == 0:
                        w2 = [bank_gate[dh]] + div_gp
                        if dh == 1:
                            w2.append(qkv_gate[0])
                    emit("tensor", w2, [("pe", 1)] if ch == NCH - 1 else [],
                         lambda e, ch=ch, tt=tt, dh=dh, ps=ps: e.matmul(
                             ps[:, 0:512],
                             ctxT[:, ch * S + tt * 128: ch * S + tt * 128 + 128],
                             wo_sb[:, ch * D + dh * 512: ch * D + dh * 512 + 512],
                             start=(ch == 0), stop=(ch == NCH - 1)))
                emit("vector", [mk("pe"), sy_m] + xbf[buf], [("dve", 1)],
                     lambda e, buf=buf, dh=dh, ps=ps: e.tensor_tensor(
                         xb_sb[:, buf * 1024 + dh * 512: buf * 1024 + dh * 512 + 512],
                         ps[:, 0:512],
                         xa_sb[:, buf * 1024 + dh * 512: buf * 1024 + dh * 512 + 512],
                         op=OP.add))
                bank_gate[dh] = mk("dve")
                if dh == 1:
                    qkv_gate[0] = mk("dve")
            xf[buf] = mk("dve")
            xa_free[buf] = mk("dve")
            sc = 64 + 2 * tt
            emit("vector", [mk("dve")], [],
                 lambda e, buf=buf: e.bn_stats(
                     st_sb[:, 36:42], xb_sb[:, buf * 1024: buf * 1024 + 512]))
            emit("vector", [], [("dve", 1)],
                 lambda e, buf=buf: e.bn_stats(
                     st_sb[:, 42:48], xb_sb[:, buf * 1024 + 512: buf * 1024 + 1024]))
            emit("vector", [mk("dve")], [("dve", 1)],
                 lambda e, sc=sc: e.bn_aggr(st_sb[:, sc:sc + 2], st_sb[:, 36:48]))
            emit("gpsimd", [mk("dve")], [(f"gx{buf}", 16)],
                 lambda e, tt=tt, buf=buf: e.dma_start(
                     out=x2_d[tt * 128: tt * 128 + 128, :],
                     in_=xb_sb[:, buf * 1024: buf * 1024 + 1024]))
            xbf[buf] = [mk(f"gx{buf}")]
            if st8 is not None:
                a_tile(st8, tt)
        if st8 is not None:
            a_end(st8)
        emit("scalar", [mk("dve")], [("act", 1)],
             lambda e: e.activation(st_sb[:, 96:112], st_sb[:, 65:97:2],
                                    AF.Ln, bias=eps_sb[:, 0:1], scale=1.0))
        emit("scalar", [mk("act")], [("act", 1)],
             lambda e: e.activation(st_sb[:, 112:128], st_sb[:, 96:112],
                                    AF.Exp, scale=-0.5))
        rstd2 = mk("act")
        emit("vector", [rstd2], [("dve", 1)],
             lambda e: e.scalar_tensor_tensor(
                 st_sb[:, 48:64], st_sb[:, 64:96:2], -1.0, st_sb[:, 112:128],
                 op0=OP.mult, op1=OP.mult))
        div_tt_gate[0] = mk("dve")
        return mk("dve"), [mk("gx0"), mk("gx1")], st8

    def phase_D(item, nm2_done, x2_ready, a_next):
        base = item * S
        load_table(id_exp)
        xbfree = [nm2_done, nm2_done]
        lnfree = [("pe", cnt["pe"]), ("pe", cnt["pe"])]
        ltfree = [("pe", 0), ("pe", 0)]
        misc_gate[0] = mk("dve")
        qkv_gate[0] = mk("dve")
        tanh_done = [("act", 0)]
        tanh_war = [mk("gdiv")]
        yfree = [("gy0", 0), ("gy1", 0)]
        for tt in range(NT):
            buf = tt % 2
            emit("sync", [xbfree[buf]] + x2_ready, [(f"sxb{buf}", 16)],
                 lambda e, tt=tt, buf=buf: e.dma_start(
                     out=xb_sb[:, buf * 1024: buf * 1024 + 1024],
                     in_=x2_d[tt * 128: tt * 128 + 128, :]))
            sy_m = mk(f"sxb{buf}")
            emit("vector", [sy_m, nm2_done, lnfree[buf]], [("dve", 1)],
                 lambda e, tt=tt, buf=buf: e.tensor_scalar(
                     qs_sb[:, buf * 1024: buf * 1024 + 1024],
                     xb_sb[:, buf * 1024: buf * 1024 + 1024],
                     st_sb[:, 112 + tt:113 + tt], st_sb[:, 48 + tt:49 + tt],
                     op0=OP.mult, op1=OP.add))
            dve_m = mk("dve")
            src_ = ap3(qs_sb, buf * 1024, [[2048, 128], [128, 8], [1, 128]])
            dst_ = ap3(lt_sb, buf * 1024, [[2048, 128], [128, 8], [1, 128]])
            wt = [dve_m, ltfree[buf]]
            if tt == 0:
                wt.extend(d_tp_fence[0])
            emit("sync", wt, [(f"tpd{buf}", 16)],
                 lambda e, dst_=dst_, src_=src_: e.dma_start(out=dst_, in_=src_,
                                                             transpose=True))
            tp_m = mk(f"tpd{buf}")
            lnfree[buf] = tp_m
            for dh in range(2):
                ps, gate_get, gate_cell = acc_pick()
                for chm in range(NCH):
                    w3 = []
                    if chm == 0:
                        w3 = [tp_m, gate_get()]
                    emit("tensor", w3, [("pe", 1)] if chm == NCH - 1 else [],
                         lambda e, chm=chm, buf=buf, dh=dh, ps=ps: e.matmul(
                             ps[:, 0:512],
                             lt_sb[:, buf * 1024 + chm * 128: buf * 1024 + chm * 128 + 128],
                             m_sb[:, chm * D + dh * 512: chm * D + dh * 512 + 512],
                             start=(chm == 0), stop=(chm == NCH - 1)))
                pe_m = mk("pe")
                emit("vector", [pe_m, tanh_done[0]], [("dve", 1)],
                     lambda e, dh=dh, ps=ps: e.tensor_tensor(
                         ctxu_sb[:, dh * 512: dh * 512 + 512], ps[:, 0:512],
                         br_sb[:, dh * 512: dh * 512 + 512], op=OP.add))
                gate_cell[0] = mk("dve")
                emit("scalar", [mk("dve"), tanh_war[0]], [("act", 1)],
                     lambda e, dh=dh: e.activation(
                         aux_sb[:, dh * 512: dh * 512 + 512],
                         ctxu_sb[:, dh * 512: dh * 512 + 512],
                         AF.Tanh, scale=1.0))
                tanh_done[0] = mk("act")
                emit("vector", [mk("act"), yfree[buf]], [("dve", 1)],
                     lambda e, buf=buf, dh=dh: e.tensor_tensor(
                         xb_sb[:, buf * 1024 + dh * 512: buf * 1024 + dh * 512 + 512],
                         xb_sb[:, buf * 1024 + dh * 512: buf * 1024 + dh * 512 + 512],
                         aux_sb[:, dh * 512: dh * 512 + 512], op=OP.add))
                tanh_war[0] = mk("dve")
            emit("gpsimd", [mk("dve")], [(f"gy{buf}", 16)],
                 lambda e, tt=tt, buf=buf, base=base: e.dma_start(
                     out=y8[base + tt * 128: base + tt * 128 + 128, :],
                     in_=xb_sb[:, buf * 1024: buf * 1024 + 1024]))
            yfree[buf] = mk(f"gy{buf}")
            xbfree[buf] = mk(f"gy{buf}")
            ltfree[buf] = ("pe", cnt["pe"])
        load_table(id_nle)
        qkv_gate[0] = mk("dve")
        misc_gate[0] = mk("dve")
        qs_freed[0] = mk("tpd0")
        qs_freed[1] = mk("tpd1")
        return

    d_tp_fence = [[]]

    a_next_state = None
    for item in range(min(B_PER, DBG_ITEMS)):
        if DBG_STAGE == "prologue":
            break
        if a_next_state is None:
            a_next_state = phase_A(item)
        if DBG_DUMP and item == DBG_DUMP_ITEM:
            emit("gpsimd", [lnT_tp[NT - 2], lnT_tp[NT - 1]], [("gdbg", 16)],
                 lambda e: e.dma_start(out=dbg["d_ln1t"], in_=ln1T[:, :]))
        if DBG_STAGE == "A":
            a_next_state = None
            continue
        bgq_flush()
        if item == 0:
            wdup_done_c[0] = [mk("gw0"), mk("gw1")]
            pt_reader[0] = mk("gw1")
            pt_reader[1] = mk("gw1")
        emit_qk_stream(0)
        for proj in range(2):
            for qb in range(4):
                emit_qkv_group(0, proj, qb)
        for tt in range(NT):
            emit_v_tile(0, tt)
        if DBG_DUMP and item == DBG_DUMP_ITEM:
            emit("gpsimd", [mk("dve")], [("gdbg", 16)],
                 lambda e: e.dma_start(out=dbg["d_qt"], in_=qt_sb[:, :]))
            emit("gpsimd", [], [("gdbg", 16)],
                 lambda e: e.dma_start(out=dbg["d_kt"], in_=kt_sb[:, :]))
            emit("gpsimd", [], [("gdbg", 16)],
                 lambda e: e.dma_start(out=dbg["d_vh"], in_=vh_sb[:, 0:8320]))
        queue_head_work(1)
        pp_tp_fence[0] = [mk("tpa0"), mk("tpa1"), mk("tpd0"), mk("tpd1"),
                          mk("tpw")]
        phase_B(item)
        if DBG_DUMP and item == DBG_DUMP_ITEM:
            emit("gpsimd", [mk("dve"), mk("gc0"), mk("gc1")], [("gdbg", 16)],
                 lambda e: e.dma_start(out=dbg["d_ctxt"], in_=ctxT[:, :]))
        if DBG_STAGE == "B":
            a_next_state = None
            continue
        nxt = item + 1 if item + 1 < min(B_PER, DBG_ITEMS) else None
        nm2_done, x2_ready, a_next_state = phase_C1(item, nxt)
        if DBG_STAGE == "C1":
            continue
        d_tp_fence[0] = [mk("gpp0"), mk("gpp1"), mk("gc0"), mk("gc1"),
                         mk("gdiv")]
        phase_D(item, nm2_done, x2_ready, a_next_state)

    emit("gpsimd", [], [], lambda e: e.wait_ge(sems["gy0"], cnt["gy0"]))
    emit("gpsimd", [], [], lambda e: e.wait_ge(sems["gy1"], cnt["gy1"]))

    if DBG_PREFIX is not None:
        del OPS[DBG_PREFIX:]
    print(f"[kernel] total ops: {len(OPS)}")

    with nc.Block() as block:
        for eng_name in ["sync", "gpsimd", "tensor", "vector", "scalar"]:
            def make(eng_name):
                def run(eng):
                    for e_, fn in OPS:
                        if e_ == eng_name:
                            fn(eng)
                return run
            getattr(block, eng_name)(make(eng_name))

    return nc


_PROG = None


def kernel(L0_attn_hidden_states, L0_attn_Wq, L0_attn_Wk, L0_attn_Wv, L0_attn_Wo,
           L0_attn_rel_emb, L0_ffn_weight, L0_ffn_aux_weight1, L0_ffn_aux_weight2,
           L0_ffn_bias):
    global _PROG
    x = np.ascontiguousarray(np.asarray(L0_attn_hidden_states, dtype=np.float32))
    onehot, eye16 = _host_consts()
    shared = dict(
        wq=np.ascontiguousarray(np.asarray(L0_attn_Wq, np.float32)),
        wk=np.ascontiguousarray(np.asarray(L0_attn_Wk, np.float32)),
        wv=np.ascontiguousarray(np.asarray(L0_attn_Wv, np.float32)),
        wo=np.ascontiguousarray(np.asarray(L0_attn_Wo, np.float32)),
        wf=np.ascontiguousarray(np.asarray(L0_ffn_weight, np.float32)),
        wa1=np.ascontiguousarray(np.asarray(L0_ffn_aux_weight1, np.float32)),
        wa2=np.ascontiguousarray(np.asarray(L0_ffn_aux_weight2, np.float32)),
        fbias=np.ascontiguousarray(np.asarray(L0_ffn_bias, np.float32).reshape(1, D)),
        rel=np.ascontiguousarray(np.asarray(L0_attn_rel_emb, np.float32)),
        onehot=onehot, eye16=eye16,
    )
    if _PROG is None:
        _PROG = build_program()
    in_maps = []
    for c in range(N_CORES):
        m = dict(shared)
        m["x"] = np.ascontiguousarray(
            x[c * B_PER:(c + 1) * B_PER].reshape(B_PER * S, D))
        in_maps.append(m)
    res = run_bass_kernel_spmd(_PROG, in_maps, core_ids=list(range(N_CORES)))
    out = np.concatenate(
        [np.asarray(r["y"]).reshape(B_PER, S, D) for r in res.results], axis=0)
    return out.astype(np.float32)


# version 09
# speedup vs baseline: 1.1565x; 1.0027x over previous
import math
import numpy as np
import bass_rust
import concourse.bass as bass
import concourse.mybir as mybir
from concourse.bass_utils import run_bass_kernel_spmd

F32 = mybir.dt.float32
BF16 = mybir.dt.bfloat16
AF = mybir.ActivationFunctionType
OP = mybir.AluOpType

N_CORES = 8
B_PER = 4
S = 2048
D = 1024
H = 16
DK = 64
NT = S // 128
NCH = D // 128
NREV = 511
C_SHIFT = 12.0
EPS = 1e-5

NUM_BUCKETS = 32
MAX_DISTANCE = 128

DBG_ITEMS = B_PER
DBG_STAGE = "full"
DBG_PREFIX = None
DBG_DUMP = False
DBG_DUMP_ITEM = 0
DBG_NO_ROT = False
DBG_OLD_RECIP = False
DBG_ACT_FLUSH = False


def _rel_bucket_host(d):
    import jax
    import jax.numpy as jnp
    with jax.default_device(jax.devices("cpu")[0]):
        num_buckets = NUM_BUCKETS // 2
        rel = jnp.asarray(d, dtype=jnp.int32)
        relative_buckets = (rel > 0).astype(jnp.int32) * num_buckets
        rp = jnp.abs(rel)
        max_exact = num_buckets // 2
        is_small = rp < max_exact
        rp_large = max_exact + (
            jnp.log(rp.astype(jnp.float32) / max_exact)
            / math.log(MAX_DISTANCE / max_exact)
            * (num_buckets - max_exact)
        ).astype(jnp.int32)
        rp_large = jnp.minimum(rp_large, num_buckets - 1)
        relative_buckets = relative_buckets + jnp.where(is_small, rp, rp_large)
        return np.asarray(relative_buckets)


def _host_consts():
    d = np.arange(NREV, dtype=np.int64) - 255
    bk = _rel_bucket_host(d)
    onehot = np.zeros((NUM_BUCKETS, NREV), dtype=np.float32)
    onehot[bk, np.arange(NREV)] = 1.0
    eye16 = np.eye(16, dtype=np.float32)
    return onehot, eye16


def build_program():
    nc = bass.Bass("TRN2", target_bir_lowering=False)

    x8 = nc.dram_tensor("x", [B_PER * S, D], F32, kind="ExternalInput").ap()
    wq_d = nc.dram_tensor("wq", [D, D], F32, kind="ExternalInput").ap()
    wk_d = nc.dram_tensor("wk", [D, D], F32, kind="ExternalInput").ap()
    wv_d = nc.dram_tensor("wv", [D, D], F32, kind="ExternalInput").ap()
    wo_d = nc.dram_tensor("wo", [D, D], F32, kind="ExternalInput").ap()
    wf_d = nc.dram_tensor("wf", [D, D], F32, kind="ExternalInput").ap()
    wa1_d = nc.dram_tensor("wa1", [D, D], F32, kind="ExternalInput").ap()
    wa2_d = nc.dram_tensor("wa2", [D, D], F32, kind="ExternalInput").ap()
    fb_d = nc.dram_tensor("fbias", [1, D], F32, kind="ExternalInput").ap()
    rel_d = nc.dram_tensor("rel", [NUM_BUCKETS, H], F32, kind="ExternalInput").ap()
    oh_d = nc.dram_tensor("onehot", [NUM_BUCKETS, NREV], F32, kind="ExternalInput").ap()
    eye_d = nc.dram_tensor("eye16", [16, 16], F32, kind="ExternalInput").ap()
    y8 = nc.dram_tensor("y", [B_PER * S, D], F32, kind="ExternalOutput").ap()

    dbg = {}
    if DBG_DUMP:
        for nm, shp, dt in [("d_ln1t", [128, NCH * S], BF16),
                            ("d_qt", [128, 2 * S], BF16),
                            ("d_kt", [128, 2 * S], BF16),
                            ("d_vh", [128, 8320], BF16),
                            ("d_pt", [128, 4096], BF16),
                            ("d_ctxt", [128, NCH * S], BF16),
                            ("d_rev", [16, NREV], F32),
                            ("d_ca", [128, 32], F32),
                            ("d_pp", [128, 768], BF16)]:
            dbg[nm] = nc.dram_tensor(nm, shp, dt, kind="ExternalOutput").ap()

    qdup_d = nc.dram_tensor("qdup", [NCH, 128, H, 128], BF16, kind="Internal").ap()
    kdup_d = nc.dram_tensor("kdup", [NCH, 128, H, 128], BF16, kind="Internal").ap()
    x2_d = nc.dram_tensor("x2scr", [S, D], F32, kind="Internal").ap()
    rcp_d = nc.dram_tensor("rcpscr", [1, 1024], F32, kind="Internal").ap()

    A = nc.alloc_sbuf_tensor
    ln1T = A("ln1T", [128, NCH * S], BF16)
    ctxT = A("ctxT", [128, NCH * S], BF16)
    wv_sb = A("wv_sb", [128, NCH * D], BF16)
    wo_sb = A("wo_sb", [128, NCH * D], BF16)
    m_sb = A("m_sb", [128, NCH * D], BF16)
    qt_sb = A("qt_sb", [128, 2 * S], BF16)
    kt_sb = A("kt_sb", [128, 2 * S], BF16)
    qs_sb = A("qs_sb", [128, 2 * NCH * 128], BF16)
    ks_sb = A("ks_sb", [128, 2 * NCH * 128], BF16)
    vh_sb = A("vh_sb", [128, 2 * 4 * (NT * 65)], BF16)
    pt_sb = A("pt_sb", [128, 4 * 1024], BF16)
    pp_sb = A("pp_sb", [128, 2 * 384], BF16)
    xa_sb = A("xa_sb", [128, 2 * 1024], F32)
    xb_sb = A("xb_sb", [128, 2 * 1024], F32)
    ctxu_sb = A("ctxu_sb", [128, 1024], F32)
    aux_sb = A("aux_sb", [128, 1024], F32)
    ln_sb = A("ln_sb", [128, 2 * 1024], BF16)
    lt_sb = A("lt_sb", [128, 2 * 1024], BF16)
    br_sb = A("br_sb", [128, D], F32)
    ca_sb = A("ca_sb", [128, 32], F32)
    st_sb = A("st_sb", [128, 160], F32)
    rev_sb = A("rev_sb", [16, NREV], F32)
    oh_sb = A("oh_sb", [NUM_BUCKETS, NREV], F32)
    rel_sb = A("rel_sb", [NUM_BUCKETS, H], F32)
    eye_sb = A("eye_sb", [16, 16], F32)
    or_sb = A("or_sb", [1, 128], F32)
    oc_sb = A("oc_sb", [128, 64], F32)
    eps_sb = A("eps_sb", [128, 1], F32)

    P = nc.alloc_psum_tensor
    s2_ps = P("s2_ps", [128, 2048], F32)
    ctx_ps = P("ctx_ps", [128, 1024], F32)
    qkv_ps = P("qkv_ps", [128, 512], F32)
    msc_ps = P("msc_ps", [128, 512], F32)

    sem_names = ["ssm", "sxa0", "sxa1", "sxb0", "sxb1",
                 "tpa0", "tpa1", "tpd0", "tpd1", "tpw",
                 "gq0", "gq1", "gw0", "gw1", "gpp0", "gpp1",
                 "gdiv", "gc0", "gc1", "gx0", "gx1", "gy0", "gy1", "gdbg",
                 "gld0", "gld1", "gld2", "gld3", "gld4", "gdv2",
                 "pe", "act", "dve"]
    sems = {n: nc.alloc_semaphore(n) for n in sem_names}
    cnt = {n: 0 for n in sem_names}
    OPS = []

    def emit(engine, waits, incs, body):
        wl = []
        for s, v in waits:
            assert v <= cnt[s], f"wait on future: {s} {v} > {cnt[s]}"
            if v > 0:
                wl.append((sems[s], v))

        def fn(eng, wl=wl, incs=incs, body=body):
            for sh, v in wl:
                eng.wait_ge(sh, v)
            inst = body(eng)
            for s, k in incs:
                inst = inst.then_inc(sems[s], k)
        OPS.append((engine, fn))
        for s, k in incs:
            cnt[s] += k

    def mk(s):
        return (s, cnt[s])

    def ap3(t, off, dims):
        return bass_rust.AP(t[:].tensor, off, dims)

    from concourse.hw_specs import get_activation_tables
    tabs = list(get_activation_tables(nc.m.arch).items())
    id_nle = next(i for i, (n, f) in enumerate(tabs) if AF.Exp in f and AF.Ln in f)
    id_exp = next(i for i, (n, f) in enumerate(tabs) if AF.Exp in f and AF.Tanh in f)

    def load_table(idx):
        emit("scalar", [], [], lambda e, idx=idx: e.add_instruction(
            mybir.InstLoadActFuncSet(name=nc.get_next_instruction_name(),
                                     act_func_set_id=idx, ins=[], outs=[])))

    emit("vector", [], [], lambda e: e.memset(or_sb[0:1, :], 1.0))
    emit("vector", [], [], lambda e: e.memset(oc_sb[:, :], 1.0))
    for g2 in range(2):
        for hl in range(4):
            off = g2 * 4160 + hl * 1040 + 64
            apm = ap3(vh_sb, off, [[8320, 128], [65, NT]])
            emit("vector", [], [], lambda e, apm=apm: e.memset(apm, 1.0))
    emit("vector", [], [], lambda e: e.memset(eps_sb[:, :], EPS))
    emit("vector", [], [("dve", 1)], lambda e: e.memset(st_sb[:, :], 0.0))
    ones_done = mk("dve")

    load_table(id_nle)

    emit("sync", [], [("ssm", 16)], lambda e: e.dma_start(out=oh_sb[:, :], in_=oh_d))
    emit("sync", [], [("ssm", 16)], lambda e: e.dma_start(out=rel_sb[:, :], in_=rel_d))
    emit("sync", [], [("ssm", 16)], lambda e: e.dma_start(out=eye_sb[:, :], in_=eye_d))
    emit("sync", [], [("ssm", 16)], lambda e: e.dma_start(out=ctxu_sb[0:1, 0:D], in_=fb_d))
    small_ld = mk("ssm")

    emit("tensor", [small_ld], [("pe", 1)],
         lambda e: e.matmul(msc_ps[0:16, 0:NREV], rel_sb[0:32, 0:16],
                            oh_sb[0:32, 0:NREV], start=True, stop=True))
    emit("scalar", [mk("pe")], [("act", 1)],
         lambda e: e.activation(rev_sb[0:16, :], msc_ps[0:16, 0:NREV], AF.Copy,
                                bias=-C_SHIFT, scale=1.0))
    rev_done = mk("act")

    pp_ready = [None, None]
    pp_freed = [("dve", 0), ("dve", 0)]

    def emit_patp(h, waits):
        pbuf = h % 2
        sem = f"gpp{pbuf}"
        src = ap3(rev_sb, h * NREV, [[NREV, 1], [1, 128], [1, 384]])
        dst = pp_sb[:, pbuf * 384:(pbuf + 1) * 384]
        emit("gpsimd", waits, [(sem, 16)],
             lambda e, dst=dst, src=src: e.dma_start(out=dst, in_=src))
        pp_ready[pbuf] = mk(sem)

    emit_patp(0, [rev_done])
    emit_patp(1, [rev_done])
    patp_prologue = [mk("gpp0"), mk("gpp1")]

    emit("tensor", [rev_done], [],
         lambda e: e.matmul(msc_ps[0:1, 0:16], rev_sb[0:16, 0:1],
                            eye_sb[0:16, 0:16], start=True, stop=True))
    emit("tensor", [], [("pe", 1)],
         lambda e: e.matmul(msc_ps[0:1, 16:32], rev_sb[0:16, 510:511],
                            eye_sb[0:16, 0:16], start=True, stop=True))
    emit("vector", [mk("pe"), ones_done], [("dve", 1)],
         lambda e: e.tensor_copy(st_sb[0:1, 0:32], msc_ps[0:1, 0:32]))
    emit("tensor", [mk("dve")], [("pe", 1)],
         lambda e: e.matmul(msc_ps[0:128, 0:32], or_sb[0:1, 0:128],
                            st_sb[0:1, 0:32], start=True, stop=True))
    emit("vector", [mk("pe")], [("dve", 1)],
         lambda e: e.tensor_copy(ca_sb[:, :], msc_ps[:, 0:32]))

    for half in range(2):
        emit("tensor", [small_ld, ones_done, mk("dve")], [("pe", 1)],
             lambda e, half=half: e.matmul(
                 qkv_ps[:, 0:512], or_sb[0:1, 0:128],
                 ctxu_sb[0:1, half * 512:(half + 1) * 512], start=True, stop=True))
        emit("vector", [mk("pe")], [("dve", 1)],
             lambda e, half=half: e.tensor_copy(
                 br_sb[:, half * 512:(half + 1) * 512], qkv_ps[:, 0:512]))
    pro1_done = mk("dve")

    wbf, abf = ctxT, ln1T
    xa_free = [pro1_done, pro1_done]

    _gld_i = [0]

    def cast_load(src_d, dst_t, dst_off, waits):
        sem = f"gld{_gld_i[0]}"
        _gld_i[0] += 1
        pitch = dst_t[:].ap[0][0]
        s_ap = bass_rust.AP(src_d.tensor, 0,
                            [[1024, 128], [128 * 1024, NCH], [1, 1024]])
        d_ap = bass_rust.AP(dst_t[:].tensor, dst_off,
                            [[pitch, 128], [1024, NCH], [1, 1024]])
        emit("gpsimd", waits, [(sem, 16)],
             lambda e, d_ap=d_ap, s_ap=s_ap: e.dma_start(out=d_ap, in_=s_ap))
        return mk(sem)

    w_cast = cast_load(wf_d, wbf, 0, [pro1_done])
    for i in range(NCH):
        src = ap3(wbf, i * 1024, [[NCH * S, 128], [128, 8], [1, 128]])
        dst = ap3(wbf, 8192 + i * 128, [[NCH * S, 128], [1024, 8], [1, 128]])
        w = [w_cast]
        if i == 0:
            w.extend(patp_prologue)
        emit("sync", w, [("tpw", 16)],
             lambda e, dst=dst, src=src: e.dma_start(out=dst, in_=src, transpose=True))
    wt_done = mk("tpw")
    a1_done = cast_load(wa1_d, abf, 0, [pro1_done])

    def mm_chain(lhs_fn, rhs_fn, ps, dst_t, dst_fn, extra_first_waits, gate0):
        gate = gate0
        for ot in range(NCH):
            for ob in range(2):
                for ch in range(NCH):
                    w = []
                    if ch == 0:
                        w = list(extra_first_waits) + [gate]
                    emit("tensor", w, [("pe", 1)] if ch == NCH - 1 else [],
                         lambda e, ot=ot, ob=ob, ch=ch, ps=ps: e.matmul(
                             ps[:, 0:512], lhs_fn(ch, ot), rhs_fn(ch, ob),
                             start=(ch == 0), stop=(ch == NCH - 1)))
                emit("vector", [mk("pe")], [("dve", 1)],
                     lambda e, ot=ot, ob=ob, dst_t=dst_t: e.tensor_copy(
                         dst_fn(dst_t, ot, ob), qkv_ps[:, 0:512] if ps is qkv_ps
                         else msc_ps[:, 0:512]))
                gate = mk("dve")
        return mk("dve")

    m1t_done = mm_chain(
        lambda ch, jt: abf[:, ch * 1024 + jt * 128: ch * 1024 + jt * 128 + 128],
        lambda ch, cb: wbf[:, 8192 + ch * 1024 + cb * 512: 8192 + ch * 1024 + cb * 512 + 512],
        msc_ps, abf,
        lambda t, jt, cb: t[:, 8192 + jt * 1024 + cb * 512: 8192 + jt * 1024 + cb * 512 + 512],
        [a1_done, wt_done], a1_done)
    a2_done = cast_load(wa2_d, abf, 0, [m1t_done])
    m_done = mm_chain(
        lambda ch, ct: abf[:, 8192 + ch * 1024 + ct * 128: 8192 + ch * 1024 + ct * 128 + 128],
        lambda ch, jb: abf[:, ch * 1024 + jb * 512: ch * 1024 + jb * 512 + 512],
        qkv_ps, m_sb,
        lambda t, ct, jb: t[:, ct * 1024 + jb * 512: ct * 1024 + jb * 512 + 512],
        [a2_done], a2_done)


    if DBG_DUMP:
        emit("gpsimd", [mk("dve"), mk("act")], [("gdbg", 16)],
             lambda e: e.dma_start(out=dbg["d_rev"], in_=rev_sb[:, :]))
        emit("gpsimd", [], [("gdbg", 16)],
             lambda e: e.dma_start(out=dbg["d_ca"], in_=ca_sb[:, :]))
        emit("gpsimd", [], [("gdbg", 16)],
             lambda e: e.dma_start(out=dbg["d_pp"], in_=pp_sb[:, :]))

    s_free = {"A": ("act", 0), "B": ("act", 0)}
    pt_reader = [("pe", 0)] * 4
    kt_slot = [0]
    qs_ready = [None, None]
    qs_freed = [("pe", 0), ("pe", 0)]
    qt_freed = [("pe", 0), ("pe", 0)]
    qkv_done = {}
    vh_ready = [("dve", 0), ("dve", 0)]
    vh_free = [("pe", 0), ("pe", 0)]
    qkv_gate = [m_done]
    lnT_tp = [("tpa0", 0)] * NT
    ctx_gate = [("dve", 0)]
    ctx_done = [("pe", 0)]
    div_tt_gate = [("dve", 0)]
    c1_xa_fence = [("gdv2", 0)]
    misc_gate = [m1t_done]
    acc_i = [0]
    div_r2_free = [("gdiv", 0)]
    div_rb_free = [("gdv2", 0)]
    div_xa_free = [("dve", 0)]
    div_pending = []
    bgq = []


    def bgq_pop(n):
        for _ in range(min(n, len(bgq))):
            bgq.pop(0)()

    def bgq_flush():
        while bgq:
            bgq.pop(0)()

    stage_free = {0: m_done, 1: m_done}
    wv_done_c = [None]
    wo_done_c = [None]
    wdup_done_c = [None]

    def q_cast_chunk(src_d, dst_t, dst_off, ch, done_cell):
        def go():
            buf = ch % 2
            emit("sync", [xa_free[buf]], [(f"sxa{buf}", 16)],
                 lambda e: e.dma_start(
                     out=xa_sb[:, buf * 1024:(buf + 1) * 1024],
                     in_=src_d[ch * 128:(ch + 1) * 128, :]))
            emit("vector", [mk(f"sxa{buf}")], [("dve", 1)],
                 lambda e: e.tensor_copy(
                     dst_t[:, dst_off + ch * 1024: dst_off + (ch + 1) * 1024],
                     xa_sb[:, buf * 1024:(buf + 1) * 1024]))
            xa_free[buf] = mk("dve")
            if done_cell is not None and ch == NCH - 1:
                done_cell[0] = mk("dve")
        return go

    def q_dup_chunk(src_d, dst_d, ch, done_cell):
        def go():
            buf = ch % 2
            st_t = lt_sb if buf == 0 else pt_sb
            emit("sync", [xa_free[buf]], [(f"sxa{buf}", 16)],
                 lambda e: e.dma_start(
                     out=xa_sb[:, buf * 1024: buf * 1024 + 1024],
                     in_=src_d[ch * 128:(ch + 1) * 128, :]))
            sy_m = mk(f"sxa{buf}")
            pitch = NCH * 512 if st_t is pt_sb else 2048
            for rep in range(2):
                src_ap = ap3(xa_sb, buf * 1024, [[2048, 128], [64, 16], [1, 64]])
                dst_ap = ap3(st_t, rep * 64, [[pitch, 128], [128, 16], [1, 64]])
                emit("vector", [sy_m, stage_free[buf]], [("dve", 1)],
                     lambda e, dst_ap=dst_ap, src_ap=src_ap:
                     e.tensor_copy(dst_ap, src_ap))
            xa_free[buf] = mk("dve")
            emit("gpsimd", [mk("dve")], [(f"gw{buf}", 16)],
                 lambda e: e.dma_start(
                     out=dst_d[ch],
                     in_=st_t[:, 0:2048].rearrange("p (h c) -> p h c", h=16)))
            stage_free[buf] = mk(f"gw{buf}")
            if done_cell is not None and ch == NCH - 1:
                done_cell[0] = None
        return go

    for _ch in range(NCH):
        bgq.append(q_dup_chunk(wq_d, qdup_d, _ch, None))
    for _ch in range(NCH):
        bgq.append(q_dup_chunk(wk_d, kdup_d, _ch, wdup_done_c))
    def _wv_item():
        wv_done_c[0] = cast_load(wv_d, wv_sb, 0, [])
    def _wo_item():
        wo_done_c[0] = cast_load(wo_d, wo_sb, 0, [])
    bgq.append(_wv_item)
    bgq.append(_wo_item)

    def emit_qk_stream(h):
        buf = h % 2
        for (src_d, dst_t) in ((qdup_d, qs_sb), (kdup_d, ks_sb)):
            src = bass_rust.AP(src_d.tensor, h * 128,
                               [[H * 128, 128], [128 * H * 128, NCH], [1, 128]])
            dst = ap3(dst_t, buf * 1024, [[2048, 128], [128, 8], [1, 128]])
            emit("gpsimd", [qs_freed[buf]] + wdup_done_c[0], [(f"gq{buf}", 16)],
                 lambda e, dst=dst, src=src: e.dma_start(out=dst, in_=src))
        qs_ready[buf] = mk(f"gq{buf}")

    def acc_pick():
        i = (acc_i[0] % 2) if not DBG_NO_ROT else 0
        acc_i[0] += 1
        if i == 0:
            return qkv_ps, (lambda: qkv_gate[0]), qkv_gate
        return msc_ps, (lambda: misc_gate[0]), misc_gate

    def emit_qkv_group(h, proj, qb):
        buf = h % 2
        w_src = qs_sb if proj == 0 else ks_sb
        t_dst = qt_sb if proj == 0 else kt_sb
        ps, gate_get, gate_cell = acc_pick()
        for ch in range(NCH):
            w = []
            if ch == 0:
                w = [qs_ready[buf], lnT_tp[qb * 4 + 2],
                     lnT_tp[qb * 4 + 3], gate_get()]
            emit("tensor", w, [("pe", 1)] if ch == NCH - 1 else [],
                 lambda e, ch=ch, qb=qb, buf=buf, w_src=w_src, ps=ps:
                 e.matmul(ps[:, 0:512],
                          w_src[:, buf * 1024 + ch * 128: buf * 1024 + ch * 128 + 128],
                          ln1T[:, ch * S + qb * 512: ch * S + qb * 512 + 512],
                          start=(ch == 0), stop=(ch == NCH - 1)))
        emit("vector", [mk("pe")], [("dve", 1)],
             lambda e, qb=qb, buf=buf, t_dst=t_dst, ps=ps:
             e.tensor_copy(t_dst[:, buf * S + qb * 512: buf * S + qb * 512 + 512],
                           ps[:, 0:512]))
        gate_cell[0] = mk("dve")
        if proj == 1 and qb == 3:
            qkv_done[h] = mk("dve")

    def emit_v_tile(g, tt):
        gb = g % 2
        ps, gate_get, gate_cell = acc_pick()
        for ch in range(NCH):
            w = []
            if ch == 0:
                w = [lnT_tp[tt], gate_get(), wv_done_c[0]]
            emit("tensor", w, [("pe", 1)] if ch == NCH - 1 else [],
                 lambda e, ch=ch, tt=tt, g=g, ps=ps:
                 e.matmul(ps[:, 0:256],
                          ln1T[:, ch * S + tt * 128: ch * S + tt * 128 + 128],
                          wv_sb[:, ch * D + g * 256: ch * D + g * 256 + 256],
                          start=(ch == 0), stop=(ch == NCH - 1)))
        src = ap3(ps, 0, [[512, 128], [64, 4], [1, 64]])
        dst = ap3(vh_sb, gb * 4160 + tt * 65, [[8320, 128], [1040, 4], [1, 64]])
        emit("vector", [mk("pe"), vh_free[gb]], [("dve", 1)],
             lambda e, dst=dst, src=src: e.tensor_copy(dst, src))
        gate_cell[0] = mk("dve")
        if tt == NT - 1:
            vh_ready[gb] = mk("dve")

    def queue_head_work(h):
        bgq.append(lambda h=h: emit_qk_stream(h))
        for proj in range(2):
            for qb in range(4):
                bgq.append(lambda h=h, p=proj, q=qb: emit_qkv_group(h, p, q))

    def a_begin(item):
        a_tp_fence[0] = [mk("gpp0"), mk("gpp1"), mk("gc0"), mk("gc1"),
                         mk("gdiv")]
        a_xa_fence[0] = mk("gdv2")
        st8 = dict(item=item, base=item * S,
                   rstd=[("act", 0)] * NT, norm=[("dve", 0)] * NT,
                   tp=[("tpa0", 0)] * NT,
                   ln_fence=[mk("tpa0"), mk("tpa1"), mk("tpd0"), mk("tpd1"),
                             ln_stage_free[0], ln_stage_free[1]],
                   pe_fence=mk("pe"))
        return st8

    def a_tile(st8, tt):
        base = st8["base"]
        buf = tt % 2
        sc = 18 * buf
        w = [xa_free[buf]]
        if tt >= 2:
            w.append(st8["norm"][tt - 2])
        else:
            w.append(a_xa_fence[0])
        emit("sync", w, [(f"sxa{buf}", 16)],
             lambda e, tt=tt, buf=buf, base=base: e.dma_start(
                 out=xa_sb[:, buf * 1024: buf * 1024 + 1024],
                 in_=x8[base + tt * 128: base + tt * 128 + 128, :]))
        sy_m = mk(f"sxa{buf}")
        wv_ = [sy_m]
        if tt >= 2:
            wv_.append(st8["rstd"][tt - 2])
        emit("vector", wv_, [],
             lambda e, buf=buf, sc=sc: e.bn_stats(
                 st_sb[:, sc:sc + 6], xa_sb[:, buf * 1024: buf * 1024 + 512]))
        emit("vector", [], [("dve", 1)],
             lambda e, buf=buf, sc=sc: e.bn_stats(
                 st_sb[:, sc + 6:sc + 12],
                 xa_sb[:, buf * 1024 + 512: buf * 1024 + 1024]))
        emit("vector", [mk("dve")], [("dve", 1)],
             lambda e, sc=sc: e.bn_aggr(st_sb[:, sc + 12:sc + 14],
                                        st_sb[:, sc:sc + 12]))
        emit("scalar", [mk("dve")], [("act", 1)],
             lambda e, sc=sc: e.activation(st_sb[:, sc + 14:sc + 15],
                                           st_sb[:, sc + 13:sc + 14],
                                           AF.Ln, bias=eps_sb[:, 0:1], scale=1.0))
        emit("scalar", [mk("act")], [("act", 1)],
             lambda e, sc=sc: e.activation(st_sb[:, sc + 15:sc + 16],
                                           st_sb[:, sc + 14:sc + 15],
                                           AF.Exp, scale=-0.5))
        st8["rstd"][tt] = mk("act")
        emit("vector", [st8["rstd"][tt], mk("dve")], [("dve", 1)],
             lambda e, sc=sc: e.tensor_scalar(
                 st_sb[:, sc + 16:sc + 17], st_sb[:, sc + 12:sc + 13],
                 st_sb[:, sc + 15:sc + 16], -1.0, op0=OP.mult, op1=OP.mult))
        wv2 = [st8["tp"][tt - 2], mk("dve")] if tt >= 2 else \
            list(st8["ln_fence"]) + [mk("dve")]
        emit("vector", wv2, [("dve", 1)],
             lambda e, buf=buf, sc=sc: e.tensor_scalar(
                 ln_sb[:, buf * 1024: buf * 1024 + 1024],
                 xa_sb[:, buf * 1024: buf * 1024 + 1024],
                 st_sb[:, sc + 15:sc + 16], st_sb[:, sc + 16:sc + 17],
                 op0=OP.mult, op1=OP.add))
        st8["norm"][tt] = mk("dve")
        xa_free[buf] = st8["norm"][tt]
        src_ = ap3(ln_sb, buf * 1024, [[2048, 128], [128, 8], [1, 128]])
        dst_ = ap3(ln1T, tt * 128, [[NCH * S, 128], [S, 8], [1, 128]])
        wt = [st8["norm"][tt]] + a_tp_fence[0]
        if tt == 0:
            wt.append(st8["pe_fence"])
        emit("sync", wt, [(f"tpa{buf}", 16)],
             lambda e, dst_=dst_, src_=src_: e.dma_start(out=dst_, in_=src_,
                                                         transpose=True))
        st8["tp"][tt] = mk(f"tpa{buf}")
        lnT_tp[tt] = st8["tp"][tt]

    def a_end(st8):
        ln_stage_free[0] = st8["tp"][NT - 1]
        ln_stage_free[1] = st8["tp"][NT - 1]

    def phase_A(item):
        st8 = a_begin(item)
        for tt in range(NT):
            a_tile(st8, tt)
            bgq_pop(4)
        a_end(st8)
        return st8

    a_tp_fence = [[]]
    a_xa_fence = [("gdv2", 0)]
    ln_stage_free = [("gc0", 0), ("gc1", 0)]

    def band_range(kt, qh):
        lo = max(0, 128 * (kt - 1), 1024 * qh)
        hi = min(S, 128 * (kt + 2), 1024 * (qh + 1))
        return lo, hi

    def emit_sweep(item, h, qh):
        buf = h % 2
        g = h // 4
        gb = g % 2
        pbase = buf * S + qh * 1024
        exp_marks = {}
        s_pe = {}
        band_dve = {}

        def s_mms(ktp):
            ktA, ktB = 2 * ktp, 2 * ktp + 1
            for (kt, p0, coff, tp) in ((ktA, 0, 0, (0, 0)), (ktB, 64, 1024, (64, 0))):
                key = "A" if coff == 0 else "B"
                for half in range(2):
                    w = []
                    if half == 0:
                        w = [s_free[key], ("dve", qkv_done[h][1])]
                    emit("tensor", w, [("pe", 1)] if half == 1 else [],
                         lambda e, kt=kt, p0=p0, coff=coff, tp=tp, half=half,
                         buf=buf, pbase=pbase:
                         e.matmul(
                             s2_ps[:, coff + half * 512: coff + half * 512 + 512],
                             kt_sb[p0:p0 + 64, buf * S + kt * 128: buf * S + kt * 128 + 128],
                             qt_sb[p0:p0 + 64, pbase + half * 512: pbase + half * 512 + 512],
                             start=True, stop=True, tile_position=tp))
                s_pe[kt] = mk("pe")

        def band_add(kt, coff):
            lo, hi = band_range(kt, qh)
            if hi <= lo:
                return
            llo, lhi = lo - 1024 * qh, hi - 1024 * qh
            r0 = 128 * kt + 255 - lo
            ln_ = hi - lo
            ppo = (h % 2) * 384
            abs_start = ppo + r0
            abs_stop = ppo + r0 - ln_
            if abs_stop < 0:
                in1 = pp_sb[:, abs_start::-1]
            else:
                in1 = pp_sb[:, abs_start: abs_stop: -1]
            emit("vector", [s_pe[kt], pp_ready[h % 2]], [("dve", 1)],
                 lambda e, coff=coff, llo=llo, lhi=lhi, in1=in1:
                 e.tensor_tensor(s2_ps[:, coff + llo: coff + lhi],
                                 s2_ps[:, coff + llo: coff + lhi], in1, op=OP.add))
            band_dve[kt] = mk("dve")

        def exps(kt, coff, slot):
            lo, hi = band_range(kt, qh)
            segs = []
            if hi <= lo:
                segs = [(0, 1024, "lo" if 128 * (kt + 2) <= 1024 * qh else "hi")]
            else:
                llo, lhi = lo - 1024 * qh, hi - 1024 * qh
                if llo > 0:
                    segs.append((0, llo, "hi"))
                segs.append((llo, lhi, "band"))
                if lhi < 1024:
                    segs.append((lhi, 1024, "lo"))
            w = [s_pe[kt], pt_reader[slot]]
            if kt in band_dve:
                w.append(band_dve[kt])
            nseg = len(segs) + (1 if DBG_ACT_FLUSH else 0)
            for i, (a, b_, kind) in enumerate(segs):
                inc = [("act", 1)] if (i == len(segs) - 1 and not DBG_ACT_FLUSH) else []
                if kind == "band":
                    bias = None
                elif kind == "hi":
                    bias = ca_sb[:, 16 + h:16 + h + 1]
                else:
                    bias = ca_sb[:, h:h + 1]
                if bias is None:
                    emit("scalar", w if i == 0 else [], inc,
                         lambda e, a=a, b_=b_, coff=coff, slot=slot:
                         e.activation(pt_sb[:, slot * 1024 + a: slot * 1024 + b_],
                                      s2_ps[:, coff + a: coff + b_],
                                      AF.Exp, scale=1.0))
                else:
                    emit("scalar", w if i == 0 else [], inc,
                         lambda e, a=a, b_=b_, coff=coff, slot=slot, bias=bias:
                         e.activation(pt_sb[:, slot * 1024 + a: slot * 1024 + b_],
                                      s2_ps[:, coff + a: coff + b_],
                                      AF.Exp, bias=bias, scale=1.0))
            if DBG_ACT_FLUSH:
                emit("scalar", [], [("act", 1)],
                     lambda e, slot=slot: e.activation(
                         st_sb[0:1, 158:159],
                         pt_sb[0:1, slot * 1024: slot * 1024 + 1],
                         AF.Identity, scale=1.0))
            exp_marks[kt] = mk("act")
            key = "A" if coff == 0 else "B"
            s_free[key] = mk("act")

        def ctx_mms(kt, slot):
            vo = gb * 4160 + (h % 4) * 1040 + kt * 65
            for half in range(2):
                w = []
                if half == 0:
                    w = [exp_marks[kt], vh_ready[gb]]
                    if kt == 0:
                        w.append(ctx_gate[0])
                emit("tensor", w, [("pe", 1)] if half == 1 else [],
                     lambda e, kt=kt, slot=slot, half=half, vo=vo:
                     e.matmul(ctx_ps[0:65, half * 512: half * 512 + 512],
                              vh_sb[:, vo: vo + 65],
                              pt_sb[:, slot * 1024 + half * 512: slot * 1024 + half * 512 + 512],
                              start=(kt == 0), stop=(kt == NT - 1)))
            pt_reader[slot] = mk("pe")
            if kt == NT - 1:
                ctx_done[0] = mk("pe")

        prev = None
        for ktp in range(8):
            ktA, ktB = 2 * ktp, 2 * ktp + 1
            slA = kt_slot[0] % 4
            kt_slot[0] += 1
            slB = kt_slot[0] % 4
            kt_slot[0] += 1
            s_mms(ktp)
            band_add(ktA, 0)
            band_add(ktB, 1024)
            exps(ktA, 0, slA)
            exps(ktB, 1024, slB)
            if DBG_DUMP and item == DBG_DUMP_ITEM and h == 0 and qh == 0 and ktp == 0:
                emit("gpsimd", [mk("act")], [("gdbg", 16)],
                     lambda e, slA=slA, slB=slB: e.dma_start(
                         out=dbg["d_pt"], in_=pt_sb[:, :]))
                pt_reader[slA] = mk("gdbg")
                pt_reader[slB] = mk("gdbg")
            if prev is not None:
                ctx_mms(prev[0], prev[2])
                ctx_mms(prev[1], prev[3])
            if div_pending:
                div_pending.pop(0)()
            bgq_pop(2 if (ktp & 1) else 1)
            prev = (ktA, ktB, slA, slB)
        ctx_mms(prev[0], prev[2])
        ctx_mms(prev[1], prev[3])

    def queue_division(item, h, qh):
        ch = h // 2
        odd = h % 2 == 1
        pe_ctx = ctx_done[0]
        st = {}

        def d1():
            emit("vector", [pe_ctx, div_tt_gate[0]], [("dve", 1)],
                 lambda e: e.tensor_copy(ctxu_sb[0:65, :], ctx_ps[0:65, :]))
            ctx_gate[0] = mk("dve")
            st["d1"] = mk("dve")

        def d2():
            scat_src = ap3(ctxu_sb, 64 * 1024, [[1024, 1], [16, 64], [1, 16]])
            emit("gpsimd", [st["d1"], div_r2_free[0]] + pp_tp_fence[0],
                 [("gdiv", 16)],
                 lambda e: e.dma_start(out=st_sb[64:128, 128:144], in_=scat_src))
            emit("vector", [mk("gdiv")], [("dve", 1)],
                 lambda e: e.reciprocal(st_sb[64:128, 128:144],
                                        st_sb[64:128, 128:144]))
            gat_src = ap3(st_sb, 64 * 160 + 128, [[160, 64], [1, 16]])
            emit("gpsimd", [mk("dve"), div_rb_free[0]], [("gdiv", 16)],
                 lambda e: e.dma_start(out=rcp_d, in_=gat_src))
            bc_src = bass_rust.AP(rcp_d.tensor, 0, [[0, 64], [1, 1024]])
            emit("gpsimd", [mk("gdiv"), div_xa_free[0]], [("gdv2", 16)],
                 lambda e: e.dma_start(out=xa_sb[0:64, 0:1024], in_=bc_src))
            st["d2"] = mk("gdv2")
            div_r2_free[0] = mk("gdiv")
            div_rb_free[0] = st["d2"]

        def d4(half):
            if odd:
                out = ln_sb[0:64, qh * 1024 + half * 512: qh * 1024 + half * 512 + 512]
            else:
                out = ctxT[0:64, ch * S + qh * 1024 + half * 512:
                           ch * S + qh * 1024 + half * 512 + 512]
            w = [st["d2"]]
            if half == 0:
                w.append(div_tt_gate[0])
            if odd:
                w.append(ln_stage_free[qh])
            emit("vector", w, [("dve", 1)],
                 lambda e, out=out, half=half: e.tensor_tensor(
                     out, ctxu_sb[0:64, half * 512: half * 512 + 512],
                     xa_sb[0:64, half * 512: half * 512 + 512], op=OP.mult))
            div_tt_gate[0] = mk("dve")
            if half == 1:
                div_xa_free[0] = mk("dve")
            st["d4"] = mk("dve")

        def d5():
            emit("gpsimd", [st["d4"]] + pp_tp_fence[0], [(f"gc{qh}", 16)],
                 lambda e: e.dma_start(
                     out=ctxT[64:128, ch * S + qh * 1024: ch * S + qh * 1024 + 1024],
                     in_=ln_sb[0:64, qh * 1024: qh * 1024 + 1024]))
            ln_stage_free[qh] = mk(f"gc{qh}")

        seq = [d1, d2, lambda: d4(0), lambda: d4(1)]
        if odd:
            seq.append(d5)
        div_pending.extend(seq)

    def phase_B(item):
        for h in range(H):
            buf = h % 2
            if 1 <= h and h + 1 < H:
                queue_head_work(h + 1)
            nh = (h + 1) % H
            emit_patp(nh, [pp_freed[nh % 2]] + pp_tp_fence[0])
            if h % 4 == 1 and h // 4 + 1 <= 3:
                g = h // 4 + 1
                for tt in range(NT):
                    bgq.append(lambda g=g, tt=tt: emit_v_tile(g, tt))
            if h not in qkv_done:
                bgq_flush()
            assert h in qkv_done, f"QKV for head {h} not emitted"
            for qh in range(2):
                emit_sweep(item, h, qh)
                queue_division(item, h, qh)
            pp_freed[h % 2] = mk("dve")
            qt_freed[buf] = mk("pe")
            qs_freed[buf] = mk("pe")
            if h % 4 == 3:
                vh_free[(h // 4) % 2] = mk("pe")
        while div_pending:
            div_pending.pop(0)()
        bgq_flush()

    pp_tp_fence = [[]]

    def phase_C1(item, next_item):
        base = item * S
        div_done = mk("dve")
        div_gp = [mk("gc0"), mk("gc1"), mk("gdiv"), wo_done_c[0]]
        misc_gate[0] = div_done
        qkv_gate[0] = max(qkv_gate[0], div_done, key=lambda t: t[1])
        bank_gate = [div_done, div_done]
        xf = [mk("dve"), mk("dve")]
        c1_xa_fence[0] = mk("gdv2")
        xbf = [[mk("gy0"), mk("gx0")], [mk("gy1"), mk("gx1")]]
        st8 = a_begin(next_item) if next_item is not None else None
        for tt in range(NT):
            buf = tt % 2
            emit("sync", [xf[buf], xa_free[buf], c1_xa_fence[0]], [(f"sxa{buf}", 16)],
                 lambda e, tt=tt, buf=buf, base=base: e.dma_start(
                     out=xa_sb[:, buf * 1024: buf * 1024 + 1024],
                     in_=x8[base + tt * 128: base + tt * 128 + 128, :]))
            sy_m = mk(f"sxa{buf}")
            for dh in range(2):
                ps = msc_ps if dh == 0 else qkv_ps
                for ch in range(NCH):
                    w2 = []
                    if ch == 0:
                        w2 = [bank_gate[dh]] + div_gp
                        if dh == 1:
                            w2.append(qkv_gate[0])
                    emit("tensor", w2, [("pe", 1)] if ch == NCH - 1 else [],
                         lambda e, ch=ch, tt=tt, dh=dh, ps=ps: e.matmul(
                             ps[:, 0:512],
                             ctxT[:, ch * S + tt * 128: ch * S + tt * 128 + 128],
                             wo_sb[:, ch * D + dh * 512: ch * D + dh * 512 + 512],
                             start=(ch == 0), stop=(ch == NCH - 1)))
                emit("vector", [mk("pe"), sy_m] + xbf[buf], [("dve", 1)],
                     lambda e, buf=buf, dh=dh, ps=ps: e.tensor_tensor(
                         xb_sb[:, buf * 1024 + dh * 512: buf * 1024 + dh * 512 + 512],
                         ps[:, 0:512],
                         xa_sb[:, buf * 1024 + dh * 512: buf * 1024 + dh * 512 + 512],
                         op=OP.add))
                bank_gate[dh] = mk("dve")
                if dh == 1:
                    qkv_gate[0] = mk("dve")
            xf[buf] = mk("dve")
            xa_free[buf] = mk("dve")
            sc = 64 + 2 * tt
            emit("vector", [mk("dve")], [],
                 lambda e, buf=buf: e.bn_stats(
                     st_sb[:, 36:42], xb_sb[:, buf * 1024: buf * 1024 + 512]))
            emit("vector", [], [("dve", 1)],
                 lambda e, buf=buf: e.bn_stats(
                     st_sb[:, 42:48], xb_sb[:, buf * 1024 + 512: buf * 1024 + 1024]))
            emit("vector", [mk("dve")], [("dve", 1)],
                 lambda e, sc=sc: e.bn_aggr(st_sb[:, sc:sc + 2], st_sb[:, 36:48]))
            emit("gpsimd", [mk("dve")], [(f"gx{buf}", 16)],
                 lambda e, tt=tt, buf=buf: e.dma_start(
                     out=x2_d[tt * 128: tt * 128 + 128, :],
                     in_=xb_sb[:, buf * 1024: buf * 1024 + 1024]))
            xbf[buf] = [mk(f"gx{buf}")]
            if st8 is not None:
                a_tile(st8, tt)
        if st8 is not None:
            a_end(st8)
        emit("scalar", [mk("dve")], [("act", 1)],
             lambda e: e.activation(st_sb[:, 96:112], st_sb[:, 65:97:2],
                                    AF.Ln, bias=eps_sb[:, 0:1], scale=1.0))
        emit("scalar", [mk("act")], [("act", 1)],
             lambda e: e.activation(st_sb[:, 112:128], st_sb[:, 96:112],
                                    AF.Exp, scale=-0.5))
        rstd2 = mk("act")
        emit("vector", [rstd2], [("dve", 1)],
             lambda e: e.scalar_tensor_tensor(
                 st_sb[:, 48:64], st_sb[:, 64:96:2], -1.0, st_sb[:, 112:128],
                 op0=OP.mult, op1=OP.mult))
        div_tt_gate[0] = mk("dve")
        return mk("dve"), [mk("gx0"), mk("gx1")], st8

    def phase_D(item, nm2_done, x2_ready, a_next):
        base = item * S
        load_table(id_exp)
        xbfree = [nm2_done, nm2_done]
        lnfree = [("pe", cnt["pe"]), ("pe", cnt["pe"])]
        ltfree = [("pe", 0), ("pe", 0)]
        misc_gate[0] = mk("dve")
        qkv_gate[0] = mk("dve")
        tanh_done = [("act", 0)]
        tanh_war = [mk("gdiv")]
        yfree = [("gy0", 0), ("gy1", 0)]
        for tt in range(NT):
            buf = tt % 2
            emit("sync", [xbfree[buf]] + x2_ready, [(f"sxb{buf}", 16)],
                 lambda e, tt=tt, buf=buf: e.dma_start(
                     out=xb_sb[:, buf * 1024: buf * 1024 + 1024],
                     in_=x2_d[tt * 128: tt * 128 + 128, :]))
            sy_m = mk(f"sxb{buf}")
            emit("vector", [sy_m, nm2_done, lnfree[buf]], [("dve", 1)],
                 lambda e, tt=tt, buf=buf: e.tensor_scalar(
                     qs_sb[:, buf * 1024: buf * 1024 + 1024],
                     xb_sb[:, buf * 1024: buf * 1024 + 1024],
                     st_sb[:, 112 + tt:113 + tt], st_sb[:, 48 + tt:49 + tt],
                     op0=OP.mult, op1=OP.add))
            dve_m = mk("dve")
            src_ = ap3(qs_sb, buf * 1024, [[2048, 128], [128, 8], [1, 128]])
            dst_ = ap3(lt_sb, buf * 1024, [[2048, 128], [128, 8], [1, 128]])
            wt = [dve_m, ltfree[buf]]
            if tt == 0:
                wt.extend(d_tp_fence[0])
            emit("sync", wt, [(f"tpd{buf}", 16)],
                 lambda e, dst_=dst_, src_=src_: e.dma_start(out=dst_, in_=src_,
                                                             transpose=True))
            tp_m = mk(f"tpd{buf}")
            lnfree[buf] = tp_m
            for dh in range(2):
                ps, gate_get, gate_cell = acc_pick()
                for chm in range(NCH):
                    w3 = []
                    if chm == 0:
                        w3 = [tp_m, gate_get()]
                    emit("tensor", w3, [("pe", 1)] if chm == NCH - 1 else [],
                         lambda e, chm=chm, buf=buf, dh=dh, ps=ps: e.matmul(
                             ps[:, 0:512],
                             lt_sb[:, buf * 1024 + chm * 128: buf * 1024 + chm * 128 + 128],
                             m_sb[:, chm * D + dh * 512: chm * D + dh * 512 + 512],
                             start=(chm == 0), stop=(chm == NCH - 1)))
                pe_m = mk("pe")
                emit("vector", [pe_m, tanh_done[0]], [("dve", 1)],
                     lambda e, dh=dh, ps=ps: e.tensor_tensor(
                         ctxu_sb[:, dh * 512: dh * 512 + 512], ps[:, 0:512],
                         br_sb[:, dh * 512: dh * 512 + 512], op=OP.add))
                gate_cell[0] = mk("dve")
                emit("scalar", [mk("dve"), tanh_war[0]], [("act", 1)],
                     lambda e, dh=dh: e.activation(
                         aux_sb[:, dh * 512: dh * 512 + 512],
                         ctxu_sb[:, dh * 512: dh * 512 + 512],
                         AF.Tanh, scale=1.0))
                tanh_done[0] = mk("act")
                emit("vector", [mk("act"), yfree[buf]], [("dve", 1)],
                     lambda e, buf=buf, dh=dh: e.tensor_tensor(
                         xb_sb[:, buf * 1024 + dh * 512: buf * 1024 + dh * 512 + 512],
                         xb_sb[:, buf * 1024 + dh * 512: buf * 1024 + dh * 512 + 512],
                         aux_sb[:, dh * 512: dh * 512 + 512], op=OP.add))
                tanh_war[0] = mk("dve")
            emit("gpsimd", [mk("dve")], [(f"gy{buf}", 16)],
                 lambda e, tt=tt, buf=buf, base=base: e.dma_start(
                     out=y8[base + tt * 128: base + tt * 128 + 128, :],
                     in_=xb_sb[:, buf * 1024: buf * 1024 + 1024]))
            yfree[buf] = mk(f"gy{buf}")
            xbfree[buf] = mk(f"gy{buf}")
            ltfree[buf] = ("pe", cnt["pe"])
        load_table(id_nle)
        qkv_gate[0] = mk("dve")
        misc_gate[0] = mk("dve")
        qs_freed[0] = mk("tpd0")
        qs_freed[1] = mk("tpd1")
        return

    d_tp_fence = [[]]

    a_next_state = None
    for item in range(min(B_PER, DBG_ITEMS)):
        if DBG_STAGE == "prologue":
            break
        if a_next_state is None:
            a_next_state = phase_A(item)
        if DBG_DUMP and item == DBG_DUMP_ITEM:
            emit("gpsimd", [lnT_tp[NT - 2], lnT_tp[NT - 1]], [("gdbg", 16)],
                 lambda e: e.dma_start(out=dbg["d_ln1t"], in_=ln1T[:, :]))
        if DBG_STAGE == "A":
            a_next_state = None
            continue
        bgq_flush()
        if item == 0:
            wdup_done_c[0] = [mk("gw0"), mk("gw1")]
            pt_reader[0] = mk("gw1")
            pt_reader[1] = mk("gw1")
        emit_qk_stream(0)
        for proj in range(2):
            for qb in range(4):
                emit_qkv_group(0, proj, qb)
        for tt in range(NT):
            emit_v_tile(0, tt)
        if DBG_DUMP and item == DBG_DUMP_ITEM:
            emit("gpsimd", [mk("dve")], [("gdbg", 16)],
                 lambda e: e.dma_start(out=dbg["d_qt"], in_=qt_sb[:, :]))
            emit("gpsimd", [], [("gdbg", 16)],
                 lambda e: e.dma_start(out=dbg["d_kt"], in_=kt_sb[:, :]))
            emit("gpsimd", [], [("gdbg", 16)],
                 lambda e: e.dma_start(out=dbg["d_vh"], in_=vh_sb[:, 0:8320]))
        queue_head_work(1)
        pp_tp_fence[0] = [mk("tpa0"), mk("tpa1"), mk("tpd0"), mk("tpd1"),
                          mk("tpw")]
        phase_B(item)
        if DBG_DUMP and item == DBG_DUMP_ITEM:
            emit("gpsimd", [mk("dve"), mk("gc0"), mk("gc1")], [("gdbg", 16)],
                 lambda e: e.dma_start(out=dbg["d_ctxt"], in_=ctxT[:, :]))
        if DBG_STAGE == "B":
            a_next_state = None
            continue
        nxt = item + 1 if item + 1 < min(B_PER, DBG_ITEMS) else None
        nm2_done, x2_ready, a_next_state = phase_C1(item, nxt)
        if DBG_STAGE == "C1":
            continue
        d_tp_fence[0] = [mk("gpp0"), mk("gpp1"), mk("gc0"), mk("gc1"),
                         mk("gdiv")]
        phase_D(item, nm2_done, x2_ready, a_next_state)

    emit("gpsimd", [], [], lambda e: e.wait_ge(sems["gy0"], cnt["gy0"]))
    emit("gpsimd", [], [], lambda e: e.wait_ge(sems["gy1"], cnt["gy1"]))

    if DBG_PREFIX is not None:
        del OPS[DBG_PREFIX:]
    print(f"[kernel] total ops: {len(OPS)}")

    with nc.Block() as block:
        for eng_name in ["sync", "gpsimd", "tensor", "vector", "scalar"]:
            def make(eng_name):
                def run(eng):
                    for e_, fn in OPS:
                        if e_ == eng_name:
                            fn(eng)
                return run
            getattr(block, eng_name)(make(eng_name))

    return nc


_PROG = None


def kernel(L0_attn_hidden_states, L0_attn_Wq, L0_attn_Wk, L0_attn_Wv, L0_attn_Wo,
           L0_attn_rel_emb, L0_ffn_weight, L0_ffn_aux_weight1, L0_ffn_aux_weight2,
           L0_ffn_bias):
    global _PROG
    x = np.ascontiguousarray(np.asarray(L0_attn_hidden_states, dtype=np.float32))
    onehot, eye16 = _host_consts()
    shared = dict(
        wq=np.ascontiguousarray(np.asarray(L0_attn_Wq, np.float32)),
        wk=np.ascontiguousarray(np.asarray(L0_attn_Wk, np.float32)),
        wv=np.ascontiguousarray(np.asarray(L0_attn_Wv, np.float32)),
        wo=np.ascontiguousarray(np.asarray(L0_attn_Wo, np.float32)),
        wf=np.ascontiguousarray(np.asarray(L0_ffn_weight, np.float32)),
        wa1=np.ascontiguousarray(np.asarray(L0_ffn_aux_weight1, np.float32)),
        wa2=np.ascontiguousarray(np.asarray(L0_ffn_aux_weight2, np.float32)),
        fbias=np.ascontiguousarray(np.asarray(L0_ffn_bias, np.float32).reshape(1, D)),
        rel=np.ascontiguousarray(np.asarray(L0_attn_rel_emb, np.float32)),
        onehot=onehot, eye16=eye16,
    )
    if _PROG is None:
        _PROG = build_program()
    in_maps = []
    for c in range(N_CORES):
        m = dict(shared)
        m["x"] = np.ascontiguousarray(
            x[c * B_PER:(c + 1) * B_PER].reshape(B_PER * S, D))
        in_maps.append(m)
    res = run_bass_kernel_spmd(_PROG, in_maps, core_ids=list(range(N_CORES)))
    out = np.concatenate(
        [np.asarray(r["y"]).reshape(B_PER, S, D) for r in res.results], axis=0)
    return out.astype(np.float32)


# version 10
# speedup vs baseline: 1.2292x; 1.0629x over previous
import math
import numpy as np
import bass_rust
import concourse.bass as bass
import concourse.mybir as mybir
from concourse.bass_utils import run_bass_kernel_spmd

F32 = mybir.dt.float32
BF16 = mybir.dt.bfloat16
AF = mybir.ActivationFunctionType
OP = mybir.AluOpType

N_CORES = 8
B_PER = 4
S = 2048
D = 1024
H = 16
DK = 64
NT = S // 128
NCH = D // 128
NREV = 511
C_SHIFT = 12.0
EPS = 1e-5

NUM_BUCKETS = 32
MAX_DISTANCE = 128

DBG_ITEMS = B_PER
DBG_STAGE = "full"
DBG_PREFIX = None
DBG_DUMP = False
DBG_DUMP_ITEM = 0
DBG_NO_ROT = False
DBG_OLD_RECIP = False
DBG_ACT_FLUSH = False


def _rel_bucket_host(d):
    import jax
    import jax.numpy as jnp
    with jax.default_device(jax.devices("cpu")[0]):
        num_buckets = NUM_BUCKETS // 2
        rel = jnp.asarray(d, dtype=jnp.int32)
        relative_buckets = (rel > 0).astype(jnp.int32) * num_buckets
        rp = jnp.abs(rel)
        max_exact = num_buckets // 2
        is_small = rp < max_exact
        rp_large = max_exact + (
            jnp.log(rp.astype(jnp.float32) / max_exact)
            / math.log(MAX_DISTANCE / max_exact)
            * (num_buckets - max_exact)
        ).astype(jnp.int32)
        rp_large = jnp.minimum(rp_large, num_buckets - 1)
        relative_buckets = relative_buckets + jnp.where(is_small, rp, rp_large)
        return np.asarray(relative_buckets)


def _host_consts():
    d = np.arange(NREV, dtype=np.int64) - 255
    bk = _rel_bucket_host(d)
    onehot = np.zeros((NUM_BUCKETS, NREV), dtype=np.float32)
    onehot[bk, np.arange(NREV)] = 1.0
    eye16 = np.eye(16, dtype=np.float32)
    return onehot, eye16


def build_program():
    nc = bass.Bass("TRN2", target_bir_lowering=False)

    x8 = nc.dram_tensor("x", [B_PER * S, D], F32, kind="ExternalInput").ap()
    wq_d = nc.dram_tensor("wq", [D, D], F32, kind="ExternalInput").ap()
    wk_d = nc.dram_tensor("wk", [D, D], F32, kind="ExternalInput").ap()
    wv_d = nc.dram_tensor("wv", [D, D], F32, kind="ExternalInput").ap()
    wo_d = nc.dram_tensor("wo", [D, D], F32, kind="ExternalInput").ap()
    wf_d = nc.dram_tensor("wf", [D, D], F32, kind="ExternalInput").ap()
    wa1_d = nc.dram_tensor("wa1", [D, D], F32, kind="ExternalInput").ap()
    wa2_d = nc.dram_tensor("wa2", [D, D], F32, kind="ExternalInput").ap()
    fb_d = nc.dram_tensor("fbias", [1, D], F32, kind="ExternalInput").ap()
    rel_d = nc.dram_tensor("rel", [NUM_BUCKETS, H], F32, kind="ExternalInput").ap()
    oh_d = nc.dram_tensor("onehot", [NUM_BUCKETS, NREV], F32, kind="ExternalInput").ap()
    eye_d = nc.dram_tensor("eye16", [16, 16], F32, kind="ExternalInput").ap()
    y8 = nc.dram_tensor("y", [B_PER * S, D], F32, kind="ExternalOutput").ap()

    dbg = {}
    if DBG_DUMP:
        for nm, shp, dt in [("d_ln1t", [128, NCH * S], BF16),
                            ("d_qt", [128, 2 * S], BF16),
                            ("d_kt", [128, 2 * S], BF16),
                            ("d_vh", [128, 8320], BF16),
                            ("d_pt", [128, 4096], BF16),
                            ("d_ctxt", [128, NCH * S], BF16),
                            ("d_rev", [16, NREV], F32),
                            ("d_ca", [128, 32], F32),
                            ("d_pp", [128, 768], BF16)]:
            dbg[nm] = nc.dram_tensor(nm, shp, dt, kind="ExternalOutput").ap()

    qdup_d = nc.dram_tensor("qdup", [NCH, 128, H, 128], BF16, kind="Internal").ap()
    kdup_d = nc.dram_tensor("kdup", [NCH, 128, H, 128], BF16, kind="Internal").ap()
    x2_d = nc.dram_tensor("x2scr", [S, D], F32, kind="Internal").ap()
    rcp_d = nc.dram_tensor("rcpscr", [1, 1024], F32, kind="Internal").ap()

    A = nc.alloc_sbuf_tensor
    ln1T = A("ln1T", [128, NCH * S], BF16)
    ctxT = A("ctxT", [128, NCH * S], BF16)
    wv_sb = A("wv_sb", [128, NCH * D], BF16)
    wo_sb = A("wo_sb", [128, NCH * D], BF16)
    m_sb = A("m_sb", [128, NCH * D], BF16)
    qt_sb = A("qt_sb", [128, 2 * S], BF16)
    kt_sb = A("kt_sb", [128, 2 * S], BF16)
    qs_sb = A("qs_sb", [128, 2 * NCH * 128], BF16)
    ks_sb = A("ks_sb", [128, 2 * NCH * 128], BF16)
    vh_sb = A("vh_sb", [128, 2 * 4 * (NT * 65)], BF16)
    pt_sb = A("pt_sb", [128, 4 * 1024], BF16)
    pp_sb = A("pp_sb", [128, 2 * 384], BF16)
    xa_sb = A("xa_sb", [128, 2 * 1024], F32)
    xb_sb = A("xb_sb", [128, 2 * 1024], F32)
    ctxu_sb = A("ctxu_sb", [128, 1024], F32)
    aux_sb = A("aux_sb", [128, 1024], F32)
    ln_sb = A("ln_sb", [128, 2 * 1024], BF16)
    lt_sb = A("lt_sb", [128, 2 * 1024], BF16)
    br_sb = A("br_sb", [128, D], F32)
    ca_sb = A("ca_sb", [128, 32], F32)
    st_sb = A("st_sb", [128, 160], F32)
    rev_sb = A("rev_sb", [16, NREV], F32)
    oh_sb = A("oh_sb", [NUM_BUCKETS, NREV], F32)
    rel_sb = A("rel_sb", [NUM_BUCKETS, H], F32)
    eye_sb = A("eye_sb", [16, 16], F32)
    or_sb = A("or_sb", [1, 128], F32)
    oc_sb = A("oc_sb", [128, 64], F32)
    eps_sb = A("eps_sb", [128, 1], F32)

    P = nc.alloc_psum_tensor
    s2_ps = P("s2_ps", [128, 2048], F32)
    ctx_ps = P("ctx_ps", [128, 1024], F32)
    qkv_ps = P("qkv_ps", [128, 512], F32)
    msc_ps = P("msc_ps", [128, 512], F32)

    sem_names = ["ssm", "sxa0", "sxa1", "sxb0", "sxb1",
                 "tpa0", "tpa1", "tpd0", "tpd1", "tpw",
                 "gq0", "gq1", "gw0", "gw1", "gpp0", "gpp1",
                 "gdiv", "gc0", "gc1", "gx0", "gx1", "gy0", "gy1", "gdbg",
                 "gld0", "gld1", "gld2", "gld3", "gld4", "gdv2",
                 "pe", "act", "dve"]
    sems = {n: nc.alloc_semaphore(n) for n in sem_names}
    cnt = {n: 0 for n in sem_names}
    OPS = []

    def emit(engine, waits, incs, body):
        wl = []
        for s, v in waits:
            assert v <= cnt[s], f"wait on future: {s} {v} > {cnt[s]}"
            if v > 0:
                wl.append((sems[s], v))

        def fn(eng, wl=wl, incs=incs, body=body):
            for sh, v in wl:
                eng.wait_ge(sh, v)
            inst = body(eng)
            for s, k in incs:
                inst = inst.then_inc(sems[s], k)
        OPS.append((engine, fn))
        for s, k in incs:
            cnt[s] += k

    def mk(s):
        return (s, cnt[s])

    def ap3(t, off, dims):
        return bass_rust.AP(t[:].tensor, off, dims)

    from concourse.hw_specs import get_activation_tables
    tabs = list(get_activation_tables(nc.m.arch).items())
    id_nle = next(i for i, (n, f) in enumerate(tabs) if AF.Exp in f and AF.Ln in f)
    id_exp = next(i for i, (n, f) in enumerate(tabs) if AF.Exp in f and AF.Tanh in f)

    def load_table(idx):
        emit("scalar", [], [], lambda e, idx=idx: e.add_instruction(
            mybir.InstLoadActFuncSet(name=nc.get_next_instruction_name(),
                                     act_func_set_id=idx, ins=[], outs=[])))

    emit("vector", [], [], lambda e: e.memset(or_sb[0:1, :], 1.0))
    emit("vector", [], [], lambda e: e.memset(oc_sb[:, :], 1.0))
    for g2 in range(2):
        for hl in range(4):
            off = g2 * 4160 + hl * 1040 + 64
            apm = ap3(vh_sb, off, [[8320, 128], [65, NT]])
            emit("vector", [], [], lambda e, apm=apm: e.memset(apm, 1.0))
    emit("vector", [], [], lambda e: e.memset(eps_sb[:, :], EPS))
    emit("vector", [], [("dve", 1)], lambda e: e.memset(st_sb[:, :], 0.0))
    ones_done = mk("dve")

    load_table(id_nle)

    emit("sync", [], [("ssm", 16)], lambda e: e.dma_start(out=oh_sb[:, :], in_=oh_d))
    emit("sync", [], [("ssm", 16)], lambda e: e.dma_start(out=rel_sb[:, :], in_=rel_d))
    emit("sync", [], [("ssm", 16)], lambda e: e.dma_start(out=eye_sb[:, :], in_=eye_d))
    emit("sync", [], [("ssm", 16)], lambda e: e.dma_start(out=ctxu_sb[0:1, 0:D], in_=fb_d))
    small_ld = mk("ssm")

    emit("tensor", [small_ld], [("pe", 1)],
         lambda e: e.matmul(msc_ps[0:16, 0:NREV], rel_sb[0:32, 0:16],
                            oh_sb[0:32, 0:NREV], start=True, stop=True))
    emit("scalar", [mk("pe")], [("act", 1)],
         lambda e: e.activation(rev_sb[0:16, :], msc_ps[0:16, 0:NREV], AF.Copy,
                                bias=-C_SHIFT, scale=1.0))
    rev_done = mk("act")

    pp_ready = [None, None]
    pp_freed = [("dve", 0), ("dve", 0)]

    def emit_patp(h, waits):
        pbuf = h % 2
        sem = f"gpp{pbuf}"
        src = ap3(rev_sb, h * NREV, [[NREV, 1], [1, 128], [1, 384]])
        dst = pp_sb[:, pbuf * 384:(pbuf + 1) * 384]
        emit("gpsimd", waits, [(sem, 16)],
             lambda e, dst=dst, src=src: e.dma_start(out=dst, in_=src))
        pp_ready[pbuf] = mk(sem)

    emit_patp(0, [rev_done])
    emit_patp(1, [rev_done])
    patp_prologue = [mk("gpp0"), mk("gpp1")]

    emit("tensor", [rev_done], [],
         lambda e: e.matmul(msc_ps[0:1, 0:16], rev_sb[0:16, 0:1],
                            eye_sb[0:16, 0:16], start=True, stop=True))
    emit("tensor", [], [("pe", 1)],
         lambda e: e.matmul(msc_ps[0:1, 16:32], rev_sb[0:16, 510:511],
                            eye_sb[0:16, 0:16], start=True, stop=True))
    emit("vector", [mk("pe"), ones_done], [("dve", 1)],
         lambda e: e.tensor_copy(st_sb[0:1, 0:32], msc_ps[0:1, 0:32]))
    emit("tensor", [mk("dve")], [("pe", 1)],
         lambda e: e.matmul(msc_ps[0:128, 0:32], or_sb[0:1, 0:128],
                            st_sb[0:1, 0:32], start=True, stop=True))
    emit("vector", [mk("pe")], [("dve", 1)],
         lambda e: e.tensor_copy(ca_sb[:, :], msc_ps[:, 0:32]))

    for half in range(2):
        emit("tensor", [small_ld, ones_done, mk("dve")], [("pe", 1)],
             lambda e, half=half: e.matmul(
                 qkv_ps[:, 0:512], or_sb[0:1, 0:128],
                 ctxu_sb[0:1, half * 512:(half + 1) * 512], start=True, stop=True))
        emit("vector", [mk("pe")], [("dve", 1)],
             lambda e, half=half: e.tensor_copy(
                 br_sb[:, half * 512:(half + 1) * 512], qkv_ps[:, 0:512]))
    pro1_done = mk("dve")

    wbf, abf = ctxT, ln1T
    xa_free = [pro1_done, pro1_done]

    _gld_i = [0]

    def cast_load(src_d, dst_t, dst_off, waits):
        sem = f"gld{_gld_i[0]}"
        _gld_i[0] += 1
        pitch = dst_t[:].ap[0][0]
        s_ap = bass_rust.AP(src_d.tensor, 0,
                            [[1024, 128], [128 * 1024, NCH], [1, 1024]])
        d_ap = bass_rust.AP(dst_t[:].tensor, dst_off,
                            [[pitch, 128], [1024, NCH], [1, 1024]])
        emit("gpsimd", waits, [(sem, 16)],
             lambda e, d_ap=d_ap, s_ap=s_ap: e.dma_start(out=d_ap, in_=s_ap))
        return mk(sem)

    w_cast = cast_load(wf_d, wbf, 0, [pro1_done])
    for i in range(NCH):
        src = ap3(wbf, i * 1024, [[NCH * S, 128], [128, 8], [1, 128]])
        dst = ap3(wbf, 8192 + i * 128, [[NCH * S, 128], [1024, 8], [1, 128]])
        w = [w_cast]
        if i == 0:
            w.extend(patp_prologue)
        emit("sync", w, [("tpw", 16)],
             lambda e, dst=dst, src=src: e.dma_start(out=dst, in_=src, transpose=True))
    wt_done = mk("tpw")
    a1_done = cast_load(wa1_d, abf, 0, [pro1_done])

    def mm_chain(lhs_fn, rhs_fn, ps, dst_t, dst_fn, extra_first_waits, gate0):
        gate = gate0
        for ot in range(NCH):
            for ob in range(2):
                for ch in range(NCH):
                    w = []
                    if ch == 0:
                        w = list(extra_first_waits) + [gate]
                    emit("tensor", w, [("pe", 1)] if ch == NCH - 1 else [],
                         lambda e, ot=ot, ob=ob, ch=ch, ps=ps: e.matmul(
                             ps[:, 0:512], lhs_fn(ch, ot), rhs_fn(ch, ob),
                             start=(ch == 0), stop=(ch == NCH - 1)))
                emit("vector", [mk("pe")], [("dve", 1)],
                     lambda e, ot=ot, ob=ob, dst_t=dst_t: e.tensor_copy(
                         dst_fn(dst_t, ot, ob), qkv_ps[:, 0:512] if ps is qkv_ps
                         else msc_ps[:, 0:512]))
                gate = mk("dve")
        return mk("dve")

    m1t_done = mm_chain(
        lambda ch, jt: abf[:, ch * 1024 + jt * 128: ch * 1024 + jt * 128 + 128],
        lambda ch, cb: wbf[:, 8192 + ch * 1024 + cb * 512: 8192 + ch * 1024 + cb * 512 + 512],
        msc_ps, abf,
        lambda t, jt, cb: t[:, 8192 + jt * 1024 + cb * 512: 8192 + jt * 1024 + cb * 512 + 512],
        [a1_done, wt_done], a1_done)
    a2_done = cast_load(wa2_d, abf, 0, [m1t_done])
    m_done = mm_chain(
        lambda ch, ct: abf[:, 8192 + ch * 1024 + ct * 128: 8192 + ch * 1024 + ct * 128 + 128],
        lambda ch, jb: abf[:, ch * 1024 + jb * 512: ch * 1024 + jb * 512 + 512],
        qkv_ps, m_sb,
        lambda t, ct, jb: t[:, ct * 1024 + jb * 512: ct * 1024 + jb * 512 + 512],
        [a2_done], a2_done)


    if DBG_DUMP:
        emit("gpsimd", [mk("dve"), mk("act")], [("gdbg", 16)],
             lambda e: e.dma_start(out=dbg["d_rev"], in_=rev_sb[:, :]))
        emit("gpsimd", [], [("gdbg", 16)],
             lambda e: e.dma_start(out=dbg["d_ca"], in_=ca_sb[:, :]))
        emit("gpsimd", [], [("gdbg", 16)],
             lambda e: e.dma_start(out=dbg["d_pp"], in_=pp_sb[:, :]))

    s_free = {"A": ("act", 0), "B": ("act", 0)}
    pt_reader = [("pe", 0)] * 4
    kt_slot = [0]
    qs_ready = [None, None]
    qs_freed = [("pe", 0), ("pe", 0)]
    qt_freed = [("pe", 0), ("pe", 0)]
    qkv_done = {}
    vh_ready = [("dve", 0), ("dve", 0)]
    vh_free = [("pe", 0), ("pe", 0)]
    qkv_gate = [m_done]
    lnT_tp = [("tpa0", 0)] * NT
    ctx_gate = [("dve", 0)]
    ctx_done = [("pe", 0)]
    div_tt_gate = [("dve", 0)]
    c1_xa_fence = [("gdv2", 0)]
    misc_gate = [m1t_done]
    acc_i = [0]
    div_r2_free = [("gdiv", 0)]
    div_rb_free = [("gdv2", 0)]
    div_xa_free = [("dve", 0)]
    div_pending = []
    bgq = []


    def bgq_pop(n):
        for _ in range(min(n, len(bgq))):
            bgq.pop(0)()

    def bgq_flush():
        while bgq:
            bgq.pop(0)()

    stage_free = {0: m_done, 1: m_done}
    wv_done_c = [None]
    wo_done_c = [None]
    wdup_done_c = [None]

    def q_cast_chunk(src_d, dst_t, dst_off, ch, done_cell):
        def go():
            buf = ch % 2
            emit("sync", [xa_free[buf]], [(f"sxa{buf}", 16)],
                 lambda e: e.dma_start(
                     out=xa_sb[:, buf * 1024:(buf + 1) * 1024],
                     in_=src_d[ch * 128:(ch + 1) * 128, :]))
            emit("vector", [mk(f"sxa{buf}")], [("dve", 1)],
                 lambda e: e.tensor_copy(
                     dst_t[:, dst_off + ch * 1024: dst_off + (ch + 1) * 1024],
                     xa_sb[:, buf * 1024:(buf + 1) * 1024]))
            xa_free[buf] = mk("dve")
            if done_cell is not None and ch == NCH - 1:
                done_cell[0] = mk("dve")
        return go

    def q_dup_chunk(src_d, dst_d, ch, done_cell):
        def go():
            buf = ch % 2
            st_t = lt_sb if buf == 0 else pt_sb
            emit("sync", [xa_free[buf]], [(f"sxa{buf}", 16)],
                 lambda e: e.dma_start(
                     out=xa_sb[:, buf * 1024: buf * 1024 + 1024],
                     in_=src_d[ch * 128:(ch + 1) * 128, :]))
            sy_m = mk(f"sxa{buf}")
            pitch = NCH * 512 if st_t is pt_sb else 2048
            for rep in range(2):
                src_ap = ap3(xa_sb, buf * 1024, [[2048, 128], [64, 16], [1, 64]])
                dst_ap = ap3(st_t, rep * 64, [[pitch, 128], [128, 16], [1, 64]])
                emit("vector", [sy_m, stage_free[buf]], [("dve", 1)],
                     lambda e, dst_ap=dst_ap, src_ap=src_ap:
                     e.tensor_copy(dst_ap, src_ap))
            xa_free[buf] = mk("dve")
            emit("gpsimd", [mk("dve")], [(f"gw{buf}", 16)],
                 lambda e: e.dma_start(
                     out=dst_d[ch],
                     in_=st_t[:, 0:2048].rearrange("p (h c) -> p h c", h=16)))
            stage_free[buf] = mk(f"gw{buf}")
            if done_cell is not None and ch == NCH - 1:
                done_cell[0] = None
        return go

    for _ch in range(NCH):
        bgq.append(q_dup_chunk(wq_d, qdup_d, _ch, None))
    for _ch in range(NCH):
        bgq.append(q_dup_chunk(wk_d, kdup_d, _ch, wdup_done_c))
    def _wv_item():
        wv_done_c[0] = cast_load(wv_d, wv_sb, 0, [])
    def _wo_item():
        wo_done_c[0] = cast_load(wo_d, wo_sb, 0, [])
    bgq.append(_wv_item)
    bgq.append(_wo_item)

    def emit_qk_stream(h):
        buf = h % 2
        for (src_d, dst_t) in ((qdup_d, qs_sb), (kdup_d, ks_sb)):
            src = bass_rust.AP(src_d.tensor, h * 128,
                               [[H * 128, 128], [128 * H * 128, NCH], [1, 128]])
            dst = ap3(dst_t, buf * 1024, [[2048, 128], [128, 8], [1, 128]])
            emit("gpsimd", [qs_freed[buf]] + wdup_done_c[0], [(f"gq{buf}", 16)],
                 lambda e, dst=dst, src=src: e.dma_start(out=dst, in_=src))
        qs_ready[buf] = mk(f"gq{buf}")

    def acc_pick():
        i = (acc_i[0] % 2) if not DBG_NO_ROT else 0
        acc_i[0] += 1
        if i == 0:
            return qkv_ps, (lambda: qkv_gate[0]), qkv_gate
        return msc_ps, (lambda: misc_gate[0]), misc_gate

    def emit_qkv_group(h, proj, qb):
        buf = h % 2
        w_src = qs_sb if proj == 0 else ks_sb
        t_dst = qt_sb if proj == 0 else kt_sb
        ps, gate_get, gate_cell = acc_pick()
        for ch in range(NCH):
            w = []
            if ch == 0:
                w = [qs_ready[buf], lnT_tp[qb * 4 + 2],
                     lnT_tp[qb * 4 + 3], gate_get()]
            emit("tensor", w, [("pe", 1)] if ch == NCH - 1 else [],
                 lambda e, ch=ch, qb=qb, buf=buf, w_src=w_src, ps=ps:
                 e.matmul(ps[:, 0:512],
                          w_src[:, buf * 1024 + ch * 128: buf * 1024 + ch * 128 + 128],
                          ln1T[:, ch * S + qb * 512: ch * S + qb * 512 + 512],
                          start=(ch == 0), stop=(ch == NCH - 1)))
        emit("vector", [mk("pe")], [("dve", 1)],
             lambda e, qb=qb, buf=buf, t_dst=t_dst, ps=ps:
             e.tensor_copy(t_dst[:, buf * S + qb * 512: buf * S + qb * 512 + 512],
                           ps[:, 0:512]))
        gate_cell[0] = mk("dve")
        if proj == 1 and qb == 3:
            qkv_done[h] = mk("dve")

    def emit_v_tile(g, tt):
        gb = g % 2
        ps, gate_get, gate_cell = acc_pick()
        for ch in range(NCH):
            w = []
            if ch == 0:
                w = [lnT_tp[tt], gate_get(), wv_done_c[0]]
            emit("tensor", w, [("pe", 1)] if ch == NCH - 1 else [],
                 lambda e, ch=ch, tt=tt, g=g, ps=ps:
                 e.matmul(ps[:, 0:256],
                          ln1T[:, ch * S + tt * 128: ch * S + tt * 128 + 128],
                          wv_sb[:, ch * D + g * 256: ch * D + g * 256 + 256],
                          start=(ch == 0), stop=(ch == NCH - 1)))
        src = ap3(ps, 0, [[512, 128], [64, 4], [1, 64]])
        dst = ap3(vh_sb, gb * 4160 + tt * 65, [[8320, 128], [1040, 4], [1, 64]])
        emit("vector", [mk("pe"), vh_free[gb]], [("dve", 1)],
             lambda e, dst=dst, src=src: e.tensor_copy(dst, src))
        gate_cell[0] = mk("dve")
        if tt == NT - 1:
            vh_ready[gb] = mk("dve")

    def queue_head_work(h):
        bgq.append(lambda h=h: emit_qk_stream(h))
        for proj in range(2):
            for qb in range(4):
                bgq.append(lambda h=h, p=proj, q=qb: emit_qkv_group(h, p, q))

    def a_begin(item):
        a_tp_fence[0] = [mk("gpp0"), mk("gpp1"), mk("gc0"), mk("gc1"),
                         mk("gdiv")]
        a_xa_fence[0] = mk("gdv2")
        st8 = dict(item=item, base=item * S,
                   rstd=[("act", 0)] * NT, norm=[("dve", 0)] * NT,
                   tp=[("tpa0", 0)] * NT,
                   ln_fence=[mk("tpa0"), mk("tpa1"), mk("tpd0"), mk("tpd1"),
                             ln_stage_free[0], ln_stage_free[1]],
                   pe_fence=mk("pe"))
        return st8

    def a_tile(st8, tt):
        base = st8["base"]
        buf = tt % 2
        sc = 18 * buf
        w = [xa_free[buf]]
        if tt >= 2:
            w.append(st8["norm"][tt - 2])
        else:
            w.append(a_xa_fence[0])
        emit("sync", w, [(f"sxa{buf}", 16)],
             lambda e, tt=tt, buf=buf, base=base: e.dma_start(
                 out=xa_sb[:, buf * 1024: buf * 1024 + 1024],
                 in_=x8[base + tt * 128: base + tt * 128 + 128, :]))
        sy_m = mk(f"sxa{buf}")
        wv_ = [sy_m]
        if tt >= 2:
            wv_.append(st8["rstd"][tt - 2])
        emit("vector", wv_, [],
             lambda e, buf=buf, sc=sc: e.bn_stats(
                 st_sb[:, sc:sc + 6], xa_sb[:, buf * 1024: buf * 1024 + 512]))
        emit("vector", [], [("dve", 1)],
             lambda e, buf=buf, sc=sc: e.bn_stats(
                 st_sb[:, sc + 6:sc + 12],
                 xa_sb[:, buf * 1024 + 512: buf * 1024 + 1024]))
        emit("vector", [mk("dve")], [("dve", 1)],
             lambda e, sc=sc: e.bn_aggr(st_sb[:, sc + 12:sc + 14],
                                        st_sb[:, sc:sc + 12]))
        emit("scalar", [mk("dve")], [("act", 1)],
             lambda e, sc=sc: e.activation(st_sb[:, sc + 14:sc + 15],
                                           st_sb[:, sc + 13:sc + 14],
                                           AF.Ln, bias=eps_sb[:, 0:1], scale=1.0))
        emit("scalar", [mk("act")], [("act", 1)],
             lambda e, sc=sc: e.activation(st_sb[:, sc + 15:sc + 16],
                                           st_sb[:, sc + 14:sc + 15],
                                           AF.Exp, scale=-0.5))
        st8["rstd"][tt] = mk("act")
        emit("vector", [st8["rstd"][tt], mk("dve")], [("dve", 1)],
             lambda e, sc=sc: e.tensor_scalar(
                 st_sb[:, sc + 16:sc + 17], st_sb[:, sc + 12:sc + 13],
                 st_sb[:, sc + 15:sc + 16], -1.0, op0=OP.mult, op1=OP.mult))
        wv2 = [st8["tp"][tt - 2], mk("dve")] if tt >= 2 else \
            list(st8["ln_fence"]) + [mk("dve")]
        emit("vector", wv2, [("dve", 1)],
             lambda e, buf=buf, sc=sc: e.tensor_scalar(
                 ln_sb[:, buf * 1024: buf * 1024 + 1024],
                 xa_sb[:, buf * 1024: buf * 1024 + 1024],
                 st_sb[:, sc + 15:sc + 16], st_sb[:, sc + 16:sc + 17],
                 op0=OP.mult, op1=OP.add))
        st8["norm"][tt] = mk("dve")
        xa_free[buf] = st8["norm"][tt]
        src_ = ap3(ln_sb, buf * 1024, [[2048, 128], [128, 8], [1, 128]])
        dst_ = ap3(ln1T, tt * 128, [[NCH * S, 128], [S, 8], [1, 128]])
        wt = [st8["norm"][tt]] + a_tp_fence[0]
        if tt == 0:
            wt.append(st8["pe_fence"])
        emit("sync", wt, [(f"tpa{buf}", 16)],
             lambda e, dst_=dst_, src_=src_: e.dma_start(out=dst_, in_=src_,
                                                         transpose=True))
        st8["tp"][tt] = mk(f"tpa{buf}")
        lnT_tp[tt] = st8["tp"][tt]

    def a_end(st8):
        ln_stage_free[0] = st8["tp"][NT - 1]
        ln_stage_free[1] = st8["tp"][NT - 1]

    def phase_A(item):
        st8 = a_begin(item)
        for tt in range(NT):
            a_tile(st8, tt)
            bgq_pop(4)
        a_end(st8)
        return st8

    a_tp_fence = [[]]
    a_xa_fence = [("gdv2", 0)]
    ln_stage_free = [("gc0", 0), ("gc1", 0)]

    def band_range(kt, qh):
        lo = max(0, 128 * (kt - 1), 1024 * qh)
        hi = min(S, 128 * (kt + 2), 1024 * (qh + 1))
        return lo, hi

    def emit_sweep(item, h, qh):
        buf = h % 2
        g = h // 4
        gb = g % 2
        pbase = buf * S + qh * 1024
        exp_marks = {}
        s_pe = {}
        band_dve = {}

        def s_mms(ktp):
            ktA, ktB = 2 * ktp, 2 * ktp + 1
            for (kt, p0, coff, tp) in ((ktA, 0, 0, (0, 0)), (ktB, 64, 1024, (64, 0))):
                key = "A" if coff == 0 else "B"
                for half in range(2):
                    w = []
                    if half == 0:
                        w = [s_free[key], ("dve", qkv_done[h][1])]
                    emit("tensor", w, [("pe", 1)] if half == 1 else [],
                         lambda e, kt=kt, p0=p0, coff=coff, tp=tp, half=half,
                         buf=buf, pbase=pbase:
                         e.matmul(
                             s2_ps[:, coff + half * 512: coff + half * 512 + 512],
                             kt_sb[p0:p0 + 64, buf * S + kt * 128: buf * S + kt * 128 + 128],
                             qt_sb[p0:p0 + 64, pbase + half * 512: pbase + half * 512 + 512],
                             start=True, stop=True, tile_position=tp))
                s_pe[kt] = mk("pe")

        def band_add(kt, coff):
            lo, hi = band_range(kt, qh)
            if hi <= lo:
                return
            llo, lhi = lo - 1024 * qh, hi - 1024 * qh
            r0 = 128 * kt + 255 - lo
            ln_ = hi - lo
            ppo = (h % 2) * 384
            abs_start = ppo + r0
            abs_stop = ppo + r0 - ln_
            if abs_stop < 0:
                in1 = pp_sb[:, abs_start::-1]
            else:
                in1 = pp_sb[:, abs_start: abs_stop: -1]
            emit("vector", [s_pe[kt], pp_ready[h % 2]], [("dve", 1)],
                 lambda e, coff=coff, llo=llo, lhi=lhi, in1=in1:
                 e.tensor_tensor(s2_ps[:, coff + llo: coff + lhi],
                                 s2_ps[:, coff + llo: coff + lhi], in1, op=OP.add))
            band_dve[kt] = mk("dve")

        def exps(kt, coff, slot):
            lo, hi = band_range(kt, qh)
            segs = []
            if hi <= lo:
                segs = [(0, 1024, "lo" if 128 * (kt + 2) <= 1024 * qh else "hi")]
            else:
                llo, lhi = lo - 1024 * qh, hi - 1024 * qh
                if llo > 0:
                    segs.append((0, llo, "hi"))
                segs.append((llo, lhi, "band"))
                if lhi < 1024:
                    segs.append((lhi, 1024, "lo"))
            w = [s_pe[kt], pt_reader[slot]]
            if kt in band_dve:
                w.append(band_dve[kt])
            nseg = len(segs) + (1 if DBG_ACT_FLUSH else 0)
            for i, (a, b_, kind) in enumerate(segs):
                inc = [("act", 1)] if (i == len(segs) - 1 and not DBG_ACT_FLUSH) else []
                if kind == "band":
                    bias = None
                elif kind == "hi":
                    bias = ca_sb[:, 16 + h:16 + h + 1]
                else:
                    bias = ca_sb[:, h:h + 1]
                if bias is None:
                    emit("scalar", w if i == 0 else [], inc,
                         lambda e, a=a, b_=b_, coff=coff, slot=slot:
                         e.activation(pt_sb[:, slot * 1024 + a: slot * 1024 + b_],
                                      s2_ps[:, coff + a: coff + b_],
                                      AF.Exp, scale=1.0))
                else:
                    emit("scalar", w if i == 0 else [], inc,
                         lambda e, a=a, b_=b_, coff=coff, slot=slot, bias=bias:
                         e.activation(pt_sb[:, slot * 1024 + a: slot * 1024 + b_],
                                      s2_ps[:, coff + a: coff + b_],
                                      AF.Exp, bias=bias, scale=1.0))
            if DBG_ACT_FLUSH:
                emit("scalar", [], [("act", 1)],
                     lambda e, slot=slot: e.activation(
                         st_sb[0:1, 158:159],
                         pt_sb[0:1, slot * 1024: slot * 1024 + 1],
                         AF.Identity, scale=1.0))
            exp_marks[kt] = mk("act")
            key = "A" if coff == 0 else "B"
            s_free[key] = mk("act")

        def ctx_mms(kt, slot):
            vo = gb * 4160 + (h % 4) * 1040 + kt * 65
            for half in range(2):
                w = []
                if half == 0:
                    w = [exp_marks[kt], vh_ready[gb]]
                    if kt == 0:
                        w.append(ctx_gate[0])
                emit("tensor", w, [("pe", 1)] if half == 1 else [],
                     lambda e, kt=kt, slot=slot, half=half, vo=vo:
                     e.matmul(ctx_ps[0:65, half * 512: half * 512 + 512],
                              vh_sb[:, vo: vo + 65],
                              pt_sb[:, slot * 1024 + half * 512: slot * 1024 + half * 512 + 512],
                              start=(kt == 0), stop=(kt == NT - 1)))
            pt_reader[slot] = mk("pe")
            if kt == NT - 1:
                ctx_done[0] = mk("pe")

        prev = None
        for ktp in range(8):
            ktA, ktB = 2 * ktp, 2 * ktp + 1
            slA = kt_slot[0] % 4
            kt_slot[0] += 1
            slB = kt_slot[0] % 4
            kt_slot[0] += 1
            s_mms(ktp)
            band_add(ktA, 0)
            band_add(ktB, 1024)
            exps(ktA, 0, slA)
            exps(ktB, 1024, slB)
            if DBG_DUMP and item == DBG_DUMP_ITEM and h == 0 and qh == 0 and ktp == 0:
                emit("gpsimd", [mk("act")], [("gdbg", 16)],
                     lambda e, slA=slA, slB=slB: e.dma_start(
                         out=dbg["d_pt"], in_=pt_sb[:, :]))
                pt_reader[slA] = mk("gdbg")
                pt_reader[slB] = mk("gdbg")
            if prev is not None:
                ctx_mms(prev[0], prev[2])
                ctx_mms(prev[1], prev[3])
            if div_pending:
                div_pending.pop(0)()
            bgq_pop(2 if (ktp & 1) else 1)
            prev = (ktA, ktB, slA, slB)
        ctx_mms(prev[0], prev[2])
        ctx_mms(prev[1], prev[3])

    def queue_division(item, h, qh):
        ch = h // 2
        odd = h % 2 == 1
        pe_ctx = ctx_done[0]
        st = {}

        def d1():
            emit("vector", [pe_ctx, div_tt_gate[0]], [("dve", 1)],
                 lambda e: e.tensor_copy(ctxu_sb[0:65, :], ctx_ps[0:65, :]))
            ctx_gate[0] = mk("dve")
            st["d1"] = mk("dve")

        def d2():
            scat_src = ap3(ctxu_sb, 64 * 1024, [[1024, 1], [16, 64], [1, 16]])
            emit("sync", [st["d1"], div_r2_free[0]] + pp_tp_fence[0],
                 [("gdiv", 16)],
                 lambda e: e.dma_start(out=st_sb[64:128, 128:144], in_=scat_src))
            emit("vector", [mk("gdiv")], [("dve", 1)],
                 lambda e: e.reciprocal(st_sb[64:128, 128:144],
                                        st_sb[64:128, 128:144]))
            gat_src = ap3(st_sb, 64 * 160 + 128, [[160, 64], [1, 16]])
            emit("sync", [mk("dve"), div_rb_free[0]], [("gdiv", 16)],
                 lambda e: e.dma_start(out=rcp_d, in_=gat_src))
            bc_src = bass_rust.AP(rcp_d.tensor, 0, [[0, 64], [1, 1024]])
            emit("sync", [mk("gdiv"), div_xa_free[0]], [("gdv2", 16)],
                 lambda e: e.dma_start(out=xa_sb[0:64, 0:1024], in_=bc_src))
            st["d2"] = mk("gdv2")
            div_r2_free[0] = mk("gdiv")
            div_rb_free[0] = st["d2"]

        def d4(half):
            if odd:
                out = ln_sb[0:64, qh * 1024 + half * 512: qh * 1024 + half * 512 + 512]
            else:
                out = ctxT[0:64, ch * S + qh * 1024 + half * 512:
                           ch * S + qh * 1024 + half * 512 + 512]
            w = [st["d2"]]
            if half == 0:
                w.append(div_tt_gate[0])
            if odd:
                w.append(ln_stage_free[qh])
            emit("vector", w, [("dve", 1)],
                 lambda e, out=out, half=half: e.tensor_tensor(
                     out, ctxu_sb[0:64, half * 512: half * 512 + 512],
                     xa_sb[0:64, half * 512: half * 512 + 512], op=OP.mult))
            div_tt_gate[0] = mk("dve")
            if half == 1:
                div_xa_free[0] = mk("dve")
            st["d4"] = mk("dve")

        def d5():
            emit("sync", [st["d4"]] + pp_tp_fence[0], [(f"gc{qh}", 16)],
                 lambda e: e.dma_start(
                     out=ctxT[64:128, ch * S + qh * 1024: ch * S + qh * 1024 + 1024],
                     in_=ln_sb[0:64, qh * 1024: qh * 1024 + 1024]))
            ln_stage_free[qh] = mk(f"gc{qh}")

        seq = [d1, d2, lambda: d4(0), lambda: d4(1)]
        if odd:
            seq.append(d5)
        div_pending.extend(seq)

    def phase_B(item):
        for h in range(H):
            buf = h % 2
            if 1 <= h and h + 1 < H:
                queue_head_work(h + 1)
            nh = (h + 1) % H
            emit_patp(nh, [pp_freed[nh % 2]] + pp_tp_fence[0])
            if h % 4 == 1 and h // 4 + 1 <= 3:
                g = h // 4 + 1
                for tt in range(NT):
                    bgq.append(lambda g=g, tt=tt: emit_v_tile(g, tt))
            if h not in qkv_done:
                bgq_flush()
            assert h in qkv_done, f"QKV for head {h} not emitted"
            for qh in range(2):
                emit_sweep(item, h, qh)
                queue_division(item, h, qh)
            pp_freed[h % 2] = mk("dve")
            qt_freed[buf] = mk("pe")
            qs_freed[buf] = mk("pe")
            if h % 4 == 3:
                vh_free[(h // 4) % 2] = mk("pe")
        while div_pending:
            div_pending.pop(0)()
        bgq_flush()

    pp_tp_fence = [[]]

    def phase_C1(item, next_item):
        base = item * S
        div_done = mk("dve")
        div_gp = [mk("gc0"), mk("gc1"), mk("gdiv"), wo_done_c[0]]
        misc_gate[0] = div_done
        qkv_gate[0] = max(qkv_gate[0], div_done, key=lambda t: t[1])
        bank_gate = [div_done, div_done]
        xf = [mk("dve"), mk("dve")]
        c1_xa_fence[0] = mk("gdv2")
        xbf = [[mk("gy0"), mk("gx0")], [mk("gy1"), mk("gx1")]]
        st8 = a_begin(next_item) if next_item is not None else None
        for tt in range(NT):
            buf = tt % 2
            emit("scalar", [xf[buf], xa_free[buf], c1_xa_fence[0]], [(f"sxa{buf}", 16)],
                 lambda e, tt=tt, buf=buf, base=base: e.dma_start(
                     out=xa_sb[:, buf * 1024: buf * 1024 + 1024],
                     in_=x8[base + tt * 128: base + tt * 128 + 128, :]))
            sy_m = mk(f"sxa{buf}")
            for dh in range(2):
                ps = msc_ps if dh == 0 else qkv_ps
                for ch in range(NCH):
                    w2 = []
                    if ch == 0:
                        w2 = [bank_gate[dh]] + div_gp
                        if dh == 1:
                            w2.append(qkv_gate[0])
                    emit("tensor", w2, [("pe", 1)] if ch == NCH - 1 else [],
                         lambda e, ch=ch, tt=tt, dh=dh, ps=ps: e.matmul(
                             ps[:, 0:512],
                             ctxT[:, ch * S + tt * 128: ch * S + tt * 128 + 128],
                             wo_sb[:, ch * D + dh * 512: ch * D + dh * 512 + 512],
                             start=(ch == 0), stop=(ch == NCH - 1)))
                emit("vector", [mk("pe"), sy_m] + xbf[buf], [("dve", 1)],
                     lambda e, buf=buf, dh=dh, ps=ps: e.tensor_tensor(
                         xb_sb[:, buf * 1024 + dh * 512: buf * 1024 + dh * 512 + 512],
                         ps[:, 0:512],
                         xa_sb[:, buf * 1024 + dh * 512: buf * 1024 + dh * 512 + 512],
                         op=OP.add))
                bank_gate[dh] = mk("dve")
                if dh == 1:
                    qkv_gate[0] = mk("dve")
            xf[buf] = mk("dve")
            xa_free[buf] = mk("dve")
            sc = 64 + 2 * tt
            emit("vector", [mk("dve")], [],
                 lambda e, buf=buf: e.bn_stats(
                     st_sb[:, 36:42], xb_sb[:, buf * 1024: buf * 1024 + 512]))
            emit("vector", [], [("dve", 1)],
                 lambda e, buf=buf: e.bn_stats(
                     st_sb[:, 42:48], xb_sb[:, buf * 1024 + 512: buf * 1024 + 1024]))
            emit("vector", [mk("dve")], [("dve", 1)],
                 lambda e, sc=sc: e.bn_aggr(st_sb[:, sc:sc + 2], st_sb[:, 36:48]))
            emit("gpsimd", [mk("dve")], [(f"gx{buf}", 16)],
                 lambda e, tt=tt, buf=buf: e.dma_start(
                     out=x2_d[tt * 128: tt * 128 + 128, :],
                     in_=xb_sb[:, buf * 1024: buf * 1024 + 1024]))
            xbf[buf] = [mk(f"gx{buf}")]
            if st8 is not None:
                a_tile(st8, tt)
        if st8 is not None:
            a_end(st8)
        emit("scalar", [mk("dve")], [("act", 1)],
             lambda e: e.activation(st_sb[:, 96:112], st_sb[:, 65:97:2],
                                    AF.Ln, bias=eps_sb[:, 0:1], scale=1.0))
        emit("scalar", [mk("act")], [("act", 1)],
             lambda e: e.activation(st_sb[:, 112:128], st_sb[:, 96:112],
                                    AF.Exp, scale=-0.5))
        rstd2 = mk("act")
        emit("vector", [rstd2], [("dve", 1)],
             lambda e: e.scalar_tensor_tensor(
                 st_sb[:, 48:64], st_sb[:, 64:96:2], -1.0, st_sb[:, 112:128],
                 op0=OP.mult, op1=OP.mult))
        div_tt_gate[0] = mk("dve")
        return mk("dve"), [mk("gx0"), mk("gx1")], st8

    def phase_D(item, nm2_done, x2_ready, a_next):
        base = item * S
        load_table(id_exp)
        xbfree = [nm2_done, nm2_done]
        lnfree = [("pe", cnt["pe"]), ("pe", cnt["pe"])]
        ltfree = [("pe", 0), ("pe", 0)]
        misc_gate[0] = mk("dve")
        qkv_gate[0] = mk("dve")
        tanh_done = [("act", 0)]
        tanh_war = [mk("gdiv")]
        yfree = [("gy0", 0), ("gy1", 0)]
        for tt in range(NT):
            buf = tt % 2
            emit("scalar", [xbfree[buf]] + x2_ready, [(f"sxb{buf}", 16)],
                 lambda e, tt=tt, buf=buf: e.dma_start(
                     out=xb_sb[:, buf * 1024: buf * 1024 + 1024],
                     in_=x2_d[tt * 128: tt * 128 + 128, :]))
            sy_m = mk(f"sxb{buf}")
            emit("vector", [sy_m, nm2_done, lnfree[buf]], [("dve", 1)],
                 lambda e, tt=tt, buf=buf: e.tensor_scalar(
                     qs_sb[:, buf * 1024: buf * 1024 + 1024],
                     xb_sb[:, buf * 1024: buf * 1024 + 1024],
                     st_sb[:, 112 + tt:113 + tt], st_sb[:, 48 + tt:49 + tt],
                     op0=OP.mult, op1=OP.add))
            dve_m = mk("dve")
            src_ = ap3(qs_sb, buf * 1024, [[2048, 128], [128, 8], [1, 128]])
            dst_ = ap3(lt_sb, buf * 1024, [[2048, 128], [128, 8], [1, 128]])
            wt = [dve_m, ltfree[buf]]
            if tt == 0:
                wt.extend(d_tp_fence[0])
            emit("sync", wt, [(f"tpd{buf}", 16)],
                 lambda e, dst_=dst_, src_=src_: e.dma_start(out=dst_, in_=src_,
                                                             transpose=True))
            tp_m = mk(f"tpd{buf}")
            lnfree[buf] = tp_m
            for dh in range(2):
                ps, gate_get, gate_cell = acc_pick()
                for chm in range(NCH):
                    w3 = []
                    if chm == 0:
                        w3 = [tp_m, gate_get()]
                    emit("tensor", w3, [("pe", 1)] if chm == NCH - 1 else [],
                         lambda e, chm=chm, buf=buf, dh=dh, ps=ps: e.matmul(
                             ps[:, 0:512],
                             lt_sb[:, buf * 1024 + chm * 128: buf * 1024 + chm * 128 + 128],
                             m_sb[:, chm * D + dh * 512: chm * D + dh * 512 + 512],
                             start=(chm == 0), stop=(chm == NCH - 1)))
                pe_m = mk("pe")
                emit("vector", [pe_m, tanh_done[0]], [("dve", 1)],
                     lambda e, dh=dh, ps=ps: e.tensor_tensor(
                         ctxu_sb[:, dh * 512: dh * 512 + 512], ps[:, 0:512],
                         br_sb[:, dh * 512: dh * 512 + 512], op=OP.add))
                gate_cell[0] = mk("dve")
                emit("scalar", [mk("dve"), tanh_war[0]], [("act", 1)],
                     lambda e, dh=dh: e.activation(
                         aux_sb[:, dh * 512: dh * 512 + 512],
                         ctxu_sb[:, dh * 512: dh * 512 + 512],
                         AF.Tanh, scale=1.0))
                tanh_done[0] = mk("act")
                emit("vector", [mk("act"), yfree[buf]], [("dve", 1)],
                     lambda e, buf=buf, dh=dh: e.tensor_tensor(
                         xb_sb[:, buf * 1024 + dh * 512: buf * 1024 + dh * 512 + 512],
                         xb_sb[:, buf * 1024 + dh * 512: buf * 1024 + dh * 512 + 512],
                         aux_sb[:, dh * 512: dh * 512 + 512], op=OP.add))
                tanh_war[0] = mk("dve")
            emit("gpsimd", [mk("dve")], [(f"gy{buf}", 16)],
                 lambda e, tt=tt, buf=buf, base=base: e.dma_start(
                     out=y8[base + tt * 128: base + tt * 128 + 128, :],
                     in_=xb_sb[:, buf * 1024: buf * 1024 + 1024]))
            yfree[buf] = mk(f"gy{buf}")
            xbfree[buf] = mk(f"gy{buf}")
            ltfree[buf] = ("pe", cnt["pe"])
        load_table(id_nle)
        qkv_gate[0] = mk("dve")
        misc_gate[0] = mk("dve")
        qs_freed[0] = mk("tpd0")
        qs_freed[1] = mk("tpd1")
        return

    d_tp_fence = [[]]

    a_next_state = None
    for item in range(min(B_PER, DBG_ITEMS)):
        if DBG_STAGE == "prologue":
            break
        if a_next_state is None:
            a_next_state = phase_A(item)
        if DBG_DUMP and item == DBG_DUMP_ITEM:
            emit("gpsimd", [lnT_tp[NT - 2], lnT_tp[NT - 1]], [("gdbg", 16)],
                 lambda e: e.dma_start(out=dbg["d_ln1t"], in_=ln1T[:, :]))
        if DBG_STAGE == "A":
            a_next_state = None
            continue
        bgq_flush()
        if item == 0:
            wdup_done_c[0] = [mk("gw0"), mk("gw1")]
            pt_reader[0] = mk("gw1")
            pt_reader[1] = mk("gw1")
        emit_qk_stream(0)
        for proj in range(2):
            for qb in range(4):
                emit_qkv_group(0, proj, qb)
        for tt in range(NT):
            emit_v_tile(0, tt)
        if DBG_DUMP and item == DBG_DUMP_ITEM:
            emit("gpsimd", [mk("dve")], [("gdbg", 16)],
                 lambda e: e.dma_start(out=dbg["d_qt"], in_=qt_sb[:, :]))
            emit("gpsimd", [], [("gdbg", 16)],
                 lambda e: e.dma_start(out=dbg["d_kt"], in_=kt_sb[:, :]))
            emit("gpsimd", [], [("gdbg", 16)],
                 lambda e: e.dma_start(out=dbg["d_vh"], in_=vh_sb[:, 0:8320]))
        queue_head_work(1)
        pp_tp_fence[0] = [mk("tpa0"), mk("tpa1"), mk("tpd0"), mk("tpd1"),
                          mk("tpw")]
        phase_B(item)
        if DBG_DUMP and item == DBG_DUMP_ITEM:
            emit("gpsimd", [mk("dve"), mk("gc0"), mk("gc1")], [("gdbg", 16)],
                 lambda e: e.dma_start(out=dbg["d_ctxt"], in_=ctxT[:, :]))
        if DBG_STAGE == "B":
            a_next_state = None
            continue
        nxt = item + 1 if item + 1 < min(B_PER, DBG_ITEMS) else None
        nm2_done, x2_ready, a_next_state = phase_C1(item, nxt)
        if DBG_STAGE == "C1":
            continue
        d_tp_fence[0] = [mk("gpp0"), mk("gpp1"), mk("gc0"), mk("gc1"),
                         mk("gdiv")]
        phase_D(item, nm2_done, x2_ready, a_next_state)

    emit("gpsimd", [], [], lambda e: e.wait_ge(sems["gy0"], cnt["gy0"]))
    emit("gpsimd", [], [], lambda e: e.wait_ge(sems["gy1"], cnt["gy1"]))

    if DBG_PREFIX is not None:
        del OPS[DBG_PREFIX:]
    print(f"[kernel] total ops: {len(OPS)}")

    with nc.Block() as block:
        for eng_name in ["sync", "gpsimd", "tensor", "vector", "scalar"]:
            def make(eng_name):
                def run(eng):
                    for e_, fn in OPS:
                        if e_ == eng_name:
                            fn(eng)
                return run
            getattr(block, eng_name)(make(eng_name))

    return nc


_PROG = None


def kernel(L0_attn_hidden_states, L0_attn_Wq, L0_attn_Wk, L0_attn_Wv, L0_attn_Wo,
           L0_attn_rel_emb, L0_ffn_weight, L0_ffn_aux_weight1, L0_ffn_aux_weight2,
           L0_ffn_bias):
    global _PROG
    x = np.ascontiguousarray(np.asarray(L0_attn_hidden_states, dtype=np.float32))
    onehot, eye16 = _host_consts()
    shared = dict(
        wq=np.ascontiguousarray(np.asarray(L0_attn_Wq, np.float32)),
        wk=np.ascontiguousarray(np.asarray(L0_attn_Wk, np.float32)),
        wv=np.ascontiguousarray(np.asarray(L0_attn_Wv, np.float32)),
        wo=np.ascontiguousarray(np.asarray(L0_attn_Wo, np.float32)),
        wf=np.ascontiguousarray(np.asarray(L0_ffn_weight, np.float32)),
        wa1=np.ascontiguousarray(np.asarray(L0_ffn_aux_weight1, np.float32)),
        wa2=np.ascontiguousarray(np.asarray(L0_ffn_aux_weight2, np.float32)),
        fbias=np.ascontiguousarray(np.asarray(L0_ffn_bias, np.float32).reshape(1, D)),
        rel=np.ascontiguousarray(np.asarray(L0_attn_rel_emb, np.float32)),
        onehot=onehot, eye16=eye16,
    )
    if _PROG is None:
        _PROG = build_program()
    in_maps = []
    for c in range(N_CORES):
        m = dict(shared)
        m["x"] = np.ascontiguousarray(
            x[c * B_PER:(c + 1) * B_PER].reshape(B_PER * S, D))
        in_maps.append(m)
    res = run_bass_kernel_spmd(_PROG, in_maps, core_ids=list(range(N_CORES)))
    out = np.concatenate(
        [np.asarray(r["y"]).reshape(B_PER, S, D) for r in res.results], axis=0)
    return out.astype(np.float32)


# version 11
# speedup vs baseline: 1.2375x; 1.0067x over previous
import math
import numpy as np
import bass_rust
import concourse.bass as bass
import concourse.mybir as mybir
from concourse.bass_utils import run_bass_kernel_spmd

F32 = mybir.dt.float32
BF16 = mybir.dt.bfloat16
AF = mybir.ActivationFunctionType
OP = mybir.AluOpType

N_CORES = 8
B_PER = 4
S = 2048
D = 1024
H = 16
DK = 64
NT = S // 128
NCH = D // 128
NREV = 511
C_SHIFT = 12.0
EPS = 1e-5

NUM_BUCKETS = 32
MAX_DISTANCE = 128

DBG_ITEMS = B_PER
DBG_STAGE = "full"
DBG_PREFIX = None
DBG_DUMP = False
DBG_DUMP_ITEM = 0
DBG_NO_ROT = False
DBG_OLD_RECIP = False
DBG_ACT_FLUSH = False


def _rel_bucket_host(d):
    import jax
    import jax.numpy as jnp
    with jax.default_device(jax.devices("cpu")[0]):
        num_buckets = NUM_BUCKETS // 2
        rel = jnp.asarray(d, dtype=jnp.int32)
        relative_buckets = (rel > 0).astype(jnp.int32) * num_buckets
        rp = jnp.abs(rel)
        max_exact = num_buckets // 2
        is_small = rp < max_exact
        rp_large = max_exact + (
            jnp.log(rp.astype(jnp.float32) / max_exact)
            / math.log(MAX_DISTANCE / max_exact)
            * (num_buckets - max_exact)
        ).astype(jnp.int32)
        rp_large = jnp.minimum(rp_large, num_buckets - 1)
        relative_buckets = relative_buckets + jnp.where(is_small, rp, rp_large)
        return np.asarray(relative_buckets)


def _host_consts():
    d = np.arange(NREV, dtype=np.int64) - 255
    bk = _rel_bucket_host(d)
    onehot = np.zeros((NUM_BUCKETS, NREV), dtype=np.float32)
    onehot[bk, np.arange(NREV)] = 1.0
    eye16 = np.eye(16, dtype=np.float32)
    return onehot, eye16


def build_program():
    nc = bass.Bass("TRN2", target_bir_lowering=False)

    x8 = nc.dram_tensor("x", [B_PER * S, D], F32, kind="ExternalInput").ap()
    wq_d = nc.dram_tensor("wq", [D, D], F32, kind="ExternalInput").ap()
    wk_d = nc.dram_tensor("wk", [D, D], F32, kind="ExternalInput").ap()
    wv_d = nc.dram_tensor("wv", [D, D], F32, kind="ExternalInput").ap()
    wo_d = nc.dram_tensor("wo", [D, D], F32, kind="ExternalInput").ap()
    wf_d = nc.dram_tensor("wf", [D, D], F32, kind="ExternalInput").ap()
    wa1_d = nc.dram_tensor("wa1", [D, D], F32, kind="ExternalInput").ap()
    wa2_d = nc.dram_tensor("wa2", [D, D], F32, kind="ExternalInput").ap()
    fb_d = nc.dram_tensor("fbias", [1, D], F32, kind="ExternalInput").ap()
    rel_d = nc.dram_tensor("rel", [NUM_BUCKETS, H], F32, kind="ExternalInput").ap()
    oh_d = nc.dram_tensor("onehot", [NUM_BUCKETS, NREV], F32, kind="ExternalInput").ap()
    eye_d = nc.dram_tensor("eye16", [16, 16], F32, kind="ExternalInput").ap()
    y8 = nc.dram_tensor("y", [B_PER * S, D], F32, kind="ExternalOutput").ap()

    dbg = {}
    if DBG_DUMP:
        for nm, shp, dt in [("d_ln1t", [128, NCH * S], BF16),
                            ("d_qt", [128, 2 * S], BF16),
                            ("d_kt", [128, 2 * S], BF16),
                            ("d_vh", [128, 8320], BF16),
                            ("d_pt", [128, 4096], BF16),
                            ("d_ctxt", [128, NCH * S], BF16),
                            ("d_rev", [16, NREV], F32),
                            ("d_ca", [128, 32], F32),
                            ("d_pp", [128, 768], BF16)]:
            dbg[nm] = nc.dram_tensor(nm, shp, dt, kind="ExternalOutput").ap()

    qdup_d = nc.dram_tensor("qdup", [NCH, 128, H, 128], BF16, kind="Internal").ap()
    kdup_d = nc.dram_tensor("kdup", [NCH, 128, H, 128], BF16, kind="Internal").ap()
    x2_d = nc.dram_tensor("x2scr", [S, D], F32, kind="Internal").ap()
    rcp_d = nc.dram_tensor("rcpscr", [1, 1024], F32, kind="Internal").ap()

    A = nc.alloc_sbuf_tensor
    ln1T = A("ln1T", [128, NCH * S], BF16)
    ctxT = A("ctxT", [128, NCH * S], BF16)
    wv_sb = A("wv_sb", [128, NCH * D], BF16)
    wo_sb = A("wo_sb", [128, NCH * D], BF16)
    m_sb = A("m_sb", [128, NCH * D], BF16)
    qt_sb = A("qt_sb", [128, 2 * S], BF16)
    kt_sb = A("kt_sb", [128, 2 * S], BF16)
    qs_sb = A("qs_sb", [128, 2 * NCH * 128], BF16)
    ks_sb = A("ks_sb", [128, 2 * NCH * 128], BF16)
    vh_sb = A("vh_sb", [128, 2 * 4 * (NT * 65)], BF16)
    pt_sb = A("pt_sb", [128, 4 * 1024], BF16)
    pp_sb = A("pp_sb", [128, 2 * 384], BF16)
    xa_sb = A("xa_sb", [128, 2 * 1024], F32)
    xb_sb = A("xb_sb", [128, 2 * 1024], F32)
    ctxu_sb = A("ctxu_sb", [128, 1024], F32)
    aux_sb = A("aux_sb", [128, 1024], F32)
    ln_sb = A("ln_sb", [128, 2 * 1024], BF16)
    lt_sb = A("lt_sb", [128, 2 * 1024], BF16)
    br_sb = A("br_sb", [128, D], F32)
    ca_sb = A("ca_sb", [128, 32], F32)
    st_sb = A("st_sb", [128, 160], F32)
    rev_sb = A("rev_sb", [16, NREV], F32)
    oh_sb = A("oh_sb", [NUM_BUCKETS, NREV], F32)
    rel_sb = A("rel_sb", [NUM_BUCKETS, H], F32)
    eye_sb = A("eye_sb", [16, 16], F32)
    or_sb = A("or_sb", [1, 128], F32)
    oc_sb = A("oc_sb", [128, 64], F32)
    eps_sb = A("eps_sb", [128, 1], F32)

    P = nc.alloc_psum_tensor
    s2_ps = P("s2_ps", [128, 2048], F32)
    ctx_ps = P("ctx_ps", [128, 1024], F32)
    qkv_ps = P("qkv_ps", [128, 512], F32)
    msc_ps = P("msc_ps", [128, 512], F32)

    sem_names = ["ssm", "sxa0", "sxa1", "sxb0", "sxb1",
                 "tpa0", "tpa1", "tpd0", "tpd1", "tpw",
                 "gq0", "gq1", "gw0", "gw1", "gpp0", "gpp1",
                 "gdiv", "gc0", "gc1", "gx0", "gx1", "gy0", "gy1", "gdbg",
                 "gld0", "gld1", "gld2", "gld3", "gld4", "gdv2",
                 "pe", "act", "dve"]
    sems = {n: nc.alloc_semaphore(n) for n in sem_names}
    cnt = {n: 0 for n in sem_names}
    OPS = []

    def emit(engine, waits, incs, body):
        wl = []
        for s, v in waits:
            assert v <= cnt[s], f"wait on future: {s} {v} > {cnt[s]}"
            if v > 0:
                wl.append((sems[s], v))

        def fn(eng, wl=wl, incs=incs, body=body):
            for sh, v in wl:
                eng.wait_ge(sh, v)
            inst = body(eng)
            for s, k in incs:
                inst = inst.then_inc(sems[s], k)
        OPS.append((engine, fn))
        for s, k in incs:
            cnt[s] += k

    def mk(s):
        return (s, cnt[s])

    def ap3(t, off, dims):
        return bass_rust.AP(t[:].tensor, off, dims)

    from concourse.hw_specs import get_activation_tables
    tabs = list(get_activation_tables(nc.m.arch).items())
    id_nle = next(i for i, (n, f) in enumerate(tabs) if AF.Exp in f and AF.Ln in f)
    id_exp = next(i for i, (n, f) in enumerate(tabs) if AF.Exp in f and AF.Tanh in f)

    def load_table(idx):
        emit("scalar", [], [], lambda e, idx=idx: e.add_instruction(
            mybir.InstLoadActFuncSet(name=nc.get_next_instruction_name(),
                                     act_func_set_id=idx, ins=[], outs=[])))

    emit("vector", [], [], lambda e: e.memset(or_sb[0:1, :], 1.0))
    emit("vector", [], [], lambda e: e.memset(oc_sb[:, :], 1.0))
    for g2 in range(2):
        for hl in range(4):
            off = g2 * 4160 + hl * 1040 + 64
            apm = ap3(vh_sb, off, [[8320, 128], [65, NT]])
            emit("vector", [], [], lambda e, apm=apm: e.memset(apm, 1.0))
    emit("vector", [], [], lambda e: e.memset(eps_sb[:, :], EPS))
    emit("vector", [], [("dve", 1)], lambda e: e.memset(st_sb[:, :], 0.0))
    ones_done = mk("dve")

    load_table(id_nle)

    emit("sync", [], [("ssm", 16)], lambda e: e.dma_start(out=oh_sb[:, :], in_=oh_d))
    emit("sync", [], [("ssm", 16)], lambda e: e.dma_start(out=rel_sb[:, :], in_=rel_d))
    emit("sync", [], [("ssm", 16)], lambda e: e.dma_start(out=eye_sb[:, :], in_=eye_d))
    emit("sync", [], [("ssm", 16)], lambda e: e.dma_start(out=ctxu_sb[0:1, 0:D], in_=fb_d))
    small_ld = mk("ssm")

    emit("tensor", [small_ld], [("pe", 1)],
         lambda e: e.matmul(msc_ps[0:16, 0:NREV], rel_sb[0:32, 0:16],
                            oh_sb[0:32, 0:NREV], start=True, stop=True))
    emit("scalar", [mk("pe")], [("act", 1)],
         lambda e: e.activation(rev_sb[0:16, :], msc_ps[0:16, 0:NREV], AF.Copy,
                                bias=-C_SHIFT, scale=1.0))
    rev_done = mk("act")

    pp_ready = [None, None]
    pp_freed = [("dve", 0), ("dve", 0)]

    def emit_patp(h, waits):
        pbuf = h % 2
        sem = f"gpp{pbuf}"
        src = ap3(rev_sb, h * NREV, [[NREV, 1], [1, 128], [1, 384]])
        dst = pp_sb[:, pbuf * 384:(pbuf + 1) * 384]
        emit("gpsimd", waits, [(sem, 16)],
             lambda e, dst=dst, src=src: e.dma_start(out=dst, in_=src))
        pp_ready[pbuf] = mk(sem)

    emit_patp(0, [rev_done])
    emit_patp(1, [rev_done])
    patp_prologue = [mk("gpp0"), mk("gpp1")]

    emit("tensor", [rev_done], [],
         lambda e: e.matmul(msc_ps[0:1, 0:16], rev_sb[0:16, 0:1],
                            eye_sb[0:16, 0:16], start=True, stop=True))
    emit("tensor", [], [("pe", 1)],
         lambda e: e.matmul(msc_ps[0:1, 16:32], rev_sb[0:16, 510:511],
                            eye_sb[0:16, 0:16], start=True, stop=True))
    emit("vector", [mk("pe"), ones_done], [("dve", 1)],
         lambda e: e.tensor_copy(st_sb[0:1, 0:32], msc_ps[0:1, 0:32]))
    emit("tensor", [mk("dve")], [("pe", 1)],
         lambda e: e.matmul(msc_ps[0:128, 0:32], or_sb[0:1, 0:128],
                            st_sb[0:1, 0:32], start=True, stop=True))
    emit("vector", [mk("pe")], [("dve", 1)],
         lambda e: e.tensor_copy(ca_sb[:, :], msc_ps[:, 0:32]))

    for half in range(2):
        emit("tensor", [small_ld, ones_done, mk("dve")], [("pe", 1)],
             lambda e, half=half: e.matmul(
                 qkv_ps[:, 0:512], or_sb[0:1, 0:128],
                 ctxu_sb[0:1, half * 512:(half + 1) * 512], start=True, stop=True))
        emit("vector", [mk("pe")], [("dve", 1)],
             lambda e, half=half: e.tensor_copy(
                 br_sb[:, half * 512:(half + 1) * 512], qkv_ps[:, 0:512]))
    pro1_done = mk("dve")

    wbf, abf = ctxT, ln1T
    xa_free = [pro1_done, pro1_done]

    _gld_i = [0]

    def cast_load(src_d, dst_t, dst_off, waits):
        sem = f"gld{_gld_i[0]}"
        _gld_i[0] += 1
        pitch = dst_t[:].ap[0][0]
        s_ap = bass_rust.AP(src_d.tensor, 0,
                            [[1024, 128], [128 * 1024, NCH], [1, 1024]])
        d_ap = bass_rust.AP(dst_t[:].tensor, dst_off,
                            [[pitch, 128], [1024, NCH], [1, 1024]])
        emit("gpsimd", waits, [(sem, 16)],
             lambda e, d_ap=d_ap, s_ap=s_ap: e.dma_start(out=d_ap, in_=s_ap))
        return mk(sem)

    w_cast = cast_load(wf_d, wbf, 0, [pro1_done])
    for i in range(NCH):
        src = ap3(wbf, i * 1024, [[NCH * S, 128], [128, 8], [1, 128]])
        dst = ap3(wbf, 8192 + i * 128, [[NCH * S, 128], [1024, 8], [1, 128]])
        w = [w_cast]
        if i == 0:
            w.extend(patp_prologue)
        emit("sync", w, [("tpw", 16)],
             lambda e, dst=dst, src=src: e.dma_start(out=dst, in_=src, transpose=True))
    wt_done = mk("tpw")
    a1_done = cast_load(wa1_d, abf, 0, [pro1_done])

    def mm_chain(lhs_fn, rhs_fn, ps, dst_t, dst_fn, extra_first_waits, gate0):
        gate = gate0
        for ot in range(NCH):
            for ob in range(2):
                for ch in range(NCH):
                    w = []
                    if ch == 0:
                        w = list(extra_first_waits) + [gate]
                    emit("tensor", w, [("pe", 1)] if ch == NCH - 1 else [],
                         lambda e, ot=ot, ob=ob, ch=ch, ps=ps: e.matmul(
                             ps[:, 0:512], lhs_fn(ch, ot), rhs_fn(ch, ob),
                             start=(ch == 0), stop=(ch == NCH - 1)))
                emit("vector", [mk("pe")], [("dve", 1)],
                     lambda e, ot=ot, ob=ob, dst_t=dst_t: e.tensor_copy(
                         dst_fn(dst_t, ot, ob), qkv_ps[:, 0:512] if ps is qkv_ps
                         else msc_ps[:, 0:512]))
                gate = mk("dve")
        return mk("dve")

    m1t_done = mm_chain(
        lambda ch, jt: abf[:, ch * 1024 + jt * 128: ch * 1024 + jt * 128 + 128],
        lambda ch, cb: wbf[:, 8192 + ch * 1024 + cb * 512: 8192 + ch * 1024 + cb * 512 + 512],
        msc_ps, abf,
        lambda t, jt, cb: t[:, 8192 + jt * 1024 + cb * 512: 8192 + jt * 1024 + cb * 512 + 512],
        [a1_done, wt_done], a1_done)
    a2_done = cast_load(wa2_d, abf, 0, [m1t_done])
    m_done = mm_chain(
        lambda ch, ct: abf[:, 8192 + ch * 1024 + ct * 128: 8192 + ch * 1024 + ct * 128 + 128],
        lambda ch, jb: abf[:, ch * 1024 + jb * 512: ch * 1024 + jb * 512 + 512],
        qkv_ps, m_sb,
        lambda t, ct, jb: t[:, ct * 1024 + jb * 512: ct * 1024 + jb * 512 + 512],
        [a2_done], a2_done)


    if DBG_DUMP:
        emit("gpsimd", [mk("dve"), mk("act")], [("gdbg", 16)],
             lambda e: e.dma_start(out=dbg["d_rev"], in_=rev_sb[:, :]))
        emit("gpsimd", [], [("gdbg", 16)],
             lambda e: e.dma_start(out=dbg["d_ca"], in_=ca_sb[:, :]))
        emit("gpsimd", [], [("gdbg", 16)],
             lambda e: e.dma_start(out=dbg["d_pp"], in_=pp_sb[:, :]))

    s_free = {"A": ("act", 0), "B": ("act", 0)}
    pt_reader = [("pe", 0)] * 4
    kt_slot = [0]
    qs_ready = [None, None]
    qs_freed = [("pe", 0), ("pe", 0)]
    qt_freed = [("pe", 0), ("pe", 0)]
    qkv_done = {}
    vh_ready = [("dve", 0), ("dve", 0)]
    vh_free = [("pe", 0), ("pe", 0)]
    qkv_gate = [m_done]
    lnT_tp = [("tpa0", 0)] * NT
    ctx_gate = [("dve", 0)]
    ctx_done = [("pe", 0)]
    div_tt_gate = [("dve", 0)]
    c1_xa_fence = [("gdv2", 0)]
    misc_gate = [m1t_done]
    acc_i = [0]
    div_r2_free = [("gdiv", 0)]
    div_rb_free = [("gdv2", 0)]
    div_xa_free = [("dve", 0)]
    div_pending = []
    bgq = []


    def bgq_pop(n):
        for _ in range(min(n, len(bgq))):
            bgq.pop(0)()

    def bgq_flush():
        while bgq:
            bgq.pop(0)()

    stage_free = {0: m_done, 1: m_done}
    wv_done_c = [None]
    wo_done_c = [None]
    wdup_done_c = [None]

    def q_cast_chunk(src_d, dst_t, dst_off, ch, done_cell):
        def go():
            buf = ch % 2
            emit("sync", [xa_free[buf]], [(f"sxa{buf}", 16)],
                 lambda e: e.dma_start(
                     out=xa_sb[:, buf * 1024:(buf + 1) * 1024],
                     in_=src_d[ch * 128:(ch + 1) * 128, :]))
            emit("vector", [mk(f"sxa{buf}")], [("dve", 1)],
                 lambda e: e.tensor_copy(
                     dst_t[:, dst_off + ch * 1024: dst_off + (ch + 1) * 1024],
                     xa_sb[:, buf * 1024:(buf + 1) * 1024]))
            xa_free[buf] = mk("dve")
            if done_cell is not None and ch == NCH - 1:
                done_cell[0] = mk("dve")
        return go

    def q_dup_chunk(src_d, dst_d, ch, done_cell):
        def go():
            buf = ch % 2
            st_t = lt_sb if buf == 0 else pt_sb
            emit("sync", [xa_free[buf]], [(f"sxa{buf}", 16)],
                 lambda e: e.dma_start(
                     out=xa_sb[:, buf * 1024: buf * 1024 + 1024],
                     in_=src_d[ch * 128:(ch + 1) * 128, :]))
            sy_m = mk(f"sxa{buf}")
            pitch = NCH * 512 if st_t is pt_sb else 2048
            for rep in range(2):
                src_ap = ap3(xa_sb, buf * 1024, [[2048, 128], [64, 16], [1, 64]])
                dst_ap = ap3(st_t, rep * 64, [[pitch, 128], [128, 16], [1, 64]])
                emit("vector", [sy_m, stage_free[buf]], [("dve", 1)],
                     lambda e, dst_ap=dst_ap, src_ap=src_ap:
                     e.tensor_copy(dst_ap, src_ap))
            xa_free[buf] = mk("dve")
            emit("gpsimd", [mk("dve")], [(f"gw{buf}", 16)],
                 lambda e: e.dma_start(
                     out=dst_d[ch],
                     in_=st_t[:, 0:2048].rearrange("p (h c) -> p h c", h=16)))
            stage_free[buf] = mk(f"gw{buf}")
            if done_cell is not None and ch == NCH - 1:
                done_cell[0] = None
        return go

    for _ch in range(NCH):
        bgq.append(q_dup_chunk(wq_d, qdup_d, _ch, None))
    for _ch in range(NCH):
        bgq.append(q_dup_chunk(wk_d, kdup_d, _ch, wdup_done_c))
    def _wv_item():
        wv_done_c[0] = cast_load(wv_d, wv_sb, 0, [])
    def _wo_item():
        wo_done_c[0] = cast_load(wo_d, wo_sb, 0, [])
    bgq.append(_wv_item)
    bgq.append(_wo_item)

    def emit_qk_stream(h):
        buf = h % 2
        for (src_d, dst_t) in ((qdup_d, qs_sb), (kdup_d, ks_sb)):
            src = bass_rust.AP(src_d.tensor, h * 128,
                               [[H * 128, 128], [128 * H * 128, NCH], [1, 128]])
            dst = ap3(dst_t, buf * 1024, [[2048, 128], [128, 8], [1, 128]])
            emit("gpsimd", [qs_freed[buf]] + wdup_done_c[0], [(f"gq{buf}", 16)],
                 lambda e, dst=dst, src=src: e.dma_start(out=dst, in_=src))
        qs_ready[buf] = mk(f"gq{buf}")

    def acc_pick():
        i = (acc_i[0] % 2) if not DBG_NO_ROT else 0
        acc_i[0] += 1
        if i == 0:
            return qkv_ps, (lambda: qkv_gate[0]), qkv_gate
        return msc_ps, (lambda: misc_gate[0]), misc_gate

    def emit_qkv_group(h, proj, qb):
        buf = h % 2
        w_src = qs_sb if proj == 0 else ks_sb
        t_dst = qt_sb if proj == 0 else kt_sb
        ps, gate_get, gate_cell = acc_pick()
        for ch in range(NCH):
            w = []
            if ch == 0:
                w = [qs_ready[buf], lnT_tp[qb * 4 + 2],
                     lnT_tp[qb * 4 + 3], gate_get()]
            emit("tensor", w, [("pe", 1)] if ch == NCH - 1 else [],
                 lambda e, ch=ch, qb=qb, buf=buf, w_src=w_src, ps=ps:
                 e.matmul(ps[:, 0:512],
                          w_src[:, buf * 1024 + ch * 128: buf * 1024 + ch * 128 + 128],
                          ln1T[:, ch * S + qb * 512: ch * S + qb * 512 + 512],
                          start=(ch == 0), stop=(ch == NCH - 1)))
        emit("vector", [mk("pe")], [("dve", 1)],
             lambda e, qb=qb, buf=buf, t_dst=t_dst, ps=ps:
             e.tensor_copy(t_dst[:, buf * S + qb * 512: buf * S + qb * 512 + 512],
                           ps[:, 0:512]))
        gate_cell[0] = mk("dve")
        if proj == 1 and qb == 3:
            qkv_done[h] = mk("dve")

    def emit_v_tile(g, tt):
        gb = g % 2
        ps, gate_get, gate_cell = acc_pick()
        for ch in range(NCH):
            w = []
            if ch == 0:
                w = [lnT_tp[tt], gate_get(), wv_done_c[0]]
            emit("tensor", w, [("pe", 1)] if ch == NCH - 1 else [],
                 lambda e, ch=ch, tt=tt, g=g, ps=ps:
                 e.matmul(ps[:, 0:256],
                          ln1T[:, ch * S + tt * 128: ch * S + tt * 128 + 128],
                          wv_sb[:, ch * D + g * 256: ch * D + g * 256 + 256],
                          start=(ch == 0), stop=(ch == NCH - 1)))
        src = ap3(ps, 0, [[512, 128], [64, 4], [1, 64]])
        dst = ap3(vh_sb, gb * 4160 + tt * 65, [[8320, 128], [1040, 4], [1, 64]])
        emit("vector", [mk("pe"), vh_free[gb]], [("dve", 1)],
             lambda e, dst=dst, src=src: e.tensor_copy(dst, src))
        gate_cell[0] = mk("dve")
        if tt == NT - 1:
            vh_ready[gb] = mk("dve")

    def queue_head_work(h):
        bgq.append(lambda h=h: emit_qk_stream(h))
        for proj in range(2):
            for qb in range(4):
                bgq.append(lambda h=h, p=proj, q=qb: emit_qkv_group(h, p, q))

    def a_begin(item):
        a_tp_fence[0] = [mk("gpp0"), mk("gpp1"), mk("gc0"), mk("gc1"),
                         mk("gdiv")]
        a_xa_fence[0] = mk("gdv2")
        st8 = dict(item=item, base=item * S,
                   rstd=[("act", 0)] * NT, norm=[("dve", 0)] * NT,
                   tp=[("tpa0", 0)] * NT,
                   ln_fence=[mk("tpa0"), mk("tpa1"), mk("tpd0"), mk("tpd1"),
                             ln_stage_free[0], ln_stage_free[1]],
                   pe_fence=mk("pe"))
        return st8

    def a_tile(st8, tt):
        base = st8["base"]
        buf = tt % 2
        sc = 18 * buf
        w = [xa_free[buf]]
        if tt >= 2:
            w.append(st8["norm"][tt - 2])
        else:
            w.append(a_xa_fence[0])
        emit("sync", w, [(f"sxa{buf}", 16)],
             lambda e, tt=tt, buf=buf, base=base: e.dma_start(
                 out=xa_sb[:, buf * 1024: buf * 1024 + 1024],
                 in_=x8[base + tt * 128: base + tt * 128 + 128, :]))
        sy_m = mk(f"sxa{buf}")
        wv_ = [sy_m]
        if tt >= 2:
            wv_.append(st8["rstd"][tt - 2])
        emit("vector", wv_, [],
             lambda e, buf=buf, sc=sc: e.bn_stats(
                 st_sb[:, sc:sc + 6], xa_sb[:, buf * 1024: buf * 1024 + 512]))
        emit("vector", [], [("dve", 1)],
             lambda e, buf=buf, sc=sc: e.bn_stats(
                 st_sb[:, sc + 6:sc + 12],
                 xa_sb[:, buf * 1024 + 512: buf * 1024 + 1024]))
        emit("vector", [mk("dve")], [("dve", 1)],
             lambda e, sc=sc: e.bn_aggr(st_sb[:, sc + 12:sc + 14],
                                        st_sb[:, sc:sc + 12]))
        emit("scalar", [mk("dve")], [("act", 1)],
             lambda e, sc=sc: e.activation(st_sb[:, sc + 14:sc + 15],
                                           st_sb[:, sc + 13:sc + 14],
                                           AF.Ln, bias=eps_sb[:, 0:1], scale=1.0))
        emit("scalar", [mk("act")], [("act", 1)],
             lambda e, sc=sc: e.activation(st_sb[:, sc + 15:sc + 16],
                                           st_sb[:, sc + 14:sc + 15],
                                           AF.Exp, scale=-0.5))
        st8["rstd"][tt] = mk("act")
        emit("vector", [st8["rstd"][tt], mk("dve")], [("dve", 1)],
             lambda e, sc=sc: e.tensor_scalar(
                 st_sb[:, sc + 16:sc + 17], st_sb[:, sc + 12:sc + 13],
                 st_sb[:, sc + 15:sc + 16], -1.0, op0=OP.mult, op1=OP.mult))
        wv2 = [st8["tp"][tt - 2], mk("dve")] if tt >= 2 else \
            list(st8["ln_fence"]) + [mk("dve")]
        emit("vector", wv2, [("dve", 1)],
             lambda e, buf=buf, sc=sc: e.tensor_scalar(
                 ln_sb[:, buf * 1024: buf * 1024 + 1024],
                 xa_sb[:, buf * 1024: buf * 1024 + 1024],
                 st_sb[:, sc + 15:sc + 16], st_sb[:, sc + 16:sc + 17],
                 op0=OP.mult, op1=OP.add))
        st8["norm"][tt] = mk("dve")
        xa_free[buf] = st8["norm"][tt]
        src_ = ap3(ln_sb, buf * 1024, [[2048, 128], [128, 8], [1, 128]])
        dst_ = ap3(ln1T, tt * 128, [[NCH * S, 128], [S, 8], [1, 128]])
        wt = [st8["norm"][tt]] + a_tp_fence[0]
        if tt == 0:
            wt.append(st8["pe_fence"])
        emit("sync", wt, [(f"tpa{buf}", 16)],
             lambda e, dst_=dst_, src_=src_: e.dma_start(out=dst_, in_=src_,
                                                         transpose=True))
        st8["tp"][tt] = mk(f"tpa{buf}")
        lnT_tp[tt] = st8["tp"][tt]

    def a_end(st8):
        ln_stage_free[0] = st8["tp"][NT - 1]
        ln_stage_free[1] = st8["tp"][NT - 1]

    def phase_A(item):
        st8 = a_begin(item)
        for tt in range(NT):
            a_tile(st8, tt)
            bgq_pop(4)
        a_end(st8)
        return st8

    a_tp_fence = [[]]
    a_xa_fence = [("gdv2", 0)]
    ln_stage_free = [("gc0", 0), ("gc1", 0)]

    def band_range(kt, qh):
        lo = max(0, 128 * (kt - 1), 1024 * qh)
        hi = min(S, 128 * (kt + 2), 1024 * (qh + 1))
        return lo, hi

    def emit_sweep(item, h, qh):
        buf = h % 2
        g = h // 4
        gb = g % 2
        pbase = buf * S + qh * 1024
        exp_marks = {}
        s_pe = {}
        band_dve = {}

        def s_mms(ktp):
            ktA, ktB = 2 * ktp, 2 * ktp + 1
            for (kt, p0, coff, tp) in ((ktA, 0, 0, (0, 0)), (ktB, 64, 1024, (64, 0))):
                key = "A" if coff == 0 else "B"
                for half in range(2):
                    w = []
                    if half == 0:
                        w = [s_free[key], ("dve", qkv_done[h][1])]
                    emit("tensor", w, [("pe", 1)] if half == 1 else [],
                         lambda e, kt=kt, p0=p0, coff=coff, tp=tp, half=half,
                         buf=buf, pbase=pbase:
                         e.matmul(
                             s2_ps[:, coff + half * 512: coff + half * 512 + 512],
                             kt_sb[p0:p0 + 64, buf * S + kt * 128: buf * S + kt * 128 + 128],
                             qt_sb[p0:p0 + 64, pbase + half * 512: pbase + half * 512 + 512],
                             start=True, stop=True, tile_position=tp))
                s_pe[kt] = mk("pe")

        def band_add(kt, coff):
            lo, hi = band_range(kt, qh)
            if hi <= lo:
                return
            llo, lhi = lo - 1024 * qh, hi - 1024 * qh
            r0 = 128 * kt + 255 - lo
            ln_ = hi - lo
            ppo = (h % 2) * 384
            abs_start = ppo + r0
            abs_stop = ppo + r0 - ln_
            if abs_stop < 0:
                in1 = pp_sb[:, abs_start::-1]
            else:
                in1 = pp_sb[:, abs_start: abs_stop: -1]
            emit("vector", [s_pe[kt], pp_ready[h % 2]], [("dve", 1)],
                 lambda e, coff=coff, llo=llo, lhi=lhi, in1=in1:
                 e.tensor_tensor(s2_ps[:, coff + llo: coff + lhi],
                                 s2_ps[:, coff + llo: coff + lhi], in1, op=OP.add))
            band_dve[kt] = mk("dve")

        def exps(kt, coff, slot):
            lo, hi = band_range(kt, qh)
            segs = []
            if hi <= lo:
                segs = [(0, 1024, "lo" if 128 * (kt + 2) <= 1024 * qh else "hi")]
            else:
                llo, lhi = lo - 1024 * qh, hi - 1024 * qh
                if llo > 0:
                    segs.append((0, llo, "hi"))
                segs.append((llo, lhi, "band"))
                if lhi < 1024:
                    segs.append((lhi, 1024, "lo"))
            w = [s_pe[kt], pt_reader[slot]]
            if kt in band_dve:
                w.append(band_dve[kt])
            nseg = len(segs) + (1 if DBG_ACT_FLUSH else 0)
            for i, (a, b_, kind) in enumerate(segs):
                inc = [("act", 1)] if (i == len(segs) - 1 and not DBG_ACT_FLUSH) else []
                if kind == "band":
                    bias = None
                elif kind == "hi":
                    bias = ca_sb[:, 16 + h:16 + h + 1]
                else:
                    bias = ca_sb[:, h:h + 1]
                if bias is None:
                    emit("scalar", w if i == 0 else [], inc,
                         lambda e, a=a, b_=b_, coff=coff, slot=slot:
                         e.activation(pt_sb[:, slot * 1024 + a: slot * 1024 + b_],
                                      s2_ps[:, coff + a: coff + b_],
                                      AF.Exp, scale=1.0))
                else:
                    emit("scalar", w if i == 0 else [], inc,
                         lambda e, a=a, b_=b_, coff=coff, slot=slot, bias=bias:
                         e.activation(pt_sb[:, slot * 1024 + a: slot * 1024 + b_],
                                      s2_ps[:, coff + a: coff + b_],
                                      AF.Exp, bias=bias, scale=1.0))
            if DBG_ACT_FLUSH:
                emit("scalar", [], [("act", 1)],
                     lambda e, slot=slot: e.activation(
                         st_sb[0:1, 158:159],
                         pt_sb[0:1, slot * 1024: slot * 1024 + 1],
                         AF.Identity, scale=1.0))
            exp_marks[kt] = mk("act")
            key = "A" if coff == 0 else "B"
            s_free[key] = mk("act")

        def ctx_mms(kt, slot):
            vo = gb * 4160 + (h % 4) * 1040 + kt * 65
            for half in range(2):
                w = []
                if half == 0:
                    w = [exp_marks[kt], vh_ready[gb]]
                    if kt == 0:
                        w.append(ctx_gate[0])
                emit("tensor", w, [("pe", 1)] if half == 1 else [],
                     lambda e, kt=kt, slot=slot, half=half, vo=vo:
                     e.matmul(ctx_ps[0:65, half * 512: half * 512 + 512],
                              vh_sb[:, vo: vo + 65],
                              pt_sb[:, slot * 1024 + half * 512: slot * 1024 + half * 512 + 512],
                              start=(kt == 0), stop=(kt == NT - 1)))
            pt_reader[slot] = mk("pe")
            if kt == NT - 1:
                ctx_done[0] = mk("pe")

        prev = None
        for ktp in range(8):
            ktA, ktB = 2 * ktp, 2 * ktp + 1
            slA = kt_slot[0] % 4
            kt_slot[0] += 1
            slB = kt_slot[0] % 4
            kt_slot[0] += 1
            s_mms(ktp)
            band_add(ktA, 0)
            band_add(ktB, 1024)
            exps(ktA, 0, slA)
            exps(ktB, 1024, slB)
            if DBG_DUMP and item == DBG_DUMP_ITEM and h == 0 and qh == 0 and ktp == 0:
                emit("gpsimd", [mk("act")], [("gdbg", 16)],
                     lambda e, slA=slA, slB=slB: e.dma_start(
                         out=dbg["d_pt"], in_=pt_sb[:, :]))
                pt_reader[slA] = mk("gdbg")
                pt_reader[slB] = mk("gdbg")
            if prev is not None:
                ctx_mms(prev[0], prev[2])
                ctx_mms(prev[1], prev[3])
            if div_pending:
                div_pending.pop(0)()
            bgq_pop(2 if (ktp & 1) else 1)
            prev = (ktA, ktB, slA, slB)
        ctx_mms(prev[0], prev[2])
        ctx_mms(prev[1], prev[3])

    def queue_division(item, h, qh):
        ch = h // 2
        odd = h % 2 == 1
        pe_ctx = ctx_done[0]
        st = {}

        def d1():
            emit("vector", [pe_ctx, div_tt_gate[0]], [("dve", 1)],
                 lambda e: e.tensor_copy(ctxu_sb[0:65, :], ctx_ps[0:65, :]))
            ctx_gate[0] = mk("dve")
            st["d1"] = mk("dve")

        def d2():
            scat_src = ap3(ctxu_sb, 64 * 1024, [[1024, 1], [16, 64], [1, 16]])
            emit("sync", [st["d1"], div_r2_free[0]] + pp_tp_fence[0],
                 [("gdiv", 16)],
                 lambda e: e.dma_start(out=st_sb[64:128, 128:144], in_=scat_src))
            emit("vector", [mk("gdiv")], [("dve", 1)],
                 lambda e: e.reciprocal(st_sb[64:128, 128:144],
                                        st_sb[64:128, 128:144]))
            gat_src = ap3(st_sb, 64 * 160 + 128, [[160, 64], [1, 16]])
            emit("sync", [mk("dve"), div_rb_free[0]], [("gdiv", 16)],
                 lambda e: e.dma_start(out=rcp_d, in_=gat_src))
            bc_src = bass_rust.AP(rcp_d.tensor, 0, [[0, 64], [1, 1024]])
            emit("sync", [mk("gdiv"), div_xa_free[0]], [("gdv2", 16)],
                 lambda e: e.dma_start(out=xa_sb[0:64, 0:1024], in_=bc_src))
            st["d2"] = mk("gdv2")
            div_r2_free[0] = mk("gdiv")
            div_rb_free[0] = st["d2"]

        def d4(half):
            if odd:
                out = ln_sb[0:64, qh * 1024 + half * 512: qh * 1024 + half * 512 + 512]
            else:
                out = ctxT[0:64, ch * S + qh * 1024 + half * 512:
                           ch * S + qh * 1024 + half * 512 + 512]
            w = [st["d2"]]
            if half == 0:
                w.append(div_tt_gate[0])
            if odd:
                w.append(ln_stage_free[qh])
            emit("vector", w, [("dve", 1)],
                 lambda e, out=out, half=half: e.tensor_tensor(
                     out, ctxu_sb[0:64, half * 512: half * 512 + 512],
                     xa_sb[0:64, half * 512: half * 512 + 512], op=OP.mult))
            div_tt_gate[0] = mk("dve")
            if half == 1:
                div_xa_free[0] = mk("dve")
            st["d4"] = mk("dve")

        def d5():
            emit("sync", [st["d4"]] + pp_tp_fence[0], [(f"gc{qh}", 16)],
                 lambda e: e.dma_start(
                     out=ctxT[64:128, ch * S + qh * 1024: ch * S + qh * 1024 + 1024],
                     in_=ln_sb[0:64, qh * 1024: qh * 1024 + 1024]))
            ln_stage_free[qh] = mk(f"gc{qh}")

        seq = [d1, d2, lambda: d4(0), lambda: d4(1)]
        if odd:
            seq.append(d5)
        div_pending.extend(seq)

    def phase_B(item):
        for h in range(H):
            buf = h % 2
            if 1 <= h and h + 1 < H:
                queue_head_work(h + 1)
            nh = (h + 1) % H
            emit_patp(nh, [pp_freed[nh % 2]] + pp_tp_fence[0])
            if h % 4 == 1 and h // 4 + 1 <= 3:
                g = h // 4 + 1
                for tt in range(NT):
                    bgq.append(lambda g=g, tt=tt: emit_v_tile(g, tt))
            if h not in qkv_done:
                bgq_flush()
            assert h in qkv_done, f"QKV for head {h} not emitted"
            for qh in range(2):
                emit_sweep(item, h, qh)
                queue_division(item, h, qh)
            pp_freed[h % 2] = mk("dve")
            qt_freed[buf] = mk("pe")
            qs_freed[buf] = mk("pe")
            if h % 4 == 3:
                vh_free[(h // 4) % 2] = mk("pe")
        while div_pending:
            div_pending.pop(0)()
        bgq_flush()

    pp_tp_fence = [[]]

    def phase_C1(item, next_item):
        base = item * S
        div_done = mk("dve")
        div_gp = [mk("gc0"), mk("gc1"), mk("gdiv"), wo_done_c[0]]
        misc_gate[0] = div_done
        qkv_gate[0] = max(qkv_gate[0], div_done, key=lambda t: t[1])
        bank_gate = [div_done, div_done]
        xf = [mk("dve"), mk("dve")]
        c1_xa_fence[0] = mk("gdv2")
        xbf = [[mk("gy0"), mk("gx0")], [mk("gy1"), mk("gx1")]]
        x2_marks = []
        st8 = a_begin(next_item) if next_item is not None else None
        for tt in range(NT):
            buf = tt % 2
            emit("scalar", [xf[buf], xa_free[buf], c1_xa_fence[0]], [(f"sxa{buf}", 16)],
                 lambda e, tt=tt, buf=buf, base=base: e.dma_start(
                     out=xa_sb[:, buf * 1024: buf * 1024 + 1024],
                     in_=x8[base + tt * 128: base + tt * 128 + 128, :]))
            sy_m = mk(f"sxa{buf}")
            for dh in range(2):
                ps = msc_ps if dh == 0 else qkv_ps
                for ch in range(NCH):
                    w2 = []
                    if ch == 0:
                        w2 = [bank_gate[dh]] + div_gp
                        if dh == 1:
                            w2.append(qkv_gate[0])
                    emit("tensor", w2, [("pe", 1)] if ch == NCH - 1 else [],
                         lambda e, ch=ch, tt=tt, dh=dh, ps=ps: e.matmul(
                             ps[:, 0:512],
                             ctxT[:, ch * S + tt * 128: ch * S + tt * 128 + 128],
                             wo_sb[:, ch * D + dh * 512: ch * D + dh * 512 + 512],
                             start=(ch == 0), stop=(ch == NCH - 1)))
                emit("vector", [mk("pe"), sy_m] + xbf[buf], [("dve", 1)],
                     lambda e, buf=buf, dh=dh, ps=ps: e.tensor_tensor(
                         xb_sb[:, buf * 1024 + dh * 512: buf * 1024 + dh * 512 + 512],
                         ps[:, 0:512],
                         xa_sb[:, buf * 1024 + dh * 512: buf * 1024 + dh * 512 + 512],
                         op=OP.add))
                bank_gate[dh] = mk("dve")
                if dh == 1:
                    qkv_gate[0] = mk("dve")
            xf[buf] = mk("dve")
            xa_free[buf] = mk("dve")
            sc = 64 + 2 * tt
            emit("vector", [mk("dve")], [],
                 lambda e, buf=buf: e.bn_stats(
                     st_sb[:, 36:42], xb_sb[:, buf * 1024: buf * 1024 + 512]))
            emit("vector", [], [("dve", 1)],
                 lambda e, buf=buf: e.bn_stats(
                     st_sb[:, 42:48], xb_sb[:, buf * 1024 + 512: buf * 1024 + 1024]))
            emit("vector", [mk("dve")], [("dve", 1)],
                 lambda e, sc=sc: e.bn_aggr(st_sb[:, sc:sc + 2], st_sb[:, 36:48]))
            emit("gpsimd", [mk("dve")], [(f"gx{buf}", 16)],
                 lambda e, tt=tt, buf=buf: e.dma_start(
                     out=x2_d[tt * 128: tt * 128 + 128, :],
                     in_=xb_sb[:, buf * 1024: buf * 1024 + 1024]))
            xbf[buf] = [mk(f"gx{buf}")]
            x2_marks.append(mk(f"gx{buf}"))
            if st8 is not None:
                a_tile(st8, tt)
        if st8 is not None:
            a_end(st8)
        emit("scalar", [mk("dve")], [("act", 1)],
             lambda e: e.activation(st_sb[:, 96:112], st_sb[:, 65:97:2],
                                    AF.Ln, bias=eps_sb[:, 0:1], scale=1.0))
        emit("scalar", [mk("act")], [("act", 1)],
             lambda e: e.activation(st_sb[:, 112:128], st_sb[:, 96:112],
                                    AF.Exp, scale=-0.5))
        rstd2 = mk("act")
        emit("vector", [rstd2], [("dve", 1)],
             lambda e: e.scalar_tensor_tensor(
                 st_sb[:, 48:64], st_sb[:, 64:96:2], -1.0, st_sb[:, 112:128],
                 op0=OP.mult, op1=OP.mult))
        div_tt_gate[0] = mk("dve")
        return mk("dve"), x2_marks, st8, xf

    def phase_D(item, nm2_done, x2_marks, a_next, c1_xf, has_next):
        base = item * S
        load_table(id_exp)
        def sl(s):
            return (xb_sb, (s % 2) * 1024) if s < 2 else (xa_sb, (s % 2) * 1024)

        fr = [[nm2_done, x2_marks[NT - 2]],
              [nm2_done, x2_marks[NT - 1]],
              [c1_xf[0], xa_free[0], div_xa_free[0]],
              [c1_xf[1], xa_free[1], div_xa_free[0]]]
        sem_of = ["sxb0", "sxb1", "sxa0", "sxa1"]
        ld_m = [None] * NT

        def emit_ld(tt):
            s = tt % 4
            t, off = sl(s)
            emit("sync", fr[s] + [x2_marks[tt]], [(sem_of[s], 16)],
                 lambda e, tt=tt, t=t, off=off: e.dma_start(
                     out=t[:, off: off + 1024],
                     in_=x2_d[tt * 128: tt * 128 + 128, :]))
            ld_m[tt] = mk(sem_of[s])

        if a_next is not None:
            lnfree = [[a_next["tp"][NT - 2]], [a_next["tp"][NT - 1]]]
        else:
            lnfree = [[ln_stage_free[0]], [ln_stage_free[1]]]
        ltfree = [("pe", 0), ("pe", 0)]
        misc_gate[0] = mk("dve")
        qkv_gate[0] = mk("dve")
        tanh_done = [("act", 0)]
        tanh_war = [mk("gdiv")]
        yfree = [("gy0", 0), ("gy1", 0), ("gy0", 0), ("gy1", 0)]
        emit_ld(0)
        emit_ld(1)
        for tt in range(NT):
            s = tt % 4
            buf = tt % 2
            t_s, off_s = sl(s)
            emit("vector", [ld_m[tt], nm2_done] + lnfree[buf], [("dve", 1)],
                 lambda e, tt=tt, buf=buf, t_s=t_s, off_s=off_s: e.tensor_scalar(
                     ln_sb[:, buf * 1024: buf * 1024 + 1024],
                     t_s[:, off_s: off_s + 1024],
                     st_sb[:, 112 + tt:113 + tt], st_sb[:, 48 + tt:49 + tt],
                     op0=OP.mult, op1=OP.add))
            dve_m = mk("dve")
            if tt + 2 < NT:
                emit_ld(tt + 2)
            src_ = ap3(ln_sb, buf * 1024, [[2048, 128], [128, 8], [1, 128]])
            dst_ = ap3(lt_sb, buf * 1024, [[2048, 128], [128, 8], [1, 128]])
            wt = [dve_m, ltfree[buf]]
            if tt == 0:
                wt.extend(d_tp_fence[0])
            emit("sync", wt, [(f"tpd{buf}", 16)],
                 lambda e, dst_=dst_, src_=src_: e.dma_start(out=dst_, in_=src_,
                                                             transpose=True))
            tp_m = mk(f"tpd{buf}")
            lnfree[buf] = [tp_m]
            if tt == 2 and has_next:
                emit_qk_stream(0)
                qk0_early[0] = True
            for dh in range(2):
                ps, gate_get, gate_cell = acc_pick()
                for chm in range(NCH):
                    w3 = []
                    if chm == 0:
                        w3 = [tp_m, gate_get()]
                    emit("tensor", w3, [("pe", 1)] if chm == NCH - 1 else [],
                         lambda e, chm=chm, buf=buf, dh=dh, ps=ps: e.matmul(
                             ps[:, 0:512],
                             lt_sb[:, buf * 1024 + chm * 128: buf * 1024 + chm * 128 + 128],
                             m_sb[:, chm * D + dh * 512: chm * D + dh * 512 + 512],
                             start=(chm == 0), stop=(chm == NCH - 1)))
                pe_m = mk("pe")
                emit("vector", [pe_m, tanh_done[0]], [("dve", 1)],
                     lambda e, dh=dh, ps=ps: e.tensor_tensor(
                         ctxu_sb[:, dh * 512: dh * 512 + 512], ps[:, 0:512],
                         br_sb[:, dh * 512: dh * 512 + 512], op=OP.add))
                gate_cell[0] = mk("dve")
                emit("scalar", [mk("dve"), tanh_war[0]], [("act", 1)],
                     lambda e, dh=dh: e.activation(
                         aux_sb[:, dh * 512: dh * 512 + 512],
                         ctxu_sb[:, dh * 512: dh * 512 + 512],
                         AF.Tanh, scale=1.0))
                tanh_done[0] = mk("act")
                emit("vector", [mk("act"), yfree[s]], [("dve", 1)],
                     lambda e, t_s=t_s, off_s=off_s, dh=dh: e.tensor_tensor(
                         t_s[:, off_s + dh * 512: off_s + dh * 512 + 512],
                         t_s[:, off_s + dh * 512: off_s + dh * 512 + 512],
                         aux_sb[:, dh * 512: dh * 512 + 512], op=OP.add))
                tanh_war[0] = mk("dve")
            emit("gpsimd", [mk("dve")], [(f"gy{buf}", 16)],
                 lambda e, tt=tt, t_s=t_s, off_s=off_s, base=base: e.dma_start(
                     out=y8[base + tt * 128: base + tt * 128 + 128, :],
                     in_=t_s[:, off_s: off_s + 1024]))
            yfree[s] = mk(f"gy{buf}")
            fr[s] = [mk(f"gy{buf}")]
            ltfree[buf] = ("pe", cnt["pe"])
        load_table(id_nle)
        qkv_gate[0] = mk("dve")
        misc_gate[0] = mk("dve")
        xa_free[0] = fr[2][0]
        xa_free[1] = fr[3][0]
        div_xa_free[0] = fr[2][0]
        ln_stage_free[0] = mk("tpd0")
        ln_stage_free[1] = mk("tpd1")
        return

    d_tp_fence = [[]]
    qk0_early = [False]

    a_next_state = None
    for item in range(min(B_PER, DBG_ITEMS)):
        if DBG_STAGE == "prologue":
            break
        if a_next_state is None:
            a_next_state = phase_A(item)
        if DBG_DUMP and item == DBG_DUMP_ITEM:
            emit("gpsimd", [lnT_tp[NT - 2], lnT_tp[NT - 1]], [("gdbg", 16)],
                 lambda e: e.dma_start(out=dbg["d_ln1t"], in_=ln1T[:, :]))
        if DBG_STAGE == "A":
            a_next_state = None
            continue
        bgq_flush()
        if item == 0:
            wdup_done_c[0] = [mk("gw0"), mk("gw1")]
            pt_reader[0] = mk("gw1")
            pt_reader[1] = mk("gw1")
        if not qk0_early[0]:
            emit_qk_stream(0)
        qk0_early[0] = False
        for proj in range(2):
            for qb in range(4):
                emit_qkv_group(0, proj, qb)
        for tt in range(NT):
            emit_v_tile(0, tt)
        if DBG_DUMP and item == DBG_DUMP_ITEM:
            emit("gpsimd", [mk("dve")], [("gdbg", 16)],
                 lambda e: e.dma_start(out=dbg["d_qt"], in_=qt_sb[:, :]))
            emit("gpsimd", [], [("gdbg", 16)],
                 lambda e: e.dma_start(out=dbg["d_kt"], in_=kt_sb[:, :]))
            emit("gpsimd", [], [("gdbg", 16)],
                 lambda e: e.dma_start(out=dbg["d_vh"], in_=vh_sb[:, 0:8320]))
        queue_head_work(1)
        pp_tp_fence[0] = [mk("tpa0"), mk("tpa1"), mk("tpd0"), mk("tpd1"),
                          mk("tpw")]
        phase_B(item)
        if DBG_DUMP and item == DBG_DUMP_ITEM:
            emit("gpsimd", [mk("dve"), mk("gc0"), mk("gc1")], [("gdbg", 16)],
                 lambda e: e.dma_start(out=dbg["d_ctxt"], in_=ctxT[:, :]))
        if DBG_STAGE == "B":
            a_next_state = None
            continue
        nxt = item + 1 if item + 1 < min(B_PER, DBG_ITEMS) else None
        nm2_done, x2_marks, a_next_state, c1_xf = phase_C1(item, nxt)
        if DBG_STAGE == "C1":
            continue
        d_tp_fence[0] = [mk("gpp0"), mk("gpp1"), mk("gc0"), mk("gc1"),
                         mk("gdiv")]
        phase_D(item, nm2_done, x2_marks, a_next_state, c1_xf,
                has_next=nxt is not None)

    emit("gpsimd", [], [], lambda e: e.wait_ge(sems["gy0"], cnt["gy0"]))
    emit("gpsimd", [], [], lambda e: e.wait_ge(sems["gy1"], cnt["gy1"]))

    if DBG_PREFIX is not None:
        del OPS[DBG_PREFIX:]
    print(f"[kernel] total ops: {len(OPS)}")

    with nc.Block() as block:
        for eng_name in ["sync", "gpsimd", "tensor", "vector", "scalar"]:
            def make(eng_name):
                def run(eng):
                    for e_, fn in OPS:
                        if e_ == eng_name:
                            fn(eng)
                return run
            getattr(block, eng_name)(make(eng_name))

    return nc


_PROG = None


def kernel(L0_attn_hidden_states, L0_attn_Wq, L0_attn_Wk, L0_attn_Wv, L0_attn_Wo,
           L0_attn_rel_emb, L0_ffn_weight, L0_ffn_aux_weight1, L0_ffn_aux_weight2,
           L0_ffn_bias):
    global _PROG
    x = np.ascontiguousarray(np.asarray(L0_attn_hidden_states, dtype=np.float32))
    onehot, eye16 = _host_consts()
    shared = dict(
        wq=np.ascontiguousarray(np.asarray(L0_attn_Wq, np.float32)),
        wk=np.ascontiguousarray(np.asarray(L0_attn_Wk, np.float32)),
        wv=np.ascontiguousarray(np.asarray(L0_attn_Wv, np.float32)),
        wo=np.ascontiguousarray(np.asarray(L0_attn_Wo, np.float32)),
        wf=np.ascontiguousarray(np.asarray(L0_ffn_weight, np.float32)),
        wa1=np.ascontiguousarray(np.asarray(L0_ffn_aux_weight1, np.float32)),
        wa2=np.ascontiguousarray(np.asarray(L0_ffn_aux_weight2, np.float32)),
        fbias=np.ascontiguousarray(np.asarray(L0_ffn_bias, np.float32).reshape(1, D)),
        rel=np.ascontiguousarray(np.asarray(L0_attn_rel_emb, np.float32)),
        onehot=onehot, eye16=eye16,
    )
    if _PROG is None:
        _PROG = build_program()
    in_maps = []
    for c in range(N_CORES):
        m = dict(shared)
        m["x"] = np.ascontiguousarray(
            x[c * B_PER:(c + 1) * B_PER].reshape(B_PER * S, D))
        in_maps.append(m)
    res = run_bass_kernel_spmd(_PROG, in_maps, core_ids=list(range(N_CORES)))
    out = np.concatenate(
        [np.asarray(r["y"]).reshape(B_PER, S, D) for r in res.results], axis=0)
    return out.astype(np.float32)


# version 12
# speedup vs baseline: 1.3409x; 1.0836x over previous
import math
import numpy as np
import bass_rust
import concourse.bass as bass
import concourse.mybir as mybir
from concourse.bass_utils import run_bass_kernel_spmd

F32 = mybir.dt.float32
BF16 = mybir.dt.bfloat16
AF = mybir.ActivationFunctionType
OP = mybir.AluOpType

N_CORES = 8
B_PER = 4
S = 2048
D = 1024
H = 16
DK = 64
NT = S // 128
NCH = D // 128
NREV = 511
C_SHIFT = 12.0
EPS = 1e-5

NUM_BUCKETS = 32
MAX_DISTANCE = 128

DBG_ITEMS = B_PER
DBG_STAGE = "full"
DBG_PREFIX = None
DBG_DUMP = False
DBG_DUMP_ITEM = 0
DBG_NO_ROT = False
DBG_OLD_RECIP = False
DBG_ACT_FLUSH = False


def _rel_bucket_host(d):
    import jax
    import jax.numpy as jnp
    with jax.default_device(jax.devices("cpu")[0]):
        num_buckets = NUM_BUCKETS // 2
        rel = jnp.asarray(d, dtype=jnp.int32)
        relative_buckets = (rel > 0).astype(jnp.int32) * num_buckets
        rp = jnp.abs(rel)
        max_exact = num_buckets // 2
        is_small = rp < max_exact
        rp_large = max_exact + (
            jnp.log(rp.astype(jnp.float32) / max_exact)
            / math.log(MAX_DISTANCE / max_exact)
            * (num_buckets - max_exact)
        ).astype(jnp.int32)
        rp_large = jnp.minimum(rp_large, num_buckets - 1)
        relative_buckets = relative_buckets + jnp.where(is_small, rp, rp_large)
        return np.asarray(relative_buckets)


def _host_consts():
    d = np.arange(NREV, dtype=np.int64) - 255
    bk = _rel_bucket_host(d)
    onehot = np.zeros((NUM_BUCKETS, NREV), dtype=np.float32)
    onehot[bk, np.arange(NREV)] = 1.0
    eye16 = np.eye(16, dtype=np.float32)
    return onehot, eye16


def build_program():
    nc = bass.Bass("TRN2", target_bir_lowering=False)

    x8 = nc.dram_tensor("x", [B_PER * S, D], F32, kind="ExternalInput").ap()
    wq_d = nc.dram_tensor("wq", [D, D], F32, kind="ExternalInput").ap()
    wk_d = nc.dram_tensor("wk", [D, D], F32, kind="ExternalInput").ap()
    wv_d = nc.dram_tensor("wv", [D, D], F32, kind="ExternalInput").ap()
    wo_d = nc.dram_tensor("wo", [D, D], F32, kind="ExternalInput").ap()
    wf_d = nc.dram_tensor("wf", [D, D], F32, kind="ExternalInput").ap()
    wa1_d = nc.dram_tensor("wa1", [D, D], F32, kind="ExternalInput").ap()
    wa2_d = nc.dram_tensor("wa2", [D, D], F32, kind="ExternalInput").ap()
    fb_d = nc.dram_tensor("fbias", [1, D], F32, kind="ExternalInput").ap()
    rel_d = nc.dram_tensor("rel", [NUM_BUCKETS, H], F32, kind="ExternalInput").ap()
    oh_d = nc.dram_tensor("onehot", [NUM_BUCKETS, NREV], F32, kind="ExternalInput").ap()
    eye_d = nc.dram_tensor("eye16", [16, 16], F32, kind="ExternalInput").ap()
    y8 = nc.dram_tensor("y", [B_PER * S, D], F32, kind="ExternalOutput").ap()

    dbg = {}
    if DBG_DUMP:
        for nm, shp, dt in [("d_ln1t", [128, NCH * S], BF16),
                            ("d_qt", [128, 2 * S], BF16),
                            ("d_kt", [128, 2 * S], BF16),
                            ("d_vh", [128, 8320], BF16),
                            ("d_pt", [128, 4096], BF16),
                            ("d_ctxt", [128, NCH * S], BF16),
                            ("d_rev", [16, NREV], F32),
                            ("d_ca", [128, 32], F32),
                            ("d_pp", [128, 768], BF16)]:
            dbg[nm] = nc.dram_tensor(nm, shp, dt, kind="ExternalOutput").ap()

    qdup_d = nc.dram_tensor("qdup", [NCH, 128, H, 128], BF16, kind="Internal").ap()
    kdup_d = nc.dram_tensor("kdup", [NCH, 128, H, 128], BF16, kind="Internal").ap()
    x2_d = nc.dram_tensor("x2scr", [S, D], F32, kind="Internal").ap()
    rcp_d = nc.dram_tensor("rcpscr", [1, 1024], F32, kind="Internal").ap()

    A = nc.alloc_sbuf_tensor
    ln1T = A("ln1T", [128, NCH * S], BF16)
    ctxT = A("ctxT", [128, NCH * S], BF16)
    wv_sb = A("wv_sb", [128, NCH * D], BF16)
    wo_sb = A("wo_sb", [128, NCH * D], BF16)
    m_sb = A("m_sb", [128, NCH * D], BF16)
    qt_sb = A("qt_sb", [128, 2 * S], BF16)
    kt_sb = A("kt_sb", [128, 2 * S], BF16)
    qs_sb = A("qs_sb", [128, 2 * NCH * 128], BF16)
    ks_sb = A("ks_sb", [128, 2 * NCH * 128], BF16)
    vh_sb = A("vh_sb", [128, 2 * 4 * (NT * 65)], BF16)
    pt_sb = A("pt_sb", [128, 4 * 1024], BF16)
    pp_sb = A("pp_sb", [128, 2 * 384], BF16)
    xa_sb = A("xa_sb", [128, 2 * 1024], F32)
    xb_sb = A("xb_sb", [128, 2 * 1024], F32)
    ctxu_sb = A("ctxu_sb", [128, 1024], F32)
    aux_sb = A("aux_sb", [128, 1024], F32)
    ln_sb = A("ln_sb", [128, 2 * 1024], BF16)
    lt_sb = A("lt_sb", [128, 2 * 1024], BF16)
    br_sb = A("br_sb", [128, D], F32)
    ca_sb = A("ca_sb", [128, 32], F32)
    st_sb = A("st_sb", [128, 160], F32)
    rev_sb = A("rev_sb", [16, NREV], F32)
    oh_sb = A("oh_sb", [NUM_BUCKETS, NREV], F32)
    rel_sb = A("rel_sb", [NUM_BUCKETS, H], F32)
    eye_sb = A("eye_sb", [16, 16], F32)
    or_sb = A("or_sb", [1, 128], F32)
    oc_sb = A("oc_sb", [128, 64], F32)
    eps_sb = A("eps_sb", [128, 1], F32)

    P = nc.alloc_psum_tensor
    s2_ps = P("s2_ps", [128, 2048], F32)
    ctx_ps = P("ctx_ps", [128, 1024], F32)
    qkv_ps = P("qkv_ps", [128, 512], F32)
    msc_ps = P("msc_ps", [128, 512], F32)

    sem_names = ["ssm", "sxa0", "sxa1", "sxb0", "sxb1",
                 "tpa0", "tpa1", "tpd0", "tpd1", "tpw",
                 "gq0", "gq1", "gw0", "gw1", "gpp0", "gpp1",
                 "gdiv", "gc0", "gc1", "gx0", "gx1", "gy0", "gy1", "gdbg",
                 "gld0", "gld1", "gld2", "gld3", "gld4", "gdv2",
                 "pe", "act", "dve"]
    sems = {n: nc.alloc_semaphore(n) for n in sem_names}
    cnt = {n: 0 for n in sem_names}
    OPS = []

    def emit(engine, waits, incs, body):
        wl = []
        for s, v in waits:
            assert v <= cnt[s], f"wait on future: {s} {v} > {cnt[s]}"
            if v > 0:
                wl.append((sems[s], v))

        def fn(eng, wl=wl, incs=incs, body=body):
            for sh, v in wl:
                eng.wait_ge(sh, v)
            inst = body(eng)
            for s, k in incs:
                inst = inst.then_inc(sems[s], k)
        OPS.append((engine, fn))
        for s, k in incs:
            cnt[s] += k

    def mk(s):
        return (s, cnt[s])

    def ap3(t, off, dims):
        return bass_rust.AP(t[:].tensor, off, dims)

    from concourse.hw_specs import get_activation_tables
    tabs = list(get_activation_tables(nc.m.arch).items())
    id_nle = next(i for i, (n, f) in enumerate(tabs) if AF.Exp in f and AF.Ln in f)
    id_exp = next(i for i, (n, f) in enumerate(tabs) if AF.Exp in f and AF.Tanh in f)

    def load_table(idx):
        emit("scalar", [], [], lambda e, idx=idx: e.add_instruction(
            mybir.InstLoadActFuncSet(name=nc.get_next_instruction_name(),
                                     act_func_set_id=idx, ins=[], outs=[])))

    emit("vector", [], [], lambda e: e.memset(or_sb[0:1, :], 1.0))
    emit("vector", [], [], lambda e: e.memset(oc_sb[:, :], 1.0))
    for g2 in range(2):
        for hl in range(4):
            off = g2 * 4160 + hl * 1040 + 64
            apm = ap3(vh_sb, off, [[8320, 128], [65, NT]])
            emit("vector", [], [], lambda e, apm=apm: e.memset(apm, 1.0))
    emit("vector", [], [], lambda e: e.memset(eps_sb[:, :], EPS))
    emit("vector", [], [("dve", 1)], lambda e: e.memset(st_sb[:, :], 0.0))
    ones_done = mk("dve")

    load_table(id_nle)

    emit("sync", [], [("ssm", 16)], lambda e: e.dma_start(out=oh_sb[:, :], in_=oh_d))
    emit("sync", [], [("ssm", 16)], lambda e: e.dma_start(out=rel_sb[:, :], in_=rel_d))
    emit("sync", [], [("ssm", 16)], lambda e: e.dma_start(out=eye_sb[:, :], in_=eye_d))
    emit("sync", [], [("ssm", 16)], lambda e: e.dma_start(out=ctxu_sb[0:1, 0:D], in_=fb_d))
    small_ld = mk("ssm")

    emit("tensor", [small_ld], [("pe", 1)],
         lambda e: e.matmul(msc_ps[0:16, 0:NREV], rel_sb[0:32, 0:16],
                            oh_sb[0:32, 0:NREV], start=True, stop=True))
    emit("scalar", [mk("pe")], [("act", 1)],
         lambda e: e.activation(rev_sb[0:16, :], msc_ps[0:16, 0:NREV], AF.Copy,
                                bias=-C_SHIFT, scale=1.0))
    rev_done = mk("act")

    pp_ready = [None, None]
    pp_freed = [("dve", 0), ("dve", 0)]

    def emit_patp(h, waits):
        pbuf = h % 2
        sem = f"gpp{pbuf}"
        src = ap3(rev_sb, h * NREV, [[NREV, 1], [1, 128], [1, 384]])
        dst = pp_sb[:, pbuf * 384:(pbuf + 1) * 384]
        emit("gpsimd", waits, [(sem, 16)],
             lambda e, dst=dst, src=src: e.dma_start(out=dst, in_=src))
        pp_ready[pbuf] = mk(sem)

    emit_patp(0, [rev_done])
    emit_patp(1, [rev_done])
    patp_prologue = [mk("gpp0"), mk("gpp1")]

    emit("tensor", [rev_done], [],
         lambda e: e.matmul(msc_ps[0:1, 0:16], rev_sb[0:16, 0:1],
                            eye_sb[0:16, 0:16], start=True, stop=True))
    emit("tensor", [], [("pe", 1)],
         lambda e: e.matmul(msc_ps[0:1, 16:32], rev_sb[0:16, 510:511],
                            eye_sb[0:16, 0:16], start=True, stop=True))
    emit("vector", [mk("pe"), ones_done], [("dve", 1)],
         lambda e: e.tensor_copy(st_sb[0:1, 0:32], msc_ps[0:1, 0:32]))
    emit("tensor", [mk("dve")], [("pe", 1)],
         lambda e: e.matmul(msc_ps[0:128, 0:32], or_sb[0:1, 0:128],
                            st_sb[0:1, 0:32], start=True, stop=True))
    emit("vector", [mk("pe")], [("dve", 1)],
         lambda e: e.tensor_copy(ca_sb[:, :], msc_ps[:, 0:32]))

    for half in range(2):
        emit("tensor", [small_ld, ones_done, mk("dve")], [("pe", 1)],
             lambda e, half=half: e.matmul(
                 qkv_ps[:, 0:512], or_sb[0:1, 0:128],
                 ctxu_sb[0:1, half * 512:(half + 1) * 512], start=True, stop=True))
        emit("vector", [mk("pe")], [("dve", 1)],
             lambda e, half=half: e.tensor_copy(
                 br_sb[:, half * 512:(half + 1) * 512], qkv_ps[:, 0:512]))
    pro1_done = mk("dve")

    wbf, abf = ctxT, ln1T
    xa_free = [pro1_done, pro1_done]

    _gld_i = [0]

    def cast_load(src_d, dst_t, dst_off, waits):
        sem = f"gld{_gld_i[0]}"
        _gld_i[0] += 1
        pitch = dst_t[:].ap[0][0]
        s_ap = bass_rust.AP(src_d.tensor, 0,
                            [[1024, 128], [128 * 1024, NCH], [1, 1024]])
        d_ap = bass_rust.AP(dst_t[:].tensor, dst_off,
                            [[pitch, 128], [1024, NCH], [1, 1024]])
        emit("gpsimd", waits, [(sem, 16)],
             lambda e, d_ap=d_ap, s_ap=s_ap: e.dma_start(out=d_ap, in_=s_ap))
        return mk(sem)

    w_cast = cast_load(wf_d, wbf, 0, [pro1_done])
    for i in range(NCH):
        src = ap3(wbf, i * 1024, [[NCH * S, 128], [128, 8], [1, 128]])
        dst = ap3(wbf, 8192 + i * 128, [[NCH * S, 128], [1024, 8], [1, 128]])
        w = [w_cast]
        if i == 0:
            w.extend(patp_prologue)
        emit("sync", w, [("tpw", 16)],
             lambda e, dst=dst, src=src: e.dma_start(out=dst, in_=src, transpose=True))
    wt_done = mk("tpw")
    a1_done = cast_load(wa1_d, abf, 0, [pro1_done])

    def mm_chain(lhs_fn, rhs_fn, ps, dst_t, dst_fn, extra_first_waits, gate0):
        gate = gate0
        for ot in range(NCH):
            for ob in range(2):
                for ch in range(NCH):
                    w = []
                    if ch == 0:
                        w = list(extra_first_waits) + [gate]
                    emit("tensor", w, [("pe", 1)] if ch == NCH - 1 else [],
                         lambda e, ot=ot, ob=ob, ch=ch, ps=ps: e.matmul(
                             ps[:, 0:512], lhs_fn(ch, ot), rhs_fn(ch, ob),
                             start=(ch == 0), stop=(ch == NCH - 1)))
                emit("vector", [mk("pe")], [("dve", 1)],
                     lambda e, ot=ot, ob=ob, dst_t=dst_t: e.tensor_copy(
                         dst_fn(dst_t, ot, ob), qkv_ps[:, 0:512] if ps is qkv_ps
                         else msc_ps[:, 0:512]))
                gate = mk("dve")
        return mk("dve")

    m1t_done = mm_chain(
        lambda ch, jt: abf[:, ch * 1024 + jt * 128: ch * 1024 + jt * 128 + 128],
        lambda ch, cb: wbf[:, 8192 + ch * 1024 + cb * 512: 8192 + ch * 1024 + cb * 512 + 512],
        msc_ps, abf,
        lambda t, jt, cb: t[:, 8192 + jt * 1024 + cb * 512: 8192 + jt * 1024 + cb * 512 + 512],
        [a1_done, wt_done], a1_done)
    a2_done = cast_load(wa2_d, abf, 0, [m1t_done])
    m_done = mm_chain(
        lambda ch, ct: abf[:, 8192 + ch * 1024 + ct * 128: 8192 + ch * 1024 + ct * 128 + 128],
        lambda ch, jb: abf[:, ch * 1024 + jb * 512: ch * 1024 + jb * 512 + 512],
        qkv_ps, m_sb,
        lambda t, ct, jb: t[:, ct * 1024 + jb * 512: ct * 1024 + jb * 512 + 512],
        [a2_done], a2_done)


    if DBG_DUMP:
        emit("gpsimd", [mk("dve"), mk("act")], [("gdbg", 16)],
             lambda e: e.dma_start(out=dbg["d_rev"], in_=rev_sb[:, :]))
        emit("gpsimd", [], [("gdbg", 16)],
             lambda e: e.dma_start(out=dbg["d_ca"], in_=ca_sb[:, :]))
        emit("gpsimd", [], [("gdbg", 16)],
             lambda e: e.dma_start(out=dbg["d_pp"], in_=pp_sb[:, :]))

    s_free = {"A": ("act", 0), "B": ("act", 0)}
    pt_reader = [("pe", 0)] * 4
    kt_slot = [0]
    qs_ready = [None, None]
    qs_freed = [("pe", 0), ("pe", 0)]
    qt_freed = [("pe", 0), ("pe", 0)]
    qkv_done = {}
    vh_ready = [("dve", 0), ("dve", 0)]
    vh_free = [("pe", 0), ("pe", 0)]
    qkv_gate = [m_done]
    lnT_tp = [("tpa0", 0)] * NT
    ctx_gate = [("dve", 0)]
    ctx_done = [("pe", 0)]
    div_tt_gate = [("dve", 0)]
    c1_xa_fence = [("gdv2", 0)]
    misc_gate = [m1t_done]
    acc_i = [0]
    div_r2_free = [("gdiv", 0)]
    div_rb_free = [("gdv2", 0)]
    div_xa_free = [("dve", 0)]
    div_pending = []
    bgq = []


    def bgq_pop(n):
        for _ in range(min(n, len(bgq))):
            bgq.pop(0)()

    def bgq_flush():
        while bgq:
            bgq.pop(0)()

    stage_free = {0: m_done, 1: m_done}
    wv_done_c = [None]
    wo_done_c = [None]
    wdup_done_c = [None]

    def q_cast_chunk(src_d, dst_t, dst_off, ch, done_cell):
        def go():
            buf = ch % 2
            emit("sync", [xa_free[buf]], [(f"sxa{buf}", 16)],
                 lambda e: e.dma_start(
                     out=xa_sb[:, buf * 1024:(buf + 1) * 1024],
                     in_=src_d[ch * 128:(ch + 1) * 128, :]))
            emit("vector", [mk(f"sxa{buf}")], [("dve", 1)],
                 lambda e: e.tensor_copy(
                     dst_t[:, dst_off + ch * 1024: dst_off + (ch + 1) * 1024],
                     xa_sb[:, buf * 1024:(buf + 1) * 1024]))
            xa_free[buf] = mk("dve")
            if done_cell is not None and ch == NCH - 1:
                done_cell[0] = mk("dve")
        return go

    def q_dup_chunk(src_d, dst_d, ch, done_cell):
        def go():
            buf = ch % 2
            st_t = lt_sb if buf == 0 else pt_sb
            emit("sync", [xa_free[buf]], [(f"sxa{buf}", 16)],
                 lambda e: e.dma_start(
                     out=xa_sb[:, buf * 1024: buf * 1024 + 1024],
                     in_=src_d[ch * 128:(ch + 1) * 128, :]))
            sy_m = mk(f"sxa{buf}")
            pitch = NCH * 512 if st_t is pt_sb else 2048
            for rep in range(2):
                src_ap = ap3(xa_sb, buf * 1024, [[2048, 128], [64, 16], [1, 64]])
                dst_ap = ap3(st_t, rep * 64, [[pitch, 128], [128, 16], [1, 64]])
                emit("vector", [sy_m, stage_free[buf]], [("dve", 1)],
                     lambda e, dst_ap=dst_ap, src_ap=src_ap:
                     e.tensor_copy(dst_ap, src_ap))
            xa_free[buf] = mk("dve")
            emit("gpsimd", [mk("dve")], [(f"gw{buf}", 16)],
                 lambda e: e.dma_start(
                     out=dst_d[ch],
                     in_=st_t[:, 0:2048].rearrange("p (h c) -> p h c", h=16)))
            stage_free[buf] = mk(f"gw{buf}")
            if done_cell is not None and ch == NCH - 1:
                done_cell[0] = None
        return go

    for _ch in range(NCH):
        bgq.append(q_dup_chunk(wq_d, qdup_d, _ch, None))
    for _ch in range(NCH):
        bgq.append(q_dup_chunk(wk_d, kdup_d, _ch, wdup_done_c))
    def _wv_item():
        wv_done_c[0] = cast_load(wv_d, wv_sb, 0, [])
    def _wo_item():
        wo_done_c[0] = cast_load(wo_d, wo_sb, 0, [])
    bgq.append(_wv_item)
    bgq.append(_wo_item)

    def emit_qk_stream(h):
        buf = h % 2
        for (src_d, dst_t) in ((qdup_d, qs_sb), (kdup_d, ks_sb)):
            src = bass_rust.AP(src_d.tensor, h * 128,
                               [[H * 128, 128], [128 * H * 128, NCH], [1, 128]])
            dst = ap3(dst_t, buf * 1024, [[2048, 128], [128, 8], [1, 128]])
            emit("gpsimd", [qs_freed[buf]] + wdup_done_c[0], [(f"gq{buf}", 16)],
                 lambda e, dst=dst, src=src: e.dma_start(out=dst, in_=src))
        qs_ready[buf] = mk(f"gq{buf}")

    def acc_pick():
        i = (acc_i[0] % 2) if not DBG_NO_ROT else 0
        acc_i[0] += 1
        if i == 0:
            return qkv_ps, (lambda: qkv_gate[0]), qkv_gate
        return msc_ps, (lambda: misc_gate[0]), misc_gate

    def emit_qkv_group(h, proj, qb):
        buf = h % 2
        w_src = qs_sb if proj == 0 else ks_sb
        t_dst = qt_sb if proj == 0 else kt_sb
        ps, gate_get, gate_cell = acc_pick()
        for ch in range(NCH):
            w = []
            if ch == 0:
                w = [qs_ready[buf], lnT_tp[qb * 4 + 2],
                     lnT_tp[qb * 4 + 3], gate_get()]
            emit("tensor", w, [("pe", 1)] if ch == NCH - 1 else [],
                 lambda e, ch=ch, qb=qb, buf=buf, w_src=w_src, ps=ps:
                 e.matmul(ps[:, 0:512],
                          w_src[:, buf * 1024 + ch * 128: buf * 1024 + ch * 128 + 128],
                          ln1T[:, ch * S + qb * 512: ch * S + qb * 512 + 512],
                          start=(ch == 0), stop=(ch == NCH - 1)))
        emit("vector", [mk("pe")], [("dve", 1)],
             lambda e, qb=qb, buf=buf, t_dst=t_dst, ps=ps:
             e.tensor_copy(t_dst[:, buf * S + qb * 512: buf * S + qb * 512 + 512],
                           ps[:, 0:512]))
        gate_cell[0] = mk("dve")
        if proj == 1 and qb == 3:
            qkv_done[h] = mk("dve")

    def emit_v_tile(g, tt):
        gb = g % 2
        ps, gate_get, gate_cell = acc_pick()
        for ch in range(NCH):
            w = []
            if ch == 0:
                w = [lnT_tp[tt], gate_get(), wv_done_c[0]]
            emit("tensor", w, [("pe", 1)] if ch == NCH - 1 else [],
                 lambda e, ch=ch, tt=tt, g=g, ps=ps:
                 e.matmul(ps[:, 0:256],
                          ln1T[:, ch * S + tt * 128: ch * S + tt * 128 + 128],
                          wv_sb[:, ch * D + g * 256: ch * D + g * 256 + 256],
                          start=(ch == 0), stop=(ch == NCH - 1)))
        src = ap3(ps, 0, [[512, 128], [64, 4], [1, 64]])
        dst = ap3(vh_sb, gb * 4160 + tt * 65, [[8320, 128], [1040, 4], [1, 64]])
        emit("vector", [mk("pe"), vh_free[gb]], [("dve", 1)],
             lambda e, dst=dst, src=src: e.tensor_copy(dst, src))
        gate_cell[0] = mk("dve")
        if tt == NT - 1:
            vh_ready[gb] = mk("dve")

    def queue_head_work(h):
        bgq.append(lambda h=h: emit_qk_stream(h))
        for proj in range(2):
            for qb in range(4):
                bgq.append(lambda h=h, p=proj, q=qb: emit_qkv_group(h, p, q))

    def a_begin(item):
        a_tp_fence[0] = [mk("gpp0"), mk("gpp1"), mk("gc0"), mk("gc1"),
                         mk("gdiv")]
        a_xa_fence[0] = mk("gdv2")
        st8 = dict(item=item, base=item * S,
                   rstd=[("act", 0)] * NT, norm=[("dve", 0)] * NT,
                   tp=[("tpa0", 0)] * NT,
                   ln_fence=[mk("tpa0"), mk("tpa1"), mk("tpd0"), mk("tpd1"),
                             ln_stage_free[0], ln_stage_free[1]],
                   pe_fence=mk("pe"))
        return st8

    def a_tile(st8, tt):
        base = st8["base"]
        buf = tt % 2
        sc = 18 * buf
        w = [xa_free[buf]]
        if tt >= 2:
            w.append(st8["norm"][tt - 2])
        else:
            w.append(a_xa_fence[0])
        emit("sync", w, [(f"sxa{buf}", 16)],
             lambda e, tt=tt, buf=buf, base=base: e.dma_start(
                 out=xa_sb[:, buf * 1024: buf * 1024 + 1024],
                 in_=x8[base + tt * 128: base + tt * 128 + 128, :]))
        sy_m = mk(f"sxa{buf}")
        wv_ = [sy_m]
        if tt >= 2:
            wv_.append(st8["rstd"][tt - 2])
        emit("vector", wv_, [],
             lambda e, buf=buf, sc=sc: e.bn_stats(
                 st_sb[:, sc:sc + 6], xa_sb[:, buf * 1024: buf * 1024 + 512]))
        emit("vector", [], [("dve", 1)],
             lambda e, buf=buf, sc=sc: e.bn_stats(
                 st_sb[:, sc + 6:sc + 12],
                 xa_sb[:, buf * 1024 + 512: buf * 1024 + 1024]))
        emit("vector", [mk("dve")], [("dve", 1)],
             lambda e, sc=sc: e.bn_aggr(st_sb[:, sc + 12:sc + 14],
                                        st_sb[:, sc:sc + 12]))
        emit("scalar", [mk("dve")], [("act", 1)],
             lambda e, sc=sc: e.activation(st_sb[:, sc + 14:sc + 15],
                                           st_sb[:, sc + 13:sc + 14],
                                           AF.Ln, bias=eps_sb[:, 0:1], scale=1.0))
        emit("scalar", [mk("act")], [("act", 1)],
             lambda e, sc=sc: e.activation(st_sb[:, sc + 15:sc + 16],
                                           st_sb[:, sc + 14:sc + 15],
                                           AF.Exp, scale=-0.5))
        st8["rstd"][tt] = mk("act")
        emit("vector", [st8["rstd"][tt], mk("dve")], [("dve", 1)],
             lambda e, sc=sc: e.tensor_scalar(
                 st_sb[:, sc + 16:sc + 17], st_sb[:, sc + 12:sc + 13],
                 st_sb[:, sc + 15:sc + 16], -1.0, op0=OP.mult, op1=OP.mult))
        wv2 = [st8["tp"][tt - 2], mk("dve")] if tt >= 2 else \
            list(st8["ln_fence"]) + [mk("dve")]
        emit("vector", wv2, [("dve", 1)],
             lambda e, buf=buf, sc=sc: e.tensor_scalar(
                 ln_sb[:, buf * 1024: buf * 1024 + 1024],
                 xa_sb[:, buf * 1024: buf * 1024 + 1024],
                 st_sb[:, sc + 15:sc + 16], st_sb[:, sc + 16:sc + 17],
                 op0=OP.mult, op1=OP.add))
        st8["norm"][tt] = mk("dve")
        xa_free[buf] = st8["norm"][tt]
        src_ = ap3(ln_sb, buf * 1024, [[2048, 128], [128, 8], [1, 128]])
        dst_ = ap3(ln1T, tt * 128, [[NCH * S, 128], [S, 8], [1, 128]])
        wt = [st8["norm"][tt]] + a_tp_fence[0]
        if tt == 0:
            wt.append(st8["pe_fence"])
        emit("sync", wt, [(f"tpa{buf}", 16)],
             lambda e, dst_=dst_, src_=src_: e.dma_start(out=dst_, in_=src_,
                                                         transpose=True))
        st8["tp"][tt] = mk(f"tpa{buf}")
        lnT_tp[tt] = st8["tp"][tt]

    def a_end(st8):
        ln_stage_free[0] = st8["tp"][NT - 1]
        ln_stage_free[1] = st8["tp"][NT - 1]

    def phase_A(item):
        st8 = a_begin(item)
        for tt in range(NT):
            a_tile(st8, tt)
            bgq_pop(4)
        a_end(st8)
        return st8

    a_tp_fence = [[]]
    a_xa_fence = [("gdv2", 0)]
    ln_stage_free = [("gc0", 0), ("gc1", 0)]

    def band_range(kt, qh):
        lo = max(0, 128 * (kt - 1), 1024 * qh)
        hi = min(S, 128 * (kt + 2), 1024 * (qh + 1))
        return lo, hi

    def emit_sweep(item, h, qh):
        buf = h % 2
        g = h // 4
        gb = g % 2
        pbase = buf * S + qh * 1024
        exp_marks = {}
        s_pe = {}
        band_dve = {}

        def s_mms(ktp):
            ktA, ktB = 2 * ktp, 2 * ktp + 1
            for (kt, p0, coff, tp) in ((ktA, 0, 0, (0, 0)), (ktB, 64, 1024, (64, 0))):
                key = "A" if coff == 0 else "B"
                for half in range(2):
                    w = []
                    if half == 0:
                        w = [s_free[key], ("dve", qkv_done[h][1])]
                    emit("tensor", w, [("pe", 1)] if half == 1 else [],
                         lambda e, kt=kt, p0=p0, coff=coff, tp=tp, half=half,
                         buf=buf, pbase=pbase:
                         e.matmul(
                             s2_ps[:, coff + half * 512: coff + half * 512 + 512],
                             kt_sb[p0:p0 + 64, buf * S + kt * 128: buf * S + kt * 128 + 128],
                             qt_sb[p0:p0 + 64, pbase + half * 512: pbase + half * 512 + 512],
                             start=True, stop=True, tile_position=tp))
                s_pe[kt] = mk("pe")

        def band_add(kt, coff):
            lo, hi = band_range(kt, qh)
            if hi <= lo:
                return
            llo, lhi = lo - 1024 * qh, hi - 1024 * qh
            r0 = 128 * kt + 255 - lo
            ln_ = hi - lo
            ppo = (h % 2) * 384
            abs_start = ppo + r0
            abs_stop = ppo + r0 - ln_
            if abs_stop < 0:
                in1 = pp_sb[:, abs_start::-1]
            else:
                in1 = pp_sb[:, abs_start: abs_stop: -1]
            emit("vector", [s_pe[kt], pp_ready[h % 2]], [("dve", 1)],
                 lambda e, coff=coff, llo=llo, lhi=lhi, in1=in1:
                 e.tensor_tensor(s2_ps[:, coff + llo: coff + lhi],
                                 s2_ps[:, coff + llo: coff + lhi], in1, op=OP.add))
            band_dve[kt] = mk("dve")

        def exps(kt, coff, slot):
            lo, hi = band_range(kt, qh)
            segs = []
            if hi <= lo:
                segs = [(0, 1024, "lo" if 128 * (kt + 2) <= 1024 * qh else "hi")]
            else:
                llo, lhi = lo - 1024 * qh, hi - 1024 * qh
                if llo > 0:
                    segs.append((0, llo, "hi"))
                segs.append((llo, lhi, "band"))
                if lhi < 1024:
                    segs.append((lhi, 1024, "lo"))
            w = [s_pe[kt], pt_reader[slot]]
            if kt in band_dve:
                w.append(band_dve[kt])
            nseg = len(segs) + (1 if DBG_ACT_FLUSH else 0)
            for i, (a, b_, kind) in enumerate(segs):
                inc = [("act", 1)] if (i == len(segs) - 1 and not DBG_ACT_FLUSH) else []
                if kind == "band":
                    bias = None
                elif kind == "hi":
                    bias = ca_sb[:, 16 + h:16 + h + 1]
                else:
                    bias = ca_sb[:, h:h + 1]
                if bias is None:
                    emit("scalar", w if i == 0 else [], inc,
                         lambda e, a=a, b_=b_, coff=coff, slot=slot:
                         e.activation(pt_sb[:, slot * 1024 + a: slot * 1024 + b_],
                                      s2_ps[:, coff + a: coff + b_],
                                      AF.Exp, scale=1.0))
                else:
                    emit("scalar", w if i == 0 else [], inc,
                         lambda e, a=a, b_=b_, coff=coff, slot=slot, bias=bias:
                         e.activation(pt_sb[:, slot * 1024 + a: slot * 1024 + b_],
                                      s2_ps[:, coff + a: coff + b_],
                                      AF.Exp, bias=bias, scale=1.0))
            if DBG_ACT_FLUSH:
                emit("scalar", [], [("act", 1)],
                     lambda e, slot=slot: e.activation(
                         st_sb[0:1, 158:159],
                         pt_sb[0:1, slot * 1024: slot * 1024 + 1],
                         AF.Identity, scale=1.0))
            exp_marks[kt] = mk("act")
            key = "A" if coff == 0 else "B"
            s_free[key] = mk("act")

        def ctx_mms(kt, slot):
            vo = gb * 4160 + (h % 4) * 1040 + kt * 65
            for half in range(2):
                w = []
                if half == 0:
                    w = [exp_marks[kt], vh_ready[gb]]
                    if kt == 0:
                        w.append(ctx_gate[0])
                emit("tensor", w, [("pe", 1)] if half == 1 else [],
                     lambda e, kt=kt, slot=slot, half=half, vo=vo:
                     e.matmul(ctx_ps[0:65, half * 512: half * 512 + 512],
                              vh_sb[:, vo: vo + 65],
                              pt_sb[:, slot * 1024 + half * 512: slot * 1024 + half * 512 + 512],
                              start=(kt == 0), stop=(kt == NT - 1)))
            pt_reader[slot] = mk("pe")
            if kt == NT - 1:
                ctx_done[0] = mk("pe")

        prev = None
        for ktp in range(8):
            ktA, ktB = 2 * ktp, 2 * ktp + 1
            slA = kt_slot[0] % 4
            kt_slot[0] += 1
            slB = kt_slot[0] % 4
            kt_slot[0] += 1
            s_mms(ktp)
            band_add(ktA, 0)
            band_add(ktB, 1024)
            exps(ktA, 0, slA)
            exps(ktB, 1024, slB)
            if DBG_DUMP and item == DBG_DUMP_ITEM and h == 0 and qh == 0 and ktp == 0:
                emit("gpsimd", [mk("act")], [("gdbg", 16)],
                     lambda e, slA=slA, slB=slB: e.dma_start(
                         out=dbg["d_pt"], in_=pt_sb[:, :]))
                pt_reader[slA] = mk("gdbg")
                pt_reader[slB] = mk("gdbg")
            if prev is not None:
                ctx_mms(prev[0], prev[2])
                ctx_mms(prev[1], prev[3])
            if div_pending:
                div_pending.pop(0)()
            bgq_pop(2 if (ktp & 1) else 1)
            prev = (ktA, ktB, slA, slB)
        ctx_mms(prev[0], prev[2])
        ctx_mms(prev[1], prev[3])

    def queue_division(item, h, qh):
        ch = h // 2
        odd = h % 2 == 1
        pe_ctx = ctx_done[0]
        st = {}

        def d1():
            emit("vector", [pe_ctx, div_tt_gate[0]], [("dve", 1)],
                 lambda e: e.tensor_copy(ctxu_sb[0:65, :], ctx_ps[0:65, :]))
            ctx_gate[0] = mk("dve")
            st["d1"] = mk("dve")

        def d2():
            scat_src = ap3(ctxu_sb, 64 * 1024, [[1024, 1], [16, 64], [1, 16]])
            emit("sync", [st["d1"], div_r2_free[0]] + pp_tp_fence[0],
                 [("gdiv", 16)],
                 lambda e: e.dma_start(out=st_sb[64:128, 128:144], in_=scat_src))
            emit("vector", [mk("gdiv")], [("dve", 1)],
                 lambda e: e.reciprocal(st_sb[64:128, 128:144],
                                        st_sb[64:128, 128:144]))
            gat_src = ap3(st_sb, 64 * 160 + 128, [[160, 64], [1, 16]])
            emit("sync", [mk("dve"), div_rb_free[0]], [("gdiv", 16)],
                 lambda e: e.dma_start(out=rcp_d, in_=gat_src))
            bc_src = bass_rust.AP(rcp_d.tensor, 0, [[0, 64], [1, 1024]])
            emit("sync", [mk("gdiv"), div_xa_free[0]], [("gdv2", 16)],
                 lambda e: e.dma_start(out=xa_sb[0:64, 0:1024], in_=bc_src))
            st["d2"] = mk("gdv2")
            div_r2_free[0] = mk("gdiv")
            div_rb_free[0] = st["d2"]

        def d4(half):
            if odd:
                out = ln_sb[0:64, qh * 1024 + half * 512: qh * 1024 + half * 512 + 512]
            else:
                out = ctxT[0:64, ch * S + qh * 1024 + half * 512:
                           ch * S + qh * 1024 + half * 512 + 512]
            w = [st["d2"]]
            if half == 0:
                w.append(div_tt_gate[0])
            if odd:
                w.append(ln_stage_free[qh])
            emit("vector", w, [("dve", 1)],
                 lambda e, out=out, half=half: e.tensor_tensor(
                     out, ctxu_sb[0:64, half * 512: half * 512 + 512],
                     xa_sb[0:64, half * 512: half * 512 + 512], op=OP.mult))
            div_tt_gate[0] = mk("dve")
            if half == 1:
                div_xa_free[0] = mk("dve")
            st["d4"] = mk("dve")

        def d5():
            emit("sync", [st["d4"]] + pp_tp_fence[0], [(f"gc{qh}", 16)],
                 lambda e: e.dma_start(
                     out=ctxT[64:128, ch * S + qh * 1024: ch * S + qh * 1024 + 1024],
                     in_=ln_sb[0:64, qh * 1024: qh * 1024 + 1024]))
            ln_stage_free[qh] = mk(f"gc{qh}")

        seq = [d1, d2, lambda: d4(0), lambda: d4(1)]
        if odd:
            seq.append(d5)
        div_pending.extend(seq)

    def phase_B(item):
        for h in range(H):
            buf = h % 2
            if 1 <= h and h + 1 < H:
                queue_head_work(h + 1)
            nh = (h + 1) % H
            emit_patp(nh, [pp_freed[nh % 2]] + pp_tp_fence[0])
            if h % 4 == 1 and h // 4 + 1 <= 3:
                g = h // 4 + 1
                for tt in range(NT):
                    bgq.append(lambda g=g, tt=tt: emit_v_tile(g, tt))
            if h not in qkv_done:
                bgq_flush()
            assert h in qkv_done, f"QKV for head {h} not emitted"
            for qh in range(2):
                emit_sweep(item, h, qh)
                queue_division(item, h, qh)
            pp_freed[h % 2] = mk("dve")
            qt_freed[buf] = mk("pe")
            qs_freed[buf] = mk("pe")
            if h % 4 == 3:
                vh_free[(h // 4) % 2] = mk("pe")
        while div_pending:
            div_pending.pop(0)()
        bgq_flush()

    pp_tp_fence = [[]]

    def phase_C1(item, next_item):
        base = item * S
        div_done = mk("dve")
        div_gp = [mk("gc0"), mk("gc1"), mk("gdiv"), wo_done_c[0]]
        misc_gate[0] = div_done
        qkv_gate[0] = max(qkv_gate[0], div_done, key=lambda t: t[1])
        bank_gate = [div_done, div_done]
        xf = [mk("dve"), mk("dve")]
        c1_xa_fence[0] = mk("gdv2")
        xbf = [[mk("gy0"), mk("gx0")], [mk("gy1"), mk("gx1")]]
        x2_marks = []
        st8 = a_begin(next_item) if next_item is not None else None
        for tt in range(NT):
            buf = tt % 2
            emit("scalar", [xf[buf], xa_free[buf], c1_xa_fence[0]], [(f"sxa{buf}", 16)],
                 lambda e, tt=tt, buf=buf, base=base: e.dma_start(
                     out=xa_sb[:, buf * 1024: buf * 1024 + 1024],
                     in_=x8[base + tt * 128: base + tt * 128 + 128, :]))
            sy_m = mk(f"sxa{buf}")
            for dh in range(2):
                ps = msc_ps if dh == 0 else qkv_ps
                for ch in range(NCH):
                    w2 = []
                    if ch == 0:
                        w2 = [bank_gate[dh]] + div_gp
                        if dh == 1:
                            w2.append(qkv_gate[0])
                    emit("tensor", w2, [("pe", 1)] if ch == NCH - 1 else [],
                         lambda e, ch=ch, tt=tt, dh=dh, ps=ps: e.matmul(
                             ps[:, 0:512],
                             ctxT[:, ch * S + tt * 128: ch * S + tt * 128 + 128],
                             wo_sb[:, ch * D + dh * 512: ch * D + dh * 512 + 512],
                             start=(ch == 0), stop=(ch == NCH - 1)))
                emit("vector", [mk("pe"), sy_m] + xbf[buf], [("dve", 1)],
                     lambda e, buf=buf, dh=dh, ps=ps: e.tensor_tensor(
                         xb_sb[:, buf * 1024 + dh * 512: buf * 1024 + dh * 512 + 512],
                         ps[:, 0:512],
                         xa_sb[:, buf * 1024 + dh * 512: buf * 1024 + dh * 512 + 512],
                         op=OP.add))
                bank_gate[dh] = mk("dve")
                if dh == 1:
                    qkv_gate[0] = mk("dve")
            xf[buf] = mk("dve")
            xa_free[buf] = mk("dve")
            sc = 64 + 2 * tt
            emit("vector", [mk("dve")], [],
                 lambda e, buf=buf: e.bn_stats(
                     st_sb[:, 36:42], xb_sb[:, buf * 1024: buf * 1024 + 512]))
            emit("vector", [], [("dve", 1)],
                 lambda e, buf=buf: e.bn_stats(
                     st_sb[:, 42:48], xb_sb[:, buf * 1024 + 512: buf * 1024 + 1024]))
            emit("vector", [mk("dve")], [("dve", 1)],
                 lambda e, sc=sc: e.bn_aggr(st_sb[:, sc:sc + 2], st_sb[:, 36:48]))
            emit("gpsimd", [mk("dve")], [(f"gx{buf}", 16)],
                 lambda e, tt=tt, buf=buf: e.dma_start(
                     out=x2_d[tt * 128: tt * 128 + 128, :],
                     in_=xb_sb[:, buf * 1024: buf * 1024 + 1024]))
            xbf[buf] = [mk(f"gx{buf}")]
            x2_marks.append(mk(f"gx{buf}"))
            if st8 is not None:
                a_tile(st8, tt)
        if st8 is not None:
            a_end(st8)
        emit("scalar", [mk("dve")], [("act", 1)],
             lambda e: e.activation(st_sb[:, 96:112], st_sb[:, 65:97:2],
                                    AF.Ln, bias=eps_sb[:, 0:1], scale=1.0))
        emit("scalar", [mk("act")], [("act", 1)],
             lambda e: e.activation(st_sb[:, 112:128], st_sb[:, 96:112],
                                    AF.Exp, scale=-0.5))
        rstd2 = mk("act")
        emit("vector", [rstd2], [("dve", 1)],
             lambda e: e.scalar_tensor_tensor(
                 st_sb[:, 48:64], st_sb[:, 64:96:2], -1.0, st_sb[:, 112:128],
                 op0=OP.mult, op1=OP.mult))
        div_tt_gate[0] = mk("dve")
        return mk("dve"), x2_marks, st8, xf

    def phase_D(item, nm2_done, x2_marks, a_next, c1_xf, has_next):
        base = item * S
        load_table(id_exp)
        def sl(s):
            return (xb_sb, (s % 2) * 1024) if s < 2 else (xa_sb, (s % 2) * 1024)

        fr = [[nm2_done, x2_marks[NT - 2]],
              [nm2_done, x2_marks[NT - 1]],
              [c1_xf[0], xa_free[0], div_xa_free[0]],
              [c1_xf[1], xa_free[1], div_xa_free[0]]]
        sem_of = ["sxb0", "sxb1", "sxa0", "sxa1"]
        ld_m = [None] * NT

        def emit_ld(tt):
            s = tt % 4
            t, off = sl(s)
            emit("sync", fr[s] + [x2_marks[tt]], [(sem_of[s], 16)],
                 lambda e, tt=tt, t=t, off=off: e.dma_start(
                     out=t[:, off: off + 1024],
                     in_=x2_d[tt * 128: tt * 128 + 128, :]))
            ld_m[tt] = mk(sem_of[s])

        if a_next is not None:
            lnfree = [[a_next["tp"][NT - 2]], [a_next["tp"][NT - 1]]]
        else:
            lnfree = [[ln_stage_free[0]], [ln_stage_free[1]]]
        ltfree = [("pe", 0), ("pe", 0)]
        misc_gate[0] = mk("dve")
        qkv_gate[0] = mk("dve")
        tanh_done = [("act", 0)]
        tanh_war = [mk("gdiv")]
        yfree = [("gy0", 0), ("gy1", 0), ("gy0", 0), ("gy1", 0)]
        tp_marks = [None] * NT

        def emit_norm(tt):
            s, buf = tt % 4, tt % 2
            t_s, off_s = sl(s)
            emit("vector", [ld_m[tt], nm2_done] + lnfree[buf], [("dve", 1)],
                 lambda e, tt=tt, buf=buf, t_s=t_s, off_s=off_s: e.tensor_scalar(
                     ln_sb[:, buf * 1024: buf * 1024 + 1024],
                     t_s[:, off_s: off_s + 1024],
                     st_sb[:, 112 + tt:113 + tt], st_sb[:, 48 + tt:49 + tt],
                     op0=OP.mult, op1=OP.add))
            dve_m = mk("dve")
            src_ = ap3(ln_sb, buf * 1024, [[2048, 128], [128, 8], [1, 128]])
            dst_ = ap3(lt_sb, buf * 1024, [[2048, 128], [128, 8], [1, 128]])
            wt = [dve_m, ltfree[buf]]
            if tt == 0:
                wt.extend(d_tp_fence[0])
            emit("sync", wt, [(f"tpd{buf}", 16)],
                 lambda e, dst_=dst_, src_=src_: e.dma_start(out=dst_, in_=src_,
                                                             transpose=True))
            tp_marks[tt] = mk(f"tpd{buf}")
            lnfree[buf] = [tp_marks[tt]]

        emit_ld(0)
        emit_ld(1)
        emit_norm(0)
        for tt in range(NT):
            s = tt % 4
            buf = tt % 2
            t_s, off_s = sl(s)
            if tt + 2 < NT:
                emit_ld(tt + 2)
            if tt + 1 < NT:
                emit_norm(tt + 1)
            tp_m = tp_marks[tt]
            if tt == 2 and has_next:
                emit_qk_stream(0)
                qk0_early[0] = True
            for dh in range(2):
                ps, gate_get, gate_cell = acc_pick()
                for chm in range(NCH):
                    w3 = []
                    if chm == 0:
                        w3 = [tp_m, gate_get()]
                    emit("tensor", w3, [("pe", 1)] if chm == NCH - 1 else [],
                         lambda e, chm=chm, buf=buf, dh=dh, ps=ps: e.matmul(
                             ps[:, 0:512],
                             lt_sb[:, buf * 1024 + chm * 128: buf * 1024 + chm * 128 + 128],
                             m_sb[:, chm * D + dh * 512: chm * D + dh * 512 + 512],
                             start=(chm == 0), stop=(chm == NCH - 1)))
                pe_m = mk("pe")
                emit("vector", [pe_m, tanh_done[0]], [("dve", 1)],
                     lambda e, dh=dh, ps=ps: e.tensor_tensor(
                         ctxu_sb[:, dh * 512: dh * 512 + 512], ps[:, 0:512],
                         br_sb[:, dh * 512: dh * 512 + 512], op=OP.add))
                gate_cell[0] = mk("dve")
                emit("scalar", [mk("dve"), tanh_war[0]], [("act", 1)],
                     lambda e, dh=dh: e.activation(
                         aux_sb[:, dh * 512: dh * 512 + 512],
                         ctxu_sb[:, dh * 512: dh * 512 + 512],
                         AF.Tanh, scale=1.0))
                tanh_done[0] = mk("act")
                emit("vector", [mk("act"), yfree[s]], [("dve", 1)],
                     lambda e, t_s=t_s, off_s=off_s, dh=dh: e.tensor_tensor(
                         t_s[:, off_s + dh * 512: off_s + dh * 512 + 512],
                         t_s[:, off_s + dh * 512: off_s + dh * 512 + 512],
                         aux_sb[:, dh * 512: dh * 512 + 512], op=OP.add))
                tanh_war[0] = mk("dve")
            emit("gpsimd", [mk("dve")], [(f"gy{buf}", 16)],
                 lambda e, tt=tt, t_s=t_s, off_s=off_s, base=base: e.dma_start(
                     out=y8[base + tt * 128: base + tt * 128 + 128, :],
                     in_=t_s[:, off_s: off_s + 1024]))
            yfree[s] = mk(f"gy{buf}")
            fr[s] = [mk(f"gy{buf}")]
            ltfree[buf] = ("pe", cnt["pe"])
        load_table(id_nle)
        qkv_gate[0] = mk("dve")
        misc_gate[0] = mk("dve")
        xa_free[0] = fr[2][0]
        xa_free[1] = fr[3][0]
        div_xa_free[0] = fr[2][0]
        ln_stage_free[0] = mk("tpd0")
        ln_stage_free[1] = mk("tpd1")
        return

    d_tp_fence = [[]]
    qk0_early = [False]

    a_next_state = None
    for item in range(min(B_PER, DBG_ITEMS)):
        if DBG_STAGE == "prologue":
            break
        if a_next_state is None:
            a_next_state = phase_A(item)
        if DBG_DUMP and item == DBG_DUMP_ITEM:
            emit("gpsimd", [lnT_tp[NT - 2], lnT_tp[NT - 1]], [("gdbg", 16)],
                 lambda e: e.dma_start(out=dbg["d_ln1t"], in_=ln1T[:, :]))
        if DBG_STAGE == "A":
            a_next_state = None
            continue
        bgq_flush()
        if item == 0:
            wdup_done_c[0] = [mk("gw0"), mk("gw1")]
            pt_reader[0] = mk("gw1")
            pt_reader[1] = mk("gw1")
        if not qk0_early[0]:
            emit_qk_stream(0)
        qk0_early[0] = False
        for proj in range(2):
            for qb in range(4):
                emit_qkv_group(0, proj, qb)
        for tt in range(NT):
            emit_v_tile(0, tt)
        if DBG_DUMP and item == DBG_DUMP_ITEM:
            emit("gpsimd", [mk("dve")], [("gdbg", 16)],
                 lambda e: e.dma_start(out=dbg["d_qt"], in_=qt_sb[:, :]))
            emit("gpsimd", [], [("gdbg", 16)],
                 lambda e: e.dma_start(out=dbg["d_kt"], in_=kt_sb[:, :]))
            emit("gpsimd", [], [("gdbg", 16)],
                 lambda e: e.dma_start(out=dbg["d_vh"], in_=vh_sb[:, 0:8320]))
        queue_head_work(1)
        pp_tp_fence[0] = [mk("tpa0"), mk("tpa1"), mk("tpd0"), mk("tpd1"),
                          mk("tpw")]
        phase_B(item)
        if DBG_DUMP and item == DBG_DUMP_ITEM:
            emit("gpsimd", [mk("dve"), mk("gc0"), mk("gc1")], [("gdbg", 16)],
                 lambda e: e.dma_start(out=dbg["d_ctxt"], in_=ctxT[:, :]))
        if DBG_STAGE == "B":
            a_next_state = None
            continue
        nxt = item + 1 if item + 1 < min(B_PER, DBG_ITEMS) else None
        nm2_done, x2_marks, a_next_state, c1_xf = phase_C1(item, nxt)
        if DBG_STAGE == "C1":
            continue
        d_tp_fence[0] = [mk("gpp0"), mk("gpp1"), mk("gc0"), mk("gc1"),
                         mk("gdiv")]
        phase_D(item, nm2_done, x2_marks, a_next_state, c1_xf,
                has_next=nxt is not None)

    emit("gpsimd", [], [], lambda e: e.wait_ge(sems["gy0"], cnt["gy0"]))
    emit("gpsimd", [], [], lambda e: e.wait_ge(sems["gy1"], cnt["gy1"]))

    if DBG_PREFIX is not None:
        del OPS[DBG_PREFIX:]
    print(f"[kernel] total ops: {len(OPS)}")

    with nc.Block() as block:
        for eng_name in ["sync", "gpsimd", "tensor", "vector", "scalar"]:
            def make(eng_name):
                def run(eng):
                    for e_, fn in OPS:
                        if e_ == eng_name:
                            fn(eng)
                return run
            getattr(block, eng_name)(make(eng_name))

    return nc


_PROG = None


def kernel(L0_attn_hidden_states, L0_attn_Wq, L0_attn_Wk, L0_attn_Wv, L0_attn_Wo,
           L0_attn_rel_emb, L0_ffn_weight, L0_ffn_aux_weight1, L0_ffn_aux_weight2,
           L0_ffn_bias):
    global _PROG
    x = np.ascontiguousarray(np.asarray(L0_attn_hidden_states, dtype=np.float32))
    onehot, eye16 = _host_consts()
    shared = dict(
        wq=np.ascontiguousarray(np.asarray(L0_attn_Wq, np.float32)),
        wk=np.ascontiguousarray(np.asarray(L0_attn_Wk, np.float32)),
        wv=np.ascontiguousarray(np.asarray(L0_attn_Wv, np.float32)),
        wo=np.ascontiguousarray(np.asarray(L0_attn_Wo, np.float32)),
        wf=np.ascontiguousarray(np.asarray(L0_ffn_weight, np.float32)),
        wa1=np.ascontiguousarray(np.asarray(L0_ffn_aux_weight1, np.float32)),
        wa2=np.ascontiguousarray(np.asarray(L0_ffn_aux_weight2, np.float32)),
        fbias=np.ascontiguousarray(np.asarray(L0_ffn_bias, np.float32).reshape(1, D)),
        rel=np.ascontiguousarray(np.asarray(L0_attn_rel_emb, np.float32)),
        onehot=onehot, eye16=eye16,
    )
    if _PROG is None:
        _PROG = build_program()
    in_maps = []
    for c in range(N_CORES):
        m = dict(shared)
        m["x"] = np.ascontiguousarray(
            x[c * B_PER:(c + 1) * B_PER].reshape(B_PER * S, D))
        in_maps.append(m)
    res = run_bass_kernel_spmd(_PROG, in_maps, core_ids=list(range(N_CORES)))
    out = np.concatenate(
        [np.asarray(r["y"]).reshape(B_PER, S, D) for r in res.results], axis=0)
    return out.astype(np.float32)


# version 13
# speedup vs baseline: 1.3468x; 1.0044x over previous
import math
import numpy as np
import bass_rust
import concourse.bass as bass
import concourse.mybir as mybir
from concourse.bass_utils import run_bass_kernel_spmd

F32 = mybir.dt.float32
BF16 = mybir.dt.bfloat16
AF = mybir.ActivationFunctionType
OP = mybir.AluOpType

N_CORES = 8
B_PER = 4
S = 2048
D = 1024
H = 16
DK = 64
NT = S // 128
NCH = D // 128
NREV = 511
C_SHIFT = 12.0
EPS = 1e-5

NUM_BUCKETS = 32
MAX_DISTANCE = 128

DBG_ITEMS = B_PER
DBG_STAGE = "full"
DBG_PREFIX = None
DBG_DUMP = False
DBG_DUMP_ITEM = 0
DBG_NO_ROT = False
DBG_OLD_RECIP = False
DBG_ACT_FLUSH = False


def _rel_bucket_host(d):
    import jax
    import jax.numpy as jnp
    with jax.default_device(jax.devices("cpu")[0]):
        num_buckets = NUM_BUCKETS // 2
        rel = jnp.asarray(d, dtype=jnp.int32)
        relative_buckets = (rel > 0).astype(jnp.int32) * num_buckets
        rp = jnp.abs(rel)
        max_exact = num_buckets // 2
        is_small = rp < max_exact
        rp_large = max_exact + (
            jnp.log(rp.astype(jnp.float32) / max_exact)
            / math.log(MAX_DISTANCE / max_exact)
            * (num_buckets - max_exact)
        ).astype(jnp.int32)
        rp_large = jnp.minimum(rp_large, num_buckets - 1)
        relative_buckets = relative_buckets + jnp.where(is_small, rp, rp_large)
        return np.asarray(relative_buckets)


def _host_consts():
    d = np.arange(NREV, dtype=np.int64) - 255
    bk = _rel_bucket_host(d)
    onehot = np.zeros((NUM_BUCKETS, NREV), dtype=np.float32)
    onehot[bk, np.arange(NREV)] = 1.0
    eye16 = np.eye(16, dtype=np.float32)
    return onehot, eye16


def build_program():
    nc = bass.Bass("TRN2", target_bir_lowering=False)

    x8 = nc.dram_tensor("x", [B_PER * S, D], F32, kind="ExternalInput").ap()
    wq_d = nc.dram_tensor("wq", [D, D], F32, kind="ExternalInput").ap()
    wk_d = nc.dram_tensor("wk", [D, D], F32, kind="ExternalInput").ap()
    wv_d = nc.dram_tensor("wv", [D, D], F32, kind="ExternalInput").ap()
    wo_d = nc.dram_tensor("wo", [D, D], F32, kind="ExternalInput").ap()
    wf_d = nc.dram_tensor("wf", [D, D], F32, kind="ExternalInput").ap()
    wa1_d = nc.dram_tensor("wa1", [D, D], F32, kind="ExternalInput").ap()
    wa2_d = nc.dram_tensor("wa2", [D, D], F32, kind="ExternalInput").ap()
    fb_d = nc.dram_tensor("fbias", [1, D], F32, kind="ExternalInput").ap()
    rel_d = nc.dram_tensor("rel", [NUM_BUCKETS, H], F32, kind="ExternalInput").ap()
    oh_d = nc.dram_tensor("onehot", [NUM_BUCKETS, NREV], F32, kind="ExternalInput").ap()
    eye_d = nc.dram_tensor("eye16", [16, 16], F32, kind="ExternalInput").ap()
    y8 = nc.dram_tensor("y", [B_PER * S, D], F32, kind="ExternalOutput").ap()

    dbg = {}
    if DBG_DUMP:
        for nm, shp, dt in [("d_ln1t", [128, NCH * S], BF16),
                            ("d_qt", [128, 2 * S], BF16),
                            ("d_kt", [128, 2 * S], BF16),
                            ("d_vh", [128, 8320], BF16),
                            ("d_pt", [128, 4096], BF16),
                            ("d_ctxt", [128, NCH * S], BF16),
                            ("d_rev", [16, NREV], F32),
                            ("d_ca", [128, 32], F32),
                            ("d_pp", [128, 768], BF16)]:
            dbg[nm] = nc.dram_tensor(nm, shp, dt, kind="ExternalOutput").ap()

    qdup_d = nc.dram_tensor("qdup", [NCH, 128, H, 128], BF16, kind="Internal").ap()
    kdup_d = nc.dram_tensor("kdup", [NCH, 128, H, 128], BF16, kind="Internal").ap()
    x2_d = nc.dram_tensor("x2scr", [S, D], F32, kind="Internal").ap()
    rcp_d = nc.dram_tensor("rcpscr", [1, 1024], F32, kind="Internal").ap()

    A = nc.alloc_sbuf_tensor
    ln1T = A("ln1T", [128, NCH * S], BF16)
    ctxT = A("ctxT", [128, NCH * S], BF16)
    wv_sb = A("wv_sb", [128, NCH * D], BF16)
    wo_sb = A("wo_sb", [128, NCH * D], BF16)
    m_sb = A("m_sb", [128, NCH * D], BF16)
    qt_sb = A("qt_sb", [128, 2 * S], BF16)
    kt_sb = A("kt_sb", [128, 2 * S], BF16)
    qs_sb = A("qs_sb", [128, 2 * NCH * 128], BF16)
    ks_sb = A("ks_sb", [128, 2 * NCH * 128], BF16)
    vh_sb = A("vh_sb", [128, 2 * 4 * (NT * 65)], BF16)
    pt_sb = A("pt_sb", [128, 4 * 1024], BF16)
    pp_sb = A("pp_sb", [128, 2 * 384], BF16)
    xa_sb = A("xa_sb", [128, 2 * 1024], F32)
    xb_sb = A("xb_sb", [128, 2 * 1024], F32)
    ctxu_sb = A("ctxu_sb", [128, 1024], F32)
    aux_sb = A("aux_sb", [128, 1024], F32)
    ln_sb = A("ln_sb", [128, 2 * 1024], BF16)
    lt_sb = A("lt_sb", [128, 2 * 1024], BF16)
    br_sb = A("br_sb", [128, D], F32)
    ca_sb = A("ca_sb", [128, 32], F32)
    st_sb = A("st_sb", [128, 160], F32)
    rev_sb = A("rev_sb", [16, NREV], F32)
    oh_sb = A("oh_sb", [NUM_BUCKETS, NREV], F32)
    rel_sb = A("rel_sb", [NUM_BUCKETS, H], F32)
    eye_sb = A("eye_sb", [16, 16], F32)
    or_sb = A("or_sb", [1, 128], F32)
    oc_sb = A("oc_sb", [128, 64], F32)
    eps_sb = A("eps_sb", [128, 1], F32)

    P = nc.alloc_psum_tensor
    s2_ps = P("s2_ps", [128, 2048], F32)
    ctx_ps = P("ctx_ps", [128, 1024], F32)
    qkv_ps = P("qkv_ps", [128, 512], F32)
    msc_ps = P("msc_ps", [128, 512], F32)

    sem_names = ["ssm", "sxa0", "sxa1", "sxb0", "sxb1",
                 "tpa0", "tpa1", "tpd0", "tpd1", "tpw",
                 "gq0", "gq1", "gw0", "gw1", "gpp0", "gpp1",
                 "gdiv", "gc0", "gc1", "gx0", "gx1", "gy0", "gy1", "gdbg",
                 "gld0", "gld1", "gld2", "gld3", "gld4", "gdv2",
                 "pe", "act", "dve"]
    sems = {n: nc.alloc_semaphore(n) for n in sem_names}
    cnt = {n: 0 for n in sem_names}
    OPS = []

    def emit(engine, waits, incs, body):
        wl = []
        for s, v in waits:
            assert v <= cnt[s], f"wait on future: {s} {v} > {cnt[s]}"
            if v > 0:
                wl.append((sems[s], v))

        def fn(eng, wl=wl, incs=incs, body=body):
            for sh, v in wl:
                eng.wait_ge(sh, v)
            inst = body(eng)
            for s, k in incs:
                inst = inst.then_inc(sems[s], k)
        OPS.append((engine, fn))
        for s, k in incs:
            cnt[s] += k

    def mk(s):
        return (s, cnt[s])

    def ap3(t, off, dims):
        return bass_rust.AP(t[:].tensor, off, dims)

    from concourse.hw_specs import get_activation_tables
    tabs = list(get_activation_tables(nc.m.arch).items())
    id_nle = next(i for i, (n, f) in enumerate(tabs) if AF.Exp in f and AF.Ln in f)
    id_exp = next(i for i, (n, f) in enumerate(tabs) if AF.Exp in f and AF.Tanh in f)

    def load_table(idx):
        emit("scalar", [], [], lambda e, idx=idx: e.add_instruction(
            mybir.InstLoadActFuncSet(name=nc.get_next_instruction_name(),
                                     act_func_set_id=idx, ins=[], outs=[])))

    emit("vector", [], [], lambda e: e.memset(or_sb[0:1, :], 1.0))
    emit("vector", [], [], lambda e: e.memset(oc_sb[:, :], 1.0))
    for g2 in range(2):
        for hl in range(4):
            off = g2 * 4160 + hl * 1040 + 64
            apm = ap3(vh_sb, off, [[8320, 128], [65, NT]])
            emit("vector", [], [], lambda e, apm=apm: e.memset(apm, 1.0))
    emit("vector", [], [], lambda e: e.memset(eps_sb[:, :], EPS))
    emit("vector", [], [("dve", 1)], lambda e: e.memset(st_sb[:, :], 0.0))
    ones_done = mk("dve")

    load_table(id_nle)

    emit("sync", [], [("ssm", 16)], lambda e: e.dma_start(out=oh_sb[:, :], in_=oh_d))
    emit("sync", [], [("ssm", 16)], lambda e: e.dma_start(out=rel_sb[:, :], in_=rel_d))
    emit("sync", [], [("ssm", 16)], lambda e: e.dma_start(out=eye_sb[:, :], in_=eye_d))
    emit("sync", [], [("ssm", 16)], lambda e: e.dma_start(out=ctxu_sb[0:1, 0:D], in_=fb_d))
    small_ld = mk("ssm")

    emit("tensor", [small_ld], [("pe", 1)],
         lambda e: e.matmul(msc_ps[0:16, 0:NREV], rel_sb[0:32, 0:16],
                            oh_sb[0:32, 0:NREV], start=True, stop=True))
    emit("scalar", [mk("pe")], [("act", 1)],
         lambda e: e.activation(rev_sb[0:16, :], msc_ps[0:16, 0:NREV], AF.Copy,
                                bias=-C_SHIFT, scale=1.0))
    rev_done = mk("act")

    pp_ready = [None, None]
    pp_freed = [("dve", 0), ("dve", 0)]

    def emit_patp(h, waits):
        pbuf = h % 2
        sem = f"gpp{pbuf}"
        src = ap3(rev_sb, h * NREV, [[NREV, 1], [1, 128], [1, 384]])
        dst = pp_sb[:, pbuf * 384:(pbuf + 1) * 384]
        emit("gpsimd", waits, [(sem, 16)],
             lambda e, dst=dst, src=src: e.dma_start(out=dst, in_=src))
        pp_ready[pbuf] = mk(sem)

    emit_patp(0, [rev_done])
    emit_patp(1, [rev_done])
    patp_prologue = [mk("gpp0"), mk("gpp1")]

    emit("tensor", [rev_done], [],
         lambda e: e.matmul(msc_ps[0:1, 0:16], rev_sb[0:16, 0:1],
                            eye_sb[0:16, 0:16], start=True, stop=True))
    emit("tensor", [], [("pe", 1)],
         lambda e: e.matmul(msc_ps[0:1, 16:32], rev_sb[0:16, 510:511],
                            eye_sb[0:16, 0:16], start=True, stop=True))
    emit("vector", [mk("pe"), ones_done], [("dve", 1)],
         lambda e: e.tensor_copy(st_sb[0:1, 0:32], msc_ps[0:1, 0:32]))
    emit("tensor", [mk("dve")], [("pe", 1)],
         lambda e: e.matmul(msc_ps[0:128, 0:32], or_sb[0:1, 0:128],
                            st_sb[0:1, 0:32], start=True, stop=True))
    emit("vector", [mk("pe")], [("dve", 1)],
         lambda e: e.tensor_copy(ca_sb[:, :], msc_ps[:, 0:32]))

    for half in range(2):
        emit("tensor", [small_ld, ones_done, mk("dve")], [("pe", 1)],
             lambda e, half=half: e.matmul(
                 qkv_ps[:, 0:512], or_sb[0:1, 0:128],
                 ctxu_sb[0:1, half * 512:(half + 1) * 512], start=True, stop=True))
        emit("vector", [mk("pe")], [("dve", 1)],
             lambda e, half=half: e.tensor_copy(
                 br_sb[:, half * 512:(half + 1) * 512], qkv_ps[:, 0:512]))
    pro1_done = mk("dve")

    wbf, abf = ctxT, ln1T
    xa_free = [pro1_done, pro1_done]

    _gld_i = [0]

    def cast_load(src_d, dst_t, dst_off, waits):
        sem = f"gld{_gld_i[0]}"
        _gld_i[0] += 1
        pitch = dst_t[:].ap[0][0]
        s_ap = bass_rust.AP(src_d.tensor, 0,
                            [[1024, 128], [128 * 1024, NCH], [1, 1024]])
        d_ap = bass_rust.AP(dst_t[:].tensor, dst_off,
                            [[pitch, 128], [1024, NCH], [1, 1024]])
        emit("gpsimd", waits, [(sem, 16)],
             lambda e, d_ap=d_ap, s_ap=s_ap: e.dma_start(out=d_ap, in_=s_ap))
        return mk(sem)

    w_cast = cast_load(wf_d, wbf, 0, [pro1_done])
    for i in range(NCH):
        src = ap3(wbf, i * 1024, [[NCH * S, 128], [128, 8], [1, 128]])
        dst = ap3(wbf, 8192 + i * 128, [[NCH * S, 128], [1024, 8], [1, 128]])
        w = [w_cast]
        if i == 0:
            w.extend(patp_prologue)
        emit("sync", w, [("tpw", 16)],
             lambda e, dst=dst, src=src: e.dma_start(out=dst, in_=src, transpose=True))
    wt_done = mk("tpw")
    a1_done = cast_load(wa1_d, abf, 0, [pro1_done])

    def mm_chain(lhs_fn, rhs_fn, ps, dst_t, dst_fn, extra_first_waits, gate0):
        gate = gate0
        for ot in range(NCH):
            for ob in range(2):
                for ch in range(NCH):
                    w = []
                    if ch == 0:
                        w = list(extra_first_waits) + [gate]
                    emit("tensor", w, [("pe", 1)] if ch == NCH - 1 else [],
                         lambda e, ot=ot, ob=ob, ch=ch, ps=ps: e.matmul(
                             ps[:, 0:512], lhs_fn(ch, ot), rhs_fn(ch, ob),
                             start=(ch == 0), stop=(ch == NCH - 1)))
                emit("vector", [mk("pe")], [("dve", 1)],
                     lambda e, ot=ot, ob=ob, dst_t=dst_t: e.tensor_copy(
                         dst_fn(dst_t, ot, ob), qkv_ps[:, 0:512] if ps is qkv_ps
                         else msc_ps[:, 0:512]))
                gate = mk("dve")
        return mk("dve")

    m1t_done = mm_chain(
        lambda ch, jt: abf[:, ch * 1024 + jt * 128: ch * 1024 + jt * 128 + 128],
        lambda ch, cb: wbf[:, 8192 + ch * 1024 + cb * 512: 8192 + ch * 1024 + cb * 512 + 512],
        msc_ps, abf,
        lambda t, jt, cb: t[:, 8192 + jt * 1024 + cb * 512: 8192 + jt * 1024 + cb * 512 + 512],
        [a1_done, wt_done], a1_done)
    a2_done = cast_load(wa2_d, abf, 0, [m1t_done])
    m_done = mm_chain(
        lambda ch, ct: abf[:, 8192 + ch * 1024 + ct * 128: 8192 + ch * 1024 + ct * 128 + 128],
        lambda ch, jb: abf[:, ch * 1024 + jb * 512: ch * 1024 + jb * 512 + 512],
        qkv_ps, m_sb,
        lambda t, ct, jb: t[:, ct * 1024 + jb * 512: ct * 1024 + jb * 512 + 512],
        [a2_done], a2_done)


    if DBG_DUMP:
        emit("gpsimd", [mk("dve"), mk("act")], [("gdbg", 16)],
             lambda e: e.dma_start(out=dbg["d_rev"], in_=rev_sb[:, :]))
        emit("gpsimd", [], [("gdbg", 16)],
             lambda e: e.dma_start(out=dbg["d_ca"], in_=ca_sb[:, :]))
        emit("gpsimd", [], [("gdbg", 16)],
             lambda e: e.dma_start(out=dbg["d_pp"], in_=pp_sb[:, :]))

    s_free = {"A": ("act", 0), "B": ("act", 0)}
    pt_reader = [("pe", 0)] * 4
    pp_reader_mark = [("dve", 0), ("dve", 0)]
    s2g = [None] * 4
    kt_slot = [0]
    qs_ready = [None, None]
    qs_freed = [("pe", 0), ("pe", 0)]
    qt_freed = [("pe", 0), ("pe", 0)]
    qkv_done = {}
    vh_ready = [("dve", 0), ("dve", 0)]
    vh_free = [("pe", 0), ("pe", 0)]
    qkv_gate = [m_done]
    lnT_tp = [("tpa0", 0)] * NT
    ctx_gate = [("dve", 0)]
    ctx_done = [("pe", 0)]
    div_tt_gate = [("dve", 0)]
    c1_xa_fence = [("gdv2", 0)]
    misc_gate = [m1t_done]
    acc_i = [0]
    div_r2_free = [("gdiv", 0)]
    div_rb_free = [("gdv2", 0)]
    div_xa_free = [("dve", 0)]
    div_pending = []
    bgq = []


    def bgq_pop(n):
        for _ in range(min(n, len(bgq))):
            bgq.pop(0)()

    def bgq_flush():
        while bgq:
            bgq.pop(0)()

    stage_free = {0: m_done, 1: m_done}
    wv_done_c = [None]
    wo_done_c = [None]
    wdup_done_c = [None]

    def q_cast_chunk(src_d, dst_t, dst_off, ch, done_cell):
        def go():
            buf = ch % 2
            emit("sync", [xa_free[buf]], [(f"sxa{buf}", 16)],
                 lambda e: e.dma_start(
                     out=xa_sb[:, buf * 1024:(buf + 1) * 1024],
                     in_=src_d[ch * 128:(ch + 1) * 128, :]))
            emit("vector", [mk(f"sxa{buf}")], [("dve", 1)],
                 lambda e: e.tensor_copy(
                     dst_t[:, dst_off + ch * 1024: dst_off + (ch + 1) * 1024],
                     xa_sb[:, buf * 1024:(buf + 1) * 1024]))
            xa_free[buf] = mk("dve")
            if done_cell is not None and ch == NCH - 1:
                done_cell[0] = mk("dve")
        return go

    def q_dup_chunk(src_d, dst_d, ch, done_cell):
        def go():
            buf = ch % 2
            st_t = lt_sb if buf == 0 else pt_sb
            emit("sync", [xa_free[buf]], [(f"sxa{buf}", 16)],
                 lambda e: e.dma_start(
                     out=xa_sb[:, buf * 1024: buf * 1024 + 1024],
                     in_=src_d[ch * 128:(ch + 1) * 128, :]))
            sy_m = mk(f"sxa{buf}")
            pitch = NCH * 512 if st_t is pt_sb else 2048
            for rep in range(2):
                src_ap = ap3(xa_sb, buf * 1024, [[2048, 128], [64, 16], [1, 64]])
                dst_ap = ap3(st_t, rep * 64, [[pitch, 128], [128, 16], [1, 64]])
                emit("vector", [sy_m, stage_free[buf]], [("dve", 1)],
                     lambda e, dst_ap=dst_ap, src_ap=src_ap:
                     e.tensor_copy(dst_ap, src_ap))
            xa_free[buf] = mk("dve")
            emit("gpsimd", [mk("dve")], [(f"gw{buf}", 16)],
                 lambda e: e.dma_start(
                     out=dst_d[ch],
                     in_=st_t[:, 0:2048].rearrange("p (h c) -> p h c", h=16)))
            stage_free[buf] = mk(f"gw{buf}")
            if done_cell is not None and ch == NCH - 1:
                done_cell[0] = None
        return go

    for _ch in range(NCH):
        bgq.append(q_dup_chunk(wq_d, qdup_d, _ch, None))
    for _ch in range(NCH):
        bgq.append(q_dup_chunk(wk_d, kdup_d, _ch, wdup_done_c))
    def _wv_item():
        wv_done_c[0] = cast_load(wv_d, wv_sb, 0, [])
    def _wo_item():
        wo_done_c[0] = cast_load(wo_d, wo_sb, 0, [])
    bgq.append(_wv_item)
    bgq.append(_wo_item)

    def emit_qk_stream(h):
        buf = h % 2
        for (src_d, dst_t) in ((qdup_d, qs_sb), (kdup_d, ks_sb)):
            src = bass_rust.AP(src_d.tensor, h * 128,
                               [[H * 128, 128], [128 * H * 128, NCH], [1, 128]])
            dst = ap3(dst_t, buf * 1024, [[2048, 128], [128, 8], [1, 128]])
            emit("gpsimd", [qs_freed[buf]] + wdup_done_c[0], [(f"gq{buf}", 16)],
                 lambda e, dst=dst, src=src: e.dma_start(out=dst, in_=src))
        qs_ready[buf] = mk(f"gq{buf}")

    def acc_pick():
        i = (acc_i[0] % 2) if not DBG_NO_ROT else 0
        acc_i[0] += 1
        if i == 0:
            return qkv_ps, (lambda: qkv_gate[0]), qkv_gate
        return msc_ps, (lambda: misc_gate[0]), misc_gate

    def emit_qkv_group(h, proj, qb):
        buf = h % 2
        w_src = qs_sb if proj == 0 else ks_sb
        t_dst = qt_sb if proj == 0 else kt_sb
        ps, gate_get, gate_cell = acc_pick()
        for ch in range(NCH):
            w = []
            if ch == 0:
                w = [qs_ready[buf], lnT_tp[qb * 4 + 2],
                     lnT_tp[qb * 4 + 3], gate_get()]
            emit("tensor", w, [("pe", 1)] if ch == NCH - 1 else [],
                 lambda e, ch=ch, qb=qb, buf=buf, w_src=w_src, ps=ps:
                 e.matmul(ps[:, 0:512],
                          w_src[:, buf * 1024 + ch * 128: buf * 1024 + ch * 128 + 128],
                          ln1T[:, ch * S + qb * 512: ch * S + qb * 512 + 512],
                          start=(ch == 0), stop=(ch == NCH - 1)))
        emit("vector", [mk("pe")], [("dve", 1)],
             lambda e, qb=qb, buf=buf, t_dst=t_dst, ps=ps:
             e.tensor_copy(t_dst[:, buf * S + qb * 512: buf * S + qb * 512 + 512],
                           ps[:, 0:512]))
        gate_cell[0] = mk("dve")
        if proj == 1 and qb == 3:
            qkv_done[h] = mk("dve")

    def emit_v_tile(g, tt):
        gb = g % 2
        ps, gate_get, gate_cell = acc_pick()
        for ch in range(NCH):
            w = []
            if ch == 0:
                w = [lnT_tp[tt], gate_get(), wv_done_c[0]]
            emit("tensor", w, [("pe", 1)] if ch == NCH - 1 else [],
                 lambda e, ch=ch, tt=tt, g=g, ps=ps:
                 e.matmul(ps[:, 0:256],
                          ln1T[:, ch * S + tt * 128: ch * S + tt * 128 + 128],
                          wv_sb[:, ch * D + g * 256: ch * D + g * 256 + 256],
                          start=(ch == 0), stop=(ch == NCH - 1)))
        src = ap3(ps, 0, [[512, 128], [64, 4], [1, 64]])
        dst = ap3(vh_sb, gb * 4160 + tt * 65, [[8320, 128], [1040, 4], [1, 64]])
        emit("vector", [mk("pe"), vh_free[gb]], [("dve", 1)],
             lambda e, dst=dst, src=src: e.tensor_copy(dst, src))
        gate_cell[0] = mk("dve")
        if tt == NT - 1:
            vh_ready[gb] = mk("dve")

    def queue_head_work(h):
        bgq.append(lambda h=h: emit_qk_stream(h))
        for proj in range(2):
            for qb in range(4):
                bgq.append(lambda h=h, p=proj, q=qb: emit_qkv_group(h, p, q))

    def a_begin(item):
        a_tp_fence[0] = [mk("gpp0"), mk("gpp1"), mk("gc0"), mk("gc1"),
                         mk("gdiv")]
        a_xa_fence[0] = mk("gdv2")
        st8 = dict(item=item, base=item * S,
                   rstd=[("act", 0)] * NT, norm=[("dve", 0)] * NT,
                   tp=[("tpa0", 0)] * NT,
                   ln_fence=[mk("tpa0"), mk("tpa1"), mk("tpd0"), mk("tpd1"),
                             ln_stage_free[0], ln_stage_free[1]],
                   pe_fence=mk("pe"))
        return st8

    def a_tile(st8, tt):
        base = st8["base"]
        buf = tt % 2
        sc = 18 * buf
        w = [xa_free[buf]]
        if tt >= 2:
            w.append(st8["norm"][tt - 2])
        else:
            w.append(a_xa_fence[0])
        emit("sync", w, [(f"sxa{buf}", 16)],
             lambda e, tt=tt, buf=buf, base=base: e.dma_start(
                 out=xa_sb[:, buf * 1024: buf * 1024 + 1024],
                 in_=x8[base + tt * 128: base + tt * 128 + 128, :]))
        sy_m = mk(f"sxa{buf}")
        wv_ = [sy_m]
        if tt >= 2:
            wv_.append(st8["rstd"][tt - 2])
        emit("vector", wv_, [],
             lambda e, buf=buf, sc=sc: e.bn_stats(
                 st_sb[:, sc:sc + 6], xa_sb[:, buf * 1024: buf * 1024 + 512]))
        emit("vector", [], [("dve", 1)],
             lambda e, buf=buf, sc=sc: e.bn_stats(
                 st_sb[:, sc + 6:sc + 12],
                 xa_sb[:, buf * 1024 + 512: buf * 1024 + 1024]))
        emit("vector", [mk("dve")], [("dve", 1)],
             lambda e, sc=sc: e.bn_aggr(st_sb[:, sc + 12:sc + 14],
                                        st_sb[:, sc:sc + 12]))
        emit("scalar", [mk("dve")], [("act", 1)],
             lambda e, sc=sc: e.activation(st_sb[:, sc + 14:sc + 15],
                                           st_sb[:, sc + 13:sc + 14],
                                           AF.Ln, bias=eps_sb[:, 0:1], scale=1.0))
        emit("scalar", [mk("act")], [("act", 1)],
             lambda e, sc=sc: e.activation(st_sb[:, sc + 15:sc + 16],
                                           st_sb[:, sc + 14:sc + 15],
                                           AF.Exp, scale=-0.5))
        st8["rstd"][tt] = mk("act")
        emit("vector", [st8["rstd"][tt], mk("dve")], [("dve", 1)],
             lambda e, sc=sc: e.tensor_scalar(
                 st_sb[:, sc + 16:sc + 17], st_sb[:, sc + 12:sc + 13],
                 st_sb[:, sc + 15:sc + 16], -1.0, op0=OP.mult, op1=OP.mult))
        wv2 = [st8["tp"][tt - 2], mk("dve")] if tt >= 2 else \
            list(st8["ln_fence"]) + [mk("dve")]
        emit("vector", wv2, [("dve", 1)],
             lambda e, buf=buf, sc=sc: e.tensor_scalar(
                 ln_sb[:, buf * 1024: buf * 1024 + 1024],
                 xa_sb[:, buf * 1024: buf * 1024 + 1024],
                 st_sb[:, sc + 15:sc + 16], st_sb[:, sc + 16:sc + 17],
                 op0=OP.mult, op1=OP.add))
        st8["norm"][tt] = mk("dve")
        xa_free[buf] = st8["norm"][tt]
        src_ = ap3(ln_sb, buf * 1024, [[2048, 128], [128, 8], [1, 128]])
        dst_ = ap3(ln1T, tt * 128, [[NCH * S, 128], [S, 8], [1, 128]])
        wt = [st8["norm"][tt]] + a_tp_fence[0]
        if tt == 0:
            wt.append(st8["pe_fence"])
        emit("sync", wt, [(f"tpa{buf}", 16)],
             lambda e, dst_=dst_, src_=src_: e.dma_start(out=dst_, in_=src_,
                                                         transpose=True))
        st8["tp"][tt] = mk(f"tpa{buf}")
        lnT_tp[tt] = st8["tp"][tt]

    def a_end(st8):
        ln_stage_free[0] = st8["tp"][NT - 1]
        ln_stage_free[1] = st8["tp"][NT - 1]

    def phase_A(item):
        st8 = a_begin(item)
        for tt in range(NT):
            a_tile(st8, tt)
            bgq_pop(4)
        a_end(st8)
        return st8

    a_tp_fence = [[]]
    a_xa_fence = [("gdv2", 0)]
    ln_stage_free = [("gc0", 0), ("gc1", 0)]

    def band_range(kt, qh):
        lo = max(0, 128 * (kt - 1), 1024 * qh)
        hi = min(S, 128 * (kt + 2), 1024 * (qh + 1))
        return lo, hi

    def emit_sweep(item, h, qh):
        buf = h % 2
        g = h // 4
        gb = g % 2
        pbase = buf * S + qh * 1024
        exp_marks = {}
        s_pe = {}
        band_dve = {}

        def s_mms(ktp):
            ktA, ktB = 2 * ktp, 2 * ktp + 1
            for (kt, p0, coff, tp) in ((ktA, 0, 0, (0, 0)), (ktB, 64, 1024, (64, 0))):
                key = "A" if coff == 0 else "B"
                for half in range(2):
                    w = []
                    if half == 0:
                        w = [s_free[key], ("dve", qkv_done[h][1])]
                    emit("tensor", w, [("pe", 1)] if half == 1 else [],
                         lambda e, kt=kt, p0=p0, coff=coff, tp=tp, half=half,
                         buf=buf, pbase=pbase:
                         e.matmul(
                             s2_ps[:, coff + half * 512: coff + half * 512 + 512],
                             kt_sb[p0:p0 + 64, buf * S + kt * 128: buf * S + kt * 128 + 128],
                             qt_sb[p0:p0 + 64, pbase + half * 512: pbase + half * 512 + 512],
                             start=True, stop=True, tile_position=tp))
                s_pe[kt] = mk("pe")

        def band_add(kt, coff):
            lo, hi = band_range(kt, qh)
            if hi <= lo:
                return
            llo, lhi = lo - 1024 * qh, hi - 1024 * qh
            r0 = 128 * kt + 255 - lo
            ln_ = hi - lo
            ppo = (h % 2) * 384
            abs_start = ppo + r0
            abs_stop = ppo + r0 - ln_
            if abs_stop < 0:
                in1 = pp_sb[:, abs_start::-1]
            else:
                in1 = pp_sb[:, abs_start: abs_stop: -1]
            emit("vector", [s_pe[kt], pp_ready[h % 2]], [("dve", 1)],
                 lambda e, coff=coff, llo=llo, lhi=lhi, in1=in1:
                 e.tensor_tensor(s2_ps[:, coff + llo: coff + lhi],
                                 s2_ps[:, coff + llo: coff + lhi], in1, op=OP.add))
            band_dve[kt] = mk("dve")
            pp_reader_mark[h % 2] = band_dve[kt]

        def exps(kt, coff, slot):
            lo, hi = band_range(kt, qh)
            segs = []
            if hi <= lo:
                segs = [(0, 1024, "lo" if 128 * (kt + 2) <= 1024 * qh else "hi")]
            else:
                llo, lhi = lo - 1024 * qh, hi - 1024 * qh
                if llo > 0:
                    segs.append((0, llo, "hi"))
                segs.append((llo, lhi, "band"))
                if lhi < 1024:
                    segs.append((lhi, 1024, "lo"))
            w = [s_pe[kt], pt_reader[slot]]
            if kt in band_dve:
                w.append(band_dve[kt])
            nseg = len(segs) + (1 if DBG_ACT_FLUSH else 0)
            for i, (a, b_, kind) in enumerate(segs):
                inc = [("act", 1)] if (i == len(segs) - 1 and not DBG_ACT_FLUSH) else []
                if kind == "band":
                    bias = None
                elif kind == "hi":
                    bias = ca_sb[:, 16 + h:16 + h + 1]
                else:
                    bias = ca_sb[:, h:h + 1]
                if bias is None:
                    emit("scalar", w if i == 0 else [], inc,
                         lambda e, a=a, b_=b_, coff=coff, slot=slot:
                         e.activation(pt_sb[:, slot * 1024 + a: slot * 1024 + b_],
                                      s2_ps[:, coff + a: coff + b_],
                                      AF.Exp, scale=1.0))
                else:
                    emit("scalar", w if i == 0 else [], inc,
                         lambda e, a=a, b_=b_, coff=coff, slot=slot, bias=bias:
                         e.activation(pt_sb[:, slot * 1024 + a: slot * 1024 + b_],
                                      s2_ps[:, coff + a: coff + b_],
                                      AF.Exp, bias=bias, scale=1.0))
            if DBG_ACT_FLUSH:
                emit("scalar", [], [("act", 1)],
                     lambda e, slot=slot: e.activation(
                         st_sb[0:1, 158:159],
                         pt_sb[0:1, slot * 1024: slot * 1024 + 1],
                         AF.Identity, scale=1.0))
            exp_marks[kt] = mk("act")
            key = "A" if coff == 0 else "B"
            s_free[key] = mk("act")

        def ctx_mms(kt, slot):
            vo = gb * 4160 + (h % 4) * 1040 + kt * 65
            for half in range(2):
                w = []
                if half == 0:
                    w = [exp_marks[kt], vh_ready[gb]]
                    if kt == 0:
                        w.append(ctx_gate[0])
                emit("tensor", w, [("pe", 1)] if half == 1 else [],
                     lambda e, kt=kt, slot=slot, half=half, vo=vo:
                     e.matmul(ctx_ps[0:65, half * 512: half * 512 + 512],
                              vh_sb[:, vo: vo + 65],
                              pt_sb[:, slot * 1024 + half * 512: slot * 1024 + half * 512 + 512],
                              start=(kt == 0), stop=(kt == NT - 1)))
            pt_reader[slot] = mk("pe")
            if kt == NT - 1:
                ctx_done[0] = mk("pe")

        prev = None
        for ktp in range(8):
            ktA, ktB = 2 * ktp, 2 * ktp + 1
            slA = kt_slot[0] % 4
            kt_slot[0] += 1
            slB = kt_slot[0] % 4
            kt_slot[0] += 1
            s_mms(ktp)
            band_add(ktA, 0)
            band_add(ktB, 1024)
            exps(ktA, 0, slA)
            exps(ktB, 1024, slB)
            if DBG_DUMP and item == DBG_DUMP_ITEM and h == 0 and qh == 0 and ktp == 0:
                emit("gpsimd", [mk("act")], [("gdbg", 16)],
                     lambda e, slA=slA, slB=slB: e.dma_start(
                         out=dbg["d_pt"], in_=pt_sb[:, :]))
                pt_reader[slA] = mk("gdbg")
                pt_reader[slB] = mk("gdbg")
            if prev is not None:
                ctx_mms(prev[0], prev[2])
                ctx_mms(prev[1], prev[3])
            if div_pending:
                div_pending.pop(0)()
            bgq_pop(2 if (ktp & 1) else 1)
            prev = (ktA, ktB, slA, slB)
        ctx_mms(prev[0], prev[2])
        ctx_mms(prev[1], prev[3])

    def queue_division(item, h, qh):
        ch = h // 2
        odd = h % 2 == 1
        pe_ctx = ctx_done[0]
        st = {}

        def d1():
            emit("vector", [pe_ctx, div_tt_gate[0]], [("dve", 1)],
                 lambda e: e.tensor_copy(ctxu_sb[0:65, :], ctx_ps[0:65, :]))
            ctx_gate[0] = mk("dve")
            st["d1"] = mk("dve")

        def d2():
            scat_src = ap3(ctxu_sb, 64 * 1024, [[1024, 1], [16, 64], [1, 16]])
            emit("sync", [st["d1"], div_r2_free[0]] + pp_tp_fence[0],
                 [("gdiv", 16)],
                 lambda e: e.dma_start(out=st_sb[64:128, 128:144], in_=scat_src))
            emit("vector", [mk("gdiv")], [("dve", 1)],
                 lambda e: e.reciprocal(st_sb[64:128, 128:144],
                                        st_sb[64:128, 128:144]))
            gat_src = ap3(st_sb, 64 * 160 + 128, [[160, 64], [1, 16]])
            emit("sync", [mk("dve"), div_rb_free[0]], [("gdiv", 16)],
                 lambda e: e.dma_start(out=rcp_d, in_=gat_src))
            bc_src = bass_rust.AP(rcp_d.tensor, 0, [[0, 64], [1, 1024]])
            emit("sync", [mk("gdiv"), div_xa_free[0]], [("gdv2", 16)],
                 lambda e: e.dma_start(out=xa_sb[0:64, 0:1024], in_=bc_src))
            st["d2"] = mk("gdv2")
            div_r2_free[0] = mk("gdiv")
            div_rb_free[0] = st["d2"]

        def d4(half):
            if odd:
                out = ln_sb[0:64, qh * 1024 + half * 512: qh * 1024 + half * 512 + 512]
            else:
                out = ctxT[0:64, ch * S + qh * 1024 + half * 512:
                           ch * S + qh * 1024 + half * 512 + 512]
            w = [st["d2"]]
            if half == 0:
                w.append(div_tt_gate[0])
            if odd:
                w.append(ln_stage_free[qh])
            emit("vector", w, [("dve", 1)],
                 lambda e, out=out, half=half: e.tensor_tensor(
                     out, ctxu_sb[0:64, half * 512: half * 512 + 512],
                     xa_sb[0:64, half * 512: half * 512 + 512], op=OP.mult))
            div_tt_gate[0] = mk("dve")
            if half == 1:
                div_xa_free[0] = mk("dve")
            st["d4"] = mk("dve")

        def d5():
            emit("sync", [st["d4"]] + pp_tp_fence[0], [(f"gc{qh}", 16)],
                 lambda e: e.dma_start(
                     out=ctxT[64:128, ch * S + qh * 1024: ch * S + qh * 1024 + 1024],
                     in_=ln_sb[0:64, qh * 1024: qh * 1024 + 1024]))
            ln_stage_free[qh] = mk(f"gc{qh}")

        seq = [d1, d2, lambda: d4(0), lambda: d4(1)]
        if odd:
            seq.append(d5)
        div_pending.extend(seq)

    def phase_B(item):
        for h in range(H):
            buf = h % 2
            if 1 <= h and h + 1 < H:
                queue_head_work(h + 1)
            nh = (h + 1) % H
            emit_patp(nh, [pp_freed[nh % 2]] + pp_tp_fence[0])
            if h % 4 == 1 and h // 4 + 1 <= 3:
                g = h // 4 + 1
                for tt in range(NT):
                    bgq.append(lambda g=g, tt=tt: emit_v_tile(g, tt))
            if h not in qkv_done:
                bgq_flush()
            assert h in qkv_done, f"QKV for head {h} not emitted"
            for qh in range(2):
                emit_sweep(item, h, qh)
                queue_division(item, h, qh)
            pp_freed[h % 2] = pp_reader_mark[h % 2]
            qt_freed[buf] = mk("pe")
            qs_freed[buf] = mk("pe")
            if h % 4 == 3:
                vh_free[(h // 4) % 2] = mk("pe")
        while div_pending:
            div_pending.pop(0)()
        bgq_flush()

    pp_tp_fence = [[]]

    def phase_C1(item, next_item):
        base = item * S
        div_done = mk("dve")
        div_gp = [mk("gc0"), mk("gc1"), mk("gdiv"), wo_done_c[0]]
        misc_gate[0] = div_done
        qkv_gate[0] = max(qkv_gate[0], div_done, key=lambda t: t[1])
        s2g[0] = s2g[1] = s_free["A"]
        s2g[2] = s2g[3] = s_free["B"]
        xf = [mk("dve"), mk("dve")]
        c1_xa_fence[0] = mk("gdv2")
        xbf = [[mk("gy0"), mk("gx0")], [mk("gy1"), mk("gx1")]]
        x2_marks = []
        st8 = a_begin(next_item) if next_item is not None else None
        for tt in range(NT):
            buf = tt % 2
            emit("scalar", [xf[buf], xa_free[buf], c1_xa_fence[0]], [(f"sxa{buf}", 16)],
                 lambda e, tt=tt, buf=buf, base=base: e.dma_start(
                     out=xa_sb[:, buf * 1024: buf * 1024 + 1024],
                     in_=x8[base + tt * 128: base + tt * 128 + 128, :]))
            sy_m = mk(f"sxa{buf}")
            for dh in range(2):
                b = (tt % 2) * 2 + dh
                for ch in range(NCH):
                    w2 = []
                    if ch == 0:
                        w2 = [s2g[b]] + div_gp
                    emit("tensor", w2, [("pe", 1)] if ch == NCH - 1 else [],
                         lambda e, ch=ch, tt=tt, dh=dh, b=b: e.matmul(
                             s2_ps[:, b * 512: b * 512 + 512],
                             ctxT[:, ch * S + tt * 128: ch * S + tt * 128 + 128],
                             wo_sb[:, ch * D + dh * 512: ch * D + dh * 512 + 512],
                             start=(ch == 0), stop=(ch == NCH - 1)))
                emit("vector", [mk("pe"), sy_m] + xbf[buf], [("dve", 1)],
                     lambda e, buf=buf, dh=dh, b=b: e.tensor_tensor(
                         xb_sb[:, buf * 1024 + dh * 512: buf * 1024 + dh * 512 + 512],
                         s2_ps[:, b * 512: b * 512 + 512],
                         xa_sb[:, buf * 1024 + dh * 512: buf * 1024 + dh * 512 + 512],
                         op=OP.add))
                s2g[b] = mk("dve")
            xf[buf] = mk("dve")
            xa_free[buf] = mk("dve")
            sc = 64 + 2 * tt
            emit("vector", [mk("dve")], [],
                 lambda e, buf=buf: e.bn_stats(
                     st_sb[:, 36:42], xb_sb[:, buf * 1024: buf * 1024 + 512]))
            emit("vector", [], [("dve", 1)],
                 lambda e, buf=buf: e.bn_stats(
                     st_sb[:, 42:48], xb_sb[:, buf * 1024 + 512: buf * 1024 + 1024]))
            emit("vector", [mk("dve")], [("dve", 1)],
                 lambda e, sc=sc: e.bn_aggr(st_sb[:, sc:sc + 2], st_sb[:, 36:48]))
            emit("gpsimd", [mk("dve")], [(f"gx{buf}", 16)],
                 lambda e, tt=tt, buf=buf: e.dma_start(
                     out=x2_d[tt * 128: tt * 128 + 128, :],
                     in_=xb_sb[:, buf * 1024: buf * 1024 + 1024]))
            xbf[buf] = [mk(f"gx{buf}")]
            x2_marks.append(mk(f"gx{buf}"))
            if st8 is not None:
                a_tile(st8, tt)
        if st8 is not None:
            a_end(st8)
        emit("scalar", [mk("dve")], [("act", 1)],
             lambda e: e.activation(st_sb[:, 96:112], st_sb[:, 65:97:2],
                                    AF.Ln, bias=eps_sb[:, 0:1], scale=1.0))
        emit("scalar", [mk("act")], [("act", 1)],
             lambda e: e.activation(st_sb[:, 112:128], st_sb[:, 96:112],
                                    AF.Exp, scale=-0.5))
        rstd2 = mk("act")
        emit("vector", [rstd2], [("dve", 1)],
             lambda e: e.scalar_tensor_tensor(
                 st_sb[:, 48:64], st_sb[:, 64:96:2], -1.0, st_sb[:, 112:128],
                 op0=OP.mult, op1=OP.mult))
        div_tt_gate[0] = mk("dve")
        return mk("dve"), x2_marks, st8, xf

    def phase_D(item, nm2_done, x2_marks, a_next, c1_xf, has_next):
        base = item * S
        load_table(id_exp)
        def sl(s):
            return (xb_sb, (s % 2) * 1024) if s < 2 else (xa_sb, (s % 2) * 1024)

        fr = [[nm2_done, x2_marks[NT - 2]],
              [nm2_done, x2_marks[NT - 1]],
              [c1_xf[0], xa_free[0], div_xa_free[0]],
              [c1_xf[1], xa_free[1], div_xa_free[0]]]
        sem_of = ["sxb0", "sxb1", "sxa0", "sxa1"]
        ld_m = [None] * NT

        def emit_ld(tt):
            s = tt % 4
            t, off = sl(s)
            emit("sync", fr[s] + [x2_marks[tt]], [(sem_of[s], 16)],
                 lambda e, tt=tt, t=t, off=off: e.dma_start(
                     out=t[:, off: off + 1024],
                     in_=x2_d[tt * 128: tt * 128 + 128, :]))
            ld_m[tt] = mk(sem_of[s])

        if a_next is not None:
            lnfree = [[a_next["tp"][NT - 2]], [a_next["tp"][NT - 1]]]
        else:
            lnfree = [[ln_stage_free[0]], [ln_stage_free[1]]]
        ltfree = [("pe", 0), ("pe", 0)]
        misc_gate[0] = mk("dve")
        qkv_gate[0] = mk("dve")
        tanh_done = [("act", 0)]
        tanh_war = [mk("gdiv")]
        yfree = [("gy0", 0), ("gy1", 0), ("gy0", 0), ("gy1", 0)]
        tp_marks = [None] * NT

        def emit_norm(tt):
            s, buf = tt % 4, tt % 2
            t_s, off_s = sl(s)
            emit("vector", [ld_m[tt], nm2_done] + lnfree[buf], [("dve", 1)],
                 lambda e, tt=tt, buf=buf, t_s=t_s, off_s=off_s: e.tensor_scalar(
                     ln_sb[:, buf * 1024: buf * 1024 + 1024],
                     t_s[:, off_s: off_s + 1024],
                     st_sb[:, 112 + tt:113 + tt], st_sb[:, 48 + tt:49 + tt],
                     op0=OP.mult, op1=OP.add))
            dve_m = mk("dve")
            src_ = ap3(ln_sb, buf * 1024, [[2048, 128], [128, 8], [1, 128]])
            dst_ = ap3(lt_sb, buf * 1024, [[2048, 128], [128, 8], [1, 128]])
            wt = [dve_m, ltfree[buf]]
            if tt == 0:
                wt.extend(d_tp_fence[0])
            emit("sync", wt, [(f"tpd{buf}", 16)],
                 lambda e, dst_=dst_, src_=src_: e.dma_start(out=dst_, in_=src_,
                                                             transpose=True))
            tp_marks[tt] = mk(f"tpd{buf}")
            lnfree[buf] = [tp_marks[tt]]

        emit_ld(0)
        emit_ld(1)
        emit_norm(0)
        for tt in range(NT):
            s = tt % 4
            buf = tt % 2
            t_s, off_s = sl(s)
            if tt + 2 < NT:
                emit_ld(tt + 2)
            if tt + 1 < NT:
                emit_norm(tt + 1)
            tp_m = tp_marks[tt]
            if tt == 2 and has_next:
                emit_qk_stream(0)
                qk0_early[0] = True
            for dh in range(2):
                b = (tt % 2) * 2 + dh
                for chm in range(NCH):
                    w3 = []
                    if chm == 0:
                        w3 = [tp_m, s2g[b]]
                    emit("tensor", w3, [("pe", 1)] if chm == NCH - 1 else [],
                         lambda e, chm=chm, buf=buf, dh=dh, b=b: e.matmul(
                             s2_ps[:, b * 512: b * 512 + 512],
                             lt_sb[:, buf * 1024 + chm * 128: buf * 1024 + chm * 128 + 128],
                             m_sb[:, chm * D + dh * 512: chm * D + dh * 512 + 512],
                             start=(chm == 0), stop=(chm == NCH - 1)))
                pe_m = mk("pe")
                emit("vector", [pe_m, tanh_done[0]], [("dve", 1)],
                     lambda e, dh=dh, b=b: e.tensor_tensor(
                         ctxu_sb[:, dh * 512: dh * 512 + 512],
                         s2_ps[:, b * 512: b * 512 + 512],
                         br_sb[:, dh * 512: dh * 512 + 512], op=OP.add))
                s2g[b] = mk("dve")
                emit("scalar", [mk("dve"), tanh_war[0]], [("act", 1)],
                     lambda e, dh=dh: e.activation(
                         aux_sb[:, dh * 512: dh * 512 + 512],
                         ctxu_sb[:, dh * 512: dh * 512 + 512],
                         AF.Tanh, scale=1.0))
                tanh_done[0] = mk("act")
                emit("vector", [mk("act"), yfree[s]], [("dve", 1)],
                     lambda e, t_s=t_s, off_s=off_s, dh=dh: e.tensor_tensor(
                         t_s[:, off_s + dh * 512: off_s + dh * 512 + 512],
                         t_s[:, off_s + dh * 512: off_s + dh * 512 + 512],
                         aux_sb[:, dh * 512: dh * 512 + 512], op=OP.add))
                tanh_war[0] = mk("dve")
            emit("gpsimd", [mk("dve")], [(f"gy{buf}", 16)],
                 lambda e, tt=tt, t_s=t_s, off_s=off_s, base=base: e.dma_start(
                     out=y8[base + tt * 128: base + tt * 128 + 128, :],
                     in_=t_s[:, off_s: off_s + 1024]))
            yfree[s] = mk(f"gy{buf}")
            fr[s] = [mk(f"gy{buf}")]
            ltfree[buf] = ("pe", cnt["pe"])
        load_table(id_nle)
        qkv_gate[0] = mk("dve")
        misc_gate[0] = mk("dve")
        s_free["A"] = max(s2g[0], s2g[1], key=lambda t: t[1])
        s_free["B"] = max(s2g[2], s2g[3], key=lambda t: t[1])
        xa_free[0] = fr[2][0]
        xa_free[1] = fr[3][0]
        div_xa_free[0] = fr[2][0]
        ln_stage_free[0] = mk("tpd0")
        ln_stage_free[1] = mk("tpd1")
        return

    d_tp_fence = [[]]
    qk0_early = [False]

    a_next_state = None
    for item in range(min(B_PER, DBG_ITEMS)):
        if DBG_STAGE == "prologue":
            break
        if a_next_state is None:
            a_next_state = phase_A(item)
        if DBG_DUMP and item == DBG_DUMP_ITEM:
            emit("gpsimd", [lnT_tp[NT - 2], lnT_tp[NT - 1]], [("gdbg", 16)],
                 lambda e: e.dma_start(out=dbg["d_ln1t"], in_=ln1T[:, :]))
        if DBG_STAGE == "A":
            a_next_state = None
            continue
        bgq_flush()
        if item == 0:
            wdup_done_c[0] = [mk("gw0"), mk("gw1")]
            pt_reader[0] = mk("gw1")
            pt_reader[1] = mk("gw1")
        if not qk0_early[0]:
            emit_qk_stream(0)
        qk0_early[0] = False
        for proj in range(2):
            for qb in range(4):
                emit_qkv_group(0, proj, qb)
        for tt in range(NT):
            emit_v_tile(0, tt)
        if DBG_DUMP and item == DBG_DUMP_ITEM:
            emit("gpsimd", [mk("dve")], [("gdbg", 16)],
                 lambda e: e.dma_start(out=dbg["d_qt"], in_=qt_sb[:, :]))
            emit("gpsimd", [], [("gdbg", 16)],
                 lambda e: e.dma_start(out=dbg["d_kt"], in_=kt_sb[:, :]))
            emit("gpsimd", [], [("gdbg", 16)],
                 lambda e: e.dma_start(out=dbg["d_vh"], in_=vh_sb[:, 0:8320]))
        queue_head_work(1)
        pp_tp_fence[0] = [mk("tpa0"), mk("tpa1"), mk("tpd0"), mk("tpd1"),
                          mk("tpw")]
        phase_B(item)
        if DBG_DUMP and item == DBG_DUMP_ITEM:
            emit("gpsimd", [mk("dve"), mk("gc0"), mk("gc1")], [("gdbg", 16)],
                 lambda e: e.dma_start(out=dbg["d_ctxt"], in_=ctxT[:, :]))
        if DBG_STAGE == "B":
            a_next_state = None
            continue
        nxt = item + 1 if item + 1 < min(B_PER, DBG_ITEMS) else None
        nm2_done, x2_marks, a_next_state, c1_xf = phase_C1(item, nxt)
        if DBG_STAGE == "C1":
            continue
        d_tp_fence[0] = [mk("gpp0"), mk("gpp1"), mk("gc0"), mk("gc1"),
                         mk("gdiv")]
        phase_D(item, nm2_done, x2_marks, a_next_state, c1_xf,
                has_next=nxt is not None)

    emit("gpsimd", [], [], lambda e: e.wait_ge(sems["gy0"], cnt["gy0"]))
    emit("gpsimd", [], [], lambda e: e.wait_ge(sems["gy1"], cnt["gy1"]))

    if DBG_PREFIX is not None:
        del OPS[DBG_PREFIX:]
    print(f"[kernel] total ops: {len(OPS)}")

    with nc.Block() as block:
        for eng_name in ["sync", "gpsimd", "tensor", "vector", "scalar"]:
            def make(eng_name):
                def run(eng):
                    for e_, fn in OPS:
                        if e_ == eng_name:
                            fn(eng)
                return run
            getattr(block, eng_name)(make(eng_name))

    return nc


_PROG = None


def kernel(L0_attn_hidden_states, L0_attn_Wq, L0_attn_Wk, L0_attn_Wv, L0_attn_Wo,
           L0_attn_rel_emb, L0_ffn_weight, L0_ffn_aux_weight1, L0_ffn_aux_weight2,
           L0_ffn_bias):
    global _PROG
    x = np.ascontiguousarray(np.asarray(L0_attn_hidden_states, dtype=np.float32))
    onehot, eye16 = _host_consts()
    shared = dict(
        wq=np.ascontiguousarray(np.asarray(L0_attn_Wq, np.float32)),
        wk=np.ascontiguousarray(np.asarray(L0_attn_Wk, np.float32)),
        wv=np.ascontiguousarray(np.asarray(L0_attn_Wv, np.float32)),
        wo=np.ascontiguousarray(np.asarray(L0_attn_Wo, np.float32)),
        wf=np.ascontiguousarray(np.asarray(L0_ffn_weight, np.float32)),
        wa1=np.ascontiguousarray(np.asarray(L0_ffn_aux_weight1, np.float32)),
        wa2=np.ascontiguousarray(np.asarray(L0_ffn_aux_weight2, np.float32)),
        fbias=np.ascontiguousarray(np.asarray(L0_ffn_bias, np.float32).reshape(1, D)),
        rel=np.ascontiguousarray(np.asarray(L0_attn_rel_emb, np.float32)),
        onehot=onehot, eye16=eye16,
    )
    if _PROG is None:
        _PROG = build_program()
    in_maps = []
    for c in range(N_CORES):
        m = dict(shared)
        m["x"] = np.ascontiguousarray(
            x[c * B_PER:(c + 1) * B_PER].reshape(B_PER * S, D))
        in_maps.append(m)
    res = run_bass_kernel_spmd(_PROG, in_maps, core_ids=list(range(N_CORES)))
    out = np.concatenate(
        [np.asarray(r["y"]).reshape(B_PER, S, D) for r in res.results], axis=0)
    return out.astype(np.float32)


# version 14
# speedup vs baseline: 1.4258x; 1.0587x over previous
import math
import numpy as np
import bass_rust
import concourse.bass as bass
import concourse.mybir as mybir
from concourse.bass_utils import run_bass_kernel_spmd

F32 = mybir.dt.float32
BF16 = mybir.dt.bfloat16
AF = mybir.ActivationFunctionType
OP = mybir.AluOpType

N_CORES = 8
B_PER = 4
S = 2048
D = 1024
H = 16
DK = 64
NT = S // 128
NCH = D // 128
NREV = 511
C_SHIFT = 12.0
EPS = 1e-5

NUM_BUCKETS = 32
MAX_DISTANCE = 128

DBG_ITEMS = B_PER
DBG_STAGE = "full"
DBG_PREFIX = None
DBG_DUMP = False
DBG_DUMP_ITEM = 0
DBG_NO_ROT = False
DBG_OLD_RECIP = False
DBG_ACT_FLUSH = False


def _rel_bucket_host(d):
    import jax
    import jax.numpy as jnp
    with jax.default_device(jax.devices("cpu")[0]):
        num_buckets = NUM_BUCKETS // 2
        rel = jnp.asarray(d, dtype=jnp.int32)
        relative_buckets = (rel > 0).astype(jnp.int32) * num_buckets
        rp = jnp.abs(rel)
        max_exact = num_buckets // 2
        is_small = rp < max_exact
        rp_large = max_exact + (
            jnp.log(rp.astype(jnp.float32) / max_exact)
            / math.log(MAX_DISTANCE / max_exact)
            * (num_buckets - max_exact)
        ).astype(jnp.int32)
        rp_large = jnp.minimum(rp_large, num_buckets - 1)
        relative_buckets = relative_buckets + jnp.where(is_small, rp, rp_large)
        return np.asarray(relative_buckets)


def _host_consts():
    d = np.arange(NREV, dtype=np.int64) - 255
    bk = _rel_bucket_host(d)
    onehot = np.zeros((NUM_BUCKETS, NREV), dtype=np.float32)
    onehot[bk, np.arange(NREV)] = 1.0
    eye16 = np.eye(16, dtype=np.float32)
    return onehot, eye16


def build_program():
    nc = bass.Bass("TRN2", target_bir_lowering=False)

    x8 = nc.dram_tensor("x", [B_PER * S, D], F32, kind="ExternalInput").ap()
    wq_d = nc.dram_tensor("wq", [D, D], F32, kind="ExternalInput").ap()
    wk_d = nc.dram_tensor("wk", [D, D], F32, kind="ExternalInput").ap()
    wv_d = nc.dram_tensor("wv", [D, D], F32, kind="ExternalInput").ap()
    wo_d = nc.dram_tensor("wo", [D, D], F32, kind="ExternalInput").ap()
    wf_d = nc.dram_tensor("wf", [D, D], F32, kind="ExternalInput").ap()
    wa1_d = nc.dram_tensor("wa1", [D, D], F32, kind="ExternalInput").ap()
    wa2_d = nc.dram_tensor("wa2", [D, D], F32, kind="ExternalInput").ap()
    fb_d = nc.dram_tensor("fbias", [1, D], F32, kind="ExternalInput").ap()
    rel_d = nc.dram_tensor("rel", [NUM_BUCKETS, H], F32, kind="ExternalInput").ap()
    oh_d = nc.dram_tensor("onehot", [NUM_BUCKETS, NREV], F32, kind="ExternalInput").ap()
    eye_d = nc.dram_tensor("eye16", [16, 16], F32, kind="ExternalInput").ap()
    y8 = nc.dram_tensor("y", [B_PER * S, D], F32, kind="ExternalOutput").ap()

    dbg = {}
    if DBG_DUMP:
        for nm, shp, dt in [("d_ln1t", [128, NCH * S], BF16),
                            ("d_qt", [128, 2 * S], BF16),
                            ("d_kt", [128, 2 * S], BF16),
                            ("d_vh", [128, 8320], BF16),
                            ("d_pt", [128, 4096], BF16),
                            ("d_ctxt", [128, NCH * S], BF16),
                            ("d_rev", [16, NREV], F32),
                            ("d_ca", [128, 32], F32),
                            ("d_pp", [128, 768], BF16)]:
            dbg[nm] = nc.dram_tensor(nm, shp, dt, kind="ExternalOutput").ap()

    qdup_d = nc.dram_tensor("qdup", [NCH, 128, H, 128], BF16, kind="Internal").ap()
    kdup_d = nc.dram_tensor("kdup", [NCH, 128, H, 128], BF16, kind="Internal").ap()
    x2_d = nc.dram_tensor("x2scr", [S, D], F32, kind="Internal").ap()
    rcp_d = nc.dram_tensor("rcpscr", [1, 1024], F32, kind="Internal").ap()

    A = nc.alloc_sbuf_tensor
    ln1T = A("ln1T", [128, NCH * S], BF16)
    ctxT = A("ctxT", [128, NCH * S], BF16)
    wv_sb = A("wv_sb", [128, NCH * D], BF16)
    wo_sb = A("wo_sb", [128, NCH * D], BF16)
    m_sb = A("m_sb", [128, NCH * D], BF16)
    qt_sb = A("qt_sb", [128, 2 * S], BF16)
    kt_sb = A("kt_sb", [128, 2 * S], BF16)
    qs_sb = A("qs_sb", [128, 2 * NCH * 128], BF16)
    ks_sb = A("ks_sb", [128, 2 * NCH * 128], BF16)
    vh_sb = A("vh_sb", [128, 2 * 4 * (NT * 65)], BF16)
    pt_sb = A("pt_sb", [128, 4 * 1024], BF16)
    pp_sb = A("pp_sb", [128, 2 * 384], BF16)
    xa_sb = A("xa_sb", [128, 2 * 1024], F32)
    xb_sb = A("xb_sb", [128, 2 * 1024], F32)
    ctxu_sb = A("ctxu_sb", [128, 1024], F32)
    aux_sb = A("aux_sb", [128, 1024], F32)
    ln_sb = A("ln_sb", [128, 2 * 1024], BF16)
    lt_sb = A("lt_sb", [128, 2 * 1024], BF16)
    br_sb = A("br_sb", [128, D], F32)
    ca_sb = A("ca_sb", [128, 32], F32)
    st_sb = A("st_sb", [128, 160], F32)
    rev_sb = A("rev_sb", [16, NREV], F32)
    oh_sb = A("oh_sb", [NUM_BUCKETS, NREV], F32)
    rel_sb = A("rel_sb", [NUM_BUCKETS, H], F32)
    eye_sb = A("eye_sb", [16, 16], F32)
    or_sb = A("or_sb", [1, 128], F32)
    oc_sb = A("oc_sb", [128, 64], F32)
    eps_sb = A("eps_sb", [128, 1], F32)

    P = nc.alloc_psum_tensor
    s2_ps = P("s2_ps", [128, 2048], F32)
    ctx_ps = P("ctx_ps", [128, 1024], F32)
    qkv_ps = P("qkv_ps", [128, 512], F32)
    msc_ps = P("msc_ps", [128, 512], F32)

    sem_names = ["ssm", "sxa0", "sxa1", "sxb0", "sxb1",
                 "tpa0", "tpa1", "tpd0", "tpd1", "tpw",
                 "gq0", "gq1", "gw0", "gw1", "gpp0", "gpp1",
                 "gdiv", "gc0", "gc1", "gx0", "gx1", "gy0", "gy1", "gdbg",
                 "gld0", "gld1", "gld2", "gld3", "gld4", "gdv2",
                 "pe", "act", "dve"]
    sems = {n: nc.alloc_semaphore(n) for n in sem_names}
    cnt = {n: 0 for n in sem_names}
    OPS = []

    def emit(engine, waits, incs, body):
        wl = []
        for s, v in waits:
            assert v <= cnt[s], f"wait on future: {s} {v} > {cnt[s]}"
            if v > 0:
                wl.append((sems[s], v))

        def fn(eng, wl=wl, incs=incs, body=body):
            for sh, v in wl:
                eng.wait_ge(sh, v)
            inst = body(eng)
            for s, k in incs:
                inst = inst.then_inc(sems[s], k)
        OPS.append((engine, fn))
        for s, k in incs:
            cnt[s] += k

    def mk(s):
        return (s, cnt[s])

    def ap3(t, off, dims):
        return bass_rust.AP(t[:].tensor, off, dims)

    from concourse.hw_specs import get_activation_tables
    tabs = list(get_activation_tables(nc.m.arch).items())
    id_nle = next(i for i, (n, f) in enumerate(tabs) if AF.Exp in f and AF.Ln in f)
    id_exp = next(i for i, (n, f) in enumerate(tabs) if AF.Exp in f and AF.Tanh in f)

    def load_table(idx):
        emit("scalar", [], [], lambda e, idx=idx: e.add_instruction(
            mybir.InstLoadActFuncSet(name=nc.get_next_instruction_name(),
                                     act_func_set_id=idx, ins=[], outs=[])))

    emit("vector", [], [], lambda e: e.memset(or_sb[0:1, :], 1.0))
    emit("vector", [], [], lambda e: e.memset(oc_sb[:, :], 1.0))
    for g2 in range(2):
        for hl in range(4):
            off = g2 * 4160 + hl * 1040 + 64
            apm = ap3(vh_sb, off, [[8320, 128], [65, NT]])
            emit("vector", [], [], lambda e, apm=apm: e.memset(apm, 1.0))
    emit("vector", [], [], lambda e: e.memset(eps_sb[:, :], EPS))
    emit("vector", [], [("dve", 1)], lambda e: e.memset(st_sb[:, :], 0.0))
    ones_done = mk("dve")

    load_table(id_nle)

    emit("sync", [], [("ssm", 16)], lambda e: e.dma_start(out=oh_sb[:, :], in_=oh_d))
    emit("sync", [], [("ssm", 16)], lambda e: e.dma_start(out=rel_sb[:, :], in_=rel_d))
    emit("sync", [], [("ssm", 16)], lambda e: e.dma_start(out=eye_sb[:, :], in_=eye_d))
    emit("sync", [], [("ssm", 16)], lambda e: e.dma_start(out=ctxu_sb[0:1, 0:D], in_=fb_d))
    small_ld = mk("ssm")

    emit("tensor", [small_ld], [("pe", 1)],
         lambda e: e.matmul(msc_ps[0:16, 0:NREV], rel_sb[0:32, 0:16],
                            oh_sb[0:32, 0:NREV], start=True, stop=True))
    emit("scalar", [mk("pe")], [("act", 1)],
         lambda e: e.activation(rev_sb[0:16, :], msc_ps[0:16, 0:NREV], AF.Copy,
                                bias=-C_SHIFT, scale=1.0))
    rev_done = mk("act")

    pp_ready = [None, None]
    pp_freed = [("dve", 0), ("dve", 0)]

    def emit_patp(h, waits):
        pbuf = h % 2
        sem = f"gpp{pbuf}"
        src = ap3(rev_sb, h * NREV, [[NREV, 1], [1, 128], [1, 384]])
        dst = pp_sb[:, pbuf * 384:(pbuf + 1) * 384]
        emit("gpsimd", waits, [(sem, 16)],
             lambda e, dst=dst, src=src: e.dma_start(out=dst, in_=src))
        pp_ready[pbuf] = mk(sem)

    emit_patp(0, [rev_done])
    emit_patp(1, [rev_done])
    patp_prologue = [mk("gpp0"), mk("gpp1")]

    emit("tensor", [rev_done], [],
         lambda e: e.matmul(msc_ps[0:1, 0:16], rev_sb[0:16, 0:1],
                            eye_sb[0:16, 0:16], start=True, stop=True))
    emit("tensor", [], [("pe", 1)],
         lambda e: e.matmul(msc_ps[0:1, 16:32], rev_sb[0:16, 510:511],
                            eye_sb[0:16, 0:16], start=True, stop=True))
    emit("vector", [mk("pe"), ones_done], [("dve", 1)],
         lambda e: e.tensor_copy(st_sb[0:1, 0:32], msc_ps[0:1, 0:32]))
    emit("tensor", [mk("dve")], [("pe", 1)],
         lambda e: e.matmul(msc_ps[0:128, 0:32], or_sb[0:1, 0:128],
                            st_sb[0:1, 0:32], start=True, stop=True))
    emit("vector", [mk("pe")], [("dve", 1)],
         lambda e: e.tensor_copy(ca_sb[:, :], msc_ps[:, 0:32]))

    for half in range(2):
        emit("tensor", [small_ld, ones_done, mk("dve")], [("pe", 1)],
             lambda e, half=half: e.matmul(
                 qkv_ps[:, 0:512], or_sb[0:1, 0:128],
                 ctxu_sb[0:1, half * 512:(half + 1) * 512], start=True, stop=True))
        emit("vector", [mk("pe")], [("dve", 1)],
             lambda e, half=half: e.tensor_copy(
                 br_sb[:, half * 512:(half + 1) * 512], qkv_ps[:, 0:512]))
    pro1_done = mk("dve")

    wbf, abf = ctxT, ln1T
    xa_free = [pro1_done, pro1_done]

    _gld_i = [0]

    def cast_load(src_d, dst_t, dst_off, waits):
        sem = f"gld{_gld_i[0]}"
        _gld_i[0] += 1
        pitch = dst_t[:].ap[0][0]
        s_ap = bass_rust.AP(src_d.tensor, 0,
                            [[1024, 128], [128 * 1024, NCH], [1, 1024]])
        d_ap = bass_rust.AP(dst_t[:].tensor, dst_off,
                            [[pitch, 128], [1024, NCH], [1, 1024]])
        emit("gpsimd", waits, [(sem, 16)],
             lambda e, d_ap=d_ap, s_ap=s_ap: e.dma_start(out=d_ap, in_=s_ap))
        return mk(sem)

    w_cast = cast_load(wf_d, wbf, 0, [pro1_done])
    for i in range(NCH):
        src = ap3(wbf, i * 1024, [[NCH * S, 128], [128, 8], [1, 128]])
        dst = ap3(wbf, 8192 + i * 128, [[NCH * S, 128], [1024, 8], [1, 128]])
        w = [w_cast]
        if i == 0:
            w.extend(patp_prologue)
        emit("sync", w, [("tpw", 16)],
             lambda e, dst=dst, src=src: e.dma_start(out=dst, in_=src, transpose=True))
    wt_done = mk("tpw")
    a1_done = cast_load(wa1_d, abf, 0, [pro1_done])

    def mm_chain(lhs_fn, rhs_fn, ps, dst_t, dst_fn, extra_first_waits, gate0):
        gate = gate0
        for ot in range(NCH):
            for ob in range(2):
                for ch in range(NCH):
                    w = []
                    if ch == 0:
                        w = list(extra_first_waits) + [gate]
                    emit("tensor", w, [("pe", 1)] if ch == NCH - 1 else [],
                         lambda e, ot=ot, ob=ob, ch=ch, ps=ps: e.matmul(
                             ps[:, 0:512], lhs_fn(ch, ot), rhs_fn(ch, ob),
                             start=(ch == 0), stop=(ch == NCH - 1)))
                emit("vector", [mk("pe")], [("dve", 1)],
                     lambda e, ot=ot, ob=ob, dst_t=dst_t: e.tensor_copy(
                         dst_fn(dst_t, ot, ob), qkv_ps[:, 0:512] if ps is qkv_ps
                         else msc_ps[:, 0:512]))
                gate = mk("dve")
        return mk("dve")

    m1t_done = mm_chain(
        lambda ch, jt: abf[:, ch * 1024 + jt * 128: ch * 1024 + jt * 128 + 128],
        lambda ch, cb: wbf[:, 8192 + ch * 1024 + cb * 512: 8192 + ch * 1024 + cb * 512 + 512],
        msc_ps, abf,
        lambda t, jt, cb: t[:, 8192 + jt * 1024 + cb * 512: 8192 + jt * 1024 + cb * 512 + 512],
        [a1_done, wt_done], a1_done)
    a2_done = cast_load(wa2_d, abf, 0, [m1t_done])
    m_done = mm_chain(
        lambda ch, ct: abf[:, 8192 + ch * 1024 + ct * 128: 8192 + ch * 1024 + ct * 128 + 128],
        lambda ch, jb: abf[:, ch * 1024 + jb * 512: ch * 1024 + jb * 512 + 512],
        qkv_ps, m_sb,
        lambda t, ct, jb: t[:, ct * 1024 + jb * 512: ct * 1024 + jb * 512 + 512],
        [a2_done], a2_done)


    if DBG_DUMP:
        emit("gpsimd", [mk("dve"), mk("act")], [("gdbg", 16)],
             lambda e: e.dma_start(out=dbg["d_rev"], in_=rev_sb[:, :]))
        emit("gpsimd", [], [("gdbg", 16)],
             lambda e: e.dma_start(out=dbg["d_ca"], in_=ca_sb[:, :]))
        emit("gpsimd", [], [("gdbg", 16)],
             lambda e: e.dma_start(out=dbg["d_pp"], in_=pp_sb[:, :]))

    s_free = {"A": ("act", 0), "B": ("act", 0)}
    pt_reader = [("pe", 0)] * 4
    pp_reader_mark = [("dve", 0), ("dve", 0)]
    s2g = [None] * 4
    kt_slot = [0]
    qs_ready = [None, None]
    qs_freed = [("pe", 0), ("pe", 0)]
    qt_freed = [("pe", 0), ("pe", 0)]
    qkv_done = {}
    vh_ready = [("dve", 0), ("dve", 0)]
    vh_free = [("pe", 0), ("pe", 0)]
    qkv_gate = [m_done]
    lnT_tp = [("tpa0", 0)] * NT
    ctx_gate = [("dve", 0)]
    ctx_done = [("pe", 0)]
    div_tt_gate = [("dve", 0)]
    c1_xa_fence = [("gdv2", 0)]
    misc_gate = [m1t_done]
    acc_i = [0]
    div_r2_free = [("gdiv", 0)]
    div_rb_free = [("gdv2", 0)]
    div_xa_free = [("dve", 0)]
    div_pending = []
    bgq = []


    def bgq_pop(n):
        for _ in range(min(n, len(bgq))):
            bgq.pop(0)()

    def bgq_flush():
        while bgq:
            bgq.pop(0)()

    stage_free = {0: m_done, 1: m_done}
    wv_done_c = [None]
    wo_done_c = [None]
    wdup_done_c = [None]

    def q_cast_chunk(src_d, dst_t, dst_off, ch, done_cell):
        def go():
            buf = ch % 2
            emit("sync", [xa_free[buf]], [(f"sxa{buf}", 16)],
                 lambda e: e.dma_start(
                     out=xa_sb[:, buf * 1024:(buf + 1) * 1024],
                     in_=src_d[ch * 128:(ch + 1) * 128, :]))
            emit("vector", [mk(f"sxa{buf}")], [("dve", 1)],
                 lambda e: e.tensor_copy(
                     dst_t[:, dst_off + ch * 1024: dst_off + (ch + 1) * 1024],
                     xa_sb[:, buf * 1024:(buf + 1) * 1024]))
            xa_free[buf] = mk("dve")
            if done_cell is not None and ch == NCH - 1:
                done_cell[0] = mk("dve")
        return go

    def q_dup_chunk(src_d, dst_d, ch, done_cell):
        def go():
            buf = ch % 2
            st_t = lt_sb if buf == 0 else pt_sb
            emit("sync", [xa_free[buf]], [(f"sxa{buf}", 16)],
                 lambda e: e.dma_start(
                     out=xa_sb[:, buf * 1024: buf * 1024 + 1024],
                     in_=src_d[ch * 128:(ch + 1) * 128, :]))
            sy_m = mk(f"sxa{buf}")
            pitch = NCH * 512 if st_t is pt_sb else 2048
            for rep in range(2):
                src_ap = ap3(xa_sb, buf * 1024, [[2048, 128], [64, 16], [1, 64]])
                dst_ap = ap3(st_t, rep * 64, [[pitch, 128], [128, 16], [1, 64]])
                emit("vector", [sy_m, stage_free[buf]], [("dve", 1)],
                     lambda e, dst_ap=dst_ap, src_ap=src_ap:
                     e.tensor_copy(dst_ap, src_ap))
            xa_free[buf] = mk("dve")
            emit("gpsimd", [mk("dve")], [(f"gw{buf}", 16)],
                 lambda e: e.dma_start(
                     out=dst_d[ch],
                     in_=st_t[:, 0:2048].rearrange("p (h c) -> p h c", h=16)))
            stage_free[buf] = mk(f"gw{buf}")
            if done_cell is not None and ch == NCH - 1:
                done_cell[0] = None
        return go

    for _ch in range(NCH):
        bgq.append(q_dup_chunk(wq_d, qdup_d, _ch, None))
    for _ch in range(NCH):
        bgq.append(q_dup_chunk(wk_d, kdup_d, _ch, wdup_done_c))
    def _wv_item():
        wv_done_c[0] = cast_load(wv_d, wv_sb, 0, [])
    def _wo_item():
        wo_done_c[0] = cast_load(wo_d, wo_sb, 0, [])
    bgq.append(_wv_item)
    bgq.append(_wo_item)

    def emit_qk_stream(h):
        buf = h % 2
        for (src_d, dst_t) in ((qdup_d, qs_sb), (kdup_d, ks_sb)):
            src = bass_rust.AP(src_d.tensor, h * 128,
                               [[H * 128, 128], [128 * H * 128, NCH], [1, 128]])
            dst = ap3(dst_t, buf * 1024, [[2048, 128], [128, 8], [1, 128]])
            emit("gpsimd", [qs_freed[buf]] + wdup_done_c[0], [(f"gq{buf}", 16)],
                 lambda e, dst=dst, src=src: e.dma_start(out=dst, in_=src))
        qs_ready[buf] = mk(f"gq{buf}")

    def acc_pick():
        i = (acc_i[0] % 2) if not DBG_NO_ROT else 0
        acc_i[0] += 1
        if i == 0:
            return qkv_ps, (lambda: qkv_gate[0]), qkv_gate
        return msc_ps, (lambda: misc_gate[0]), misc_gate

    def emit_qkv_group(h, proj, qb):
        buf = h % 2
        w_src = qs_sb if proj == 0 else ks_sb
        t_dst = qt_sb if proj == 0 else kt_sb
        ps, gate_get, gate_cell = acc_pick()
        for ch in range(NCH):
            w = []
            if ch == 0:
                w = [qs_ready[buf], lnT_tp[qb * 4 + 2],
                     lnT_tp[qb * 4 + 3], gate_get()]
            emit("tensor", w, [("pe", 1)] if ch == NCH - 1 else [],
                 lambda e, ch=ch, qb=qb, buf=buf, w_src=w_src, ps=ps:
                 e.matmul(ps[:, 0:512],
                          w_src[:, buf * 1024 + ch * 128: buf * 1024 + ch * 128 + 128],
                          ln1T[:, ch * S + qb * 512: ch * S + qb * 512 + 512],
                          start=(ch == 0), stop=(ch == NCH - 1)))
        emit("vector", [mk("pe")], [("dve", 1)],
             lambda e, qb=qb, buf=buf, t_dst=t_dst, ps=ps:
             e.tensor_copy(t_dst[:, buf * S + qb * 512: buf * S + qb * 512 + 512],
                           ps[:, 0:512]))
        gate_cell[0] = mk("dve")
        if proj == 1 and qb == 3:
            qkv_done[h] = mk("dve")

    def emit_v_tile(g, tt):
        gb = g % 2
        ps, gate_get, gate_cell = acc_pick()
        for ch in range(NCH):
            w = []
            if ch == 0:
                w = [lnT_tp[tt], gate_get(), wv_done_c[0]]
            emit("tensor", w, [("pe", 1)] if ch == NCH - 1 else [],
                 lambda e, ch=ch, tt=tt, g=g, ps=ps:
                 e.matmul(ps[:, 0:256],
                          ln1T[:, ch * S + tt * 128: ch * S + tt * 128 + 128],
                          wv_sb[:, ch * D + g * 256: ch * D + g * 256 + 256],
                          start=(ch == 0), stop=(ch == NCH - 1)))
        src = ap3(ps, 0, [[512, 128], [64, 4], [1, 64]])
        dst = ap3(vh_sb, gb * 4160 + tt * 65, [[8320, 128], [1040, 4], [1, 64]])
        emit("vector", [mk("pe"), vh_free[gb]], [("dve", 1)],
             lambda e, dst=dst, src=src: e.tensor_copy(dst, src))
        gate_cell[0] = mk("dve")
        if tt == NT - 1:
            vh_ready[gb] = mk("dve")

    def queue_head_work(h, with_stream=False):
        if with_stream:
            bgq.append(lambda h=h: emit_qk_stream(h))
        for proj in range(2):
            for qb in range(4):
                bgq.append(lambda h=h, p=proj, q=qb: emit_qkv_group(h, p, q))

    def a_begin(item):
        a_tp_fence[0] = [mk("gpp0"), mk("gpp1"), mk("gc0"), mk("gc1"),
                         mk("gdiv")]
        a_xa_fence[0] = mk("gdv2")
        st8 = dict(item=item, base=item * S,
                   rstd=[("act", 0)] * NT, norm=[("dve", 0)] * NT,
                   tp=[("tpa0", 0)] * NT,
                   ln_fence=[mk("tpa0"), mk("tpa1"), mk("tpd0"), mk("tpd1"),
                             ln_stage_free[0], ln_stage_free[1]],
                   pe_fence=mk("pe"))
        return st8

    def a_tile(st8, tt):
        base = st8["base"]
        buf = tt % 2
        sc = 18 * buf
        w = [xa_free[buf]]
        if tt >= 2:
            w.append(st8["norm"][tt - 2])
        else:
            w.append(a_xa_fence[0])
        emit("sync", w, [(f"sxa{buf}", 16)],
             lambda e, tt=tt, buf=buf, base=base: e.dma_start(
                 out=xa_sb[:, buf * 1024: buf * 1024 + 1024],
                 in_=x8[base + tt * 128: base + tt * 128 + 128, :]))
        sy_m = mk(f"sxa{buf}")
        wv_ = [sy_m]
        if tt >= 2:
            wv_.append(st8["rstd"][tt - 2])
        emit("vector", wv_, [],
             lambda e, buf=buf, sc=sc: e.bn_stats(
                 st_sb[:, sc:sc + 6], xa_sb[:, buf * 1024: buf * 1024 + 512]))
        emit("vector", [], [("dve", 1)],
             lambda e, buf=buf, sc=sc: e.bn_stats(
                 st_sb[:, sc + 6:sc + 12],
                 xa_sb[:, buf * 1024 + 512: buf * 1024 + 1024]))
        emit("vector", [mk("dve")], [("dve", 1)],
             lambda e, sc=sc: e.bn_aggr(st_sb[:, sc + 12:sc + 14],
                                        st_sb[:, sc:sc + 12]))
        emit("scalar", [mk("dve")], [("act", 1)],
             lambda e, sc=sc: e.activation(st_sb[:, sc + 14:sc + 15],
                                           st_sb[:, sc + 13:sc + 14],
                                           AF.Ln, bias=eps_sb[:, 0:1], scale=1.0))
        emit("scalar", [mk("act")], [("act", 1)],
             lambda e, sc=sc: e.activation(st_sb[:, sc + 15:sc + 16],
                                           st_sb[:, sc + 14:sc + 15],
                                           AF.Exp, scale=-0.5))
        st8["rstd"][tt] = mk("act")
        emit("vector", [st8["rstd"][tt], mk("dve")], [("dve", 1)],
             lambda e, sc=sc: e.tensor_scalar(
                 st_sb[:, sc + 16:sc + 17], st_sb[:, sc + 12:sc + 13],
                 st_sb[:, sc + 15:sc + 16], -1.0, op0=OP.mult, op1=OP.mult))
        wv2 = [st8["tp"][tt - 2], mk("dve")] if tt >= 2 else \
            list(st8["ln_fence"]) + [mk("dve")]
        emit("vector", wv2, [("dve", 1)],
             lambda e, buf=buf, sc=sc: e.tensor_scalar(
                 ln_sb[:, buf * 1024: buf * 1024 + 1024],
                 xa_sb[:, buf * 1024: buf * 1024 + 1024],
                 st_sb[:, sc + 15:sc + 16], st_sb[:, sc + 16:sc + 17],
                 op0=OP.mult, op1=OP.add))
        st8["norm"][tt] = mk("dve")
        xa_free[buf] = st8["norm"][tt]
        src_ = ap3(ln_sb, buf * 1024, [[2048, 128], [128, 8], [1, 128]])
        dst_ = ap3(ln1T, tt * 128, [[NCH * S, 128], [S, 8], [1, 128]])
        wt = [st8["norm"][tt]] + a_tp_fence[0]
        if tt == 0:
            wt.append(st8["pe_fence"])
        emit("sync", wt, [(f"tpa{buf}", 16)],
             lambda e, dst_=dst_, src_=src_: e.dma_start(out=dst_, in_=src_,
                                                         transpose=True))
        st8["tp"][tt] = mk(f"tpa{buf}")
        lnT_tp[tt] = st8["tp"][tt]

    def a_end(st8):
        ln_stage_free[0] = st8["tp"][NT - 1]
        ln_stage_free[1] = st8["tp"][NT - 1]

    def phase_A(item):
        st8 = a_begin(item)
        for tt in range(NT):
            a_tile(st8, tt)
            bgq_pop(4)
        a_end(st8)
        return st8

    a_tp_fence = [[]]
    a_xa_fence = [("gdv2", 0)]
    ln_stage_free = [("gc0", 0), ("gc1", 0)]

    def band_range(kt, qh):
        lo = max(0, 128 * (kt - 1), 1024 * qh)
        hi = min(S, 128 * (kt + 2), 1024 * (qh + 1))
        return lo, hi

    def emit_sweep(item, h, qh):
        buf = h % 2
        g = h // 4
        gb = g % 2
        pbase = buf * S + qh * 1024
        exp_marks = {}
        s_pe = {}
        band_dve = {}

        def s_mms(ktp):
            ktA, ktB = 2 * ktp, 2 * ktp + 1
            for (kt, p0, coff, tp) in ((ktA, 0, 0, (0, 0)), (ktB, 64, 1024, (64, 0))):
                key = "A" if coff == 0 else "B"
                for half in range(2):
                    w = []
                    if half == 0:
                        w = [s_free[key], ("dve", qkv_done[h][1])]
                    emit("tensor", w, [("pe", 1)] if half == 1 else [],
                         lambda e, kt=kt, p0=p0, coff=coff, tp=tp, half=half,
                         buf=buf, pbase=pbase:
                         e.matmul(
                             s2_ps[:, coff + half * 512: coff + half * 512 + 512],
                             kt_sb[p0:p0 + 64, buf * S + kt * 128: buf * S + kt * 128 + 128],
                             qt_sb[p0:p0 + 64, pbase + half * 512: pbase + half * 512 + 512],
                             start=True, stop=True, tile_position=tp))
                s_pe[kt] = mk("pe")

        def band_add(kt, coff):
            lo, hi = band_range(kt, qh)
            if hi <= lo:
                return
            llo, lhi = lo - 1024 * qh, hi - 1024 * qh
            r0 = 128 * kt + 255 - lo
            ln_ = hi - lo
            ppo = (h % 2) * 384
            abs_start = ppo + r0
            abs_stop = ppo + r0 - ln_
            if abs_stop < 0:
                in1 = pp_sb[:, abs_start::-1]
            else:
                in1 = pp_sb[:, abs_start: abs_stop: -1]
            emit("vector", [s_pe[kt], pp_ready[h % 2]], [("dve", 1)],
                 lambda e, coff=coff, llo=llo, lhi=lhi, in1=in1:
                 e.tensor_tensor(s2_ps[:, coff + llo: coff + lhi],
                                 s2_ps[:, coff + llo: coff + lhi], in1, op=OP.add))
            band_dve[kt] = mk("dve")
            pp_reader_mark[h % 2] = band_dve[kt]

        def exps(kt, coff, slot):
            lo, hi = band_range(kt, qh)
            segs = []
            if hi <= lo:
                segs = [(0, 1024, "lo" if 128 * (kt + 2) <= 1024 * qh else "hi")]
            else:
                llo, lhi = lo - 1024 * qh, hi - 1024 * qh
                if llo > 0:
                    segs.append((0, llo, "hi"))
                segs.append((llo, lhi, "band"))
                if lhi < 1024:
                    segs.append((lhi, 1024, "lo"))
            w = [s_pe[kt], pt_reader[slot]]
            if kt in band_dve:
                w.append(band_dve[kt])
            nseg = len(segs) + (1 if DBG_ACT_FLUSH else 0)
            for i, (a, b_, kind) in enumerate(segs):
                inc = [("act", 1)] if (i == len(segs) - 1 and not DBG_ACT_FLUSH) else []
                if kind == "band":
                    bias = None
                elif kind == "hi":
                    bias = ca_sb[:, 16 + h:16 + h + 1]
                else:
                    bias = ca_sb[:, h:h + 1]
                if bias is None:
                    emit("scalar", w if i == 0 else [], inc,
                         lambda e, a=a, b_=b_, coff=coff, slot=slot:
                         e.activation(pt_sb[:, slot * 1024 + a: slot * 1024 + b_],
                                      s2_ps[:, coff + a: coff + b_],
                                      AF.Exp, scale=1.0))
                else:
                    emit("scalar", w if i == 0 else [], inc,
                         lambda e, a=a, b_=b_, coff=coff, slot=slot, bias=bias:
                         e.activation(pt_sb[:, slot * 1024 + a: slot * 1024 + b_],
                                      s2_ps[:, coff + a: coff + b_],
                                      AF.Exp, bias=bias, scale=1.0))
            if DBG_ACT_FLUSH:
                emit("scalar", [], [("act", 1)],
                     lambda e, slot=slot: e.activation(
                         st_sb[0:1, 158:159],
                         pt_sb[0:1, slot * 1024: slot * 1024 + 1],
                         AF.Identity, scale=1.0))
            exp_marks[kt] = mk("act")
            key = "A" if coff == 0 else "B"
            s_free[key] = mk("act")

        def ctx_mms(kt, slot):
            vo = gb * 4160 + (h % 4) * 1040 + kt * 65
            for half in range(2):
                w = []
                if half == 0:
                    w = [exp_marks[kt], vh_ready[gb]]
                    if kt == 0:
                        w.append(ctx_gate[0])
                emit("tensor", w, [("pe", 1)] if half == 1 else [],
                     lambda e, kt=kt, slot=slot, half=half, vo=vo:
                     e.matmul(ctx_ps[0:65, half * 512: half * 512 + 512],
                              vh_sb[:, vo: vo + 65],
                              pt_sb[:, slot * 1024 + half * 512: slot * 1024 + half * 512 + 512],
                              start=(kt == 0), stop=(kt == NT - 1)))
            pt_reader[slot] = mk("pe")
            if kt == NT - 1:
                ctx_done[0] = mk("pe")

        prev = None
        for ktp in range(8):
            ktA, ktB = 2 * ktp, 2 * ktp + 1
            slA = kt_slot[0] % 4
            kt_slot[0] += 1
            slB = kt_slot[0] % 4
            kt_slot[0] += 1
            s_mms(ktp)
            band_add(ktA, 0)
            band_add(ktB, 1024)
            exps(ktA, 0, slA)
            exps(ktB, 1024, slB)
            if DBG_DUMP and item == DBG_DUMP_ITEM and h == 0 and qh == 0 and ktp == 0:
                emit("gpsimd", [mk("act")], [("gdbg", 16)],
                     lambda e, slA=slA, slB=slB: e.dma_start(
                         out=dbg["d_pt"], in_=pt_sb[:, :]))
                pt_reader[slA] = mk("gdbg")
                pt_reader[slB] = mk("gdbg")
            if prev is not None:
                ctx_mms(prev[0], prev[2])
                ctx_mms(prev[1], prev[3])
            if div_pending:
                div_pending.pop(0)()
            bgq_pop(2 if (ktp & 1) else 1)
            prev = (ktA, ktB, slA, slB)
        ctx_mms(prev[0], prev[2])
        ctx_mms(prev[1], prev[3])

    def queue_division(item, h, qh):
        ch = h // 2
        odd = h % 2 == 1
        pe_ctx = ctx_done[0]
        st = {}

        def d1():
            emit("vector", [pe_ctx, div_tt_gate[0]], [("dve", 1)],
                 lambda e: e.tensor_copy(ctxu_sb[0:65, :], ctx_ps[0:65, :]))
            ctx_gate[0] = mk("dve")
            st["d1"] = mk("dve")

        def d2():
            scat_src = ap3(ctxu_sb, 64 * 1024, [[1024, 1], [16, 64], [1, 16]])
            emit("sync", [st["d1"], div_r2_free[0]] + pp_tp_fence[0],
                 [("gdiv", 16)],
                 lambda e: e.dma_start(out=st_sb[64:128, 128:144], in_=scat_src))
            emit("vector", [mk("gdiv")], [("dve", 1)],
                 lambda e: e.reciprocal(st_sb[64:128, 128:144],
                                        st_sb[64:128, 128:144]))
            gat_src = ap3(st_sb, 64 * 160 + 128, [[160, 64], [1, 16]])
            emit("sync", [mk("dve"), div_rb_free[0]], [("gdiv", 16)],
                 lambda e: e.dma_start(out=rcp_d, in_=gat_src))
            bc_src = bass_rust.AP(rcp_d.tensor, 0, [[0, 64], [1, 1024]])
            emit("sync", [mk("gdiv"), div_xa_free[0]], [("gdv2", 16)],
                 lambda e: e.dma_start(out=xa_sb[0:64, 0:1024], in_=bc_src))
            st["d2"] = mk("gdv2")
            div_r2_free[0] = mk("gdiv")
            div_rb_free[0] = st["d2"]

        def d4(half):
            if odd:
                out = ln_sb[0:64, qh * 1024 + half * 512: qh * 1024 + half * 512 + 512]
            else:
                out = ctxT[0:64, ch * S + qh * 1024 + half * 512:
                           ch * S + qh * 1024 + half * 512 + 512]
            w = [st["d2"]]
            if half == 0:
                w.append(div_tt_gate[0])
            if odd:
                w.append(ln_stage_free[qh])
            emit("vector", w, [("dve", 1)],
                 lambda e, out=out, half=half: e.tensor_tensor(
                     out, ctxu_sb[0:64, half * 512: half * 512 + 512],
                     xa_sb[0:64, half * 512: half * 512 + 512], op=OP.mult))
            div_tt_gate[0] = mk("dve")
            if half == 1:
                div_xa_free[0] = mk("dve")
            st["d4"] = mk("dve")

        def d5():
            emit("sync", [st["d4"]] + pp_tp_fence[0], [(f"gc{qh}", 16)],
                 lambda e: e.dma_start(
                     out=ctxT[64:128, ch * S + qh * 1024: ch * S + qh * 1024 + 1024],
                     in_=ln_sb[0:64, qh * 1024: qh * 1024 + 1024]))
            ln_stage_free[qh] = mk(f"gc{qh}")

        seq = [d1, d2, lambda: d4(0), lambda: d4(1)]
        if odd:
            seq.append(d5)
        div_pending.extend(seq)

    def phase_B(item):
        for h in range(H):
            buf = h % 2
            if 1 <= h and h + 1 < H:
                queue_head_work(h + 1)
            nh = (h + 1) % H
            emit_patp(nh, [pp_freed[nh % 2]] + pp_tp_fence[0])
            if h % 4 == 1 and h // 4 + 1 <= 3:
                g = h // 4 + 1
                for tt in range(NT):
                    bgq.append(lambda g=g, tt=tt: emit_v_tile(g, tt))
            if h not in qkv_done:
                bgq_flush()
            assert h in qkv_done, f"QKV for head {h} not emitted"
            qs_freed[buf] = qkv_done[h]
            if h + 2 < H:
                emit_qk_stream(h + 2)
            for qh in range(2):
                emit_sweep(item, h, qh)
                queue_division(item, h, qh)
            pp_freed[h % 2] = pp_reader_mark[h % 2]
            qt_freed[buf] = mk("pe")
            if h % 4 == 3:
                vh_free[(h // 4) % 2] = mk("pe")
        while div_pending:
            div_pending.pop(0)()
        bgq_flush()

    pp_tp_fence = [[]]

    def phase_C1(item, next_item):
        base = item * S
        div_done = mk("dve")
        div_gp = [mk("gc0"), mk("gc1"), mk("gdiv"), wo_done_c[0]]
        misc_gate[0] = div_done
        qkv_gate[0] = max(qkv_gate[0], div_done, key=lambda t: t[1])
        s2g[0] = s2g[1] = s_free["A"]
        s2g[2] = s2g[3] = s_free["B"]
        xf = [mk("dve"), mk("dve")]
        c1_xa_fence[0] = mk("gdv2")
        xbf = [[mk("gy0"), mk("gx0")], [mk("gy1"), mk("gx1")]]
        x2_marks = []
        st8 = a_begin(next_item) if next_item is not None else None
        for tt in range(NT):
            buf = tt % 2
            emit("scalar", [xf[buf], xa_free[buf], c1_xa_fence[0]], [(f"sxa{buf}", 16)],
                 lambda e, tt=tt, buf=buf, base=base: e.dma_start(
                     out=xa_sb[:, buf * 1024: buf * 1024 + 1024],
                     in_=x8[base + tt * 128: base + tt * 128 + 128, :]))
            sy_m = mk(f"sxa{buf}")
            for dh in range(2):
                b = (tt % 2) * 2 + dh
                for ch in range(NCH):
                    w2 = []
                    if ch == 0:
                        w2 = [s2g[b]] + div_gp
                    emit("tensor", w2, [("pe", 1)] if ch == NCH - 1 else [],
                         lambda e, ch=ch, tt=tt, dh=dh, b=b: e.matmul(
                             s2_ps[:, b * 512: b * 512 + 512],
                             ctxT[:, ch * S + tt * 128: ch * S + tt * 128 + 128],
                             wo_sb[:, ch * D + dh * 512: ch * D + dh * 512 + 512],
                             start=(ch == 0), stop=(ch == NCH - 1)))
                emit("vector", [mk("pe"), sy_m] + xbf[buf], [("dve", 1)],
                     lambda e, buf=buf, dh=dh, b=b: e.tensor_tensor(
                         xb_sb[:, buf * 1024 + dh * 512: buf * 1024 + dh * 512 + 512],
                         s2_ps[:, b * 512: b * 512 + 512],
                         xa_sb[:, buf * 1024 + dh * 512: buf * 1024 + dh * 512 + 512],
                         op=OP.add))
                s2g[b] = mk("dve")
            xf[buf] = mk("dve")
            xa_free[buf] = mk("dve")
            sc = 64 + 2 * tt
            emit("vector", [mk("dve")], [],
                 lambda e, buf=buf: e.bn_stats(
                     st_sb[:, 36:42], xb_sb[:, buf * 1024: buf * 1024 + 512]))
            emit("vector", [], [("dve", 1)],
                 lambda e, buf=buf: e.bn_stats(
                     st_sb[:, 42:48], xb_sb[:, buf * 1024 + 512: buf * 1024 + 1024]))
            emit("vector", [mk("dve")], [("dve", 1)],
                 lambda e, sc=sc: e.bn_aggr(st_sb[:, sc:sc + 2], st_sb[:, 36:48]))
            emit("gpsimd", [mk("dve")], [(f"gx{buf}", 16)],
                 lambda e, tt=tt, buf=buf: e.dma_start(
                     out=x2_d[tt * 128: tt * 128 + 128, :],
                     in_=xb_sb[:, buf * 1024: buf * 1024 + 1024]))
            xbf[buf] = [mk(f"gx{buf}")]
            x2_marks.append(mk(f"gx{buf}"))
            if st8 is not None:
                a_tile(st8, tt)
        if st8 is not None:
            a_end(st8)
        emit("scalar", [mk("dve")], [("act", 1)],
             lambda e: e.activation(st_sb[:, 96:112], st_sb[:, 65:97:2],
                                    AF.Ln, bias=eps_sb[:, 0:1], scale=1.0))
        emit("scalar", [mk("act")], [("act", 1)],
             lambda e: e.activation(st_sb[:, 112:128], st_sb[:, 96:112],
                                    AF.Exp, scale=-0.5))
        rstd2 = mk("act")
        emit("vector", [rstd2], [("dve", 1)],
             lambda e: e.scalar_tensor_tensor(
                 st_sb[:, 48:64], st_sb[:, 64:96:2], -1.0, st_sb[:, 112:128],
                 op0=OP.mult, op1=OP.mult))
        div_tt_gate[0] = mk("dve")
        return mk("dve"), x2_marks, st8, xf

    def phase_D(item, nm2_done, x2_marks, a_next, c1_xf, has_next):
        base = item * S
        load_table(id_exp)
        def sl(s):
            return (xb_sb, (s % 2) * 1024) if s < 2 else (xa_sb, (s % 2) * 1024)

        fr = [[nm2_done, x2_marks[NT - 2]],
              [nm2_done, x2_marks[NT - 1]],
              [c1_xf[0], xa_free[0], div_xa_free[0]],
              [c1_xf[1], xa_free[1], div_xa_free[0]]]
        sem_of = ["sxb0", "sxb1", "sxa0", "sxa1"]
        ld_m = [None] * NT

        def emit_ld(tt):
            s = tt % 4
            t, off = sl(s)
            emit("sync", fr[s] + [x2_marks[tt]], [(sem_of[s], 16)],
                 lambda e, tt=tt, t=t, off=off: e.dma_start(
                     out=t[:, off: off + 1024],
                     in_=x2_d[tt * 128: tt * 128 + 128, :]))
            ld_m[tt] = mk(sem_of[s])

        if a_next is not None:
            lnfree = [[a_next["tp"][NT - 2]], [a_next["tp"][NT - 1]]]
        else:
            lnfree = [[ln_stage_free[0]], [ln_stage_free[1]]]
        ltfree = [("pe", 0), ("pe", 0)]
        misc_gate[0] = mk("dve")
        qkv_gate[0] = mk("dve")
        tanh_done = [("act", 0)]
        tanh_war = [mk("gdiv")]
        yfree = [("gy0", 0), ("gy1", 0), ("gy0", 0), ("gy1", 0)]
        tp_marks = [None] * NT

        def emit_norm(tt):
            s, buf = tt % 4, tt % 2
            t_s, off_s = sl(s)
            emit("vector", [ld_m[tt], nm2_done] + lnfree[buf], [("dve", 1)],
                 lambda e, tt=tt, buf=buf, t_s=t_s, off_s=off_s: e.tensor_scalar(
                     ln_sb[:, buf * 1024: buf * 1024 + 1024],
                     t_s[:, off_s: off_s + 1024],
                     st_sb[:, 112 + tt:113 + tt], st_sb[:, 48 + tt:49 + tt],
                     op0=OP.mult, op1=OP.add))
            dve_m = mk("dve")
            src_ = ap3(ln_sb, buf * 1024, [[2048, 128], [128, 8], [1, 128]])
            dst_ = ap3(lt_sb, buf * 1024, [[2048, 128], [128, 8], [1, 128]])
            wt = [dve_m, ltfree[buf]]
            if tt == 0:
                wt.extend(d_tp_fence[0])
            emit("sync", wt, [(f"tpd{buf}", 16)],
                 lambda e, dst_=dst_, src_=src_: e.dma_start(out=dst_, in_=src_,
                                                             transpose=True))
            tp_marks[tt] = mk(f"tpd{buf}")
            lnfree[buf] = [tp_marks[tt]]

        emit_ld(0)
        emit_ld(1)
        emit_norm(0)
        for tt in range(NT):
            s = tt % 4
            buf = tt % 2
            t_s, off_s = sl(s)
            if tt + 2 < NT:
                emit_ld(tt + 2)
            if tt + 1 < NT:
                emit_norm(tt + 1)
            tp_m = tp_marks[tt]
            if tt == 2 and has_next:
                emit_qk_stream(0)
                qk0_early[0] = True
            for dh in range(2):
                b = (tt % 2) * 2 + dh
                for chm in range(NCH):
                    w3 = []
                    if chm == 0:
                        w3 = [tp_m, s2g[b]]
                    emit("tensor", w3, [("pe", 1)] if chm == NCH - 1 else [],
                         lambda e, chm=chm, buf=buf, dh=dh, b=b: e.matmul(
                             s2_ps[:, b * 512: b * 512 + 512],
                             lt_sb[:, buf * 1024 + chm * 128: buf * 1024 + chm * 128 + 128],
                             m_sb[:, chm * D + dh * 512: chm * D + dh * 512 + 512],
                             start=(chm == 0), stop=(chm == NCH - 1)))
                pe_m = mk("pe")
                emit("vector", [pe_m, tanh_done[0]], [("dve", 1)],
                     lambda e, dh=dh, b=b: e.tensor_tensor(
                         ctxu_sb[:, dh * 512: dh * 512 + 512],
                         s2_ps[:, b * 512: b * 512 + 512],
                         br_sb[:, dh * 512: dh * 512 + 512], op=OP.add))
                s2g[b] = mk("dve")
                emit("scalar", [mk("dve"), tanh_war[0]], [("act", 1)],
                     lambda e, dh=dh: e.activation(
                         aux_sb[:, dh * 512: dh * 512 + 512],
                         ctxu_sb[:, dh * 512: dh * 512 + 512],
                         AF.Tanh, scale=1.0))
                tanh_done[0] = mk("act")
                emit("vector", [mk("act"), yfree[s]], [("dve", 1)],
                     lambda e, t_s=t_s, off_s=off_s, dh=dh: e.tensor_tensor(
                         t_s[:, off_s + dh * 512: off_s + dh * 512 + 512],
                         t_s[:, off_s + dh * 512: off_s + dh * 512 + 512],
                         aux_sb[:, dh * 512: dh * 512 + 512], op=OP.add))
                tanh_war[0] = mk("dve")
            emit("gpsimd", [mk("dve")], [(f"gy{buf}", 16)],
                 lambda e, tt=tt, t_s=t_s, off_s=off_s, base=base: e.dma_start(
                     out=y8[base + tt * 128: base + tt * 128 + 128, :],
                     in_=t_s[:, off_s: off_s + 1024]))
            yfree[s] = mk(f"gy{buf}")
            fr[s] = [mk(f"gy{buf}")]
            ltfree[buf] = ("pe", cnt["pe"])
        load_table(id_nle)
        qkv_gate[0] = mk("dve")
        misc_gate[0] = mk("dve")
        s_free["A"] = max(s2g[0], s2g[1], key=lambda t: t[1])
        s_free["B"] = max(s2g[2], s2g[3], key=lambda t: t[1])
        xa_free[0] = fr[2][0]
        xa_free[1] = fr[3][0]
        div_xa_free[0] = fr[2][0]
        ln_stage_free[0] = mk("tpd0")
        ln_stage_free[1] = mk("tpd1")
        return

    d_tp_fence = [[]]
    qk0_early = [False]

    a_next_state = None
    for item in range(min(B_PER, DBG_ITEMS)):
        if DBG_STAGE == "prologue":
            break
        if a_next_state is None:
            a_next_state = phase_A(item)
        if DBG_DUMP and item == DBG_DUMP_ITEM:
            emit("gpsimd", [lnT_tp[NT - 2], lnT_tp[NT - 1]], [("gdbg", 16)],
                 lambda e: e.dma_start(out=dbg["d_ln1t"], in_=ln1T[:, :]))
        if DBG_STAGE == "A":
            a_next_state = None
            continue
        bgq_flush()
        if item == 0:
            wdup_done_c[0] = [mk("gw0"), mk("gw1")]
            pt_reader[0] = mk("gw1")
            pt_reader[1] = mk("gw1")
        if not qk0_early[0]:
            emit_qk_stream(0)
        qk0_early[0] = False
        for proj in range(2):
            for qb in range(4):
                emit_qkv_group(0, proj, qb)
        for tt in range(NT):
            emit_v_tile(0, tt)
        if DBG_DUMP and item == DBG_DUMP_ITEM:
            emit("gpsimd", [mk("dve")], [("gdbg", 16)],
                 lambda e: e.dma_start(out=dbg["d_qt"], in_=qt_sb[:, :]))
            emit("gpsimd", [], [("gdbg", 16)],
                 lambda e: e.dma_start(out=dbg["d_kt"], in_=kt_sb[:, :]))
            emit("gpsimd", [], [("gdbg", 16)],
                 lambda e: e.dma_start(out=dbg["d_vh"], in_=vh_sb[:, 0:8320]))
        queue_head_work(1, with_stream=True)
        pp_tp_fence[0] = [mk("tpa0"), mk("tpa1"), mk("tpd0"), mk("tpd1"),
                          mk("tpw")]
        phase_B(item)
        if DBG_DUMP and item == DBG_DUMP_ITEM:
            emit("gpsimd", [mk("dve"), mk("gc0"), mk("gc1")], [("gdbg", 16)],
                 lambda e: e.dma_start(out=dbg["d_ctxt"], in_=ctxT[:, :]))
        if DBG_STAGE == "B":
            a_next_state = None
            continue
        nxt = item + 1 if item + 1 < min(B_PER, DBG_ITEMS) else None
        nm2_done, x2_marks, a_next_state, c1_xf = phase_C1(item, nxt)
        if DBG_STAGE == "C1":
            continue
        d_tp_fence[0] = [mk("gpp0"), mk("gpp1"), mk("gc0"), mk("gc1"),
                         mk("gdiv")]
        phase_D(item, nm2_done, x2_marks, a_next_state, c1_xf,
                has_next=nxt is not None)

    emit("gpsimd", [], [], lambda e: e.wait_ge(sems["gy0"], cnt["gy0"]))
    emit("gpsimd", [], [], lambda e: e.wait_ge(sems["gy1"], cnt["gy1"]))

    if DBG_PREFIX is not None:
        del OPS[DBG_PREFIX:]
    print(f"[kernel] total ops: {len(OPS)}")

    with nc.Block() as block:
        for eng_name in ["sync", "gpsimd", "tensor", "vector", "scalar"]:
            def make(eng_name):
                def run(eng):
                    for e_, fn in OPS:
                        if e_ == eng_name:
                            fn(eng)
                return run
            getattr(block, eng_name)(make(eng_name))

    return nc


_PROG = None


def kernel(L0_attn_hidden_states, L0_attn_Wq, L0_attn_Wk, L0_attn_Wv, L0_attn_Wo,
           L0_attn_rel_emb, L0_ffn_weight, L0_ffn_aux_weight1, L0_ffn_aux_weight2,
           L0_ffn_bias):
    global _PROG
    x = np.ascontiguousarray(np.asarray(L0_attn_hidden_states, dtype=np.float32))
    onehot, eye16 = _host_consts()
    shared = dict(
        wq=np.ascontiguousarray(np.asarray(L0_attn_Wq, np.float32)),
        wk=np.ascontiguousarray(np.asarray(L0_attn_Wk, np.float32)),
        wv=np.ascontiguousarray(np.asarray(L0_attn_Wv, np.float32)),
        wo=np.ascontiguousarray(np.asarray(L0_attn_Wo, np.float32)),
        wf=np.ascontiguousarray(np.asarray(L0_ffn_weight, np.float32)),
        wa1=np.ascontiguousarray(np.asarray(L0_ffn_aux_weight1, np.float32)),
        wa2=np.ascontiguousarray(np.asarray(L0_ffn_aux_weight2, np.float32)),
        fbias=np.ascontiguousarray(np.asarray(L0_ffn_bias, np.float32).reshape(1, D)),
        rel=np.ascontiguousarray(np.asarray(L0_attn_rel_emb, np.float32)),
        onehot=onehot, eye16=eye16,
    )
    if _PROG is None:
        _PROG = build_program()
    in_maps = []
    for c in range(N_CORES):
        m = dict(shared)
        m["x"] = np.ascontiguousarray(
            x[c * B_PER:(c + 1) * B_PER].reshape(B_PER * S, D))
        in_maps.append(m)
    res = run_bass_kernel_spmd(_PROG, in_maps, core_ids=list(range(N_CORES)))
    out = np.concatenate(
        [np.asarray(r["y"]).reshape(B_PER, S, D) for r in res.results], axis=0)
    return out.astype(np.float32)
